# version 06
# baseline (speedup 1.0000x reference)
import contextlib
import math
import numpy as np
import concourse.bass as bass
import concourse.mybir as mybir
from concourse.bass_utils import run_bass_kernel_spmd
from concourse.bass_core_rust import IndirectOffsetOnAxis

F32 = mybir.dt.float32
BF16 = mybir.dt.bfloat16
I32 = mybir.dt.int32
I16 = mybir.dt.int16
U32 = mybir.dt.uint32
U8 = mybir.dt.uint8
F16 = mybir.dt.float16
AF = mybir.ActivationFunctionType
ALU = mybir.AluOpType
AX = mybir.AxisListType

DBG = []
SKIP = set()
NCORES = 8
T = 2048
D = 1024
ND = D // 128

ENGINES = ("T", "V", "S", "G", "Q")
BODY = {"T": "tensor", "V": "vector", "S": "scalar", "G": "gpsimd", "Q": "sync"}
SELF_SYNC = {"T": True, "V": True, "S": True, "G": True, "Q": True}


DMA_SEM_K = 8


class Em:

    def __init__(self, nc):
        self.nc = nc
        self.ops = {e: [] for e in ENGINES}
        self.cnt = {}
        for e in ENGINES:
            self.cnt[e] = 0
        self.dma_i = {e: 0 for e in ENGINES}
        self.res = {}
        self.done = {e: {} for e in ENGINES}
        self.floor = {}

    @staticmethod
    def key(r):
        return r if isinstance(r, str) else r.name

    def emit(self, eng, fn, reads=(), writes=(), dma=False, tag=""):
        need = dict(self.floor)

        def want(f, c):
            if f == eng and not SELF_SYNC[eng]:
                return
            if need.get(f, 0) < c:
                need[f] = c
        for r in reads:
            st = self.res.get(self.key(r))
            if st and st["w"]:
                want(*st["w"])
        for w in writes:
            st = self.res.get(self.key(w))
            if st:
                if st["w"]:
                    want(*st["w"])
                for rd in st["r"]:
                    want(*rd)
        if dma:
            s = f"{eng}d{self.dma_i[eng] % DMA_SEM_K}"
            self.dma_i[eng] += 1
            prev = self.cnt.get(s, 0)
            if prev:
                want(s, prev)
            self.cnt[s] = prev + 16
            mark = (s, prev + 16)
        else:
            self.cnt[eng] += 1
            mark = (eng, self.cnt[eng])
        waits = []
        for f, v in sorted(need.items()):
            if v > 0 and self.done[eng].get(f, 0) < v:
                waits.append((f, v))
                self.done[eng][f] = v
        self.ops[eng].append((waits, fn, mark, tag))
        for r in reads:
            k = self.key(r)
            self.res.setdefault(k, {"w": None, "r": []})["r"].append(mark)
        for w in writes:
            self.res[self.key(w)] = {"w": mark, "r": []}
        return mark

    def barrier(self):
        self.floor = dict(self.cnt)



def cdiv(a, b):
    return (a + b - 1) // b


class Prog:
    ARENA_BYTES = 206 * 1024

    def __init__(self):
        self.nc = bass.Bass("TRN2", target_bir_lowering=False)
        self.em = Em(self.nc)
        self.ctx = contextlib.ExitStack()
        self.uid = 0
        self.freelist = [(0, self.ARENA_BYTES)]
        self.dirty = []
        self.vlist = []
        self.live = {}
        self.peak = 0
        self.psums = []
        self.consts_done = False

    def u(self, pfx):
        self.uid += 1
        return f"{pfx}.{self.uid}"

    def buf(self, name, shape, dtype):
        per = int(np.prod(shape[1:])) * mybir.dt.size(dtype)
        per = (per + 63) & ~63
        fits = [(ln, i) for i, (off, ln) in enumerate(self.freelist) if ln >= per]
        fits.sort()
        for _, i in fits[:1]:
            off, ln = self.freelist[i]
            if True:
                if any(off < do + dl and do < off + per for do, dl in self.dirty):
                    self.phase()
                if ln == per:
                    self.freelist.pop(i)
                else:
                    self.freelist[i] = (off + per, ln - per)
                h = self.nc.virtual_sbuf_tensor(name, list(shape), dtype)
                self.vlist.append((h, off))
                self.live[h.name] = (off, per)
                used = self.ARENA_BYTES - sum(l for _, l in self.freelist)
                self.peak = max(self.peak, used)
                return h
        live = sorted(self.live.items(), key=lambda kv: -kv[1][1])
        raise MemoryError(f"arena OOM for {name}: {per}B, free={self.freelist}, "
                          f"live={[(k, v[1]) for k, v in live[:40]]}")

    def free(self, *handles):
        for h in handles:
            name = h.name if hasattr(h, "name") else h
            off, per = self.live.pop(name)
            self.freelist.append((off, per))
            self.dirty.append((off, per))
        self.freelist.sort()
        out = []
        for off, ln in self.freelist:
            if out and out[-1][0] + out[-1][1] == off:
                out[-1] = (out[-1][0], out[-1][1] + ln)
            else:
                out.append((off, ln))
        self.freelist = [tuple(x) for x in out]

    def phase(self):
        self.em.barrier()
        self.dirty = []

    def psum(self, i, dtype=F32):
        while len(self.psums) <= i:
            j = len(self.psums)
            if j == 7:
                self.psums.append(self.ctx.enter_context(
                    self.nc.psum_tensor("ps7bf", [128, 1024], BF16)))
            else:
                self.psums.append(self.ctx.enter_context(
                    self.nc.psum_tensor(f"ps{j}", [128, 512], F32)))
        return self.psums[i]

    def tap(self, name, h, rows=128):
        sh = list(h.shape)
        o = self.dout("dbg_" + name, sh, F32)
        if h.dtype != F32:
            if not hasattr(self, "_tapscratch"):
                self._tapscratch = self.buf("tapscratch", [128, T], F32)
            tmp = self._tapscratch
            self.em.emit("V", lambda e, h=h, t=tmp: e.tensor_copy(t[0:sh[0], 0:sh[1]], h[:]),
                         reads=[h[:]], writes=[tmp[:]])
            src_ = tmp
        else:
            src_ = h
        self.em.emit("Q", lambda e, s=src_, o=o, r=sh[0], cc=sh[1]: e.dma_start(
            out=o, in_=s[0:r, 0:cc]),
            reads=[src_[:]], writes=["dbgout." + name], dma=True)

    def din(self, name, shape, dtype=F32):
        return self.nc.dram_tensor(name, list(shape), dtype, kind="ExternalInput").ap()

    def dout(self, name, shape, dtype=F32):
        return self.nc.dram_tensor(name, list(shape), dtype, kind="ExternalOutput").ap()

    def cval(self, value, dtype=F32):
        key = (float(value), dtype)
        if not hasattr(self, "_cvals"):
            self._cvals = {}
        if key not in self._cvals:
            t = self.buf(self.u("c_v"), [128, 1], dtype)
            self.em.emit("G", lambda e, t=t, v=float(value): e.memset(t[:], v),
                         writes=[t[:]])
            self._cvals[key] = t
        return self._cvals[key]

    def consts(self):
        if self.consts_done:
            return
        self.consts_done = True
        em = self.em
        self.ones_col = self.buf("c_ones_col", [128, 1], F32)
        self.ones_row = self.buf("c_ones_row", [1, 128], F32)
        self.ones_colbf = self.buf("c_ones_colbf", [128, 1], BF16)
        em.emit("G", lambda e: e.memset(self.ones_col[:], 1.0), writes=[self.ones_col[:]])
        em.emit("G", lambda e: e.memset(self.ones_row[:], 1.0), writes=[self.ones_row[:]])
        em.emit("G", lambda e: e.memset(self.ones_colbf[:], 1.0), writes=[self.ones_colbf[:]])
        self.ident = self.buf("c_ident", [128, 128], F32)
        em.emit("G", lambda e: e.memset(self.ident[:], 1.0), writes=[self.ident[:]])
        em.emit("G", lambda e: e.affine_select(
            self.ident[:], self.ident[:], pattern=[[1, 128]], compare_op=ALU.is_equal,
            fill=0.0, base=0, channel_multiplier=-1),
            reads=[self.ident[:]], writes=[self.ident[:]])
        self.ident_bf = self.buf("c_identbf", [128, 128], BF16)
        em.emit("V", lambda e: e.tensor_copy(self.ident_bf[:], self.ident[:]),
                reads=[self.ident[:]], writes=[self.ident_bf[:]])
        self.lane = self.buf("c_lane", [128, 1], F32)
        self.lane1m = self.buf("c_lane1m", [128, 1], F32)
        em.emit("G", lambda e: e.memset(self.lane[:], 1.0), writes=[self.lane[:]])
        em.emit("V", lambda e: e.tensor_single_scalar(self.lane[:], self.lane[:], 1e-12, ALU.add),
                reads=[self.lane[:]], writes=[self.lane[:]])
        em.emit("V", lambda e: e.tensor_single_scalar(self.lane[:], self.lane[:], 1.0, ALU.is_gt),
                reads=[self.lane[:]], writes=[self.lane[:]])
        em.emit("V", lambda e: e.tensor_scalar(self.lane1m[:], self.lane[:], -1.0, 1.0,
                                               ALU.mult, ALU.add),
                reads=[self.lane[:]], writes=[self.lane1m[:]])

    def round_ip(self, xh, sh, cols, tag="rnd"):
        em = self.em

        def b1(e, x=xh[:, 0:cols], s=sh[:, 0:cols], ln=self.lane):
            return e.tensor_scalar(s, x, ln[:], M64, ALU.mult, ALU.add)
        em.emit("V", b1, reads=[xh[:], self.lane[:]], writes=[sh[:]], tag=tag + ".b1")
        em.emit("V", lambda e, s=sh[:, 0:cols]: e.tensor_single_scalar(
            s, s, M64, ALU.subtract), reads=[sh[:]], writes=[sh[:]], tag=tag + ".b2")

        def a1(e, x=xh[:, 0:cols], ln=self.lane1m):
            return e.tensor_scalar(x, x, ln[:], MAGIC, ALU.mult, ALU.add)
        em.emit("V", a1, reads=[xh[:], self.lane1m[:]], writes=[xh[:]], tag=tag + ".a1")
        em.emit("V", lambda e, x=xh[:, 0:cols]: e.tensor_single_scalar(
            x, x, MAGIC, ALU.subtract), reads=[xh[:]], writes=[xh[:]], tag=tag + ".a2")
        em.emit("V", lambda e, x=xh[:, 0:cols], s=sh[:, 0:cols]: e.tensor_tensor(
            x, x, s, ALU.add), reads=[xh[:], sh[:]], writes=[xh[:]], tag=tag + ".c")

    def row_from_parts(self, src, row, T_, tag="rfp"):
        em = self.em
        nch = T_ // 128
        scol = self.buf(self.u(tag + ".scol"), [128, nch], F32)
        pt = self.psum(3)
        for c in range(nch):
            em.emit("T", lambda e, c=c: e.transpose(
                pt[:, 0:128], src[:, c * 128:(c + 1) * 128], self.ident[:]),
                reads=[src[:], self.ident[:]], writes=[pt[:]], tag=tag + ".t")
            em.emit("V", lambda e, c=c: e.reduce_max(
                scol[:, c:c + 1], pt[:, 0:128], axis=AX.X),
                reads=[pt[:]], writes=[scol[:]], tag=tag + ".r")
        s16 = self.buf(self.u(tag + ".s16"), [max(nch, 1), 128], F32)
        em.emit("T", lambda e: e.transpose(pt[0:nch, 0:128], scol[:], self.ident[:]),
                reads=[scol[:], self.ident[:]], writes=[pt[:]], tag=tag + ".t2")
        em.emit("V", lambda e: e.tensor_copy(s16[:], pt[0:nch, 0:128]),
                reads=[pt[:]], writes=[s16[:]], tag=tag + ".c2")
        em.emit("G", lambda e: e.dma_start(out=row[0:1, 0:T_], in_=s16[:]),
                reads=[s16[:]], writes=[row[:]], dma=True, tag=tag + ".d")

    def materialize(self):
        ah = self.ctx.enter_context(self.nc.sbuf_tensor(
            "arena", [128, self.ARENA_BYTES], U8))
        base = self.nc.lookup_mloc(ah).addr
        for h, off in self.vlist:
            self.nc._tensor(h.name, h.shape, h.dtype, type="SB")
            mloc = self.nc.lookup_mloc(h)
            mloc.allocated = not self.nc.target_bir_lowering
            mloc.addr = base + off

    def finish(self):
        em = self.em
        waits = [(k, v) for k, v in sorted(em.cnt.items())
                 if "d" in k and len(k) > 1 and v > 0]
        em.ops["Q"].append((waits, lambda e: None, ("Q", em.cnt["Q"] + 1), "final"))
        em.cnt["Q"] += 1
        self.materialize()
        nc = self.nc
        sems = {k: self.ctx.enter_context(nc.semaphore("sem_" + k))
                for k, v in em.cnt.items() if v > 0}
        with nc.Block() as block:
            for eng in ENGINES:
                ops = self.em.ops[eng]

                def body(e, ops=ops, eng=eng):
                    for waits, fn, mark, tag in ops:
                        for f, v in waits:
                            e.wait_ge(sems[f], v)
                        inst = fn(e)
                        if inst is None:
                            inst = e.nop()
                        s, v = mark
                        inst.then_inc(sems[s], 16 if s != eng else 1)
                getattr(block, BODY[eng])(body)
        self.ctx.close()
        return self.nc

    def ops_for(self, eng):
        return self.em.ops[eng]

    def matmul_to(self, wdram, nK, nO, xt, T_, evac, wdtype=BF16, tag="mm",
                  ps_idx=(0, 1), xtag=None, ow=128, xsel=False):
        em = self.em
        wb = [self.buf(self.u(tag + ".w"), [128, nK * ow], wdtype) for _ in range(2)]
        nch = cdiv(T_, 512)
        xreads = [x[:] for x in xt] if xtag is None else xtag
        for ot in range(nO):
            b = wb[ot % 2]
            em.emit("G", lambda e, ot=ot, b=b: e.dma_start(out=b[:], in_=wdram[ot]),
                    writes=[b[:]], dma=True, tag=tag + ".wld")
            for c in range(nch):
                c0, c1 = c * 512, min((c + 1) * 512, T_)
                ps_ = self.psum(ps_idx[(ot * nch + c) % 2])

                def mm(e, b=b, c0=c0, c1=c1, ps_=ps_, tiles=tuple(xt)):
                    last = None
                    for k in range(nK):
                        mv = (tiles[k][:, k:k + 1] if xsel else tiles[k][:, c0:c1])
                        last = e.matmul(ps_[0:ow, 0:c1 - c0], b[:, k * ow:(k + 1) * ow],
                                        mv, start=(k == 0), stop=(k == nK - 1))
                    return last
                em.emit("T", mm, reads=[b[:]] + list(xreads), writes=[ps_[:]], tag=tag + ".mm")
                evac(ot, c0, c1, ps_)
        self.free(*wb)

    def bcast_col(self, sval, dst, tag="bcc"):
        em = self.em
        p_ = self.psum(4)
        em.emit("T", lambda e: e.matmul(p_[:, 0:1], self.ones_row[:], sval[0:1, 0:1],
                                        start=True, stop=True),
                reads=[sval[:], self.ones_row[:]], writes=[p_[:]], tag=tag)
        em.emit("S", lambda e: e.copy(dst[:], p_[:, 0:1]),
                reads=[p_[:]], writes=[dst[:]], tag=tag + "c")

    def bcast_row(self, row, dst, T_, ps_i=5, dt_src=None, tag="bc"):
        em = self.em
        p_ = self.psum(ps_i)
        for c in range(cdiv(T_, 512)):
            c0, c1 = c * 512, min((c + 1) * 512, T_)
            em.emit("T", lambda e, c0=c0, c1=c1: e.matmul(
                p_[:, 0:c1 - c0], self.ones_row[:], row[:, c0:c1], start=True, stop=True),
                reads=[row[:], self.ones_row[:]], writes=[p_[:]], tag=tag)
            em.emit("S", lambda e, c0=c0, c1=c1: e.copy(dst[:, c0:c1], p_[:, 0:c1 - c0]),
                    reads=[p_[:]], writes=[dst[:]], tag=tag + "c")

    def layernorm(self, xt, T_, xn, eps=1e-5, tag="ln", scr=None):
        em = self.em
        n = len(xt)
        nd = float(n * 128)
        ones_in = self.ones_col if xt[0].dtype == F32 else self.ones_colbf
        ssum = self.buf(self.u(tag + ".sum"), [1, T_], F32)
        ssq = self.buf(self.u(tag + ".ssq"), [1, T_], F32)
        sq = [self.buf(self.u(tag + ".sq"), [128, 512], F32) for _ in range(2)]
        p1, p2 = self.psum(5), self.psum(6)
        nch = cdiv(T_, 512)
        for c in range(nch):
            c0 = c * 512

            def mm_sum(e, c0=c0, tiles=tuple(xt), p1=p1, oc=ones_in):
                last = None
                for k in range(n):
                    last = e.matmul(p1[0:1, :], oc[:], tiles[k][:, c0:c0 + 512],
                                    start=(k == 0), stop=(k == n - 1))
                return last
            em.emit("T", mm_sum, reads=[x[:] for x in xt] + [ones_in[:]],
                    writes=[p1[:]], tag=tag + ".sm")
            em.emit("V", lambda e, c0=c0: e.tensor_copy(ssum[:, c0:c0 + 512], p1[0:1, :]),
                    reads=[p1[:]], writes=[ssum[:]], tag=tag + ".cp")
            for k in range(n):
                s = sq[k % 2]
                em.emit("S", lambda e, t=xt[k], c0=c0, s=s: e.activation(
                    s[:], t[:, c0:c0 + 512], AF.Square),
                    reads=[xt[k][:]], writes=[s[:]], tag=tag + ".sq")
                em.emit("T", lambda e, k=k, s=s: e.matmul(
                    p2[0:1, :], self.ones_col[:], s[:], start=(k == 0), stop=(k == n - 1)),
                    reads=[s[:], self.ones_col[:]], writes=[p2[:]], tag=tag + ".sqm")
            em.emit("V", lambda e, c0=c0: e.tensor_copy(ssq[:, c0:c0 + 512], p2[0:1, :]),
                    reads=[p2[:]], writes=[ssq[:]], tag=tag + ".cp2")
        mrs = self.buf(self.u(tag + ".mrs"), [1, T_], F32)
        mean, rstd = ssum, ssq
        em.emit("V", lambda e: e.tensor_scalar_mul(mean[:], ssum[:], 1.0 / nd),
                reads=[ssum[:]], writes=[mean[:]])
        em.emit("V", lambda e: e.tensor_tensor(mrs[:], mean[:], mean[:], ALU.mult),
                reads=[mean[:]], writes=[mrs[:]])
        em.emit("V", lambda e: e.scalar_tensor_tensor(
            rstd[:], ssq[:], 1.0 / nd, mrs[:], ALU.mult, ALU.subtract),
            reads=[ssq[:], mrs[:]], writes=[rstd[:]])
        ec = self.cval(eps)
        em.emit("S", lambda e: e.activation(rstd[:], rstd[:], AF.Sqrt, bias=ec[0:1, :]),
                reads=[rstd[:], ec[:]], writes=[rstd[:]])
        em.emit("V", lambda e: e.reciprocal(rstd[:], rstd[:]),
                reads=[rstd[:]], writes=[rstd[:]])
        em.emit("V", lambda e: e.tensor_tensor(mrs[:], mean[:], rstd[:], ALU.mult),
                reads=[mean[:], rstd[:]], writes=[mrs[:]])
        rb = self.buf(self.u(tag + ".rb"), [128, T_], F32)
        mb = self.buf(self.u(tag + ".mb"), [128, T_], F32)
        self.bcast_row(rstd, rb, T_, ps_i=5, tag=tag + ".b1")
        self.bcast_row(mrs, mb, T_, ps_i=6, tag=tag + ".b2")
        tmp = self.buf(self.u(tag + ".tmp"), [128, T_], F32)
        for k in range(n):
            em.emit("V", lambda e, t=xt[k]: e.tensor_tensor(tmp[:], t[:], rb[:], ALU.mult),
                    reads=[xt[k][:], rb[:]], writes=[tmp[:]])
            em.emit("V", lambda e, t=xn[k]: e.tensor_tensor(t[:], tmp[:], mb[:], ALU.subtract),
                    reads=[tmp[:], mb[:]], writes=[xn[k][:]])
        self.free(ssum, ssq, sq[0], sq[1], mrs, rb, mb, tmp)



def emit_l0(p, xt, y_t=None):
    nc, em = p.nc, p.em
    DI, NDI = 2048, 16
    w_in = p.din("l0_w_in", [32, 128, ND * 128])
    w_xp = p.din("l0_w_xp", [1, 128, NDI * 96])
    w_out = p.din("l0_w_out", [ND, 128, NDI * 128])
    w_dt = p.din("l0_w_dt", [NDI, 128, 64])
    dt_b = p.din("l0_dt_b", [1, DI])
    conv_w = p.din("l0_conv_w", [128, NDI * 3])
    conv_b = p.din("l0_conv_b", [128, NDI])
    d_par = p.din("l0_dpar", [128, NDI])
    w_r = p.din("l0_w_r", [ND, 128, ND * 128])
    w_k = p.din("l0_w_k", [ND, 128, ND * 128])
    w_v = p.din("l0_w_v", [ND, 128, ND * 128])
    mu_r = p.din("l0_mu_r", [128, ND])
    mu_k = p.din("l0_mu_k", [128, ND])

    sm = {}
    for nm, src, sh in (("convw", conv_w, [128, NDI * 3]), ("convb", conv_b, [128, NDI]),
                        ("dpar", d_par, [128, NDI]), ("mur", mu_r, [128, ND]),
                        ("muk", mu_k, [128, ND]), ("dtb", dt_b, [1, DI])):
        dst = sm[nm] = p.buf("l0s." + nm, sh, F32)
        em.emit("Q", lambda e, d=dst, s=src: e.dma_start(out=d[:], in_=s),
                writes=[dst[:]], dma=True)

    xn = [p.buf(f"l0.xn{k}", [128, T], BF16) for k in range(ND)]
    p.layernorm(xt, T, xn, tag="l0ln")
    if "xn0" in DBG:
        p.tap("xn0", xn[0])
    p.phase()

    xs0 = [p.buf(f"l0.xs{k}", [128, T], BF16) for k in range(NDI)]

    def evac_xs(ot, c0, c1, ps_):
        em.emit("S", lambda e: e.copy(xs0[ot][:, c0:c1], ps_[:, 0:c1 - c0]),
                reads=[ps_[:]], writes=[xs0[ot][:]], tag="l0.xs")
    p.matmul_to(w_in[0:NDI], ND, NDI, xn, T, evac_xs, tag="l0in")

    if "xs0" in DBG:
        p.tap("xs0", xs0[0])
    if "convw" in DBG:
        p.tap("convw", sm["convw"])
        p.tap("convb", sm["convb"])
    spare = p.buf("l0.convspare", [128, T], BF16)
    cs1 = p.buf("l0.cs1", [128, T], F32)
    cs2 = p.buf("l0.cs2", [128, T], F32)
    xc = []
    dst = spare
    for k in range(NDI):
        src = xs0[k]
        em.emit("V", lambda e, k=k, src=src: e.tensor_scalar(
            cs1[:], src[:], sm["convw"][:, 3 * k + 1:3 * k + 2],
            sm["convb"][:, k:k + 1], ALU.mult, ALU.add),
            reads=[src[:], sm["convb"][:], sm["convw"][:]], writes=[cs1[:]], tag="cv1")
        em.emit("V", lambda e, k=k, src=src: e.scalar_tensor_tensor(
            cs2[:, 0:T - 1], src[:, 1:T], sm["convw"][:, 3 * k + 2:3 * k + 3],
            cs1[:, 0:T - 1], ALU.mult, ALU.add),
            reads=[src[:], cs1[:], sm["convw"][:]], writes=[cs2[:]], tag="cv2")
        em.emit("V", lambda e: e.tensor_copy(cs2[:, T - 1:T], cs1[:, T - 1:T]),
                reads=[cs1[:]], writes=[cs2[:]], tag="cv2b")
        em.emit("V", lambda e, k=k, src=src, dst=dst: e.scalar_tensor_tensor(
            dst[:, 1:T], src[:, 0:T - 1], sm["convw"][:, 3 * k:3 * k + 1],
            cs2[:, 1:T], ALU.mult, ALU.add),
            reads=[src[:], cs2[:], sm["convw"][:]], writes=[dst[:]], tag="cv3")
        em.emit("V", lambda e, dst=dst: e.tensor_copy(dst[:, 0:1], cs2[:, 0:1]),
                reads=[cs2[:]], writes=[dst[:]], tag="cv3b")
        if k == 0 and "cs1" in DBG:
            p.tap("cs1", cs1)
            p.tap("cs2", cs2)
        xc.append(dst)
        dst = src
    p.free(dst, cs1, cs2)

    if "xc0" in DBG:
        p.tap("xc0", xc[0])
    xdbl = p.buf("l0.xdbl", [96, T], F32)

    def evac_xdbl(ot, c0, c1, ps_):
        em.emit("S", lambda e: e.copy(xdbl[:, c0:c1], ps_[0:96, 0:c1 - c0]),
                reads=[ps_[:]], writes=[xdbl[:]], tag="l0.xdbl")
    p.matmul_to(w_xp, NDI, 1, xc, T, evac_xdbl, tag="l0xp", ow=96)

    m_colbf = p.buf("l0.mcol", [64, 1], BF16)
    p_m = p.psum(5)
    wdt_b = [p.buf(f"l0.wdtb{i}", [128, 64], BF16) for i in range(2)]
    for k in range(NDI):
        b = wdt_b[k % 2]
        em.emit("G", lambda e, k=k, b=b: e.dma_start(out=b[:], in_=w_dt[k]),
                writes=[b[:]], dma=True)
        em.emit("T", lambda e, k=k, b=b: e.matmul(
            p_m[0:64, 0:1], b[:], p.ones_colbf[:], start=(k == 0), stop=(k == NDI - 1)),
            reads=[b[:], p.ones_colbf[:]], writes=[p_m[:]], tag="l0.m")
    em.emit("V", lambda e: e.tensor_copy(m_colbf[:], p_m[0:64, 0:1]),
            reads=[p_m[:]], writes=[m_colbf[:]])
    sb_sum = p.buf("l0.sbsum", [1, 1], F32)
    dtbh = sm["dtb"]
    em.emit("V", lambda e, d=dtbh: e.reduce_sum(sb_sum[:], d[:], axis=AX.X),
            reads=[dtbh[:]], writes=[sb_sum[:]])
    em.emit("V", lambda e: e.tensor_scalar_mul(sb_sum[:], sb_sum[:], 1.0 / DI),
            reads=[sb_sum[:]], writes=[sb_sum[:]])

    xdbl_bf = p.buf("l0.xdblbf", [64, T], BF16)
    em.emit("V", lambda e: e.tensor_copy(xdbl_bf[:], xdbl[0:64, :]),
            reads=[xdbl[:]], writes=[xdbl_bf[:]])
    g_row = p.buf("l0.grow", [1, T], F32)
    p_g = p.psum(6)
    for c in range(T // 512):
        c0 = c * 512
        em.emit("T", lambda e, c0=c0: e.matmul(
            p_g[0:1, :], m_colbf[:], xdbl_bf[:, c0:c0 + 512], start=True, stop=True),
            reads=[m_colbf[:], xdbl_bf[:]], writes=[p_g[:]], tag="l0.g")
        sc_c = p.cval(1.0 / DI)
        em.emit("S", lambda e, c0=c0: e.activation(
            g_row[:, c0:c0 + 512], p_g[0:1, :], AF.Sigmoid,
            bias=sb_sum[:], scale=sc_c[0:1, :]),
            reads=[p_g[:], sb_sum[:], sc_c[:]], writes=[g_row[:]], tag="l0.gs")
    if "grow" in DBG:
        p.tap("grow", g_row)
    if "xdbl" in DBG:
        p.tap("xdbl", xdbl)
    p.free(xdbl, xdbl_bf, wdt_b[0], wdt_b[1], m_colbf)

    p.free(sm.pop("dtb"))
    gb = p.buf("l0.gb", [128, T], F32)
    p.bcast_row(g_row, gb, T, ps_i=5, tag="l0.gb")
    p.free(g_row, sb_sum)

    utmp = p.buf("l0.utmp", [128, T], F32)
    ysc = p.buf("l0.ysc", [128, T], F32)
    zwb = [p.buf(f"l0.zw{i}", [128, ND * 128], BF16) for i in range(2)]
    y = []
    for k in range(NDI):
        em.emit("V", lambda e, k=k: e.tensor_tensor(utmp[:], gb[:], xc[k][:], ALU.mult),
                reads=[gb[:], xc[k][:]], writes=[utmp[:]], tag="l0.u")
        em.emit("V", lambda e, k=k: e.tensor_tensor(utmp[:], xc[k][:], utmp[:], ALU.subtract),
                reads=[xc[k][:], utmp[:]], writes=[utmp[:]], tag="l0.u2")
        em.emit("V", lambda e: e.tensor_tensor_scan(
            ysc[:], gb[:], utmp[:], 0.0, ALU.mult, ALU.add),
            reads=[gb[:], utmp[:]], writes=[ysc[:]], tag="l0.scan")
        b = zwb[k % 2]
        em.emit("G", lambda e, k=k, b=b: e.dma_start(out=b[:], in_=w_in[NDI + k]),
                writes=[b[:]], dma=True, tag="l0.zwld")
        for c in range(T // 512):
            c0 = c * 512
            ps_ = p.psum((c % 2))

            def mmz(e, b=b, c0=c0, ps_=ps_, tiles=tuple(xn)):
                last = None
                for kk in range(ND):
                    last = e.matmul(ps_[:, :], b[:, kk * 128:(kk + 1) * 128],
                                    tiles[kk][:, c0:c0 + 512], start=(kk == 0),
                                    stop=(kk == ND - 1))
                return last
            em.emit("T", mmz, reads=[b[:]] + [x[:] for x in xn], writes=[ps_[:]],
                    tag="l0.zmm")
            em.emit("S", lambda e, c0=c0, ps_=ps_: e.activation(
                ps_[:, :], ps_[:, :], AF.Silu),
                reads=[ps_[:]], writes=[ps_[:]], tag="l0.zsilu")
            em.emit("V", lambda e, c0=c0, ps_=ps_: e.tensor_tensor(
                ysc[:, c0:c0 + 512], ysc[:, c0:c0 + 512], ps_[:, :], ALU.mult),
                reads=[ps_[:], ysc[:]], writes=[ysc[:]], tag="l0.ymul")
        em.emit("V", lambda e, k=k: e.scalar_tensor_tensor(
            xc[k][:], xc[k][:], sm["dpar"][:, k:k + 1], ysc[:], ALU.mult, ALU.add),
            reads=[xc[k][:], sm["dpar"][:], ysc[:]], writes=[xc[k][:]], tag="l0.yf")
        y.append(xc[k])
    if "y0" in DBG:
        p.tap("y0", y[0])
    p.free(utmp, ysc, zwb[0], zwb[1], gb)
    p.phase()

    def evac_out(ot, c0, c1, ps_):
        em.emit("V", lambda e, t=xt[ot]: e.tensor_tensor(t[:, c0:c1], t[:, c0:c1],
                                                         ps_[:, 0:c1 - c0], ALU.add),
                reads=[ps_[:], xt[ot][:]], writes=[xt[ot][:]], tag="l0.oadd")
    p.matmul_to(w_out, NDI, ND, y, T, evac_out, tag="l0out")
    if "xattn0" in DBG:
        p.tap("xattn0", xt[0])
    p.free(*[c for c in xc])
    p.phase()

    if "l0ffn" in SKIP:
        for v in sm.values():
            p.free(v)
        p.free(*xn)
        p.phase()
        return
    dtmp = p.buf("l0.dtmp", [128, T], F32)

    def mix(dst, mucol):
        for k in range(ND):
            em.emit("V", lambda e, k=k: e.tensor_tensor(
                dtmp[:, 1:T], xn[k][:, 1:T], xn[k][:, 0:T - 1], ALU.subtract),
                reads=[xn[k][:]], writes=[dtmp[:]], tag="l0.mx1")
            em.emit("V", lambda e, k=k, dst=dst: e.scalar_tensor_tensor(
                dst[k][:, 1:T], dtmp[:, 1:T], mucol[:, k:k + 1], xn[k][:, 0:T - 1],
                ALU.mult, ALU.add),
                reads=[dtmp[:], mucol, xn[k][:]], writes=[dst[k][:]], tag="l0.mx2")
            em.emit("V", lambda e, k=k, dst=dst: e.tensor_scalar(
                dst[k][:, 0:1], xn[k][:, 0:1], mucol[:, k:k + 1], None, ALU.mult),
                reads=[xn[k][:], mucol], writes=[dst[k][:]], tag="l0.mx3")

    xr = [p.buf(f"l0.xr{k}", [128, T], BF16) for k in range(ND)]
    mix(xr, sm["mur"][:])
    rs = [p.buf(f"l0.rs{k}", [128, T], BF16) for k in range(ND)]

    def evac_r(ot, c0, c1, ps_):
        em.emit("S", lambda e: e.activation(rs[ot][:, c0:c1], ps_[:, 0:c1 - c0], AF.Sigmoid),
                reads=[ps_[:]], writes=[rs[ot][:]], tag="l0.r")
    if "xr0" in DBG:
        p.tap("xr0", xr[0])
    p.matmul_to(w_r, ND, ND, xr, T, evac_r, tag="l0wr")
    p.free(*xr)
    p.phase()

    xk = [p.buf(f"l0.xk{k}", [128, T], BF16) for k in range(ND)]
    mix(xk, sm["muk"][:])
    p.free(dtmp)
    ka = [p.buf(f"l0.ka{k}", [128, T], BF16) for k in range(ND)]

    def evac_k(ot, c0, c1, ps_):
        em.emit("S", lambda e: e.activation(ka[ot][:, c0:c1], ps_[:, 0:c1 - c0], AF.Relu),
                reads=[ps_[:]], writes=[ka[ot][:]], tag="l0.k")
        em.emit("S", lambda e: e.activation(ka[ot][:, c0:c1], ka[ot][:, c0:c1], AF.Square),
                reads=[ka[ot][:]], writes=[ka[ot][:]], tag="l0.k2")
    p.matmul_to(w_k, ND, ND, xk, T, evac_k, tag="l0wk")
    p.free(*xk)
    p.phase()

    vtmp = p.buf("l0.vtmp", [128, 512], F32)

    def evac_v(ot, c0, c1, ps_):
        em.emit("V", lambda e: e.tensor_tensor(vtmp[:, 0:c1 - c0], ps_[:, 0:c1 - c0],
                                               rs[ot][:, c0:c1], ALU.mult),
                reads=[ps_[:], rs[ot][:]], writes=[vtmp[:]], tag="l0.v")
        em.emit("V", lambda e, t=xt[ot]: e.tensor_tensor(t[:, c0:c1], t[:, c0:c1],
                                                         vtmp[:, 0:c1 - c0], ALU.add),
                reads=[vtmp[:], xt[ot][:]], writes=[xt[ot][:]], tag="l0.vadd")
    if "rs0" in DBG:
        p.tap("rs0", rs[0])
    if "ka0" in DBG:
        p.tap("ka0", ka[0])
    p.matmul_to(w_v, ND, ND, ka, T, evac_v, tag="l0wv")
    p.free(*ka, *rs, vtmp, *xn)
    for v in sm.values():
        p.free(v)
    p.phase()





MAGIC = 12582912.0
M64 = 6755399441055744.0


def emit_l1(p, xt, y_t, reload=False):
    nc, em = p.nc, p.em
    DH, NDH = 4096, 32
    pr = p.din("l1_pr", [128, ND]); pi = p.din("l1_pi", [128, ND])
    ar = p.din("l1_ar", [128, ND]); ai = p.din("l1_ai", [128, ND])
    lr = p.din("l1_lr", [128, ND]); li = p.din("l1_li", [128, ND])
    n1w = p.din("l1_n1w", [128, ND])
    w1 = p.din("l1_w1", [NDH, 128, ND * 128])
    asc = p.din("l1_asc", [1, 1]); abi = p.din("l1_abi", [1, 1])
    n2w = p.din("l1_n2w", [128, NDH])
    w2 = p.din("l1_w2", [ND, 128, NDH * 128])

    sm = {}
    for nm, srcd, sh in (("pr", pr, [128, ND]), ("pi", pi, [128, ND]),
                         ("ar", ar, [128, ND]), ("ai", ai, [128, ND]),
                         ("lr", lr, [128, ND]), ("li", li, [128, ND]),
                         ("n1w", n1w, [128, ND]), ("n2w", n2w, [128, NDH]),
                         ("asc", asc, [1, 1]), ("abi", abi, [1, 1])):
        dst = sm[nm] = p.buf("l1s." + nm, sh, F32)
        em.emit("Q", lambda e, d=dst, s=srcd: e.dma_start(out=d[:], in_=s),
                writes=[dst[:]], dma=True)

    xn = [p.buf(f"l1.xn{k}", [128, T], F32) for k in range(ND)]
    p.layernorm(xt, T, xn, tag="l1ln")
    p.phase()

    skip_attn = "l1attn" in SKIP
    ws1_scr = []
    sw1, swi1 = emit_wscale(p, w1, NDH, ND * 128, 4096 * 1024, tag="l1.ws1",
                            scratch_out=ws1_scr)
    def tile(nm):
        t = p.buf("l1c." + nm, [128, ND], F32)
        return t
    m2, r_, mag, minv, cosv, sinv = (tile(x) for x in
                                     ("m2", "r", "mag", "minv", "cos", "sin"))
    t1, t2 = tile("t1"), tile("t2")
    em.emit("S", lambda e: e.activation(m2[:], sm["pr"][:], AF.Square), reads=[sm["pr"][:]], writes=[m2[:]])
    em.emit("S", lambda e: e.activation(t1[:], sm["pi"][:], AF.Square), reads=[sm["pi"][:]], writes=[t1[:]])
    em.emit("V", lambda e: e.tensor_add(m2[:], m2[:], t1[:]), reads=[m2[:], t1[:]], writes=[m2[:]])
    cm1 = p.cval(-1.0)
    em.emit("S", lambda e: e.activation(r_[:], m2[:], AF.Exp, scale=cm1[:]),
            reads=[m2[:], cm1[:]], writes=[r_[:]])
    em.emit("S", lambda e: e.activation(mag[:], m2[:], AF.Sqrt), reads=[m2[:]], writes=[mag[:]])
    em.emit("V", lambda e: e.reciprocal(minv[:], mag[:]), reads=[mag[:]], writes=[minv[:]])
    em.emit("V", lambda e: e.tensor_mul(cosv[:], sm["pr"][:], minv[:]), reads=[sm["pr"][:], minv[:]], writes=[cosv[:]])
    em.emit("V", lambda e: e.tensor_mul(sinv[:], sm["pi"][:], minv[:]), reads=[sm["pi"][:], minv[:]], writes=[sinv[:]])
    phr, phi = tile("phr"), tile("phi")
    em.emit("V", lambda e: e.tensor_mul(phr[:], r_[:], cosv[:]), reads=[r_[:], cosv[:]], writes=[phr[:]])
    em.emit("V", lambda e: e.tensor_mul(phi[:], r_[:], sinv[:]), reads=[r_[:], sinv[:]], writes=[phi[:]])
    sr, si = tile("sr"), tile("si")
    em.emit("V", lambda e: e.tensor_mul(t1[:], sm["lr"][:], phr[:]), reads=[sm["lr"][:], phr[:]], writes=[t1[:]])
    em.emit("V", lambda e: e.tensor_mul(t2[:], sm["li"][:], phi[:]), reads=[sm["li"][:], phi[:]], writes=[t2[:]])
    em.emit("V", lambda e: e.tensor_sub(sr[:], t1[:], t2[:]), reads=[t1[:], t2[:]], writes=[sr[:]])
    em.emit("V", lambda e: e.tensor_mul(t1[:], sm["lr"][:], phi[:]), reads=[sm["lr"][:], phi[:]], writes=[t1[:]])
    em.emit("V", lambda e: e.tensor_mul(t2[:], sm["li"][:], phr[:]), reads=[sm["li"][:], phr[:]], writes=[t2[:]])
    em.emit("V", lambda e: e.tensor_add(si[:], t1[:], t2[:]), reads=[t1[:], t2[:]], writes=[si[:]])
    NS = 11
    mrs = p.buf("l1c.mrs", [128, ND * NS], F32)
    mis = p.buf("l1c.mis", [128, ND * NS], F32)
    nms = p.buf("l1c.nms", [128, ND * NS], F32)
    em.emit("V", lambda e: e.tensor_copy(mrs[:, 0:ND], phr[:]), reads=[phr[:]], writes=[mrs[:]])
    em.emit("V", lambda e: e.tensor_copy(mis[:, 0:ND], phi[:]), reads=[phi[:]], writes=[mis[:]])
    for s in range(1, NS):
        a, b = (s - 1) * ND, s * ND
        em.emit("V", lambda e, a=a, b=b: e.tensor_mul(t1[:], mrs[:, a:a + ND], mrs[:, a:a + ND]),
                reads=[mrs[:]], writes=[t1[:]])
        em.emit("V", lambda e, a=a, b=b: e.tensor_mul(t2[:], mis[:, a:a + ND], mis[:, a:a + ND]),
                reads=[mis[:]], writes=[t2[:]])
        em.emit("V", lambda e, a=a, b=b: e.tensor_sub(mrs[:, b:b + ND], t1[:], t2[:]),
                reads=[t1[:], t2[:]], writes=[mrs[:]])
        em.emit("V", lambda e, a=a, b=b: e.tensor_mul(t1[:], mrs[:, a:a + ND], mis[:, a:a + ND]),
                reads=[mrs[:], mis[:]], writes=[t1[:]])
        em.emit("V", lambda e, a=a, b=b: e.tensor_scalar_mul(mis[:, b:b + ND], t1[:], 2.0),
                reads=[t1[:]], writes=[mis[:]])
    em.emit("V", lambda e: e.tensor_scalar_mul(nms[:], mis[:], -1.0), reads=[mis[:]], writes=[nms[:]])

    Hr = [p.buf("l1.Hr0", [128, T], F32)]
    Hi = [p.buf("l1.Hi0", [128, T], F32)]
    Hr2 = [p.buf("l1.Hr20", [128, T], F32)]
    Hi2 = [p.buf("l1.Hi20", [128, T], F32)]
    for k in range(ND if not skip_attn else 0):
        g = 0
        E = "V"
        hr, hi, hr2, hi2 = Hr[g], Hi[g], Hr2[g], Hi2[g]
        em.emit(E, lambda e, k=k, hr=hr: e.tensor_scalar(
            hr[:], xn[k][:], sm["ar"][:, k:k + 1], None, ALU.mult),
            reads=[xn[k][:], sm["ar"][:]], writes=[hr[:]], tag="l1.u")
        em.emit(E, lambda e, k=k, hi=hi: e.tensor_scalar(
            hi[:], xn[k][:], sm["ai"][:, k:k + 1], None, ALU.mult),
            reads=[xn[k][:], sm["ai"][:]], writes=[hi[:]], tag="l1.u")
        em.emit(E, lambda e, k=k, hr=hr: e.tensor_tensor(
            hr[:, 0:1], hr[:, 0:1], sr[:, k:k + 1], ALU.add),
            reads=[hr[:], sr[:]], writes=[hr[:]], tag="l1.seed")
        em.emit(E, lambda e, k=k, hi=hi: e.tensor_tensor(
            hi[:, 0:1], hi[:, 0:1], si[:, k:k + 1], ALU.add),
            reads=[hi[:], si[:]], writes=[hi[:]], tag="l1.seed")
        for s in range(NS):
            m = 1 << s
            c = s * ND + k
            em.emit(E, lambda e, hr=hr, hr2=hr2, m=m: e.tensor_copy(
                hr2[:, 0:m], hr[:, 0:m]), reads=[hr[:]], writes=[hr2[:]], tag="l1.dc")
            em.emit(E, lambda e, hr=hr, hr2=hr2, m=m, c=c: e.scalar_tensor_tensor(
                hr2[:, m:T], hr[:, 0:T - m], mrs[:, c:c + 1], hr[:, m:T], ALU.mult, ALU.add),
                reads=[hr[:], mrs[:]], writes=[hr2[:]], tag="l1.d1")
            em.emit(E, lambda e, hi=hi, hr2=hr2, m=m, c=c: e.scalar_tensor_tensor(
                hr2[:, m:T], hi[:, 0:T - m], nms[:, c:c + 1], hr2[:, m:T], ALU.mult, ALU.add),
                reads=[hi[:], nms[:], hr2[:]], writes=[hr2[:]], tag="l1.d2")
            em.emit(E, lambda e, hi=hi, hi2=hi2, m=m: e.tensor_copy(
                hi2[:, 0:m], hi[:, 0:m]), reads=[hi[:]], writes=[hi2[:]], tag="l1.dc")
            em.emit(E, lambda e, hr=hr, hi=hi, hi2=hi2, m=m, c=c: e.scalar_tensor_tensor(
                hi2[:, m:T], hr[:, 0:T - m], mis[:, c:c + 1], hi[:, m:T], ALU.mult, ALU.add),
                reads=[hr[:], hi[:], mis[:]], writes=[hi2[:]], tag="l1.d3")
            em.emit(E, lambda e, hi=hi, hi2=hi2, m=m, c=c: e.scalar_tensor_tensor(
                hi2[:, m:T], hi[:, 0:T - m], mrs[:, c:c + 1], hi2[:, m:T], ALU.mult, ALU.add),
                reads=[hi[:], mrs[:], hi2[:]], writes=[hi2[:]], tag="l1.d4")
            hr, hr2 = hr2, hr
            hi, hi2 = hi2, hi
        em.emit(E, lambda e, t=xt[k], hr=hr: e.tensor_tensor(
            t[:], t[:], hr[:], ALU.add),
            reads=[xt[k][:], hr[:]], writes=[xt[k][:]], tag="l1.addx")
    p.free(Hr[0], Hi[0], Hr2[0], Hi2[0])
    sm_freed = set()
    for nm in ("pr", "pi", "ar", "ai", "lr", "li"):
        p.free(sm[nm])
        sm_freed.add(nm)
    for tl in (m2, r_, mag, minv, cosv, sinv, phr, phi, sr, si, t1, t2, mrs, mis, nms):
        p.free(tl)
    p.phase()

    for k in range(ND):
        em.emit("Q", lambda e, k=k, t=xt[k]: e.dma_start(
            out=y_t[k * 128:(k + 1) * 128, :], in_=t[:]),
            reads=[xt[k][:]], writes=[f"o.y.{k}"], dma=True)
    for k in range(ND):
        p.free(xt[k])
        xt[k] = None
    p.phase()

    if "l1ffn" in SKIP:
        if reload:
            for k in range(ND):
                xt[k] = p.buf(f"x{k}b", [128, T], F32)
                em.emit("Q", lambda e, k=k, t=xt[k]: e.dma_start(
                    out=t[:], in_=y_t[k * 128:(k + 1) * 128, :]),
                    reads=[f"o.y.{k}"], writes=[xt[k][:]], dma=True)
        return
    s2row = p.buf("l1.s2row", [1, T], F32)
    sqs = p.buf("l1.sqs", [128, 512], F32)
    p2 = p.psum(6)
    for c in range(T // 512):
        c0 = c * 512
        for k in range(ND):
            em.emit("S", lambda e, k=k, c0=c0: e.activation(
                sqs[:], xn[k][:, c0:c0 + 512], AF.Square),
                reads=[xn[k][:]], writes=[sqs[:]], tag="l1.q1s")
            em.emit("T", lambda e, k=k: e.matmul(
                p2[0:1, :], p.ones_col[:], sqs[:], start=(k == 0), stop=(k == ND - 1)),
                reads=[sqs[:], p.ones_col[:]], writes=[p2[:]], tag="l1.q1m")
        em.emit("V", lambda e, c0=c0: e.tensor_copy(s2row[:, c0:c0 + 512], p2[0:1, :]),
                reads=[p2[:]], writes=[s2row[:]], tag="l1.q1c")
    rstd1 = p.buf("l1.rstd1", [1, T], F32)
    e8 = p.cval(1e-8)
    em.emit("V", lambda e: e.tensor_scalar_mul(rstd1[:], s2row[:], 1.0 / D),
            reads=[s2row[:]], writes=[rstd1[:]])
    em.emit("S", lambda e: e.activation(rstd1[:], rstd1[:], AF.Sqrt, bias=e8[0:1, :]),
            reads=[rstd1[:], e8[:]], writes=[rstd1[:]])
    em.emit("V", lambda e: e.reciprocal(rstd1[:], rstd1[:]), reads=[rstd1[:]], writes=[rstd1[:]])
    rstd1b = p.buf("l1.rstd1b", [128, T], F32)
    p.bcast_row(rstd1, rstd1b, T, ps_i=5, tag="l1.rb")
    xnorm = [p.buf(f"l1.xm{k}", [128, T], F32) for k in range(ND)]
    maxa = p.buf("l1.maxa", [128, T], F32)
    xsc = p.buf("l1.xsc", [128, T], F32)
    for k in range(ND):
        em.emit("V", lambda e, k=k: e.tensor_mul(xsc[:], xn[k][:], rstd1b[:]),
                reads=[xn[k][:], rstd1b[:]], writes=[xsc[:]], tag="l1.xh1")
        em.emit("V", lambda e, k=k: e.tensor_scalar(
            xnorm[k][:], xsc[:], sm["n1w"][:, k:k + 1], None, ALU.mult),
            reads=[xsc[:], sm["n1w"][:]], writes=[xnorm[k][:]], tag="l1.xh2")
        p.free(xn[k])
        em.emit("S", lambda e, t=xnorm[k]: e.activation(xsc[:], t[:], AF.Abs),
                reads=[xnorm[k][:]], writes=[xsc[:]], tag="l1.mxa")
        if k == 0:
            em.emit("V", lambda e: e.tensor_copy(maxa[:], xsc[:]),
                    reads=[xsc[:]], writes=[maxa[:]], tag="l1.mx")
        else:
            em.emit("V", lambda e: e.tensor_tensor(
                maxa[:], maxa[:], xsc[:], ALU.max),
                reads=[maxa[:], xsc[:]], writes=[maxa[:]], tag="l1.mx")
    mrow = p.buf("l1.mrow", [1, T], F32)
    p.row_from_parts(maxa, mrow, T, tag="l1.rfp")
    sxrow = p.buf("l1.sxrow", [1, T], F32)
    sxirow = p.buf("l1.sxirow", [1, T], F32)
    em.emit("V", lambda e: e.tensor_single_scalar(mrow[:], mrow[:], 1e-5, ALU.max),
            reads=[mrow[:]], writes=[mrow[:]])
    em.emit("V", lambda e: e.tensor_scalar_mul(sxirow[:], mrow[:], 1.0 / 127.0),
            reads=[mrow[:]], writes=[sxirow[:]])
    em.emit("V", lambda e: e.reciprocal(sxrow[:], mrow[:]), reads=[mrow[:]], writes=[sxrow[:]])
    em.emit("V", lambda e: e.tensor_scalar_mul(sxrow[:], sxrow[:], 127.0),
            reads=[sxrow[:]], writes=[sxrow[:]])
    sxb = p.buf("l1.sxb", [128, T], F32)
    sxib = p.buf("l1.sxib", [128, T], F32)
    p.bcast_row(sxrow, sxb, T, ps_i=5, tag="l1.sb")
    p.bcast_row(sxirow, sxib, T, ps_i=6, tag="l1.sib")
    p.free(sxirow)
    xq = []
    rsc = p.buf("l1.rsc", [128, T], F32)
    for k in range(ND):
        em.emit("V", lambda e, k=k: e.tensor_mul(xsc[:], xnorm[k][:], sxb[:]),
                reads=[xnorm[k][:], sxb[:]], writes=[xsc[:]], tag="l1.qq1")
        p.round_ip(xsc, rsc, T, tag="l1.qq2")
        em.emit("V", lambda e: e.tensor_single_scalar(xsc[:], xsc[:], -128.0, ALU.max),
                reads=[xsc[:]], writes=[xsc[:]], tag="l1.qq3")
        xqk = p.buf(f"l1.xq{k}", [128, T], BF16)
        em.emit("V", lambda e, t=xqk: e.tensor_single_scalar(t[:], xsc[:], 127.0, ALU.min),
                reads=[xsc[:]], writes=[xqk[:]], tag="l1.qq4")
        xq.append(xqk)
        p.free(xnorm[k])
    p.free(maxa, mrow, sxrow, rstd1, rstd1b, s2row, sqs, xsc, sxb, rsc)
    p.phase()

    p.free(*ws1_scr)
    h = [p.buf(f"l1.h{j}", [128, T], F16) for j in range(NDH)]
    s2h = p.buf("l1.s2h", [1, T], F32)
    em.emit("V", lambda e: e.memset(s2h[:], 0.0), writes=[s2h[:]])
    maxh = p.buf("l1.maxh", [128, T], F32)
    em.emit("V", lambda e: e.memset(maxh[:], 0.0), writes=[maxh[:]])
    an2w = p.buf("l1.an2w", [128, NDH], F32)
    em.emit("S", lambda e: e.activation(an2w[:], sm["n2w"][:], AF.Abs),
            reads=[sm["n2w"][:]], writes=[an2w[:]])
    ascb = p.buf("l1.ascb", [128, 1], F32)
    abib = p.buf("l1.abib", [128, 1], F32)
    p.bcast_col(sm["asc"], ascb)
    p.bcast_col(sm["abi"], abib)
    hevt = p.buf("l1.hev", [128, 512], F32)
    sqh = p.buf("l1.sqh", [128, 512], F32)
    p2b = p.psum(6)

    def evac_h(ot, c0, c1, ps_):
        em.emit("V", lambda e: e.scalar_tensor_tensor(
            hevt[:, 0:c1 - c0], ps_[:, 0:c1 - c0], swi1[:], sxib[:, c0:c1],
            ALU.mult, ALU.mult),
            reads=[ps_[:], swi1[:], sxib[:]], writes=[hevt[:]], tag="l1.he1")
        em.emit("S", lambda e: e.activation(hevt[:, 0:c1 - c0], hevt[:, 0:c1 - c0], AF.Relu),
                reads=[hevt[:]], writes=[hevt[:]], tag="l1.he2")
        em.emit("S", lambda e: e.activation(hevt[:, 0:c1 - c0], hevt[:, 0:c1 - c0], AF.Square),
                reads=[hevt[:]], writes=[hevt[:]], tag="l1.he3")
        em.emit("V", lambda e: e.tensor_scalar(
            h[ot][:, c0:c1], hevt[:, 0:c1 - c0], ascb[:], abib[:], ALU.mult, ALU.add),
            reads=[hevt[:], ascb[:], abib[:]], writes=[h[ot][:]], tag="l1.he4")
        em.emit("S", lambda e: e.activation(sqh[:], h[ot][:, c0:c1], AF.Square),
                reads=[h[ot][:]], writes=[sqh[:]], tag="l1.he5")
        em.emit("T", lambda e: e.matmul(p2b[0:1, 0:c1 - c0], p.ones_col[:], sqh[:, 0:c1 - c0],
                                        start=True, stop=True),
                reads=[sqh[:], p.ones_col[:]], writes=[p2b[:]], tag="l1.he6")
        em.emit("V", lambda e: e.tensor_tensor(s2h[:, c0:c1], s2h[:, c0:c1],
                                               p2b[0:1, 0:c1 - c0], ALU.add),
                reads=[p2b[:], s2h[:]], writes=[s2h[:]], tag="l1.he7")
        em.emit("S", lambda e, ot=ot: e.activation(sqh[:, 0:c1 - c0], h[ot][:, c0:c1], AF.Abs),
                reads=[h[ot][:]], writes=[sqh[:]], tag="l1.he8a")
        em.emit("V", lambda e, ot=ot: e.tensor_scalar(
            sqh[:, 0:c1 - c0], sqh[:, 0:c1 - c0], an2w[:, ot:ot + 1], None, ALU.mult),
            reads=[sqh[:], an2w[:]], writes=[sqh[:]], tag="l1.he8b")
        em.emit("V", lambda e: e.tensor_tensor(
            maxh[:, c0:c1], maxh[:, c0:c1], sqh[:, 0:c1 - c0], ALU.max),
            reads=[maxh[:], sqh[:]], writes=[maxh[:]], tag="l1.he8c")
    emit_bitlinear_mm(p, w1, ND, NDH, xq, sw1, T, evac_h, tag="l1.w1")
    ws2_scr = []
    sw2, swi2 = emit_wscale(p, w2, ND, NDH * 128, 1024 * 4096, tag="l1.ws2",
                            scratch_out=ws2_scr)
    for k in range(ND):
        p.free(xq[k])
    p.free(sxib, ascb, abib, hevt, sqh)
    p.phase()

    rstd2 = p.buf("l1.rstd2", [1, T], F32)
    em.emit("V", lambda e: e.tensor_scalar_mul(rstd2[:], s2h[:], 1.0 / DH),
            reads=[s2h[:]], writes=[rstd2[:]])
    em.emit("S", lambda e: e.activation(rstd2[:], rstd2[:], AF.Sqrt, bias=e8[0:1, :]),
            reads=[rstd2[:], e8[:]], writes=[rstd2[:]])
    em.emit("V", lambda e: e.reciprocal(rstd2[:], rstd2[:]), reads=[rstd2[:]], writes=[rstd2[:]])
    mrow2 = p.buf("l1.mrow2", [1, T], F32)
    p.row_from_parts(maxh, mrow2, T, tag="l1.rf2")
    em.emit("V", lambda e: e.tensor_tensor(mrow2[:], mrow2[:], rstd2[:], ALU.mult),
            reads=[mrow2[:], rstd2[:]], writes=[mrow2[:]])
    sx2row = p.buf("l1.sx2row", [1, T], F32)
    sx2irow = p.buf("l1.sx2irow", [1, T], F32)
    em.emit("V", lambda e: e.tensor_single_scalar(mrow2[:], mrow2[:], 1e-5, ALU.max),
            reads=[mrow2[:]], writes=[mrow2[:]])
    em.emit("V", lambda e: e.tensor_scalar_mul(sx2irow[:], mrow2[:], 1.0 / 127.0),
            reads=[mrow2[:]], writes=[sx2irow[:]])
    em.emit("V", lambda e: e.reciprocal(sx2row[:], mrow2[:]), reads=[mrow2[:]], writes=[sx2row[:]])
    em.emit("V", lambda e: e.tensor_scalar_mul(sx2row[:], sx2row[:], 127.0),
            reads=[sx2row[:]], writes=[sx2row[:]])
    em.emit("V", lambda e: e.tensor_tensor(sx2row[:], sx2row[:], rstd2[:], ALU.mult),
            reads=[sx2row[:], rstd2[:]], writes=[sx2row[:]])
    p.free(maxh, mrow2, rstd2, s2h)
    psb = p.buf("l1.psb", [128, T], F32)
    sx2ib = p.buf("l1.sx2ib", [128, T], F32)
    p.bcast_row(sx2row, psb, T, ps_i=5, tag="l1.s2b")
    p.bcast_row(sx2irow, sx2ib, T, ps_i=6, tag="l1.s2ib")
    p.free(sx2irow)
    hq = h
    hsc = p.buf("l1.hsc", [128, T], F32)
    hrs = p.buf("l1.hrs", [128, T], F32)
    for j in range(NDH):
        em.emit("V", lambda e, j=j: e.tensor_scalar(
            hsc[:], h[j][:], sm["n2w"][:, j:j + 1], None, ALU.mult),
            reads=[h[j][:], sm["n2w"][:]], writes=[hsc[:]], tag="l1.q21")
        em.emit("V", lambda e: e.tensor_mul(hsc[:], hsc[:], psb[:]),
                reads=[hsc[:], psb[:]], writes=[hsc[:]], tag="l1.q22")
        p.round_ip(hsc, hrs, T, tag="l1.q23")
        em.emit("V", lambda e: e.tensor_single_scalar(hsc[:], hsc[:], -128.0, ALU.max),
                reads=[hsc[:]], writes=[hsc[:]], tag="l1.q24")
        em.emit("V", lambda e, j=j: e.tensor_single_scalar(hq[j][:], hsc[:], 127.0, ALU.min),
                reads=[hsc[:]], writes=[hq[j][:]], tag="l1.q25")
    p.free(psb, hsc, hrs, sx2row)
    p.phase()

    p.free(*ws2_scr)
    ochunk = p.buf("l1.oc", [128, 512], F32)

    def evac_o(ot, c0, c1, ps_):
        em.emit("V", lambda e: e.scalar_tensor_tensor(
            ochunk[:, 0:c1 - c0], ps_[:, 0:c1 - c0], swi2[:], sx2ib[:, c0:c1],
            ALU.mult, ALU.mult),
            reads=[ps_[:], swi2[:], sx2ib[:]], writes=[ochunk[:]], tag="l1.oe1")
        em.emit("G", lambda e, ot=ot, c0=c0, c1=c1: e.dma_start(
            out=y_t[ot * 128:(ot + 1) * 128, c0:c1], in_=ochunk[:, 0:c1 - c0],
            accum_op=ALU.add),
            reads=[ochunk[:]], writes=[f"o.y.{ot}"], dma=True, tag="l1.oacc")
    emit_bitlinear_mm(p, w2, NDH, ND, hq, sw2, T, evac_o, tag="l1.w2")
    for j in range(NDH):
        p.free(hq[j])
    p.free(ochunk, sx2ib, an2w, sw1, swi1, sw2, swi2)
    for kk, v in sm.items():
        if kk not in sm_freed:
            p.free(v)
    p.phase()

    if reload:
        for k in range(ND):
            xt[k] = p.buf(f"x{k}b", [128, T], F32)
            em.emit("Q", lambda e, k=k, t=xt[k]: e.dma_start(
                out=t[:], in_=y_t[k * 128:(k + 1) * 128, :]),
                reads=[f"o.y.{k}"], writes=[xt[k][:]], dma=True)


def emit_wscale(p, wdram, nO, bw, nelem, tag, scratch_out=None):
    em = p.em
    wb = [p.buf(p.u(tag + ".w"), [128, 512], F32) for i in range(2)]
    ab = p.buf(p.u(tag + ".a"), [128, 512], F32)
    pscale = p.psum(3)
    nch = cdiv(bw, 512)
    for ot in range(nO):
        for c in range(nch):
            c0, c1 = c * 512, min((c + 1) * 512, bw)
            b = wb[(ot * nch + c) % 2]
            em.emit("G", lambda e, ot=ot, b=b, c0=c0, c1=c1: e.dma_start(
                out=b[:, 0:c1 - c0], in_=wdram[ot][:, c0:c1]),
                writes=[b[:]], dma=True, tag=tag + ".ld")
            em.emit("S", lambda e, b=b, c0=c0, c1=c1: e.activation(
                ab[:, 0:c1 - c0], b[:, 0:c1 - c0], AF.Abs),
                reads=[b[:]], writes=[ab[:]], tag=tag + ".abs")
            em.emit("T", lambda e, ot=ot, c=c, c0=c0, c1=c1: e.matmul(
                pscale[0:1, 0:c1 - c0], p.ones_col[:], ab[:, 0:c1 - c0],
                start=(ot == 0 and c == 0), stop=(ot == nO - 1 and c == nch - 1)),
                reads=[ab[:], p.ones_col[:]], writes=[pscale[:]], tag=tag + ".mm")
    srow = p.buf(p.u(tag + ".srow"), [1, 512], F32)
    em.emit("V", lambda e: e.tensor_copy(srow[:], pscale[0:1, 0:512]),
            reads=[pscale[:]], writes=[srow[:]])
    sval = p.buf(p.u(tag + ".sv"), [1, 1], F32)
    em.emit("V", lambda e: e.reduce_sum(sval[:], srow[:, 0:min(512, bw)], axis=AX.X),
            reads=[srow[:]], writes=[sval[:]])
    em.emit("V", lambda e: e.tensor_scalar_mul(sval[:], sval[:], 1.0 / nelem),
            reads=[sval[:]], writes=[sval[:]])
    em.emit("V", lambda e: e.tensor_single_scalar(sval[:], sval[:], 1e-5, ALU.max),
            reads=[sval[:]], writes=[sval[:]])
    sinv = p.buf(p.u(tag + ".si"), [1, 1], F32)
    em.emit("V", lambda e: e.tensor_copy(sinv[:], sval[:]), reads=[sval[:]], writes=[sinv[:]])
    em.emit("V", lambda e: e.reciprocal(sval[:], sval[:]), reads=[sval[:]], writes=[sval[:]])
    swc = p.buf(p.u(tag + ".swc"), [128, 1], F32)
    swic = p.buf(p.u(tag + ".swic"), [128, 1], F32)
    p.bcast_col(sval, swc)
    p.bcast_col(sinv, swic)
    if scratch_out is not None:
        scratch_out.extend([wb[0], wb[1], ab, srow, sval, sinv])
    else:
        p.free(wb[0], wb[1], ab, srow, sval, sinv)
    return swc, swic


def emit_bitlinear_mm(p, wdram, nK, nO, xq, sw_col, T_, evac, tag):
    em = p.em
    bw = nK * 128
    wf = [p.buf(p.u(tag + ".wf"), [128, 512], F32) for _ in range(2)]
    wb = [p.buf(p.u(tag + ".w"), [128, bw], BF16) for _ in range(2)]
    qs = p.buf(p.u(tag + ".qs"), [128, 512], F32)
    nch = cdiv(T_, 512)
    for ot in range(nO):
        b = wb[ot % 2]
        for q0 in range(0, bw, 512):
            q1 = min(q0 + 512, bw)
            bf_ = wf[(q0 // 512) % 2]
            em.emit("G", lambda e, ot=ot, bf_=bf_, q0=q0, q1=q1: e.dma_start(
                out=bf_[:, 0:q1 - q0], in_=wdram[ot][:, q0:q1]),
                writes=[bf_[:]], dma=True, tag=tag + ".ld")
            em.emit("V", lambda e, bf_=bf_, q0=q0, q1=q1: e.tensor_scalar(
                qs[:, 0:q1 - q0], bf_[:, 0:q1 - q0], sw_col[:], None, ALU.mult),
                reads=[bf_[:], sw_col[:]], writes=[qs[:]], tag=tag + ".q1")
            em.emit("V", lambda e, b=b, q0=q0, q1=q1: e.tensor_single_scalar(
                b[:, q0:q1], qs[:, 0:q1 - q0], 0.5, ALU.is_ge),
                reads=[qs[:]], writes=[b[:]], tag=tag + ".q2")
            em.emit("V", lambda e, q0=q0, q1=q1: e.tensor_single_scalar(
                qs[:, 0:q1 - q0], qs[:, 0:q1 - q0], -0.5, ALU.is_ge),
                reads=[qs[:]], writes=[qs[:]], tag=tag + ".q2b")
            em.emit("V", lambda e, b=b, q0=q0, q1=q1: e.tensor_tensor(
                qs[:, 0:q1 - q0], qs[:, 0:q1 - q0], b[:, q0:q1], ALU.add),
                reads=[qs[:], b[:]], writes=[qs[:]], tag=tag + ".q3")
            em.emit("V", lambda e, b=b, q0=q0, q1=q1: e.tensor_single_scalar(
                b[:, q0:q1], qs[:, 0:q1 - q0], 1.0, ALU.subtract),
                reads=[qs[:]], writes=[b[:]], tag=tag + ".q4")
        for c in range(nch):
            c0, c1 = c * 512, min((c + 1) * 512, T_)
            ps_ = p.psum((ot * nch + c) % 2)

            def mm(e, b=b, c0=c0, c1=c1, ps_=ps_, tiles=tuple(xq)):
                last = None
                for k in range(nK):
                    last = e.matmul(ps_[:, 0:c1 - c0], b[:, k * 128:(k + 1) * 128],
                                    tiles[k][:, c0:c1], start=(k == 0), stop=(k == nK - 1))
                return last
            em.emit("T", mm, reads=[b[:]] + [x[:] for x in xq], writes=[ps_[:]],
                    tag=tag + ".mm")
            evac(ot, c0, c1, ps_)
    p.free(wf[0], wf[1], wb[0], wb[1], qs)





def emit_l2a(p, xt, y_t):
    nc, em = p.nc, p.em
    H, WIN = 16, 16
    wq_c = p.din("l2_wq_c", [ND, 128, ND * 128])
    wk_c = p.din("l2_wk_c", [ND, 128, ND * 128])
    wv_ct = p.din("l2_wv_ct", [ND, 128, D])
    wq_l = p.din("l2_wq_l", [ND, 128, ND * 128])
    wk_l = p.din("l2_wk_l", [ND, 128, ND * 128])
    wv_lt = p.din("l2_wv_lt", [ND, 128, D])
    wo_c = p.din("l2_wo_c", [ND, 128, ND * 128])
    wo_l = p.din("l2_wo_l", [ND, 128, ND * 128])
    bqc = p.din("l2_bqc", [128, ND]); bkc = p.din("l2_bkc", [128, ND])
    bvc = p.din("l2_bvc", [128, ND]); boc = p.din("l2_boc", [128, ND])
    bql = p.din("l2_bql", [128, ND]); bkl = p.din("l2_bkl", [128, ND])
    bvl = p.din("l2_bvl", [128, ND]); bol = p.din("l2_bol", [128, ND])
    xn_t = p.dout("xn_t", [D, T], BF16)
    xn_tm_t = p.dout("xn_tm_t", [T, D], BF16)
    bnsum = p.dout("bnsum", [128, 2 * ND])

    sm = {}
    for nm, srcd in (("bqc", bqc), ("bkc", bkc), ("bvc", bvc), ("boc", boc),
                     ("bql", bql), ("bkl", bkl), ("bvl", bvl), ("bol", bol)):
        dst = sm[nm] = p.buf("l2s." + nm, [128, ND], F32)
        em.emit("Q", lambda e, d=dst, s=srcd: e.dma_start(out=d[:], in_=s),
                writes=[dst[:]], dma=True)

    xn = [p.buf(f"l2.xn{k}", [128, T], BF16) for k in range(ND)]
    p.layernorm(xt, T, xn, tag="l2ln")
    for k in range(ND):
        em.emit("Q", lambda e, k=k, t=xn[k]: e.dma_start(
            out=xn_t[k * 128:(k + 1) * 128, :], in_=t[:]),
            reads=[xn[k][:]], writes=[f"o.xn.{k}"], dma=True)
    bns = p.buf("l2.bns", [128, 2 * ND], F32)
    scr = p.buf("l2.scr", [128, 512], F32)
    col = p.buf("l2.col", [128, 1], F32)
    for k in range(ND):
        em.emit("V", lambda e, k=k, t=xn[k]: e.reduce_sum(bns[:, k:k + 1], t[:], axis=AX.X),
                reads=[xn[k][:]], writes=[bns[:]], tag="l2.bn1")
        for c in range(4):
            em.emit("S", lambda e, k=k, c=c, t=xn[k]: e.activation(
                scr[:], t[:, c * 512:(c + 1) * 512], AF.Square),
                reads=[xn[k][:]], writes=[scr[:]], tag="l2.bn2")
            em.emit("V", lambda e: e.reduce_sum(col[:], scr[:], axis=AX.X),
                    reads=[scr[:]], writes=[col[:]], tag="l2.bn3")
            if c == 0:
                em.emit("V", lambda e, k=k: e.tensor_copy(bns[:, ND + k:ND + k + 1], col[:]),
                        reads=[col[:]], writes=[bns[:]], tag="l2.bn4")
            else:
                em.emit("V", lambda e, k=k: e.tensor_tensor(
                    bns[:, ND + k:ND + k + 1], bns[:, ND + k:ND + k + 1], col[:], ALU.add),
                    reads=[col[:], bns[:]], writes=[bns[:]], tag="l2.bn4")
    em.emit("Q", lambda e: e.dma_start(out=bnsum, in_=bns[:]),
            reads=[bns[:]], writes=["o.bns"], dma=True)
    ttile = p.buf("l2.ttile", [128, D], BF16)
    pbf = p.psum(7)
    for c in range(T // 128):
        for k in range(ND):
            em.emit("T", lambda e, c=c, k=k, t=xn[k]: e.transpose(
                pbf[:, k * 128:(k + 1) * 128], t[:, c * 128:(c + 1) * 128], p.ident_bf[:]),
                reads=[xn[k][:], p.ident_bf[:]], writes=[pbf[:]], tag="l2.tm1")
        em.emit("V", lambda e: e.tensor_copy(ttile[:], pbf[:]),
                reads=[pbf[:]], writes=[ttile[:]], tag="l2.tm2")
        em.emit("Q", lambda e, c=c: e.dma_start(
            out=xn_tm_t[c * 128:(c + 1) * 128, :], in_=ttile[:]),
            reads=[ttile[:]], writes=["o.xntm"], dma=True, tag="l2.tm3")
    p.free(scr, col, bns, ttile)

    for k in range(ND):
        em.emit("Q", lambda e, k=k, t=xt[k]: e.dma_start(
            out=y_t[k * 128:(k + 1) * 128, :], in_=t[:]),
            reads=[xt[k][:]], writes=[f"o.y.{k}"], dma=True)
    for k in range(ND):
        p.free(xt[k])
        xt[k] = None
    p.phase()

    if "l2cls" in SKIP:
        for k in range(ND):
            p.free(xn[k])
        for v in sm.values():
            p.free(v)
        p.phase()
        return
    kc = [p.buf(f"l2.kc{k}", [128, T], BF16) for k in range(ND)]

    def evac_kc(ot, c0, c1, ps_):
        em.emit("V", lambda e, ot=ot: e.tensor_scalar(
            kc[ot][:, c0:c1], ps_[:, 0:c1 - c0], 1.0, sm["bkc"][:, ot:ot + 1],
            ALU.mult, ALU.add),
            reads=[ps_[:], sm["bkc"][:]], writes=[kc[ot][:]], tag="l2.kc")
    if "l2nokc" not in SKIP:
        p.matmul_to(wk_c, ND, ND, xn, T, evac_kc, tag="l2kc")

    if "l2c1" in SKIP:
        return
    def make_vtm(wvt_dram, tag):
        NCH = 2 if "l2vtm2" in SKIP else T // 128
        EV = "V" if "l2vtmV" in SKIP else "S"
        vt = [p.buf(f"{tag}{c}", [128, D], BF16) for c in range(T // 128)]
        wvb = [p.buf(p.u(tag + ".w"), [128, D], BF16) for _ in range(ND)]
        for k in range(ND):
            em.emit("G", lambda e, k=k, b=wvb[k], w=wvt_dram: e.dma_start(out=b[:], in_=w[k]),
                    writes=[wvb[k][:]], dma=True, tag=tag + ".ld")
        for c in range(NCH):
            for hf in range(2):
                ps_ = p.psum(hf)

                def mmv(e, c=c, hf=hf, xs=tuple(xn), ws=tuple(wvb), ps_=ps_):
                    last = None
                    for k in range(ND):
                        last = e.matmul(ps_[:, :], xs[k][:, c * 128:(c + 1) * 128],
                                        ws[k][:, hf * 512:(hf + 1) * 512],
                                        start=(k == 0), stop=(k == ND - 1))
                    return last
                em.emit("T", mmv, reads=[w[:] for w in wvb] + [x[:] for x in xn],
                        writes=[ps_[:]], tag=tag + ".mm")
                if EV == "S":
                    em.emit("S", lambda e, c=c, hf=hf, vtc=vt[c], ps_=ps_: e.copy(
                        vtc[:, hf * 512:(hf + 1) * 512], ps_[:, :]),
                        reads=[ps_[:]], writes=[vt[c][:]], tag=tag + ".ev")
                else:
                    em.emit("V", lambda e, c=c, hf=hf, vtc=vt[c], ps_=ps_: e.tensor_copy(
                        vtc[:, hf * 512:(hf + 1) * 512], ps_[:, :]),
                        reads=[ps_[:]], writes=[vt[c][:]], tag=tag + ".ev")
        p.free(*wvb)
        return vt
    vct = make_vtm(wv_ct, "l2.vct")

    if "l2c2" in SKIP:
        return
    q0 = p.buf("l2.q0", [128, ND], F32)

    def evac_q0(ot, c0, c1, ps_):
        em.emit("V", lambda e, ot=ot: e.tensor_scalar(
            q0[:, ot:ot + 1], ps_[:, 0:1], 1.0, sm["bqc"][:, ot:ot + 1],
            ALU.mult, ALU.add),
            reads=[ps_[:], sm["bqc"][:]], writes=[q0[:]], tag="l2.q0")
    p.matmul_to(wq_c, ND, ND, [x for x in xn], 1, evac_q0, tag="l2q0")
    em.emit("V", lambda e: e.tensor_scalar_mul(q0[:], q0[:], 0.125),
            reads=[q0[:]], writes=[q0[:]])
    q0b = p.buf("l2.q0b", [128, ND], BF16)
    em.emit("V", lambda e: e.tensor_copy(q0b[:], q0[:]), reads=[q0[:]], writes=[q0b[:]])

    qblk = [p.buf(f"l2.qblk{k}", [128, 16], BF16) for k in range(ND)]
    for k in range(ND):
        em.emit("G", lambda e, t=qblk[k]: e.memset(t[:], 0.0), writes=[qblk[k][:]])
        em.emit("V", lambda e, k=k, t=qblk[k]: e.tensor_copy(
            t[0:64, 2 * k:2 * k + 1], q0b[0:64, k:k + 1]),
            reads=[q0b[:]], writes=[qblk[k][:]], tag="l2.qb1")
        em.emit("V", lambda e, k=k, t=qblk[k]: e.tensor_copy(
            t[64:128, 2 * k + 1:2 * k + 2], q0b[64:128, k:k + 1]),
            reads=[q0b[:]], writes=[qblk[k][:]], tag="l2.qb2")
    if "l2c3" in SKIP:
        return
    if "tapq0" in DBG:
        p.tap("q0", q0)
    if "tapqb" in DBG:
        p.tap("qblk0", qblk[0])
        p.tap("qblk1", qblk[1])
        p.tap("kc0", kc[0])
    sc_cls = p.buf("l2.sccls", [16, T], F32)
    for c in range(4):
        ps_ = p.psum(2)

        def mmsc(e, c=c, tiles=tuple(kc), qb=tuple(qblk), ps_=ps_):
            last = None
            for k in range(ND):
                last = e.matmul(ps_[0:16, :], qb[k][:], tiles[k][:, c * 512:(c + 1) * 512],
                                start=(k == 0), stop=(k == ND - 1))
            return last
        em.emit("T", mmsc, reads=[t[:] for t in qblk] + [t[:] for t in kc],
                writes=[ps_[:]], tag="l2.scmm")
        em.emit("V", lambda e, c=c, ps_=ps_: e.tensor_copy(sc_cls[:, c * 512:(c + 1) * 512],
                                                           ps_[0:16, :]),
                reads=[ps_[:]], writes=[sc_cls[:]], tag="l2.scev")
    for k in range(ND):
        p.free(qblk[k], kc[k])
    rmx = p.buf("l2.rmx", [16, 1], F32)
    rsm = p.buf("l2.rsm", [16, 1], F32)
    em.emit("V", lambda e: e.reduce_max(rmx[:], sc_cls[:], axis=AX.X),
            reads=[sc_cls[:]], writes=[rmx[:]])
    em.emit("V", lambda e: e.tensor_scalar_mul(rmx[:], rmx[:], -1.0),
            reads=[rmx[:]], writes=[rmx[:]])
    em.emit("S", lambda e: e.activation(sc_cls[:], sc_cls[:], AF.Exp, bias=rmx[0:16, :]),
            reads=[sc_cls[:], rmx[:]], writes=[sc_cls[:]])
    em.emit("V", lambda e: e.reduce_sum(rsm[:], sc_cls[:], axis=AX.X),
            reads=[sc_cls[:]], writes=[rsm[:]])
    em.emit("V", lambda e: e.reciprocal(rsm[:], rsm[:]), reads=[rsm[:]], writes=[rsm[:]])
    awc = p.buf("l2.awc", [16, T], BF16)
    em.emit("V", lambda e: e.tensor_scalar(awc[:], sc_cls[:], rsm[0:16, :], None, ALU.mult),
            reads=[sc_cls[:], rsm[:]], writes=[awc[:]])
    p.free(sc_cls)
    if "tapsc" in DBG:
        p.tap("sccls", sc_cls)
    if "l2c4" in SKIP:
        return
    awT = p.buf("l2.awT", [128, 16 * (T // 128)], BF16)
    pbf2 = p.psum(7)
    for c in range(T // 128):
        em.emit("T", lambda e, c=c: e.transpose(
            pbf2[0:128, 0:16], awc[0:16, c * 128:(c + 1) * 128], p.ident_bf[0:16, 0:16]),
            reads=[awc[:], p.ident_bf[:]], writes=[pbf2[:]], tag="l2.awt")
        em.emit("V", lambda e, c=c: e.tensor_copy(awT[:, c * 16:(c + 1) * 16], pbf2[0:128, 0:16]),
                reads=[pbf2[:]], writes=[awT[:]], tag="l2.awtev")
    ctxall = p.buf("l2.ctxall", [16, D], F32)
    for hf in range(2):
        ps_ = p.psum(3)

        def mmctx(e, hf=hf, vt=tuple(vct), ps_=ps_):
            last = None
            for c in range(T // 128):
                last = e.matmul(ps_[0:16, :], awT[:, c * 16:(c + 1) * 16],
                                vt[c][:, hf * 512:(hf + 1) * 512],
                                start=(c == 0), stop=(c == T // 128 - 1))
            return last
        em.emit("T", mmctx, reads=[awT[:]] + [v[:] for v in vct], writes=[ps_[:]],
                tag="l2.ctxmm")
        em.emit("V", lambda e, hf=hf, ps_=ps_: e.tensor_copy(
            ctxall[:, hf * 512:(hf + 1) * 512], ps_[0:16, :]),
            reads=[ps_[:]], writes=[ctxall[:]], tag="l2.ctxev")
    for c in range(T // 128):
        p.free(vct[c])
    p.free(awc, awT, rmx, rsm, q0, q0b)
    if "l2c5" in SKIP:
        return
    if "tapctx" in DBG:
        p.tap("ctxall", ctxall)
    ctx_fm = p.buf("l2.ctxfm", [128, ND], F32)
    for h in range(16):
        em.emit("G", lambda e, h=h: e.dma_start(
            out=ctx_fm[(h % 2) * 64:(h % 2) * 64 + 64, h // 2:h // 2 + 1],
            in_=ctxall[h:h + 1, h * 64:(h + 1) * 64]),
            reads=[ctxall[:]], writes=[ctx_fm[:]], dma=True, tag="l2.diag")
    em.emit("V", lambda e: e.tensor_tensor(ctx_fm[:], ctx_fm[:], sm["bvc"][:], ALU.add),
            reads=[ctx_fm[:], sm["bvc"][:]], writes=[ctx_fm[:]])
    ctx_fmb = p.buf("l2.ctxfmb", [128, ND], BF16)
    em.emit("V", lambda e: e.tensor_copy(ctx_fmb[:], ctx_fm[:]),
            reads=[ctx_fm[:]], writes=[ctx_fmb[:]])
    if "l2c6" in SKIP:
        return
    gcol = p.buf("l2.gcol", [128, ND], F32)

    def evac_go(ot, c0, c1, ps_):
        em.emit("V", lambda e, ot=ot: e.tensor_scalar(
            gcol[:, ot:ot + 1], ps_[:, 0:1], 1.0, sm["boc"][:, ot:ot + 1],
            ALU.mult, ALU.add),
            reads=[ps_[:], sm["boc"][:]], writes=[gcol[:]], tag="l2.go")
    p.matmul_to(wo_c, ND, ND, [ctx_fmb[:, k:k + 1].unsqueeze(0).squeeze(0) if False else ctx_fmb for k in range(ND)], 1, evac_go, tag="l2go", xsel=True)
    for ot in range(ND):
        em.emit("G", lambda e, ot=ot: e.dma_start(
            out=y_t[ot * 128:(ot + 1) * 128, 0:1], in_=gcol[:, ot:ot + 1],
            accum_op=ALU.add, allow_slow_non_contiguous=True),
            reads=[gcol[:]], writes=[f"o.y.{ot}"], dma=True, tag="l2.goacc")
    p.free(ctxall, ctx_fm, ctx_fmb, gcol)
    p.phase()

    if "l2loc" in SKIP:
        for k in range(ND):
            p.free(xn[k])
        for v in sm.values():
            p.free(v)
        p.phase()
        return
    ql = [p.buf(f"l2.ql{k}", [128, T], BF16) for k in range(ND)]

    def evac_ql(ot, c0, c1, ps_):
        em.emit("V", lambda e, ot=ot: e.tensor_scalar(
            ql[ot][:, c0:c1], ps_[:, 0:c1 - c0], sm["bql"][:, ot:ot + 1], 0.125,
            ALU.add, ALU.mult),
            reads=[ps_[:], sm["bql"][:]], writes=[ql[ot][:]], tag="l2.ql")
    p.matmul_to(wq_l, ND, ND, xn, T, evac_ql, tag="l2ql")
    kl = [p.buf(f"l2.kl{k}", [128, T], BF16) for k in range(ND)]

    def evac_kl(ot, c0, c1, ps_):
        em.emit("V", lambda e, ot=ot: e.tensor_scalar(
            kl[ot][:, c0:c1], ps_[:, 0:c1 - c0], sm["bkl"][:, ot:ot + 1], None, ALU.add),
            reads=[ps_[:], sm["bkl"][:]], writes=[kl[ot][:]], tag="l2.kl")
    p.matmul_to(wk_l, ND, ND, xn, T, evac_kl, tag="l2kl")
    vlt = make_vtm(wv_lt, "l2.vlt")
    for k in range(ND):
        p.free(xn[k])
    p.phase()

    WW = 384
    mgen = p.buf("l2.mgen", [128, WW], F32)
    mb0 = p.buf("l2.mb0", [128, WW], F32)
    NEG = -3.0e30
    em.emit("G", lambda e: e.memset(mgen[:], 0.0), writes=[mgen[:]])
    em.emit("G", lambda e: e.affine_select(mgen[:], mgen[:], pattern=[[1, WW]],
                                           compare_op=ALU.is_ge, fill=NEG,
                                           base=-113, channel_multiplier=-1),
            reads=[mgen[:]], writes=[mgen[:]])
    em.emit("G", lambda e: e.affine_select(mgen[:], mgen[:], pattern=[[-1, WW]],
                                           compare_op=ALU.is_ge, fill=NEG,
                                           base=145, channel_multiplier=1),
            reads=[mgen[:]], writes=[mgen[:]])
    em.emit("G", lambda e: e.memset(mb0[:], 0.0), writes=[mb0[:]])
    em.emit("G", lambda e: e.affine_select(mb0[:], mb0[:], pattern=[[1, WW]],
                                           compare_op=ALU.is_ge, fill=NEG,
                                           base=15, channel_multiplier=-1),
            reads=[mb0[:]], writes=[mb0[:]])
    em.emit("G", lambda e: e.affine_select(mb0[:], mb0[:], pattern=[[-1, WW]],
                                           compare_op=ALU.is_ge, fill=NEG,
                                           base=17, channel_multiplier=1),
            reads=[mb0[:]], writes=[mb0[:]])
    em.emit("G", lambda e: e.affine_select(mb0[:], mb0[:], pattern=[[1, WW]],
                                           compare_op=ALU.is_ge, fill=NEG,
                                           base=-1, channel_multiplier=0),
            reads=[mb0[:]], writes=[mb0[:]])

    ctx_loc = [p.buf(f"l2.cl{k}", [128, T], BF16) for k in range(ND)]
    aws2 = [p.buf(f"l2.aws{i}", [128, WW], F32) for i in range(2)]
    awb22 = [p.buf(f"l2.awb2{i}", [128, WW], BF16) for i in range(2)]
    awT22 = [p.buf(f"l2.awT2{i}", [128, WW], BF16) for i in range(2)]
    lrmx2 = [p.buf(f"l2.lrmx{i}", [128, 1], F32) for i in range(2)]
    lrsm2 = [p.buf(f"l2.lrsm{i}", [128, 1], F32) for i in range(2)]
    pbf3 = p.psum(7)
    for b in range(16):
        nq = 128 if b < 15 else 127
        t0 = 1 + 128 * b
        cA = max(b - 1, 0)
        chunks = [cA, cA + 1, cA + 2]
        live = [c for c in chunks if c < 16]
        msk = mb0 if b == 0 else mgen
        for h in range(16):
            th, pb = h // 2, (h % 2) * 64
            par = (b * 16 + h) % 2
            aws, awb2, awT2 = aws2[par], awb22[par], awT22[par]
            lrmx, lrsm = lrmx2[par], lrsm2[par]
            ps_s = p.psum(2 + 2 * par)

            def mms(e, b=b, th=th, pb=pb, t0=t0, nq=nq, chunks=tuple(chunks), ps_s=ps_s):
                last = None
                for jj, cc in enumerate(chunks):
                    if cc >= 16:
                        continue
                    last = e.matmul(ps_s[0:nq, jj * 128:(jj + 1) * 128],
                                    ql[th][pb:pb + 64, t0:t0 + nq],
                                    kl[th][pb:pb + 64, cc * 128:(cc + 1) * 128],
                                    start=True, stop=True)
                return last
            em.emit("T", mms, reads=[ql[th][:], kl[th][:]], writes=[ps_s[:]], tag="l2.lsc")
            if len(live) < 3:
                em.emit("V", lambda e, nq=nq, ps_s=ps_s: e.memset(ps_s[0:nq, 256:384], -3.0e30),
                        reads=[], writes=[ps_s[:]], tag="l2.lz")
            em.emit("V", lambda e, nq=nq, msk=msk, ps_s=ps_s, aws=aws: e.tensor_tensor(
                aws[0:nq, :], ps_s[0:nq, 0:WW], msk[0:nq, :], ALU.add),
                reads=[ps_s[:], msk[:]], writes=[aws[:]], tag="l2.lmask")
            em.emit("V", lambda e, nq=nq, aws=aws, lrmx=lrmx: e.reduce_max(
                lrmx[0:nq, :], aws[0:nq, :], axis=AX.X),
                reads=[aws[:]], writes=[lrmx[:]], tag="l2.lmax")
            em.emit("V", lambda e, nq=nq, lrmx=lrmx: e.tensor_scalar_mul(
                lrmx[0:nq, :], lrmx[0:nq, :], -1.0),
                reads=[lrmx[:]], writes=[lrmx[:]], tag="l2.lneg")
            em.emit("S", lambda e, nq=nq, aws=aws, lrmx=lrmx: e.activation(
                aws[0:nq, :], aws[0:nq, :], AF.Exp, bias=lrmx[0:nq, :]),
                reads=[aws[:], lrmx[:]], writes=[aws[:]], tag="l2.lexp")
            em.emit("V", lambda e, nq=nq, aws=aws, lrsm=lrsm: e.reduce_sum(
                lrsm[0:nq, :], aws[0:nq, :], axis=AX.X),
                reads=[aws[:]], writes=[lrsm[:]], tag="l2.lsum")
            em.emit("V", lambda e, nq=nq, lrsm=lrsm: e.reciprocal(
                lrsm[0:nq, :], lrsm[0:nq, :]),
                reads=[lrsm[:]], writes=[lrsm[:]], tag="l2.lrec")
            if nq < 128:
                em.emit("G", lambda e, awb2=awb2: e.memset(awb2[:], 0.0),
                        writes=[awb2[:]], tag="l2.lpad")
            em.emit("V", lambda e, nq=nq, aws=aws, lrsm=lrsm, awb2=awb2: e.tensor_scalar(
                awb2[0:nq, :], aws[0:nq, :], lrsm[0:nq, :], None, ALU.mult),
                reads=[aws[:], lrsm[:]], writes=[awb2[:]], tag="l2.lnorm")
            po = par * 384
            for j in range(3):
                em.emit("T", lambda e, j=j, po=po, awb2=awb2: e.transpose(
                    pbf3[:, po + j * 128:po + (j + 1) * 128],
                    awb2[:, j * 128:(j + 1) * 128], p.ident_bf[:]),
                    reads=[awb2[:], p.ident_bf[:]], writes=[f"ps7.{par}"], tag="l2.lt")
                em.emit("V", lambda e, j=j, po=po, awT2=awT2: e.tensor_copy(
                    awT2[:, j * 128:(j + 1) * 128], pbf3[:, po + j * 128:po + j * 128 + 128]),
                    reads=[f"ps7.{par}"], writes=[awT2[:]], tag="l2.ltev")
            ps_c = p.psum(3 + 2 * par)

            def mmc(e, h=h, nq=nq, chunks=tuple(chunks), vt=tuple(vlt), ps_c=ps_c,
                    awT2=awT2):
                last = None
                first = True
                for jj, cc in enumerate(chunks):
                    if cc >= 16:
                        continue
                    last = e.matmul(ps_c[0:64, 0:nq], vt[cc][:, h * 64:(h + 1) * 64],
                                    awT2[:, jj * 128:jj * 128 + nq],
                                    start=first, stop=(jj == len(chunks) - 1 or cc == 15))
                    first = False
                return last
            em.emit("T", mmc, reads=[vlt[c][:] for c in live] + [awT2[:]],
                    writes=[ps_c[:]], tag="l2.lctx")
            em.emit("V", lambda e, th=th, pb=pb, t0=t0, nq=nq, ps_c=ps_c: e.tensor_copy(
                ctx_loc[th][pb:pb + 64, t0:t0 + nq], ps_c[0:64, 0:nq]),
                reads=[ps_c[:]], writes=[ctx_loc[th][:]], tag="l2.lcev")
    for k in range(ND):
        p.free(ql[k], kl[k])
    for c in range(T // 128):
        p.free(vlt[c])
    p.free(aws2[0], aws2[1], awb22[0], awb22[1], awT22[0], awT22[1],
           lrmx2[0], lrmx2[1], lrsm2[0], lrsm2[1], mgen, mb0)
    for k in range(ND):
        em.emit("G", lambda e, k=k, t=ctx_loc[k]: e.memset(t[:, 0:1], 0.0),
                writes=[ctx_loc[k][:]], tag="l2.c0")
        em.emit("V", lambda e, k=k, t=ctx_loc[k]: e.tensor_scalar(
            t[:, 1:T], t[:, 1:T], sm["bvl"][:, k:k + 1], None, ALU.add),
            reads=[ctx_loc[k][:], sm["bvl"][:]], writes=[ctx_loc[k][:]], tag="l2.cbias")
    p.phase()

    och = p.buf("l2.och", [128, 512], F32)

    def evac_ol(ot, c0, c1, ps_):
        em.emit("V", lambda e, ot=ot, c0=c0, c1=c1: e.tensor_scalar(
            och[:, 0:c1 - c0], ps_[:, 0:c1 - c0], sm["bol"][:, ot:ot + 1], None, ALU.add),
            reads=[ps_[:], sm["bol"][:]], writes=[och[:]], tag="l2.ol1")
        lo = 1 if c0 == 0 else 0
        em.emit("G", lambda e, ot=ot, c0=c0, c1=c1, lo=lo: e.dma_start(
            out=y_t[ot * 128:(ot + 1) * 128, c0 + lo:c1], in_=och[:, lo:c1 - c0],
            accum_op=ALU.add),
            reads=[och[:]], writes=[f"o.y.{ot}"], dma=True, tag="l2.olacc")
    p.matmul_to(wo_l, ND, ND, ctx_loc, T, evac_ol, tag="l2ol")
    for k in range(ND):
        p.free(ctx_loc[k])
    p.free(och)
    for v in sm.values():
        p.free(v)
    p.phase()




def emit_l2b(p, xt, y_t):
    nc, em = p.nc, p.em
    NH, NK, KTOP = 8, 256, 8
    NS = T // 128
    xn_in = p.din("xn_in", [D, T], BF16)
    xn_tm_in = p.din("xn_tm_in", [T, D], BF16)
    bn_all = p.din("bn_all", [128, NCORES * 16])
    wq_d = p.din("l2_wq", [16, 128, ND * 128])
    keys_d = p.din("l2_keys", [16, 128, NK])
    gam = p.din("l2_gam", [128, ND])
    bet = p.din("l2_bet", [128, ND])
    down_d = p.din("l2_down", [65536, D])
    up_d = p.din("l2_up", [65536, D])

    xn = [p.buf(f"l2b.xn{k}", [128, T], BF16) for k in range(ND)]
    for k in range(ND):
        em.emit("Q", lambda e, k=k, t=xn[k]: e.dma_start(
            out=t[:], in_=xn_in[k * 128:(k + 1) * 128, :]),
            writes=[xn[k][:]], dma=True)
    bns = p.buf("l2b.bns", [128, NCORES * 16], F32)
    em.emit("Q", lambda e: e.dma_start(out=bns[:], in_=bn_all), writes=[bns[:]], dma=True)
    sgam = p.buf("l2b.gam", [128, ND], F32)
    sbet = p.buf("l2b.bet", [128, ND], F32)
    em.emit("Q", lambda e: e.dma_start(out=sgam[:], in_=gam), writes=[sgam[:]], dma=True)
    em.emit("Q", lambda e: e.dma_start(out=sbet[:], in_=bet), writes=[sbet[:]], dma=True)
    keys = [p.buf(f"l2b.keys{i}", [128, NK], BF16) for i in range(16)]
    for i in range(16):
        em.emit("G", lambda e, i=i, t=keys[i]: e.dma_start(out=t[:], in_=keys_d[i]),
                writes=[keys[i][:]], dma=True)

    tots = p.buf("l2b.tots", [128, 16], F32)
    em.emit("V", lambda e: e.tensor_copy(tots[:], bns[:, 0:16]), reads=[bns[:]], writes=[tots[:]])
    for c in range(1, NCORES):
        em.emit("V", lambda e, c=c: e.tensor_tensor(tots[:], tots[:], bns[:, c * 16:(c + 1) * 16],
                                                    ALU.add),
                reads=[tots[:], bns[:]], writes=[tots[:]])
    NTOK = float(NCORES * T)
    mean = p.buf("l2b.mean", [128, ND], F32)
    var = p.buf("l2b.var", [128, ND], F32)
    acoef = p.buf("l2b.ac", [128, ND], F32)
    bcoef = p.buf("l2b.bc", [128, ND], F32)
    em.emit("V", lambda e: e.tensor_scalar_mul(mean[:], tots[:, 0:8], 1.0 / NTOK),
            reads=[tots[:]], writes=[mean[:]])
    em.emit("V", lambda e: e.tensor_tensor(var[:], mean[:], mean[:], ALU.mult),
            reads=[mean[:]], writes=[var[:]])
    em.emit("V", lambda e: e.scalar_tensor_tensor(var[:], tots[:, 8:16], 1.0 / NTOK, var[:],
                                                  ALU.mult, ALU.subtract),
            reads=[tots[:], var[:]], writes=[var[:]])
    e5 = p.cval(1e-5)
    em.emit("S", lambda e: e.activation(var[:], var[:], AF.Sqrt, bias=e5[:]),
            reads=[var[:], e5[:]], writes=[var[:]])
    em.emit("V", lambda e: e.reciprocal(var[:], var[:]), reads=[var[:]], writes=[var[:]])
    em.emit("V", lambda e: e.tensor_tensor(acoef[:], var[:], sgam[:], ALU.mult),
            reads=[var[:], sgam[:]], writes=[acoef[:]])
    em.emit("V", lambda e: e.tensor_tensor(bcoef[:], mean[:], acoef[:], ALU.mult),
            reads=[mean[:], acoef[:]], writes=[bcoef[:]])
    em.emit("V", lambda e: e.tensor_tensor(bcoef[:], sbet[:], bcoef[:], ALU.subtract),
            reads=[sbet[:], bcoef[:]], writes=[bcoef[:]])
    xnb = [p.buf(f"l2b.xnb{k}", [128, T], BF16) for k in range(ND)]
    for k in range(ND):
        em.emit("V", lambda e, k=k, t=xnb[k], s=xn[k]: e.tensor_scalar(
            t[:], s[:], acoef[:, k:k + 1], bcoef[:, k:k + 1], ALU.mult, ALU.add),
            reads=[xn[k][:], acoef[:], bcoef[:]], writes=[xnb[k][:]], tag="l2b.xnb")
    for k in range(ND):
        p.free(xn[k])
    p.phase()

    qbs = [p.buf(f"l2b.qb{i}", [128, T], BF16) for i in range(2)]
    wqb = [p.buf(f"l2b.wq{i}", [128, ND * 128], BF16) for i in range(2)]
    scfs = [p.buf(f"l2b.scf{i}", [128, NK], F32) for i in range(2)]
    sc_p = [p.buf(f"l2b.scp{pp}", [128, NS * 64], F32) for pp in range(2)]
    idx_p = [p.buf(f"l2b.idxp{pp}", [128, NS * 64], U32) for pp in range(2)]
    for i in range(16):
        pp, h = i // 8, i % 8
        wb = wqb[i % 2]
        qb = qbs[i % 2]
        em.emit("G", lambda e, i=i, wb=wb: e.dma_start(out=wb[:], in_=wq_d[i]),
                writes=[wb[:]], dma=True, tag="l2b.wqld")
        for c in range(4):
            ps_ = p.psum(c % 2)

            def mmq(e, c=c, wb=wb, tiles=tuple(xnb), ps_=ps_):
                last = None
                for k in range(ND):
                    last = e.matmul(ps_[:, :], wb[:, k * 128:(k + 1) * 128],
                                    tiles[k][:, c * 512:(c + 1) * 512],
                                    start=(k == 0), stop=(k == ND - 1))
                return last
            em.emit("T", mmq, reads=[wb[:]] + [x[:] for x in xnb], writes=[ps_[:]],
                    tag="l2b.qmm")
            em.emit("S", lambda e, c=c, qb=qb, ps_=ps_: e.copy(qb[:, c * 512:(c + 1) * 512], ps_[:, :]),
                    reads=[ps_[:]], writes=[qb[:]], tag="l2b.qev")
        kt = keys[h * 2 + pp]
        for st in range(NS):
            ps2 = p.psum(2 + st % 2)
            scf = scfs[st % 2]
            em.emit("T", lambda e, st=st, qb=qb, kt=kt, ps2=ps2: e.matmul(
                ps2[:, 0:NK], qb[:, st * 128:(st + 1) * 128], kt[:],
                start=True, stop=True),
                reads=[qb[:], kt[:]], writes=[ps2[:]], tag="l2b.sim")
            em.emit("V", lambda e, scf=scf, ps2=ps2: e.tensor_copy(scf[:], ps2[:, 0:NK]),
                    reads=[ps2[:]], writes=[scf[:]], tag="l2b.simev")
            o0 = st * 64 + h * 8
            em.emit("V", lambda e, pp=pp, o0=o0, scf=scf: e.max(sc_p[pp][:, o0:o0 + 8], scf[:]),
                    reads=[scf[:]], writes=[sc_p[pp][:]], tag="l2b.max8")
            em.emit("V", lambda e, pp=pp, o0=o0, scf=scf: e.max_index(
                idx_p[pp][:, o0:o0 + 8], sc_p[pp][:, o0:o0 + 8], scf[:]),
                reads=[scf[:], sc_p[pp][:]], writes=[idx_p[pp][:]], tag="l2b.idx8")
    p.free(qbs[0], qbs[1], wqb[0], wqb[1], scfs[0], scfs[1])
    for i in range(16):
        p.free(keys[i])
    p.phase()

    allsc = p.buf("l2b.allsc", [128, 512], F32)
    allidx = p.buf("l2b.allidx", [128, 512], U32)
    tmpu = p.buf("l2b.tmpu", [128, 512], U32)
    scg = p.buf("l2b.scg", [128, 64], F32)
    pkg = p.buf("l2b.pkg", [128, 64], U32)
    rowg = p.buf("l2b.rowg", [128, 64], U32)
    colg = p.buf("l2b.colg", [128, 64], U32)
    accr = p.buf("l2b.accr", [128, 64], U32)
    accc = p.buf("l2b.accc", [128, 64], U32)
    mm_ = p.buf("l2b.mm", [128, 64], U32)
    idxall = p.buf("l2b.idxall", [128, NS * 64], I32)
    wsg = p.buf("l2b.wsg", [128, NS * 64], F32)
    for st in range(NS):
        s0 = st * 64
        A = sc_p[0][:, s0:s0 + 64].rearrange("p (h i) -> p h i", h=8).unsqueeze(3).broadcast_to((128, 8, 8, 8))
        B = sc_p[1][:, s0:s0 + 64].rearrange("p (h j) -> p h j", h=8).unsqueeze(2).broadcast_to((128, 8, 8, 8))
        em.emit("V", lambda e, A=A, B=B: e.tensor_tensor(
            allsc[:].rearrange("p (h i j) -> p h i j", h=8, i=8), A, B, ALU.add),
            reads=[sc_p[0][:], sc_p[1][:]], writes=[allsc[:]], tag="l2b.comb")
        em.emit("V", lambda e, s0=s0: e.tensor_single_scalar(
            tmpu[:, 0:64], idx_p[0][:, s0:s0 + 64], 8, ALU.logical_shift_left),
            reads=[idx_p[0][:]], writes=[tmpu[:]], tag="l2b.shl")
        Ai = tmpu[:, 0:64].rearrange("p (h i) -> p h i", h=8).unsqueeze(3).broadcast_to((128, 8, 8, 8))
        Bi = idx_p[1][:, s0:s0 + 64].rearrange("p (h j) -> p h j", h=8).unsqueeze(2).broadcast_to((128, 8, 8, 8))
        em.emit("V", lambda e, Ai=Ai, Bi=Bi: e.tensor_tensor(
            allidx[:].rearrange("p (h i j) -> p h i j", h=8, i=8), Ai, Bi, ALU.bitwise_or),
            reads=[tmpu[:], idx_p[1][:]], writes=[allidx[:]], tag="l2b.or")
        for h in range(NH):
            em.emit("V", lambda e, h=h: e.max(scg[:, h * 8:(h + 1) * 8],
                                              allsc[:, h * 64:(h + 1) * 64]),
                    reads=[allsc[:]], writes=[scg[:]], tag="l2b.m2")
            em.emit("V", lambda e, h=h: e.max_index(
                pkg[:, h * 8:(h + 1) * 8], scg[:, h * 8:(h + 1) * 8],
                allsc[:, h * 64:(h + 1) * 64]),
                reads=[allsc[:], scg[:]], writes=[pkg[:]], tag="l2b.i2")
        em.emit("V", lambda e: e.tensor_single_scalar(rowg[:, 0:64], pkg[:, 0:64], 3,
                                                      ALU.logical_shift_right),
                reads=[pkg[:]], writes=[rowg[:]], tag="l2b.row")
        em.emit("V", lambda e: e.tensor_single_scalar(colg[:, 0:64], pkg[:, 0:64], 7,
                                                      ALU.bitwise_and),
                reads=[pkg[:]], writes=[colg[:]], tag="l2b.col")
        for w in range(8):
            i0w = tmpu[:, 0:64].rearrange("p (h i) -> p h i", h=8)[:, :, w:w + 1].broadcast_to((128, 8, 8))
            i1w = idx_p[1][:, s0:s0 + 64].rearrange("p (h j) -> p h j", h=8)[:, :, w:w + 1].broadcast_to((128, 8, 8))
            em.emit("V", lambda e, w=w: e.tensor_single_scalar(
                mm_[:, 0:64], rowg[:, 0:64], w, ALU.is_equal),
                reads=[rowg[:]], writes=[mm_[:]], tag="l2b.oh1")
            em.emit("V", lambda e, w=w, i0w=i0w: e.tensor_tensor(
                mm_[:].rearrange("p (h i) -> p h i", h=8),
                mm_[:].rearrange("p (h i) -> p h i", h=8), i0w, ALU.mult),
                reads=[mm_[:], tmpu[:]], writes=[mm_[:]], tag="l2b.oh2")
            if w == 0:
                em.emit("V", lambda e: e.tensor_copy(accr[:], mm_[:]),
                        reads=[mm_[:]], writes=[accr[:]], tag="l2b.oh3")
            else:
                em.emit("V", lambda e: e.tensor_tensor(accr[:], accr[:], mm_[:], ALU.add),
                        reads=[mm_[:], accr[:]], writes=[accr[:]], tag="l2b.oh3")
            em.emit("V", lambda e, w=w: e.tensor_single_scalar(
                mm_[:, 0:64], colg[:, 0:64], w, ALU.is_equal),
                reads=[colg[:]], writes=[mm_[:]], tag="l2b.oh4")
            em.emit("V", lambda e, w=w, i1w=i1w: e.tensor_tensor(
                mm_[:].rearrange("p (h i) -> p h i", h=8),
                mm_[:].rearrange("p (h i) -> p h i", h=8), i1w, ALU.mult),
                reads=[mm_[:], idx_p[1][:]], writes=[mm_[:]], tag="l2b.oh5")
            if w == 0:
                em.emit("V", lambda e: e.tensor_copy(accc[:], mm_[:]),
                        reads=[mm_[:]], writes=[accc[:]], tag="l2b.oh6")
            else:
                em.emit("V", lambda e: e.tensor_tensor(accc[:], accc[:], mm_[:], ALU.add),
                        reads=[mm_[:], accc[:]], writes=[accc[:]], tag="l2b.oh6")
        em.emit("V", lambda e: e.tensor_tensor(accr[:], accr[:], accc[:], ALU.bitwise_or),
                reads=[accr[:], accc[:]], writes=[accr[:]], tag="l2b.fin")
        em.emit("V", lambda e, s0=s0: e.tensor_copy(idxall[:, s0:s0 + 64], accr[:]),
                reads=[accr[:]], writes=[idxall[:]], tag="l2b.fin2")
        em.emit("S", lambda e, s0=s0: e.activation(wsg[:, s0:s0 + 64], scg[:, 0:64], AF.Sigmoid),
                reads=[scg[:]], writes=[wsg[:]], tag="l2b.sig")
    p.free(allsc, allidx, tmpu, scg, pkg, rowg, colg, accr, accc, mm_)
    p.free(sc_p[0], sc_p[1], idx_p[0], idx_p[1])
    p.phase()

    NB = 6
    xtm = [p.buf(f"l2b.xtm{i}", [128, D], BF16) for i in range(2)]
    gd = [p.buf(f"l2b.gd{i}", [128, D], BF16) for i in range(NB)]
    gu = [p.buf(f"l2b.gu{i}", [128, D], BF16) for i in range(NB)]
    hmid = [p.buf(f"l2b.hmid{i}", [128, 64], F32) for i in range(2)]
    wfin = [p.buf(f"l2b.wfin{i}", [128, 64], F32) for i in range(2)]
    dscr = p.buf("l2b.dscr", [128, D], F32)
    dg = [p.buf(f"l2b.dg{i}", [128, 128], BF16) for i in range(2)]
    otm = [p.buf(f"l2b.otm{i}", [128, D], F32) for i in range(2)]
    for st in range(NS):
        pi = st % 2
        xb, hm, wf, ot = xtm[pi], hmid[pi], wfin[pi], otm[pi]
        pa, pb = (p.psum(0), p.psum(1)) if pi == 0 else (p.psum(2), p.psum(3))
        em.emit("Q", lambda e, st=st, xb=xb: e.dma_start(
            out=xb[:], in_=xn_tm_in[st * 128:(st + 1) * 128, :]),
            writes=[xb[:]], dma=True, tag="l2b.xtmld")
        for j in range(64):
            g = gd[j % NB]
            em.emit("G", lambda e, st=st, j=j, g=g: e.indirect_dma_start(
                g[:], None, down_d,
                IndirectOffsetOnAxis(ap=idxall[:, st * 64 + j:st * 64 + j + 1], axis=0)),
                reads=[idxall[:]], writes=[g[:]], dma=True, tag="l2b.gdown")
            em.emit("V", lambda e, j=j, g=g, xb=xb, hm=hm: e.scalar_tensor_tensor(
                dscr[:], g[:], 1.0, xb[:], ALU.mult, ALU.mult,
                accum_out=hm[:, j:j + 1]),
                reads=[g[:], xb[:]], writes=[dscr[:], hm[:]], tag="l2b.dot")
        em.emit("S", lambda e, hm=hm, wf=wf: e.activation(wf[:], hm[:], AF.Gelu),
                reads=[hm[:]], writes=[wf[:]], tag="l2b.gelu")
        em.emit("V", lambda e, st=st, wf=wf: e.tensor_tensor(
            wf[:], wf[:], wsg[:, st * 64:(st + 1) * 64], ALU.mult),
            reads=[wf[:], wsg[:]], writes=[wf[:]], tag="l2b.wm")
        for j in range(64):
            g = gu[j % NB]
            d_ = dg[j % 2]
            em.emit("G", lambda e, st=st, j=j, g=g: e.indirect_dma_start(
                g[:], None, up_d,
                IndirectOffsetOnAxis(ap=idxall[:, st * 64 + j:st * 64 + j + 1], axis=0)),
                reads=[idxall[:]], writes=[g[:]], dma=True, tag="l2b.gup")
            em.emit("V", lambda e, j=j, d_=d_, wf=wf: e.tensor_tensor(
                d_[:], wf[:, j:j + 1].broadcast_to((128, 128)), p.ident_bf[:], ALU.mult),
                reads=[wf[:], p.ident_bf[:]], writes=[d_[:]], tag="l2b.dg")

            def mmup(e, j=j, g=g, d_=d_, pa=pa, pb=pb):
                e.matmul(pa[:, :], d_[:], g[:, 0:512],
                         start=(j == 0), stop=(j == 63))
                return e.matmul(pb[:, :], d_[:], g[:, 512:1024],
                                start=(j == 0), stop=(j == 63))
            em.emit("T", mmup, reads=[d_[:], g[:]],
                    writes=[pa[:], pb[:]], tag="l2b.upmm")
        for hf in range(2):
            ph = pa if hf == 0 else pb
            em.emit("V", lambda e, hf=hf, ph=ph, ot=ot: e.tensor_copy(
                ot[:, hf * 512:(hf + 1) * 512], ph[:, :]),
                reads=[ph[:]], writes=[ot[:]], tag="l2b.oev")
        for k in range(ND):
            ps2 = p.psum(4 + k % 2)
            em.emit("T", lambda e, k=k, ps2=ps2, ot=ot: e.transpose(
                ps2[:, 0:128], ot[:, k * 128:(k + 1) * 128], p.ident[:]),
                reads=[ot[:], p.ident[:]], writes=[ps2[:]], tag="l2b.otp")
            em.emit("V", lambda e, k=k, st=st, ps2=ps2, t=xt[k]: e.tensor_tensor(
                t[:, st * 128:(st + 1) * 128], t[:, st * 128:(st + 1) * 128],
                ps2[:, 0:128], ALU.add),
                reads=[ps2[:], xt[k][:]], writes=[xt[k][:]], tag="l2b.oadd")
    p.free(xtm[0], xtm[1], hmid[0], hmid[1], wfin[0], wfin[1], dscr,
           dg[0], dg[1], otm[0], otm[1], *gd, *gu)
    for k in range(ND):
        p.free(xnb[k])
    p.free(idxall, wsg, bns, sgam, sbet, tots, mean, var, acoef, bcoef)
    p.phase()




def emit_l3(p, xt, y_t):
    nc, em = p.nc, p.em
    DH, NDH = 4096, 32
    tdec = p.din("l3_tdec", [64, 1])
    tfir = p.din("l3_tfir", [64, 1])
    w_k = p.din("l3_wk", [1, 128, ND * 64])
    w_v = p.din("l3_wv", [1, 128, ND * 64])
    w_o = p.din("l3_wo", [ND, 64, 128])
    tsg = p.din("l3_tsg", [128, ND])
    lnw = p.din("l3_lnw", [128, ND])
    lnb = p.din("l3_lnb", [128, ND])
    w1d = p.din("l3_w1", [NDH, 128, ND * 128])
    w2d = p.din("l3_w2", [NDH, 128, ND * 128])
    h1d = p.din("l3_h1", [NDH, 128, NDH * 128])
    h2d = p.din("l3_h2", [NDH, 128, NDH * 128])
    w3d = p.din("l3_w3", [ND, 128, NDH * 128])
    s1 = p.din("l3_s1", [128, NDH])
    s2 = p.din("l3_s2", [128, NDH])
    sh1 = p.din("l3_sh1", [128, NDH])
    sh2 = p.din("l3_sh2", [128, NDH])
    s3 = p.din("l3_s3", [128, ND])

    sm = {}
    for nm, srcd, sh in (("tdec", tdec, [64, 1]), ("tfir", tfir, [64, 1]),
                         ("tsg", tsg, [128, ND]), ("lnw", lnw, [128, ND]),
                         ("lnb", lnb, [128, ND]), ("s1", s1, [128, NDH]),
                         ("s2", s2, [128, NDH]), ("sh1", sh1, [128, NDH]),
                         ("sh2", sh2, [128, NDH]), ("s3", s3, [128, ND])):
        dst = sm[nm] = p.buf("l3s." + nm, sh, F32)
        em.emit("Q", lambda e, d=dst, s=srcd: e.dma_start(out=d[:], in_=s),
                writes=[dst[:]], dma=True)

    xn = [p.buf(f"l3.xn{k}", [128, T], BF16) for k in range(ND)]
    p.layernorm(xt, T, xn, tag="l3ln")
    for k in range(ND):
        em.emit("Q", lambda e, k=k, t=xt[k]: e.dma_start(
            out=y_t[k * 128:(k + 1) * 128, :], in_=t[:]),
            reads=[xt[k][:]], writes=[f"o.y.{k}"], dma=True)
    for k in range(ND):
        p.free(xt[k])
        xt[k] = None
    p.phase()

    gcol = p.buf("l3.gcol", [128, ND], F32)
    em.emit("S", lambda e: e.activation(gcol[:], sm["tsg"][:], AF.Sigmoid),
            reads=[sm["tsg"][:]], writes=[gcol[:]])
    SH = T // 2
    xs = [p.buf(f"l3.xs{k}", [128, T], BF16) for k in range(ND)]
    dtm = p.buf("l3.dtm", [128, T], F32)
    for k in range(ND):
        em.emit("V", lambda e, k=k, t=xn[k]: e.tensor_tensor(
            dtm[:, 0:SH], t[:, SH:T], t[:, 0:SH], ALU.subtract),
            reads=[xn[k][:]], writes=[dtm[:]], tag="l3.sh1")
        em.emit("V", lambda e, k=k, t=xn[k]: e.tensor_tensor(
            dtm[:, SH:T], t[:, 0:SH], t[:, SH:T], ALU.subtract),
            reads=[xn[k][:]], writes=[dtm[:]], tag="l3.sh2")
        em.emit("V", lambda e, k=k, t=xn[k], o=xs[k]: e.scalar_tensor_tensor(
            o[:], dtm[:], gcol[:, k:k + 1], t[:], ALU.mult, ALU.add),
            reads=[dtm[:], gcol[:], xn[k][:]], writes=[xs[k][:]], tag="l3.shm")
    p.free(dtm)
    xm = [p.buf(f"l3.xm{k}", [128, T], BF16) for k in range(ND)]
    p.layernorm(xs, T, xm, tag="l3ln2")
    for k in range(ND):
        em.emit("V", lambda e, k=k, t=xm[k]: e.tensor_scalar(
            t[:], t[:], sm["lnw"][:, k:k + 1], sm["lnb"][:, k:k + 1], ALU.mult, ALU.add),
            reads=[xm[k][:], sm["lnw"][:], sm["lnb"][:]], writes=[xm[k][:]], tag="l3.lnwb")
    for k in range(ND):
        p.free(xs[k])
    bscr3 = p.buf("l3.bscr3", [128, 512], F32)
    kv = p.buf("l3.k", [64, T], F32)
    vv = p.buf("l3.v", [64, T], F32)

    def evac_k3(ot, c0, c1, ps_):
        em.emit("V", lambda e: e.tensor_copy(kv[:, c0:c1], ps_[0:64, 0:c1 - c0]),
                reads=[ps_[:]], writes=[kv[:]], tag="l3.kev")
    p.matmul_to(w_k, ND, 1, xm, T, evac_k3, tag="l3k", ow=64)

    def evac_v3(ot, c0, c1, ps_):
        em.emit("V", lambda e: e.tensor_copy(vv[:, c0:c1], ps_[0:64, 0:c1 - c0]),
                reads=[ps_[:]], writes=[vv[:]], tag="l3.vev")
    p.matmul_to(w_v, ND, 1, xm, T, evac_v3, tag="l3v", ow=64)
    etf = p.buf("l3.etf", [64, 1], F32)
    em.emit("S", lambda e: e.activation(etf[:], sm["tfir"][0:64, :], AF.Exp),
            reads=[sm["tfir"][:]], writes=[etf[:]])
    em.emit("V", lambda e: e.tensor_scalar_mul(etf[:], etf[:], -1.0),
            reads=[etf[:]], writes=[etf[:]])
    em.emit("S", lambda e: e.activation(kv[:], kv[:], AF.Exp, scale=etf[0:64, :]),
            reads=[kv[:], etf[:]], writes=[kv[:]], tag="l3.wkv1")
    em.emit("V", lambda e: e.tensor_tensor(kv[:], kv[:], vv[:], ALU.mult),
            reads=[kv[:], vv[:]], writes=[kv[:]], tag="l3.wkv2")
    wdec = p.buf("l3.wdec", [64, 1], F32)
    em.emit("S", lambda e: e.activation(wdec[:], sm["tdec"][0:64, :], AF.Exp),
            reads=[sm["tdec"][:]], writes=[wdec[:]])
    wdb = p.buf("l3.wdb", [64, T], F32)
    em.emit("V", lambda e: e.tensor_copy(wdb[:], wdec[0:64, :].broadcast_to((64, T))),
            reads=[wdec[:]], writes=[wdb[:]], tag="l3.wdb")
    em.emit("V", lambda e: e.tensor_tensor_scan(vv[:], wdb[:], kv[:], 0.0, ALU.mult, ALU.add),
            reads=[wdb[:], kv[:]], writes=[vv[:]], tag="l3.scan")
    vvb = p.buf("l3.vvb", [64, T], BF16)
    em.emit("V", lambda e: e.tensor_copy(vvb[:], vv[:]), reads=[vv[:]], writes=[vvb[:]])
    wob = p.buf("l3.wob", [64, ND * 128], BF16)
    em.emit("G", lambda e: e.dma_start(out=wob[:].rearrange("b (a c) -> b a c", a=ND), in_=w_o.rearrange("a b c -> b a c")),
            writes=[wob[:]], dma=True, tag="l3.wold")
    for ot in range(ND):
        for c in range(4):
            ps_ = p.psum((ot * 4 + c) % 2)
            em.emit("T", lambda e, ot=ot, c=c, ps_=ps_: e.matmul(
                ps_[:, :], wob[0:64, ot * 128:(ot + 1) * 128],
                vvb[0:64, c * 512:(c + 1) * 512], start=True, stop=True),
                reads=[wob[:], vvb[:]], writes=[ps_[:]], tag="l3.omm")
            em.emit("V", lambda e, ps_=ps_: e.tensor_copy(bscr3[:], ps_[:, :]),
                    reads=[ps_[:]], writes=[bscr3[:]], tag="l3.oev")
            em.emit("G", lambda e, ot=ot, c=c: e.dma_start(
                out=y_t[ot * 128:(ot + 1) * 128, c * 512:(c + 1) * 512],
                in_=bscr3[:], accum_op=ALU.add),
                reads=[bscr3[:]], writes=[f"o.y.{ot}"], dma=True, tag="l3.oacc")
    for k in range(ND):
        p.free(xm[k])
    p.free(kv, vv, vvb, wob, wdb, wdec, etf, gcol, bscr3)
    p.phase()

    och = p.buf("l3.och", [128, 512], F32)
    bscr = p.buf("l3.bscr", [128, 512], F32)
    for half in range(2):
        h0 = half * (T // 2)
        TH = T // 2
        xnh = [xn[k][:, h0:h0 + TH] for k in range(ND)]
        hT = [p.buf(f"l3.h{j}", [128, TH], BF16) for j in range(NDH)]
        w1b = [p.buf(p.u("l3.w1b"), [128, ND * 128], BF16) for _ in range(2)]
        w2b = [p.buf(p.u("l3.w2b"), [128, ND * 128], BF16) for _ in range(2)]
        for ot in range(NDH):
            b1, b2 = w1b[ot % 2], w2b[ot % 2]
            em.emit("G", lambda e, ot=ot, b1=b1: e.dma_start(out=b1[:], in_=w1d[ot]),
                    writes=[b1[:]], dma=True, tag="l3.w1ld")
            em.emit("G", lambda e, ot=ot, b2=b2: e.dma_start(out=b2[:], in_=w2d[ot]),
                    writes=[b2[:]], dma=True, tag="l3.w2ld")
            for c in range(TH // 512):
                c0 = c * 512
                psA, psB = p.psum(0), p.psum(1)

                def mm1(e, b1=b1, c0=c0, xs_=tuple(xn), psA=psA, h0=h0):
                    last = None
                    for k in range(ND):
                        last = e.matmul(psA[:, :], b1[:, k * 128:(k + 1) * 128],
                                        xs_[k][:, h0 + c0:h0 + c0 + 512],
                                        start=(k == 0), stop=(k == ND - 1))
                    return last
                em.emit("T", mm1, reads=[b1[:]] + [x[:] for x in xn], writes=[psA[:]],
                        tag="l3.mm1")

                def mm2(e, b2=b2, c0=c0, xs_=tuple(xn), psB=psB, h0=h0):
                    last = None
                    for k in range(ND):
                        last = e.matmul(psB[:, :], b2[:, k * 128:(k + 1) * 128],
                                        xs_[k][:, h0 + c0:h0 + c0 + 512],
                                        start=(k == 0), stop=(k == ND - 1))
                    return last
                em.emit("T", mm2, reads=[b2[:]] + [x[:] for x in xn], writes=[psB[:]],
                        tag="l3.mm2")
                em.emit("S", lambda e, ot=ot, psA=psA: e.activation(
                    bscr[:], psA[:, :], AF.Silu, scale=sm["s1"][:, ot:ot + 1]),
                    reads=[psA[:], sm["s1"][:]], writes=[bscr[:]], tag="l3.silu")
                em.emit("V", lambda e, ot=ot, psB=psB: e.tensor_scalar(
                    och[:], psB[:, :], sm["s2"][:, ot:ot + 1], None, ALU.mult),
                    reads=[psB[:], sm["s2"][:]], writes=[och[:]], tag="l3.s2m")
                em.emit("V", lambda e, ot=ot, c0=c0, ht=hT[ot]: e.tensor_tensor(
                    ht[:, c0:c0 + 512], bscr[:], och[:], ALU.mult),
                    reads=[bscr[:], och[:]], writes=[hT[ot][:]], tag="l3.hmul")
        p.free(w1b[0], w1b[1], w2b[0], w2b[1])
        g1 = [p.buf(f"l3.g1{j}", [128, TH], BF16) for j in range(NDH)]

        def big_mm(wdram, scol, src_tiles, dst_tiles, tag):
            wb = [p.buf(p.u(tag + ".w"), [128, NDH * 128], BF16) for _ in range(2)]
            for ot in range(NDH):
                b = wb[ot % 2]
                em.emit("G", lambda e, ot=ot, b=b: e.dma_start(out=b[:], in_=wdram[ot]),
                        writes=[b[:]], dma=True, tag=tag + ".ld")
                for c in range(TH // 512):
                    c0 = c * 512
                    ps_ = p.psum((ot * (TH // 512) + c) % 2)

                    def mmb(e, b=b, c0=c0, ts=tuple(src_tiles), ps_=ps_):
                        last = None
                        for k in range(NDH):
                            last = e.matmul(ps_[:, :], b[:, k * 128:(k + 1) * 128],
                                            ts[k][:, c0:c0 + 512],
                                            start=(k == 0), stop=(k == NDH - 1))
                        return last
                    em.emit("T", mmb, reads=[b[:]] + [t[:] for t in src_tiles],
                            writes=[ps_[:]], tag=tag + ".mm")
                    em.emit("S", lambda e, ot=ot, c0=c0, dt_=dst_tiles[ot], ps_=ps_: e.activation(
                        dt_[:, c0:c0 + 512], ps_[:, :], AF.Gelu, scale=scol[:, ot:ot + 1]),
                        reads=[ps_[:], scol], writes=[dst_tiles[ot][:]], tag=tag + ".ev")
            p.free(wb[0], wb[1])
        big_mm(h1d, sm["sh1"][:], hT, g1, "l3.h1")
        big_mm(h2d, sm["sh2"][:], g1, hT, "l3.h2")
        for j in range(NDH):
            p.free(g1[j])
        w3b = [p.buf(p.u("l3.w3b"), [128, NDH * 128], BF16) for _ in range(2)]
        for ot in range(ND):
            b = w3b[ot % 2]
            em.emit("G", lambda e, ot=ot, b=b: e.dma_start(out=b[:], in_=w3d[ot]),
                    writes=[b[:]], dma=True, tag="l3.w3ld")
            for c in range(TH // 512):
                c0 = c * 512
                ps_ = p.psum((ot * (TH // 512) + c) % 2)

                def mm3(e, b=b, c0=c0, ts=tuple(hT), ps_=ps_):
                    last = None
                    for k in range(NDH):
                        last = e.matmul(ps_[:, :], b[:, k * 128:(k + 1) * 128],
                                        ts[k][:, c0:c0 + 512],
                                        start=(k == 0), stop=(k == NDH - 1))
                    return last
                em.emit("T", mm3, reads=[b[:]] + [t[:] for t in hT], writes=[ps_[:]],
                        tag="l3.mm3")
                em.emit("V", lambda e, ot=ot, ps_=ps_: e.tensor_scalar(
                    och[:], ps_[:, :], sm["s3"][:, ot:ot + 1], None, ALU.mult),
                    reads=[ps_[:], sm["s3"][:]], writes=[och[:]], tag="l3.oev")
                em.emit("G", lambda e, ot=ot, c0=c0, h0=h0: e.dma_start(
                    out=y_t[ot * 128:(ot + 1) * 128, h0 + c0:h0 + c0 + 512],
                    in_=och[:], accum_op=ALU.add),
                    reads=[och[:]], writes=[f"o.y.{ot}"], dma=True, tag="l3.oacc")
        p.free(w3b[0], w3b[1])
        for j in range(NDH):
            p.free(hT[j])
        p.phase()
    for k in range(ND):
        p.free(xn[k])
    for v in sm.values():
        p.free(v)
    p.phase()

def build_chain(emitters):
    p = Prog()
    em = p.em
    x_t = p.din("x_t", [D, T])
    y_t = p.dout("y_t", [D, T])
    p.consts()
    xt = [p.buf(f"x{k}", [128, T], F32) for k in range(ND)]
    for k in range(ND):
        em.emit("Q", lambda e, k=k, t=xt[k]: e.dma_start(
            out=t[:], in_=x_t[k * 128:(k + 1) * 128, :]),
            writes=[xt[k][:]], dma=True)
    for emit_fn in emitters:
        emit_fn(p, xt, y_t)
    for k in range(ND):
        if xt[k] is not None:
            em.emit("Q", lambda e, k=k: e.dma_start(out=y_t[k * 128:(k + 1) * 128, :],
                                                    in_=xt[k][:]),
                    reads=[xt[k][:]], writes=[f"o.y.{k}"], dma=True)
    return p.finish()



def colblocks(wt):
    K, O = wt.shape
    nK, nO = K // 128, O // 128
    a = wt.reshape(nK, 128, nO, 128).transpose(2, 1, 0, 3).reshape(nO, 128, nK * 128)
    return np.ascontiguousarray(a)


def pcol(v):
    n = v.shape[0]
    return np.ascontiguousarray(v.reshape(n // 128, 128).T)


def prep_l0_weights(ins):
    f = lambda k: np.ascontiguousarray(np.asarray(ins[k], dtype=np.float32))
    w = {}
    w["l0_w_in"] = colblocks(np.ascontiguousarray(f("L0_attn_in_proj_w").T))
    xpt = np.ascontiguousarray(f("L0_attn_x_proj_w").T)
    K = xpt.shape[0]
    w["l0_w_xp"] = np.ascontiguousarray(
        xpt.reshape(K // 128, 128, 96).transpose(1, 0, 2).reshape(1, 128, (K // 128) * 96))
    w["l0_w_out"] = colblocks(np.ascontiguousarray(f("L0_attn_out_proj_w").T))
    w["l0_w_dt"] = np.ascontiguousarray(f("L0_attn_dt_proj_w").reshape(16, 128, 64))
    w["l0_dt_b"] = np.ascontiguousarray(f("L0_attn_dt_proj_b").reshape(1, 2048))
    cw = f("L0_attn_conv_w").reshape(2048, 3)
    w["l0_conv_w"] = np.ascontiguousarray(cw.reshape(16, 128, 3).transpose(1, 0, 2).reshape(128, 48))
    w["l0_conv_b"] = pcol(f("L0_attn_conv_b"))
    w["l0_dpar"] = pcol(f("L0_attn_D_param"))
    w["l0_w_r"] = colblocks(np.ascontiguousarray(f("L0_ffn_W_r").T))
    w["l0_w_k"] = colblocks(np.ascontiguousarray(f("L0_ffn_W_k").T))
    w["l0_w_v"] = colblocks(np.ascontiguousarray(f("L0_ffn_W_v").T))
    w["l0_mu_r"] = pcol(f("L0_ffn_mu_r"))
    w["l0_mu_k"] = pcol(f("L0_ffn_mu_k"))
    return w


def prep_l1_weights(ins):
    f = lambda k: np.ascontiguousarray(np.asarray(ins[k], dtype=np.float32))
    w = {}
    w["l1_pr"] = pcol(f("L1_attn_phazor_real"))
    w["l1_pi"] = pcol(f("L1_attn_phazor_imag"))
    w["l1_ar"] = pcol(f("L1_attn_phazor_init_real"))
    w["l1_ai"] = pcol(f("L1_attn_phazor_init_imag"))
    w["l1_lr"] = pcol(f("L1_attn_last_conv_real"))
    w["l1_li"] = pcol(f("L1_attn_last_conv_imag"))
    w["l1_n1w"] = pcol(f("L1_ffn_norm1_w"))
    w["l1_w1"] = colblocks(np.ascontiguousarray(f("L1_ffn_w1").T))
    w["l1_asc"] = f("L1_ffn_act_scale").reshape(1, 1)
    w["l1_abi"] = f("L1_ffn_act_bias").reshape(1, 1)
    w["l1_n2w"] = pcol(f("L1_ffn_norm2_w"))
    w["l1_w2"] = colblocks(np.ascontiguousarray(f("L1_ffn_w2").T))
    return w


def prep_l2a_weights(ins):
    f = lambda k: np.ascontiguousarray(np.asarray(ins[k], dtype=np.float32))
    w = {}
    iwc, ibc = f("L2_attn_in_w_cls"), f("L2_attn_in_b_cls")
    iwl, ibl = f("L2_attn_in_w_loc"), f("L2_attn_in_b_loc")
    E = 1024
    w["l2_wq_c"] = colblocks(np.ascontiguousarray(iwc[0:E].T))
    w["l2_wk_c"] = colblocks(np.ascontiguousarray(iwc[E:2 * E].T))
    w["l2_wv_ct"] = np.ascontiguousarray(iwc[2 * E:3 * E].T.reshape(8, 128, E))
    w["l2_wq_l"] = colblocks(np.ascontiguousarray(iwl[0:E].T))
    w["l2_wk_l"] = colblocks(np.ascontiguousarray(iwl[E:2 * E].T))
    w["l2_wv_lt"] = np.ascontiguousarray(iwl[2 * E:3 * E].T.reshape(8, 128, E))
    w["l2_wo_c"] = colblocks(np.ascontiguousarray(f("L2_attn_out_w_cls").T))
    w["l2_wo_l"] = colblocks(np.ascontiguousarray(f("L2_attn_out_w_loc").T))
    w["l2_bqc"] = pcol(ibc[0:E]); w["l2_bkc"] = pcol(ibc[E:2 * E])
    w["l2_bvc"] = pcol(ibc[2 * E:3 * E]); w["l2_boc"] = pcol(f("L2_attn_out_b_cls"))
    w["l2_bql"] = pcol(ibl[0:E]); w["l2_bkl"] = pcol(ibl[E:2 * E])
    w["l2_bvl"] = pcol(ibl[2 * E:3 * E]); w["l2_bol"] = pcol(f("L2_attn_out_b_loc"))
    return w


def run_l2a(inputs, x, ncores=NCORES, raw=False):
    w = prep_l2a_weights(inputs)
    nc = build_chain([emit_l2a])
    in_maps = []
    for c in range(ncores):
        m = {"x_t": np.ascontiguousarray(x[c].T)}
        m.update(w)
        in_maps.append(m)
    res = run_bass_kernel_spmd(nc, in_maps, core_ids=list(range(ncores)))
    if raw:
        return res
    return (np.stack([np.asarray(res.results[c]["y_t"]).T for c in range(ncores)]),
            np.stack([np.asarray(res.results[c]["xn_t"]) for c in range(ncores)]),
            np.stack([np.asarray(res.results[c]["xn_tm_t"]) for c in range(ncores)]),
            np.stack([np.asarray(res.results[c]["bnsum"]) for c in range(ncores)]))


def prep_l2b_weights(ins):
    f = lambda k: np.ascontiguousarray(np.asarray(ins[k], dtype=np.float32))
    w = {}
    w["l2_wq"] = colblocks(np.ascontiguousarray(f("L2_ffn_w_q").T))
    w["l2_keys"] = np.ascontiguousarray(
        f("L2_ffn_keys_p").transpose(0, 2, 3, 1).reshape(16, 128, 256))
    w["l2_gam"] = pcol(f("L2_ffn_bn_gamma"))
    w["l2_bet"] = pcol(f("L2_ffn_bn_beta"))
    w["l2_down"] = f("L2_ffn_down")
    w["l2_up"] = f("L2_ffn_up")
    return w


def run_l2b(inputs, x3p, xn_t, xn_tm, bnsum, ncores=NCORES, raw=False):
    w = prep_l2b_weights(inputs)
    bn_cat = np.ascontiguousarray(np.concatenate(list(bnsum), axis=1))
    nc = build_chain([emit_l2b])
    in_maps = []
    for c in range(ncores):
        m = {"x_t": np.ascontiguousarray(x3p[c].T), "xn_in": xn_t[c],
             "xn_tm_in": xn_tm[c], "bn_all": bn_cat}
        m.update(w)
        in_maps.append(m)
    res = run_bass_kernel_spmd(nc, in_maps, core_ids=list(range(ncores)))
    if raw:
        return res
    return np.stack([np.asarray(res.results[c]["y_t"]).T for c in range(ncores)])


def prep_l3_weights(ins):
    f = lambda k: np.ascontiguousarray(np.asarray(ins[k], dtype=np.float32))
    w = {}
    w["l3_tdec"] = f("L3_attn_time_decay").reshape(64, 1)
    w["l3_tfir"] = f("L3_attn_time_first").reshape(64, 1)
    kT = np.ascontiguousarray(f("L3_attn_key_w").T)
    w["l3_wk"] = np.ascontiguousarray(
        kT.reshape(8, 128, 64).transpose(1, 0, 2).reshape(1, 128, 8 * 64))
    vT = np.ascontiguousarray(f("L3_attn_value_w").T)
    w["l3_wv"] = np.ascontiguousarray(
        vT.reshape(8, 128, 64).transpose(1, 0, 2).reshape(1, 128, 8 * 64))
    w["l3_wo"] = np.ascontiguousarray(f("L3_attn_output_w").T.reshape(64, 8, 128).transpose(1, 0, 2))
    w["l3_tsg"] = pcol(f("L3_attn_time_shift_gates"))
    w["l3_lnw"] = pcol(f("L3_attn_ln_w"))
    w["l3_lnb"] = pcol(f("L3_attn_ln_b"))
    w["l3_w1"] = colblocks(np.ascontiguousarray(f("L3_ffn_w1_w").T))
    w["l3_w2"] = colblocks(np.ascontiguousarray(f("L3_ffn_w2_w").T))
    w["l3_h1"] = colblocks(np.ascontiguousarray(f("L3_ffn_h1_w").T))
    w["l3_h2"] = colblocks(np.ascontiguousarray(f("L3_ffn_h2_w").T))
    w["l3_w3"] = colblocks(np.ascontiguousarray(f("L3_ffn_w3_w").T))
    w["l3_s1"] = pcol(f("L3_ffn_w1_s"))
    w["l3_s2"] = pcol(f("L3_ffn_w2_s"))
    w["l3_sh1"] = pcol(f("L3_ffn_h1_s"))
    w["l3_sh2"] = pcol(f("L3_ffn_h2_s"))
    w["l3_s3"] = pcol(f("L3_ffn_w3_s"))
    return w


def run_l3(inputs, x, ncores=NCORES, raw=False):
    w = prep_l3_weights(inputs)
    nc = build_chain([emit_l3])
    in_maps = []
    for c in range(ncores):
        m = {"x_t": np.ascontiguousarray(x[c].T)}
        m.update(w)
        in_maps.append(m)
    res = run_bass_kernel_spmd(nc, in_maps, core_ids=list(range(ncores)))
    if raw:
        return res
    return np.stack([np.asarray(res.results[c]["y_t"]).T for c in range(ncores)])


def run_l1(inputs, x, ncores=NCORES, raw=False):
    w = prep_l1_weights(inputs)
    nc = build_chain([emit_l1])
    in_maps = []
    for c in range(ncores):
        m = {"x_t": np.ascontiguousarray(x[c].T)}
        m.update(w)
        in_maps.append(m)
    res = run_bass_kernel_spmd(nc, in_maps, core_ids=list(range(ncores)))
    if raw:
        return res
    return np.stack([np.asarray(res.results[c]["y_t"]).T for c in range(ncores)])


def run_l0(inputs, x, ncores=NCORES, raw=False):
    w = prep_l0_weights(inputs)
    nc = build_chain([emit_l0])
    in_maps = []
    for c in range(ncores):
        m = {"x_t": np.ascontiguousarray(x[c].T)}
        m.update(w)
        in_maps.append(m)
    res = run_bass_kernel_spmd(nc, in_maps, core_ids=list(range(ncores)))
    if raw:
        return res
    return np.stack([np.asarray(res.results[c]["y_t"]).T for c in range(ncores)])


import functools


def kernel(**inputs):
    x = np.asarray(inputs["L0_attn_x"], dtype=np.float32)
    w = {}
    w.update(prep_l0_weights(inputs))
    w.update(prep_l1_weights(inputs))
    w.update(prep_l2a_weights(inputs))
    nc1 = build_chain([emit_l0, functools.partial(emit_l1, reload=True), emit_l2a])
    in_maps = []
    for c in range(NCORES):
        m = {"x_t": np.ascontiguousarray(x[c].T)}
        m.update(w)
        in_maps.append(m)
    res1 = run_bass_kernel_spmd(nc1, in_maps, core_ids=list(range(NCORES)))
    y1 = [np.asarray(res1.results[c]["y_t"]) for c in range(NCORES)]
    xn_t = [np.asarray(res1.results[c]["xn_t"]) for c in range(NCORES)]
    xn_tm = [np.asarray(res1.results[c]["xn_tm_t"]) for c in range(NCORES)]
    bn_cat = np.ascontiguousarray(np.concatenate(
        [np.asarray(res1.results[c]["bnsum"]) for c in range(NCORES)], axis=1))
    w2 = {}
    w2.update(prep_l2b_weights(inputs))
    w2.update(prep_l3_weights(inputs))
    nc2 = build_chain([emit_l2b, emit_l3])
    in_maps2 = []
    for c in range(NCORES):
        m = {"x_t": y1[c], "xn_in": xn_t[c], "xn_tm_in": xn_tm[c], "bn_all": bn_cat}
        m.update(w2)
        in_maps2.append(m)
    res2 = run_bass_kernel_spmd(nc2, in_maps2, core_ids=list(range(NCORES)))
    x4 = np.stack([np.asarray(res2.results[c]["y_t"]).T for c in range(NCORES)])
    return np.ascontiguousarray(x4.astype(np.float32))


# version 07
# speedup vs baseline: 1.0044x; 1.0044x over previous
import contextlib
import math
import numpy as np
import concourse.bass as bass
import concourse.mybir as mybir
from concourse.bass_utils import run_bass_kernel_spmd
from concourse.bass_core_rust import IndirectOffsetOnAxis

F32 = mybir.dt.float32
BF16 = mybir.dt.bfloat16
I32 = mybir.dt.int32
I16 = mybir.dt.int16
U32 = mybir.dt.uint32
U8 = mybir.dt.uint8
F16 = mybir.dt.float16
AF = mybir.ActivationFunctionType
ALU = mybir.AluOpType
AX = mybir.AxisListType

DBG = []
SKIP = set()
NCORES = 8
T = 2048
D = 1024
ND = D // 128

ENGINES = ("T", "V", "S", "G", "Q")
BODY = {"T": "tensor", "V": "vector", "S": "scalar", "G": "gpsimd", "Q": "sync"}
SELF_SYNC = {"T": True, "V": True, "S": True, "G": True, "Q": True}


DMA_SEM_K = 8


class Em:

    def __init__(self, nc):
        self.nc = nc
        self.ops = {e: [] for e in ENGINES}
        self.cnt = {}
        for e in ENGINES:
            self.cnt[e] = 0
        self.dma_i = {e: 0 for e in ENGINES}
        self.res = {}
        self.done = {e: {} for e in ENGINES}
        self.floor = {}

    @staticmethod
    def key(r):
        return r if isinstance(r, str) else r.name

    def emit(self, eng, fn, reads=(), writes=(), dma=False, tag=""):
        need = dict(self.floor)

        def want(f, c):
            if f == eng and not SELF_SYNC[eng]:
                return
            if need.get(f, 0) < c:
                need[f] = c
        for r in reads:
            st = self.res.get(self.key(r))
            if st and st["w"]:
                want(*st["w"])
        for w in writes:
            st = self.res.get(self.key(w))
            if st:
                if st["w"]:
                    want(*st["w"])
                for rd in st["r"]:
                    want(*rd)
        if dma:
            s = f"{eng}d{self.dma_i[eng] % DMA_SEM_K}"
            self.dma_i[eng] += 1
            prev = self.cnt.get(s, 0)
            if prev:
                want(s, prev)
            self.cnt[s] = prev + 16
            mark = (s, prev + 16)
        else:
            self.cnt[eng] += 1
            mark = (eng, self.cnt[eng])
        waits = []
        for f, v in sorted(need.items()):
            if v > 0 and self.done[eng].get(f, 0) < v:
                waits.append((f, v))
                self.done[eng][f] = v
        self.ops[eng].append((waits, fn, mark, tag))
        for r in reads:
            k = self.key(r)
            self.res.setdefault(k, {"w": None, "r": []})["r"].append(mark)
        for w in writes:
            self.res[self.key(w)] = {"w": mark, "r": []}
        return mark

    def barrier(self):
        self.floor = dict(self.cnt)



def cdiv(a, b):
    return (a + b - 1) // b


class Prog:
    ARENA_BYTES = 206 * 1024

    def __init__(self):
        self.nc = bass.Bass("TRN2", target_bir_lowering=False)
        self.em = Em(self.nc)
        self.ctx = contextlib.ExitStack()
        self.uid = 0
        self.freelist = [(0, self.ARENA_BYTES)]
        self.dirty = []
        self.vlist = []
        self.live = {}
        self.peak = 0
        self.psums = []
        self.consts_done = False

    def u(self, pfx):
        self.uid += 1
        return f"{pfx}.{self.uid}"

    def buf(self, name, shape, dtype):
        per = int(np.prod(shape[1:])) * mybir.dt.size(dtype)
        per = (per + 63) & ~63
        fits = [(ln, i) for i, (off, ln) in enumerate(self.freelist) if ln >= per]
        fits.sort()
        for _, i in fits[:1]:
            off, ln = self.freelist[i]
            if True:
                if any(off < do + dl and do < off + per for do, dl in self.dirty):
                    self.phase()
                if ln == per:
                    self.freelist.pop(i)
                else:
                    self.freelist[i] = (off + per, ln - per)
                h = self.nc.virtual_sbuf_tensor(name, list(shape), dtype)
                self.vlist.append((h, off))
                self.live[h.name] = (off, per)
                used = self.ARENA_BYTES - sum(l for _, l in self.freelist)
                self.peak = max(self.peak, used)
                return h
        live = sorted(self.live.items(), key=lambda kv: -kv[1][1])
        raise MemoryError(f"arena OOM for {name}: {per}B, free={self.freelist}, "
                          f"live={[(k, v[1]) for k, v in live[:40]]}")

    def free(self, *handles):
        for h in handles:
            name = h.name if hasattr(h, "name") else h
            off, per = self.live.pop(name)
            self.freelist.append((off, per))
            self.dirty.append((off, per))
        self.freelist.sort()
        out = []
        for off, ln in self.freelist:
            if out and out[-1][0] + out[-1][1] == off:
                out[-1] = (out[-1][0], out[-1][1] + ln)
            else:
                out.append((off, ln))
        self.freelist = [tuple(x) for x in out]

    def phase(self):
        self.em.barrier()
        self.dirty = []

    def psum(self, i, dtype=F32):
        while len(self.psums) <= i:
            j = len(self.psums)
            if j == 7:
                self.psums.append(self.ctx.enter_context(
                    self.nc.psum_tensor("ps7bf", [128, 1024], BF16)))
            else:
                self.psums.append(self.ctx.enter_context(
                    self.nc.psum_tensor(f"ps{j}", [128, 512], F32)))
        return self.psums[i]

    def tap(self, name, h, rows=128):
        sh = list(h.shape)
        o = self.dout("dbg_" + name, sh, F32)
        if h.dtype != F32:
            if not hasattr(self, "_tapscratch"):
                self._tapscratch = self.buf("tapscratch", [128, T], F32)
            tmp = self._tapscratch
            self.em.emit("V", lambda e, h=h, t=tmp: e.tensor_copy(t[0:sh[0], 0:sh[1]], h[:]),
                         reads=[h[:]], writes=[tmp[:]])
            src_ = tmp
        else:
            src_ = h
        self.em.emit("Q", lambda e, s=src_, o=o, r=sh[0], cc=sh[1]: e.dma_start(
            out=o, in_=s[0:r, 0:cc]),
            reads=[src_[:]], writes=["dbgout." + name], dma=True)

    def din(self, name, shape, dtype=F32):
        return self.nc.dram_tensor(name, list(shape), dtype, kind="ExternalInput").ap()

    def dout(self, name, shape, dtype=F32):
        return self.nc.dram_tensor(name, list(shape), dtype, kind="ExternalOutput").ap()

    def cval(self, value, dtype=F32):
        key = (float(value), dtype)
        if not hasattr(self, "_cvals"):
            self._cvals = {}
        if key not in self._cvals:
            t = self.buf(self.u("c_v"), [128, 1], dtype)
            self.em.emit("G", lambda e, t=t, v=float(value): e.memset(t[:], v),
                         writes=[t[:]])
            self._cvals[key] = t
        return self._cvals[key]

    def consts(self):
        if self.consts_done:
            return
        self.consts_done = True
        em = self.em
        self.ones_col = self.buf("c_ones_col", [128, 1], F32)
        self.ones_row = self.buf("c_ones_row", [1, 128], F32)
        self.ones_colbf = self.buf("c_ones_colbf", [128, 1], BF16)
        em.emit("G", lambda e: e.memset(self.ones_col[:], 1.0), writes=[self.ones_col[:]])
        em.emit("G", lambda e: e.memset(self.ones_row[:], 1.0), writes=[self.ones_row[:]])
        em.emit("G", lambda e: e.memset(self.ones_colbf[:], 1.0), writes=[self.ones_colbf[:]])
        self.ident = self.buf("c_ident", [128, 128], F32)
        em.emit("G", lambda e: e.memset(self.ident[:], 1.0), writes=[self.ident[:]])
        em.emit("G", lambda e: e.affine_select(
            self.ident[:], self.ident[:], pattern=[[1, 128]], compare_op=ALU.is_equal,
            fill=0.0, base=0, channel_multiplier=-1),
            reads=[self.ident[:]], writes=[self.ident[:]])
        self.ident_bf = self.buf("c_identbf", [128, 128], BF16)
        em.emit("V", lambda e: e.tensor_copy(self.ident_bf[:], self.ident[:]),
                reads=[self.ident[:]], writes=[self.ident_bf[:]])
        self.lane = self.buf("c_lane", [128, 1], F32)
        self.lane1m = self.buf("c_lane1m", [128, 1], F32)
        em.emit("G", lambda e: e.memset(self.lane[:], 1.0), writes=[self.lane[:]])
        em.emit("V", lambda e: e.tensor_single_scalar(self.lane[:], self.lane[:], 1e-12, ALU.add),
                reads=[self.lane[:]], writes=[self.lane[:]])
        em.emit("V", lambda e: e.tensor_single_scalar(self.lane[:], self.lane[:], 1.0, ALU.is_gt),
                reads=[self.lane[:]], writes=[self.lane[:]])
        em.emit("V", lambda e: e.tensor_scalar(self.lane1m[:], self.lane[:], -1.0, 1.0,
                                               ALU.mult, ALU.add),
                reads=[self.lane[:]], writes=[self.lane1m[:]])

    def round_ip(self, xh, sh, cols, tag="rnd"):
        em = self.em

        def b1(e, x=xh[:, 0:cols], s=sh[:, 0:cols], ln=self.lane):
            return e.tensor_scalar(s, x, ln[:], M64, ALU.mult, ALU.add)
        em.emit("V", b1, reads=[xh[:], self.lane[:]], writes=[sh[:]], tag=tag + ".b1")
        em.emit("V", lambda e, s=sh[:, 0:cols]: e.tensor_single_scalar(
            s, s, M64, ALU.subtract), reads=[sh[:]], writes=[sh[:]], tag=tag + ".b2")

        def a1(e, x=xh[:, 0:cols], ln=self.lane1m):
            return e.tensor_scalar(x, x, ln[:], MAGIC, ALU.mult, ALU.add)
        em.emit("V", a1, reads=[xh[:], self.lane1m[:]], writes=[xh[:]], tag=tag + ".a1")
        em.emit("V", lambda e, x=xh[:, 0:cols]: e.tensor_single_scalar(
            x, x, MAGIC, ALU.subtract), reads=[xh[:]], writes=[xh[:]], tag=tag + ".a2")
        em.emit("V", lambda e, x=xh[:, 0:cols], s=sh[:, 0:cols]: e.tensor_tensor(
            x, x, s, ALU.add), reads=[xh[:], sh[:]], writes=[xh[:]], tag=tag + ".c")

    def row_from_parts(self, src, row, T_, tag="rfp"):
        em = self.em
        nch = T_ // 128
        scol = self.buf(self.u(tag + ".scol"), [128, nch], F32)
        pt = self.psum(3)
        for c in range(nch):
            em.emit("T", lambda e, c=c: e.transpose(
                pt[:, 0:128], src[:, c * 128:(c + 1) * 128], self.ident[:]),
                reads=[src[:], self.ident[:]], writes=[pt[:]], tag=tag + ".t")
            em.emit("V", lambda e, c=c: e.reduce_max(
                scol[:, c:c + 1], pt[:, 0:128], axis=AX.X),
                reads=[pt[:]], writes=[scol[:]], tag=tag + ".r")
        s16 = self.buf(self.u(tag + ".s16"), [max(nch, 1), 128], F32)
        em.emit("T", lambda e: e.transpose(pt[0:nch, 0:128], scol[:], self.ident[:]),
                reads=[scol[:], self.ident[:]], writes=[pt[:]], tag=tag + ".t2")
        em.emit("V", lambda e: e.tensor_copy(s16[:], pt[0:nch, 0:128]),
                reads=[pt[:]], writes=[s16[:]], tag=tag + ".c2")
        em.emit("G", lambda e: e.dma_start(out=row[0:1, 0:T_], in_=s16[:]),
                reads=[s16[:]], writes=[row[:]], dma=True, tag=tag + ".d")

    def materialize(self):
        ah = self.ctx.enter_context(self.nc.sbuf_tensor(
            "arena", [128, self.ARENA_BYTES], U8))
        base = self.nc.lookup_mloc(ah).addr
        for h, off in self.vlist:
            self.nc._tensor(h.name, h.shape, h.dtype, type="SB")
            mloc = self.nc.lookup_mloc(h)
            mloc.allocated = not self.nc.target_bir_lowering
            mloc.addr = base + off

    def finish(self):
        em = self.em
        waits = [(k, v) for k, v in sorted(em.cnt.items())
                 if "d" in k and len(k) > 1 and v > 0]
        em.ops["Q"].append((waits, lambda e: None, ("Q", em.cnt["Q"] + 1), "final"))
        em.cnt["Q"] += 1
        self.materialize()
        nc = self.nc
        sems = {k: self.ctx.enter_context(nc.semaphore("sem_" + k))
                for k, v in em.cnt.items() if v > 0}
        with nc.Block() as block:
            for eng in ENGINES:
                ops = self.em.ops[eng]

                def body(e, ops=ops, eng=eng):
                    for waits, fn, mark, tag in ops:
                        for f, v in waits:
                            e.wait_ge(sems[f], v)
                        inst = fn(e)
                        if inst is None:
                            inst = e.nop()
                        s, v = mark
                        inst.then_inc(sems[s], 16 if s != eng else 1)
                getattr(block, BODY[eng])(body)
        self.ctx.close()
        return self.nc

    def ops_for(self, eng):
        return self.em.ops[eng]

    def matmul_to(self, wdram, nK, nO, xt, T_, evac, wdtype=BF16, tag="mm",
                  ps_idx=(0, 1), xtag=None, ow=128, xsel=False):
        em = self.em
        wb = [self.buf(self.u(tag + ".w"), [128, nK * ow], wdtype) for _ in range(2)]
        nch = cdiv(T_, 512)
        xreads = [x[:] for x in xt] if xtag is None else xtag
        for ot in range(nO):
            b = wb[ot % 2]
            em.emit("G", lambda e, ot=ot, b=b: e.dma_start(out=b[:], in_=wdram[ot]),
                    writes=[b[:]], dma=True, tag=tag + ".wld")
            for c in range(nch):
                c0, c1 = c * 512, min((c + 1) * 512, T_)
                ps_ = self.psum(ps_idx[(ot * nch + c) % 2])

                def mm(e, b=b, c0=c0, c1=c1, ps_=ps_, tiles=tuple(xt)):
                    last = None
                    for k in range(nK):
                        mv = (tiles[k][:, k:k + 1] if xsel else tiles[k][:, c0:c1])
                        last = e.matmul(ps_[0:ow, 0:c1 - c0], b[:, k * ow:(k + 1) * ow],
                                        mv, start=(k == 0), stop=(k == nK - 1))
                    return last
                em.emit("T", mm, reads=[b[:]] + list(xreads), writes=[ps_[:]], tag=tag + ".mm")
                evac(ot, c0, c1, ps_)
        self.free(*wb)

    def bcast_col(self, sval, dst, tag="bcc"):
        em = self.em
        p_ = self.psum(4)
        em.emit("T", lambda e: e.matmul(p_[:, 0:1], self.ones_row[:], sval[0:1, 0:1],
                                        start=True, stop=True),
                reads=[sval[:], self.ones_row[:]], writes=[p_[:]], tag=tag)
        em.emit("S", lambda e: e.copy(dst[:], p_[:, 0:1]),
                reads=[p_[:]], writes=[dst[:]], tag=tag + "c")

    def bcast_row(self, row, dst, T_, ps_i=5, dt_src=None, tag="bc"):
        em = self.em
        p_ = self.psum(ps_i)
        for c in range(cdiv(T_, 512)):
            c0, c1 = c * 512, min((c + 1) * 512, T_)
            em.emit("T", lambda e, c0=c0, c1=c1: e.matmul(
                p_[:, 0:c1 - c0], self.ones_row[:], row[:, c0:c1], start=True, stop=True),
                reads=[row[:], self.ones_row[:]], writes=[p_[:]], tag=tag)
            em.emit("S", lambda e, c0=c0, c1=c1: e.copy(dst[:, c0:c1], p_[:, 0:c1 - c0]),
                    reads=[p_[:]], writes=[dst[:]], tag=tag + "c")

    def layernorm(self, xt, T_, xn, eps=1e-5, tag="ln", scr=None):
        em = self.em
        n = len(xt)
        nd = float(n * 128)
        ones_in = self.ones_col if xt[0].dtype == F32 else self.ones_colbf
        ssum = self.buf(self.u(tag + ".sum"), [1, T_], F32)
        ssq = self.buf(self.u(tag + ".ssq"), [1, T_], F32)
        sq = [self.buf(self.u(tag + ".sq"), [128, 512], F32) for _ in range(2)]
        p1, p2 = self.psum(5), self.psum(6)
        nch = cdiv(T_, 512)
        for c in range(nch):
            c0 = c * 512

            def mm_sum(e, c0=c0, tiles=tuple(xt), p1=p1, oc=ones_in):
                last = None
                for k in range(n):
                    last = e.matmul(p1[0:1, :], oc[:], tiles[k][:, c0:c0 + 512],
                                    start=(k == 0), stop=(k == n - 1))
                return last
            em.emit("T", mm_sum, reads=[x[:] for x in xt] + [ones_in[:]],
                    writes=[p1[:]], tag=tag + ".sm")
            em.emit("V", lambda e, c0=c0: e.tensor_copy(ssum[:, c0:c0 + 512], p1[0:1, :]),
                    reads=[p1[:]], writes=[ssum[:]], tag=tag + ".cp")
            for k in range(n):
                s = sq[k % 2]
                em.emit("S", lambda e, t=xt[k], c0=c0, s=s: e.activation(
                    s[:], t[:, c0:c0 + 512], AF.Square),
                    reads=[xt[k][:]], writes=[s[:]], tag=tag + ".sq")
                em.emit("T", lambda e, k=k, s=s: e.matmul(
                    p2[0:1, :], self.ones_col[:], s[:], start=(k == 0), stop=(k == n - 1)),
                    reads=[s[:], self.ones_col[:]], writes=[p2[:]], tag=tag + ".sqm")
            em.emit("V", lambda e, c0=c0: e.tensor_copy(ssq[:, c0:c0 + 512], p2[0:1, :]),
                    reads=[p2[:]], writes=[ssq[:]], tag=tag + ".cp2")
        mrs = self.buf(self.u(tag + ".mrs"), [1, T_], F32)
        mean, rstd = ssum, ssq
        em.emit("V", lambda e: e.tensor_scalar_mul(mean[:], ssum[:], 1.0 / nd),
                reads=[ssum[:]], writes=[mean[:]])
        em.emit("V", lambda e: e.tensor_tensor(mrs[:], mean[:], mean[:], ALU.mult),
                reads=[mean[:]], writes=[mrs[:]])
        em.emit("V", lambda e: e.scalar_tensor_tensor(
            rstd[:], ssq[:], 1.0 / nd, mrs[:], ALU.mult, ALU.subtract),
            reads=[ssq[:], mrs[:]], writes=[rstd[:]])
        ec = self.cval(eps)
        em.emit("S", lambda e: e.activation(rstd[:], rstd[:], AF.Sqrt, bias=ec[0:1, :]),
                reads=[rstd[:], ec[:]], writes=[rstd[:]])
        em.emit("V", lambda e: e.reciprocal(rstd[:], rstd[:]),
                reads=[rstd[:]], writes=[rstd[:]])
        em.emit("V", lambda e: e.tensor_tensor(mrs[:], mean[:], rstd[:], ALU.mult),
                reads=[mean[:], rstd[:]], writes=[mrs[:]])
        rb = self.buf(self.u(tag + ".rb"), [128, T_], F32)
        mb = self.buf(self.u(tag + ".mb"), [128, T_], F32)
        self.bcast_row(rstd, rb, T_, ps_i=5, tag=tag + ".b1")
        self.bcast_row(mrs, mb, T_, ps_i=6, tag=tag + ".b2")
        tmp = self.buf(self.u(tag + ".tmp"), [128, T_], F32)
        for k in range(n):
            em.emit("V", lambda e, t=xt[k]: e.tensor_tensor(tmp[:], t[:], rb[:], ALU.mult),
                    reads=[xt[k][:], rb[:]], writes=[tmp[:]])
            em.emit("V", lambda e, t=xn[k]: e.tensor_tensor(t[:], tmp[:], mb[:], ALU.subtract),
                    reads=[tmp[:], mb[:]], writes=[xn[k][:]])
        self.free(ssum, ssq, sq[0], sq[1], mrs, rb, mb, tmp)



def emit_l0(p, xt, y_t=None):
    nc, em = p.nc, p.em
    DI, NDI = 2048, 16
    w_in = p.din("l0_w_in", [32, 128, ND * 128])
    w_xp = p.din("l0_w_xp", [1, 128, NDI * 96])
    w_out = p.din("l0_w_out", [ND, 128, NDI * 128])
    w_dt = p.din("l0_w_dt", [NDI, 128, 64])
    dt_b = p.din("l0_dt_b", [1, DI])
    conv_w = p.din("l0_conv_w", [128, NDI * 3])
    conv_b = p.din("l0_conv_b", [128, NDI])
    d_par = p.din("l0_dpar", [128, NDI])
    w_r = p.din("l0_w_r", [ND, 128, ND * 128])
    w_k = p.din("l0_w_k", [ND, 128, ND * 128])
    w_v = p.din("l0_w_v", [ND, 128, ND * 128])
    mu_r = p.din("l0_mu_r", [128, ND])
    mu_k = p.din("l0_mu_k", [128, ND])

    sm = {}
    for nm, src, sh in (("convw", conv_w, [128, NDI * 3]), ("convb", conv_b, [128, NDI]),
                        ("dpar", d_par, [128, NDI]), ("mur", mu_r, [128, ND]),
                        ("muk", mu_k, [128, ND]), ("dtb", dt_b, [1, DI])):
        dst = sm[nm] = p.buf("l0s." + nm, sh, F32)
        em.emit("Q", lambda e, d=dst, s=src: e.dma_start(out=d[:], in_=s),
                writes=[dst[:]], dma=True)

    xn = [p.buf(f"l0.xn{k}", [128, T], BF16) for k in range(ND)]
    p.layernorm(xt, T, xn, tag="l0ln")
    if "xn0" in DBG:
        p.tap("xn0", xn[0])
    p.phase()

    xs0 = [p.buf(f"l0.xs{k}", [128, T], BF16) for k in range(NDI)]

    def evac_xs(ot, c0, c1, ps_):
        em.emit("S", lambda e: e.copy(xs0[ot][:, c0:c1], ps_[:, 0:c1 - c0]),
                reads=[ps_[:]], writes=[xs0[ot][:]], tag="l0.xs")
    p.matmul_to(w_in[0:NDI], ND, NDI, xn, T, evac_xs, tag="l0in")

    if "xs0" in DBG:
        p.tap("xs0", xs0[0])
    if "convw" in DBG:
        p.tap("convw", sm["convw"])
        p.tap("convb", sm["convb"])
    spare = p.buf("l0.convspare", [128, T], BF16)
    cs1 = p.buf("l0.cs1", [128, T], F32)
    cs2 = p.buf("l0.cs2", [128, T], F32)
    xc = []
    dst = spare
    for k in range(NDI):
        src = xs0[k]
        em.emit("V", lambda e, k=k, src=src: e.tensor_scalar(
            cs1[:], src[:], sm["convw"][:, 3 * k + 1:3 * k + 2],
            sm["convb"][:, k:k + 1], ALU.mult, ALU.add),
            reads=[src[:], sm["convb"][:], sm["convw"][:]], writes=[cs1[:]], tag="cv1")
        em.emit("V", lambda e, k=k, src=src: e.scalar_tensor_tensor(
            cs2[:, 0:T - 1], src[:, 1:T], sm["convw"][:, 3 * k + 2:3 * k + 3],
            cs1[:, 0:T - 1], ALU.mult, ALU.add),
            reads=[src[:], cs1[:], sm["convw"][:]], writes=[cs2[:]], tag="cv2")
        em.emit("V", lambda e: e.tensor_copy(cs2[:, T - 1:T], cs1[:, T - 1:T]),
                reads=[cs1[:]], writes=[cs2[:]], tag="cv2b")
        em.emit("V", lambda e, k=k, src=src, dst=dst: e.scalar_tensor_tensor(
            dst[:, 1:T], src[:, 0:T - 1], sm["convw"][:, 3 * k:3 * k + 1],
            cs2[:, 1:T], ALU.mult, ALU.add),
            reads=[src[:], cs2[:], sm["convw"][:]], writes=[dst[:]], tag="cv3")
        em.emit("V", lambda e, dst=dst: e.tensor_copy(dst[:, 0:1], cs2[:, 0:1]),
                reads=[cs2[:]], writes=[dst[:]], tag="cv3b")
        if k == 0 and "cs1" in DBG:
            p.tap("cs1", cs1)
            p.tap("cs2", cs2)
        xc.append(dst)
        dst = src
    p.free(dst, cs1, cs2)

    if "xc0" in DBG:
        p.tap("xc0", xc[0])
    xdbl = p.buf("l0.xdbl", [96, T], F32)

    def evac_xdbl(ot, c0, c1, ps_):
        em.emit("S", lambda e: e.copy(xdbl[:, c0:c1], ps_[0:96, 0:c1 - c0]),
                reads=[ps_[:]], writes=[xdbl[:]], tag="l0.xdbl")
    p.matmul_to(w_xp, NDI, 1, xc, T, evac_xdbl, tag="l0xp", ow=96)

    m_colbf = p.buf("l0.mcol", [64, 1], BF16)
    p_m = p.psum(5)
    wdt_b = [p.buf(f"l0.wdtb{i}", [128, 64], BF16) for i in range(2)]
    for k in range(NDI):
        b = wdt_b[k % 2]
        em.emit("G", lambda e, k=k, b=b: e.dma_start(out=b[:], in_=w_dt[k]),
                writes=[b[:]], dma=True)
        em.emit("T", lambda e, k=k, b=b: e.matmul(
            p_m[0:64, 0:1], b[:], p.ones_colbf[:], start=(k == 0), stop=(k == NDI - 1)),
            reads=[b[:], p.ones_colbf[:]], writes=[p_m[:]], tag="l0.m")
    em.emit("V", lambda e: e.tensor_copy(m_colbf[:], p_m[0:64, 0:1]),
            reads=[p_m[:]], writes=[m_colbf[:]])
    sb_sum = p.buf("l0.sbsum", [1, 1], F32)
    dtbh = sm["dtb"]
    em.emit("V", lambda e, d=dtbh: e.reduce_sum(sb_sum[:], d[:], axis=AX.X),
            reads=[dtbh[:]], writes=[sb_sum[:]])
    em.emit("V", lambda e: e.tensor_scalar_mul(sb_sum[:], sb_sum[:], 1.0 / DI),
            reads=[sb_sum[:]], writes=[sb_sum[:]])

    xdbl_bf = p.buf("l0.xdblbf", [64, T], BF16)
    em.emit("V", lambda e: e.tensor_copy(xdbl_bf[:], xdbl[0:64, :]),
            reads=[xdbl[:]], writes=[xdbl_bf[:]])
    g_row = p.buf("l0.grow", [1, T], F32)
    p_g = p.psum(6)
    for c in range(T // 512):
        c0 = c * 512
        em.emit("T", lambda e, c0=c0: e.matmul(
            p_g[0:1, :], m_colbf[:], xdbl_bf[:, c0:c0 + 512], start=True, stop=True),
            reads=[m_colbf[:], xdbl_bf[:]], writes=[p_g[:]], tag="l0.g")
        sc_c = p.cval(1.0 / DI)
        em.emit("S", lambda e, c0=c0: e.activation(
            g_row[:, c0:c0 + 512], p_g[0:1, :], AF.Sigmoid,
            bias=sb_sum[:], scale=sc_c[0:1, :]),
            reads=[p_g[:], sb_sum[:], sc_c[:]], writes=[g_row[:]], tag="l0.gs")
    if "grow" in DBG:
        p.tap("grow", g_row)
    if "xdbl" in DBG:
        p.tap("xdbl", xdbl)
    p.free(xdbl, xdbl_bf, wdt_b[0], wdt_b[1], m_colbf)

    p.free(sm.pop("dtb"))
    gb = p.buf("l0.gb", [128, T], F32)
    p.bcast_row(g_row, gb, T, ps_i=5, tag="l0.gb")
    p.free(g_row, sb_sum)

    utmp = p.buf("l0.utmp", [128, T], F32)
    ysc = p.buf("l0.ysc", [128, T], F32)
    zwb = [p.buf(f"l0.zw{i}", [128, ND * 128], BF16) for i in range(2)]
    y = []
    for k in range(NDI):
        em.emit("V", lambda e, k=k: e.tensor_tensor(utmp[:], gb[:], xc[k][:], ALU.mult),
                reads=[gb[:], xc[k][:]], writes=[utmp[:]], tag="l0.u")
        em.emit("V", lambda e, k=k: e.tensor_tensor(utmp[:], xc[k][:], utmp[:], ALU.subtract),
                reads=[xc[k][:], utmp[:]], writes=[utmp[:]], tag="l0.u2")
        em.emit("V", lambda e: e.tensor_tensor_scan(
            ysc[:], gb[:], utmp[:], 0.0, ALU.mult, ALU.add),
            reads=[gb[:], utmp[:]], writes=[ysc[:]], tag="l0.scan")
        b = zwb[k % 2]
        em.emit("G", lambda e, k=k, b=b: e.dma_start(out=b[:], in_=w_in[NDI + k]),
                writes=[b[:]], dma=True, tag="l0.zwld")
        for c in range(T // 512):
            c0 = c * 512
            ps_ = p.psum((c % 2))

            def mmz(e, b=b, c0=c0, ps_=ps_, tiles=tuple(xn)):
                last = None
                for kk in range(ND):
                    last = e.matmul(ps_[:, :], b[:, kk * 128:(kk + 1) * 128],
                                    tiles[kk][:, c0:c0 + 512], start=(kk == 0),
                                    stop=(kk == ND - 1))
                return last
            em.emit("T", mmz, reads=[b[:]] + [x[:] for x in xn], writes=[ps_[:]],
                    tag="l0.zmm")
            em.emit("S", lambda e, c0=c0, ps_=ps_: e.activation(
                ps_[:, :], ps_[:, :], AF.Silu),
                reads=[ps_[:]], writes=[ps_[:]], tag="l0.zsilu")
            em.emit("V", lambda e, c0=c0, ps_=ps_: e.tensor_tensor(
                ysc[:, c0:c0 + 512], ysc[:, c0:c0 + 512], ps_[:, :], ALU.mult),
                reads=[ps_[:], ysc[:]], writes=[ysc[:]], tag="l0.ymul")
        em.emit("V", lambda e, k=k: e.scalar_tensor_tensor(
            xc[k][:], xc[k][:], sm["dpar"][:, k:k + 1], ysc[:], ALU.mult, ALU.add),
            reads=[xc[k][:], sm["dpar"][:], ysc[:]], writes=[xc[k][:]], tag="l0.yf")
        y.append(xc[k])
    if "y0" in DBG:
        p.tap("y0", y[0])
    p.free(utmp, ysc, zwb[0], zwb[1], gb)
    p.phase()

    def evac_out(ot, c0, c1, ps_):
        em.emit("V", lambda e, t=xt[ot]: e.tensor_tensor(t[:, c0:c1], t[:, c0:c1],
                                                         ps_[:, 0:c1 - c0], ALU.add),
                reads=[ps_[:], xt[ot][:]], writes=[xt[ot][:]], tag="l0.oadd")
    p.matmul_to(w_out, NDI, ND, y, T, evac_out, tag="l0out")
    if "xattn0" in DBG:
        p.tap("xattn0", xt[0])
    p.free(*[c for c in xc])
    p.phase()

    if "l0ffn" in SKIP:
        for v in sm.values():
            p.free(v)
        p.free(*xn)
        p.phase()
        return
    dtmp = p.buf("l0.dtmp", [128, T], F32)

    def mix(dst, mucol):
        for k in range(ND):
            em.emit("V", lambda e, k=k: e.tensor_tensor(
                dtmp[:, 1:T], xn[k][:, 1:T], xn[k][:, 0:T - 1], ALU.subtract),
                reads=[xn[k][:]], writes=[dtmp[:]], tag="l0.mx1")
            em.emit("V", lambda e, k=k, dst=dst: e.scalar_tensor_tensor(
                dst[k][:, 1:T], dtmp[:, 1:T], mucol[:, k:k + 1], xn[k][:, 0:T - 1],
                ALU.mult, ALU.add),
                reads=[dtmp[:], mucol, xn[k][:]], writes=[dst[k][:]], tag="l0.mx2")
            em.emit("V", lambda e, k=k, dst=dst: e.tensor_scalar(
                dst[k][:, 0:1], xn[k][:, 0:1], mucol[:, k:k + 1], None, ALU.mult),
                reads=[xn[k][:], mucol], writes=[dst[k][:]], tag="l0.mx3")

    xr = [p.buf(f"l0.xr{k}", [128, T], BF16) for k in range(ND)]
    mix(xr, sm["mur"][:])
    rs = [p.buf(f"l0.rs{k}", [128, T], BF16) for k in range(ND)]

    def evac_r(ot, c0, c1, ps_):
        em.emit("S", lambda e: e.activation(rs[ot][:, c0:c1], ps_[:, 0:c1 - c0], AF.Sigmoid),
                reads=[ps_[:]], writes=[rs[ot][:]], tag="l0.r")
    if "xr0" in DBG:
        p.tap("xr0", xr[0])
    p.matmul_to(w_r, ND, ND, xr, T, evac_r, tag="l0wr")
    p.free(*xr)
    p.phase()

    xk = [p.buf(f"l0.xk{k}", [128, T], BF16) for k in range(ND)]
    mix(xk, sm["muk"][:])
    p.free(dtmp)
    ka = [p.buf(f"l0.ka{k}", [128, T], BF16) for k in range(ND)]

    def evac_k(ot, c0, c1, ps_):
        em.emit("S", lambda e: e.activation(ka[ot][:, c0:c1], ps_[:, 0:c1 - c0], AF.Relu),
                reads=[ps_[:]], writes=[ka[ot][:]], tag="l0.k")
        em.emit("S", lambda e: e.activation(ka[ot][:, c0:c1], ka[ot][:, c0:c1], AF.Square),
                reads=[ka[ot][:]], writes=[ka[ot][:]], tag="l0.k2")
    p.matmul_to(w_k, ND, ND, xk, T, evac_k, tag="l0wk")
    p.free(*xk)
    p.phase()

    vtmp = p.buf("l0.vtmp", [128, 512], F32)

    def evac_v(ot, c0, c1, ps_):
        em.emit("V", lambda e: e.tensor_tensor(vtmp[:, 0:c1 - c0], ps_[:, 0:c1 - c0],
                                               rs[ot][:, c0:c1], ALU.mult),
                reads=[ps_[:], rs[ot][:]], writes=[vtmp[:]], tag="l0.v")
        em.emit("V", lambda e, t=xt[ot]: e.tensor_tensor(t[:, c0:c1], t[:, c0:c1],
                                                         vtmp[:, 0:c1 - c0], ALU.add),
                reads=[vtmp[:], xt[ot][:]], writes=[xt[ot][:]], tag="l0.vadd")
    if "rs0" in DBG:
        p.tap("rs0", rs[0])
    if "ka0" in DBG:
        p.tap("ka0", ka[0])
    p.matmul_to(w_v, ND, ND, ka, T, evac_v, tag="l0wv")
    p.free(*ka, *rs, vtmp, *xn)
    for v in sm.values():
        p.free(v)
    p.phase()





MAGIC = 12582912.0
M64 = 6755399441055744.0


def emit_l1(p, xt, y_t, reload=False):
    nc, em = p.nc, p.em
    DH, NDH = 4096, 32
    pr = p.din("l1_pr", [128, ND]); pi = p.din("l1_pi", [128, ND])
    ar = p.din("l1_ar", [128, ND]); ai = p.din("l1_ai", [128, ND])
    lr = p.din("l1_lr", [128, ND]); li = p.din("l1_li", [128, ND])
    n1w = p.din("l1_n1w", [128, ND])
    w1 = p.din("l1_w1", [NDH, 128, ND * 128])
    asc = p.din("l1_asc", [1, 1]); abi = p.din("l1_abi", [1, 1])
    n2w = p.din("l1_n2w", [128, NDH])
    w2 = p.din("l1_w2", [ND, 128, NDH * 128])

    sm = {}
    for nm, srcd, sh in (("pr", pr, [128, ND]), ("pi", pi, [128, ND]),
                         ("ar", ar, [128, ND]), ("ai", ai, [128, ND]),
                         ("lr", lr, [128, ND]), ("li", li, [128, ND]),
                         ("n1w", n1w, [128, ND]), ("n2w", n2w, [128, NDH]),
                         ("asc", asc, [1, 1]), ("abi", abi, [1, 1])):
        dst = sm[nm] = p.buf("l1s." + nm, sh, F32)
        em.emit("Q", lambda e, d=dst, s=srcd: e.dma_start(out=d[:], in_=s),
                writes=[dst[:]], dma=True)

    xn = [p.buf(f"l1.xn{k}", [128, T], F32) for k in range(ND)]
    p.layernorm(xt, T, xn, tag="l1ln")
    p.phase()

    skip_attn = "l1attn" in SKIP
    def tile(nm):
        t = p.buf("l1c." + nm, [128, ND], F32)
        return t
    m2, r_, mag, minv, cosv, sinv = (tile(x) for x in
                                     ("m2", "r", "mag", "minv", "cos", "sin"))
    t1, t2 = tile("t1"), tile("t2")
    phr, phi = tile("phr"), tile("phi")
    sr, si = tile("sr"), tile("si")
    NS = 11
    mrs = p.buf("l1c.mrs", [128, ND * NS], F32)
    mis = p.buf("l1c.mis", [128, ND * NS], F32)
    nms = p.buf("l1c.nms", [128, ND * NS], F32)
    Hr = [p.buf("l1.Hr0", [128, T], F32)]
    Hi = [p.buf("l1.Hi0", [128, T], F32)]
    Hr2 = [p.buf("l1.Hr20", [128, T], F32)]
    Hi2 = [p.buf("l1.Hi20", [128, T], F32)]
    ws1_scr = []
    sw1, swi1 = emit_wscale(p, w1, NDH, ND * 128, 4096 * 1024, tag="l1.ws1",
                            scratch_out=ws1_scr)
    em.emit("S", lambda e: e.activation(m2[:], sm["pr"][:], AF.Square), reads=[sm["pr"][:]], writes=[m2[:]])
    em.emit("S", lambda e: e.activation(t1[:], sm["pi"][:], AF.Square), reads=[sm["pi"][:]], writes=[t1[:]])
    em.emit("V", lambda e: e.tensor_add(m2[:], m2[:], t1[:]), reads=[m2[:], t1[:]], writes=[m2[:]])
    cm1 = p.cval(-1.0)
    em.emit("S", lambda e: e.activation(r_[:], m2[:], AF.Exp, scale=cm1[:]),
            reads=[m2[:], cm1[:]], writes=[r_[:]])
    em.emit("S", lambda e: e.activation(mag[:], m2[:], AF.Sqrt), reads=[m2[:]], writes=[mag[:]])
    em.emit("V", lambda e: e.reciprocal(minv[:], mag[:]), reads=[mag[:]], writes=[minv[:]])
    em.emit("V", lambda e: e.tensor_mul(cosv[:], sm["pr"][:], minv[:]), reads=[sm["pr"][:], minv[:]], writes=[cosv[:]])
    em.emit("V", lambda e: e.tensor_mul(sinv[:], sm["pi"][:], minv[:]), reads=[sm["pi"][:], minv[:]], writes=[sinv[:]])
    em.emit("V", lambda e: e.tensor_mul(phr[:], r_[:], cosv[:]), reads=[r_[:], cosv[:]], writes=[phr[:]])
    em.emit("V", lambda e: e.tensor_mul(phi[:], r_[:], sinv[:]), reads=[r_[:], sinv[:]], writes=[phi[:]])
    em.emit("V", lambda e: e.tensor_mul(t1[:], sm["lr"][:], phr[:]), reads=[sm["lr"][:], phr[:]], writes=[t1[:]])
    em.emit("V", lambda e: e.tensor_mul(t2[:], sm["li"][:], phi[:]), reads=[sm["li"][:], phi[:]], writes=[t2[:]])
    em.emit("V", lambda e: e.tensor_sub(sr[:], t1[:], t2[:]), reads=[t1[:], t2[:]], writes=[sr[:]])
    em.emit("V", lambda e: e.tensor_mul(t1[:], sm["lr"][:], phi[:]), reads=[sm["lr"][:], phi[:]], writes=[t1[:]])
    em.emit("V", lambda e: e.tensor_mul(t2[:], sm["li"][:], phr[:]), reads=[sm["li"][:], phr[:]], writes=[t2[:]])
    em.emit("V", lambda e: e.tensor_add(si[:], t1[:], t2[:]), reads=[t1[:], t2[:]], writes=[si[:]])
    em.emit("V", lambda e: e.tensor_copy(mrs[:, 0:ND], phr[:]), reads=[phr[:]], writes=[mrs[:]])
    em.emit("V", lambda e: e.tensor_copy(mis[:, 0:ND], phi[:]), reads=[phi[:]], writes=[mis[:]])
    for s in range(1, NS):
        a, b = (s - 1) * ND, s * ND
        em.emit("V", lambda e, a=a, b=b: e.tensor_mul(t1[:], mrs[:, a:a + ND], mrs[:, a:a + ND]),
                reads=[mrs[:]], writes=[t1[:]])
        em.emit("V", lambda e, a=a, b=b: e.tensor_mul(t2[:], mis[:, a:a + ND], mis[:, a:a + ND]),
                reads=[mis[:]], writes=[t2[:]])
        em.emit("V", lambda e, a=a, b=b: e.tensor_sub(mrs[:, b:b + ND], t1[:], t2[:]),
                reads=[t1[:], t2[:]], writes=[mrs[:]])
        em.emit("V", lambda e, a=a, b=b: e.tensor_mul(t1[:], mrs[:, a:a + ND], mis[:, a:a + ND]),
                reads=[mrs[:], mis[:]], writes=[t1[:]])
        em.emit("V", lambda e, a=a, b=b: e.tensor_scalar_mul(mis[:, b:b + ND], t1[:], 2.0),
                reads=[t1[:]], writes=[mis[:]])
    em.emit("V", lambda e: e.tensor_scalar_mul(nms[:], mis[:], -1.0), reads=[mis[:]], writes=[nms[:]])

    for k in range(ND if not skip_attn else 0):
        g = 0
        E = "V"
        hr, hi, hr2, hi2 = Hr[g], Hi[g], Hr2[g], Hi2[g]
        em.emit(E, lambda e, k=k, hr=hr: e.tensor_scalar(
            hr[:], xn[k][:], sm["ar"][:, k:k + 1], None, ALU.mult),
            reads=[xn[k][:], sm["ar"][:]], writes=[hr[:]], tag="l1.u")
        em.emit(E, lambda e, k=k, hi=hi: e.tensor_scalar(
            hi[:], xn[k][:], sm["ai"][:, k:k + 1], None, ALU.mult),
            reads=[xn[k][:], sm["ai"][:]], writes=[hi[:]], tag="l1.u")
        em.emit(E, lambda e, k=k, hr=hr: e.tensor_tensor(
            hr[:, 0:1], hr[:, 0:1], sr[:, k:k + 1], ALU.add),
            reads=[hr[:], sr[:]], writes=[hr[:]], tag="l1.seed")
        em.emit(E, lambda e, k=k, hi=hi: e.tensor_tensor(
            hi[:, 0:1], hi[:, 0:1], si[:, k:k + 1], ALU.add),
            reads=[hi[:], si[:]], writes=[hi[:]], tag="l1.seed")
        for s in range(NS):
            m = 1 << s
            c = s * ND + k
            em.emit(E, lambda e, hr=hr, hr2=hr2, m=m: e.tensor_copy(
                hr2[:, 0:m], hr[:, 0:m]), reads=[hr[:]], writes=[hr2[:]], tag="l1.dc")
            em.emit(E, lambda e, hr=hr, hr2=hr2, m=m, c=c: e.scalar_tensor_tensor(
                hr2[:, m:T], hr[:, 0:T - m], mrs[:, c:c + 1], hr[:, m:T], ALU.mult, ALU.add),
                reads=[hr[:], mrs[:]], writes=[hr2[:]], tag="l1.d1")
            em.emit(E, lambda e, hi=hi, hr2=hr2, m=m, c=c: e.scalar_tensor_tensor(
                hr2[:, m:T], hi[:, 0:T - m], nms[:, c:c + 1], hr2[:, m:T], ALU.mult, ALU.add),
                reads=[hi[:], nms[:], hr2[:]], writes=[hr2[:]], tag="l1.d2")
            em.emit(E, lambda e, hi=hi, hi2=hi2, m=m: e.tensor_copy(
                hi2[:, 0:m], hi[:, 0:m]), reads=[hi[:]], writes=[hi2[:]], tag="l1.dc")
            em.emit(E, lambda e, hr=hr, hi=hi, hi2=hi2, m=m, c=c: e.scalar_tensor_tensor(
                hi2[:, m:T], hr[:, 0:T - m], mis[:, c:c + 1], hi[:, m:T], ALU.mult, ALU.add),
                reads=[hr[:], hi[:], mis[:]], writes=[hi2[:]], tag="l1.d3")
            em.emit(E, lambda e, hi=hi, hi2=hi2, m=m, c=c: e.scalar_tensor_tensor(
                hi2[:, m:T], hi[:, 0:T - m], mrs[:, c:c + 1], hi2[:, m:T], ALU.mult, ALU.add),
                reads=[hi[:], mrs[:], hi2[:]], writes=[hi2[:]], tag="l1.d4")
            hr, hr2 = hr2, hr
            hi, hi2 = hi2, hi
        em.emit(E, lambda e, t=xt[k], hr=hr: e.tensor_tensor(
            t[:], t[:], hr[:], ALU.add),
            reads=[xt[k][:], hr[:]], writes=[xt[k][:]], tag="l1.addx")
    p.free(Hr[0], Hi[0], Hr2[0], Hi2[0])
    sm_freed = set()
    for nm in ("pr", "pi", "ar", "ai", "lr", "li"):
        p.free(sm[nm])
        sm_freed.add(nm)
    for tl in (m2, r_, mag, minv, cosv, sinv, phr, phi, sr, si, t1, t2, mrs, mis, nms):
        p.free(tl)
    p.phase()

    for k in range(ND):
        em.emit("Q", lambda e, k=k, t=xt[k]: e.dma_start(
            out=y_t[k * 128:(k + 1) * 128, :], in_=t[:]),
            reads=[xt[k][:]], writes=[f"o.y.{k}"], dma=True)
    for k in range(ND):
        p.free(xt[k])
        xt[k] = None
    p.phase()

    if "l1ffn" in SKIP:
        if reload:
            for k in range(ND):
                xt[k] = p.buf(f"x{k}b", [128, T], F32)
                em.emit("Q", lambda e, k=k, t=xt[k]: e.dma_start(
                    out=t[:], in_=y_t[k * 128:(k + 1) * 128, :]),
                    reads=[f"o.y.{k}"], writes=[xt[k][:]], dma=True)
        return
    s2row = p.buf("l1.s2row", [1, T], F32)
    sqs = p.buf("l1.sqs", [128, 512], F32)
    p2 = p.psum(6)
    for c in range(T // 512):
        c0 = c * 512
        for k in range(ND):
            em.emit("S", lambda e, k=k, c0=c0: e.activation(
                sqs[:], xn[k][:, c0:c0 + 512], AF.Square),
                reads=[xn[k][:]], writes=[sqs[:]], tag="l1.q1s")
            em.emit("T", lambda e, k=k: e.matmul(
                p2[0:1, :], p.ones_col[:], sqs[:], start=(k == 0), stop=(k == ND - 1)),
                reads=[sqs[:], p.ones_col[:]], writes=[p2[:]], tag="l1.q1m")
        em.emit("V", lambda e, c0=c0: e.tensor_copy(s2row[:, c0:c0 + 512], p2[0:1, :]),
                reads=[p2[:]], writes=[s2row[:]], tag="l1.q1c")
    rstd1 = p.buf("l1.rstd1", [1, T], F32)
    e8 = p.cval(1e-8)
    em.emit("V", lambda e: e.tensor_scalar_mul(rstd1[:], s2row[:], 1.0 / D),
            reads=[s2row[:]], writes=[rstd1[:]])
    em.emit("S", lambda e: e.activation(rstd1[:], rstd1[:], AF.Sqrt, bias=e8[0:1, :]),
            reads=[rstd1[:], e8[:]], writes=[rstd1[:]])
    em.emit("V", lambda e: e.reciprocal(rstd1[:], rstd1[:]), reads=[rstd1[:]], writes=[rstd1[:]])
    rstd1b = p.buf("l1.rstd1b", [128, T], F32)
    p.bcast_row(rstd1, rstd1b, T, ps_i=5, tag="l1.rb")
    xnorm = [p.buf(f"l1.xm{k}", [128, T], F32) for k in range(ND)]
    maxa = p.buf("l1.maxa", [128, T], F32)
    xsc = p.buf("l1.xsc", [128, T], F32)
    for k in range(ND):
        em.emit("V", lambda e, k=k: e.tensor_mul(xsc[:], xn[k][:], rstd1b[:]),
                reads=[xn[k][:], rstd1b[:]], writes=[xsc[:]], tag="l1.xh1")
        em.emit("V", lambda e, k=k: e.tensor_scalar(
            xnorm[k][:], xsc[:], sm["n1w"][:, k:k + 1], None, ALU.mult),
            reads=[xsc[:], sm["n1w"][:]], writes=[xnorm[k][:]], tag="l1.xh2")
        p.free(xn[k])
        em.emit("S", lambda e, t=xnorm[k]: e.activation(xsc[:], t[:], AF.Abs),
                reads=[xnorm[k][:]], writes=[xsc[:]], tag="l1.mxa")
        if k == 0:
            em.emit("V", lambda e: e.tensor_copy(maxa[:], xsc[:]),
                    reads=[xsc[:]], writes=[maxa[:]], tag="l1.mx")
        else:
            em.emit("V", lambda e: e.tensor_tensor(
                maxa[:], maxa[:], xsc[:], ALU.max),
                reads=[maxa[:], xsc[:]], writes=[maxa[:]], tag="l1.mx")
    mrow = p.buf("l1.mrow", [1, T], F32)
    p.row_from_parts(maxa, mrow, T, tag="l1.rfp")
    sxrow = p.buf("l1.sxrow", [1, T], F32)
    sxirow = p.buf("l1.sxirow", [1, T], F32)
    em.emit("V", lambda e: e.tensor_single_scalar(mrow[:], mrow[:], 1e-5, ALU.max),
            reads=[mrow[:]], writes=[mrow[:]])
    em.emit("V", lambda e: e.tensor_scalar_mul(sxirow[:], mrow[:], 1.0 / 127.0),
            reads=[mrow[:]], writes=[sxirow[:]])
    em.emit("V", lambda e: e.reciprocal(sxrow[:], mrow[:]), reads=[mrow[:]], writes=[sxrow[:]])
    em.emit("V", lambda e: e.tensor_scalar_mul(sxrow[:], sxrow[:], 127.0),
            reads=[sxrow[:]], writes=[sxrow[:]])
    sxb = p.buf("l1.sxb", [128, T], F32)
    sxib = p.buf("l1.sxib", [128, T], F32)
    p.bcast_row(sxrow, sxb, T, ps_i=5, tag="l1.sb")
    p.bcast_row(sxirow, sxib, T, ps_i=6, tag="l1.sib")
    p.free(sxirow)
    xq = []
    rsc = p.buf("l1.rsc", [128, T], F32)
    for k in range(ND):
        em.emit("V", lambda e, k=k: e.tensor_mul(xsc[:], xnorm[k][:], sxb[:]),
                reads=[xnorm[k][:], sxb[:]], writes=[xsc[:]], tag="l1.qq1")
        p.round_ip(xsc, rsc, T, tag="l1.qq2")
        em.emit("V", lambda e: e.tensor_single_scalar(xsc[:], xsc[:], -128.0, ALU.max),
                reads=[xsc[:]], writes=[xsc[:]], tag="l1.qq3")
        xqk = p.buf(f"l1.xq{k}", [128, T], BF16)
        em.emit("V", lambda e, t=xqk: e.tensor_single_scalar(t[:], xsc[:], 127.0, ALU.min),
                reads=[xsc[:]], writes=[xqk[:]], tag="l1.qq4")
        xq.append(xqk)
        p.free(xnorm[k])
    p.free(maxa, mrow, sxrow, rstd1, rstd1b, s2row, sqs, xsc, sxb, rsc)
    p.phase()

    p.free(*ws1_scr)
    h = [p.buf(f"l1.h{j}", [128, T], F16) for j in range(NDH)]
    s2h = p.buf("l1.s2h", [1, T], F32)
    em.emit("V", lambda e: e.memset(s2h[:], 0.0), writes=[s2h[:]])
    maxh = p.buf("l1.maxh", [128, T], F32)
    em.emit("V", lambda e: e.memset(maxh[:], 0.0), writes=[maxh[:]])
    an2w = p.buf("l1.an2w", [128, NDH], F32)
    em.emit("S", lambda e: e.activation(an2w[:], sm["n2w"][:], AF.Abs),
            reads=[sm["n2w"][:]], writes=[an2w[:]])
    ascb = p.buf("l1.ascb", [128, 1], F32)
    abib = p.buf("l1.abib", [128, 1], F32)
    p.bcast_col(sm["asc"], ascb)
    p.bcast_col(sm["abi"], abib)
    hevt = p.buf("l1.hev", [128, 512], F32)
    sqh = p.buf("l1.sqh", [128, 512], F32)
    p2b = p.psum(6)

    def evac_h(ot, c0, c1, ps_):
        em.emit("V", lambda e: e.scalar_tensor_tensor(
            hevt[:, 0:c1 - c0], ps_[:, 0:c1 - c0], swi1[:], sxib[:, c0:c1],
            ALU.mult, ALU.mult),
            reads=[ps_[:], swi1[:], sxib[:]], writes=[hevt[:]], tag="l1.he1")
        em.emit("S", lambda e: e.activation(hevt[:, 0:c1 - c0], hevt[:, 0:c1 - c0], AF.Relu),
                reads=[hevt[:]], writes=[hevt[:]], tag="l1.he2")
        em.emit("S", lambda e: e.activation(hevt[:, 0:c1 - c0], hevt[:, 0:c1 - c0], AF.Square),
                reads=[hevt[:]], writes=[hevt[:]], tag="l1.he3")
        em.emit("V", lambda e: e.tensor_scalar(
            h[ot][:, c0:c1], hevt[:, 0:c1 - c0], ascb[:], abib[:], ALU.mult, ALU.add),
            reads=[hevt[:], ascb[:], abib[:]], writes=[h[ot][:]], tag="l1.he4")
        em.emit("S", lambda e: e.activation(sqh[:], h[ot][:, c0:c1], AF.Square),
                reads=[h[ot][:]], writes=[sqh[:]], tag="l1.he5")
        em.emit("T", lambda e: e.matmul(p2b[0:1, 0:c1 - c0], p.ones_col[:], sqh[:, 0:c1 - c0],
                                        start=True, stop=True),
                reads=[sqh[:], p.ones_col[:]], writes=[p2b[:]], tag="l1.he6")
        em.emit("V", lambda e: e.tensor_tensor(s2h[:, c0:c1], s2h[:, c0:c1],
                                               p2b[0:1, 0:c1 - c0], ALU.add),
                reads=[p2b[:], s2h[:]], writes=[s2h[:]], tag="l1.he7")
        em.emit("S", lambda e, ot=ot: e.activation(sqh[:, 0:c1 - c0], h[ot][:, c0:c1], AF.Abs),
                reads=[h[ot][:]], writes=[sqh[:]], tag="l1.he8a")
        em.emit("V", lambda e, ot=ot: e.tensor_scalar(
            sqh[:, 0:c1 - c0], sqh[:, 0:c1 - c0], an2w[:, ot:ot + 1], None, ALU.mult),
            reads=[sqh[:], an2w[:]], writes=[sqh[:]], tag="l1.he8b")
        em.emit("V", lambda e: e.tensor_tensor(
            maxh[:, c0:c1], maxh[:, c0:c1], sqh[:, 0:c1 - c0], ALU.max),
            reads=[maxh[:], sqh[:]], writes=[maxh[:]], tag="l1.he8c")
    emit_bitlinear_mm(p, w1, ND, NDH, xq, sw1, T, evac_h, tag="l1.w1")
    ws2_scr = []
    sw2, swi2 = emit_wscale(p, w2, ND, NDH * 128, 1024 * 4096, tag="l1.ws2",
                            scratch_out=ws2_scr)
    for k in range(ND):
        p.free(xq[k])
    p.free(sxib, ascb, abib, hevt, sqh)
    p.phase()

    rstd2 = p.buf("l1.rstd2", [1, T], F32)
    em.emit("V", lambda e: e.tensor_scalar_mul(rstd2[:], s2h[:], 1.0 / DH),
            reads=[s2h[:]], writes=[rstd2[:]])
    em.emit("S", lambda e: e.activation(rstd2[:], rstd2[:], AF.Sqrt, bias=e8[0:1, :]),
            reads=[rstd2[:], e8[:]], writes=[rstd2[:]])
    em.emit("V", lambda e: e.reciprocal(rstd2[:], rstd2[:]), reads=[rstd2[:]], writes=[rstd2[:]])
    mrow2 = p.buf("l1.mrow2", [1, T], F32)
    p.row_from_parts(maxh, mrow2, T, tag="l1.rf2")
    em.emit("V", lambda e: e.tensor_tensor(mrow2[:], mrow2[:], rstd2[:], ALU.mult),
            reads=[mrow2[:], rstd2[:]], writes=[mrow2[:]])
    sx2row = p.buf("l1.sx2row", [1, T], F32)
    sx2irow = p.buf("l1.sx2irow", [1, T], F32)
    em.emit("V", lambda e: e.tensor_single_scalar(mrow2[:], mrow2[:], 1e-5, ALU.max),
            reads=[mrow2[:]], writes=[mrow2[:]])
    em.emit("V", lambda e: e.tensor_scalar_mul(sx2irow[:], mrow2[:], 1.0 / 127.0),
            reads=[mrow2[:]], writes=[sx2irow[:]])
    em.emit("V", lambda e: e.reciprocal(sx2row[:], mrow2[:]), reads=[mrow2[:]], writes=[sx2row[:]])
    em.emit("V", lambda e: e.tensor_scalar_mul(sx2row[:], sx2row[:], 127.0),
            reads=[sx2row[:]], writes=[sx2row[:]])
    em.emit("V", lambda e: e.tensor_tensor(sx2row[:], sx2row[:], rstd2[:], ALU.mult),
            reads=[sx2row[:], rstd2[:]], writes=[sx2row[:]])
    p.free(maxh, mrow2, rstd2, s2h)
    psb = p.buf("l1.psb", [128, T], F32)
    sx2ib = p.buf("l1.sx2ib", [128, T], F32)
    p.bcast_row(sx2row, psb, T, ps_i=5, tag="l1.s2b")
    p.bcast_row(sx2irow, sx2ib, T, ps_i=6, tag="l1.s2ib")
    p.free(sx2irow)
    hq = h
    hsc = p.buf("l1.hsc", [128, T], F32)
    hrs = p.buf("l1.hrs", [128, T], F32)
    for j in range(NDH):
        em.emit("V", lambda e, j=j: e.tensor_scalar(
            hsc[:], h[j][:], sm["n2w"][:, j:j + 1], None, ALU.mult),
            reads=[h[j][:], sm["n2w"][:]], writes=[hsc[:]], tag="l1.q21")
        em.emit("V", lambda e: e.tensor_mul(hsc[:], hsc[:], psb[:]),
                reads=[hsc[:], psb[:]], writes=[hsc[:]], tag="l1.q22")
        p.round_ip(hsc, hrs, T, tag="l1.q23")
        em.emit("V", lambda e: e.tensor_single_scalar(hsc[:], hsc[:], -128.0, ALU.max),
                reads=[hsc[:]], writes=[hsc[:]], tag="l1.q24")
        em.emit("V", lambda e, j=j: e.tensor_single_scalar(hq[j][:], hsc[:], 127.0, ALU.min),
                reads=[hsc[:]], writes=[hq[j][:]], tag="l1.q25")
    p.free(psb, hsc, hrs, sx2row)
    p.phase()

    p.free(*ws2_scr)
    ochunk = p.buf("l1.oc", [128, 512], F32)

    def evac_o(ot, c0, c1, ps_):
        em.emit("V", lambda e: e.scalar_tensor_tensor(
            ochunk[:, 0:c1 - c0], ps_[:, 0:c1 - c0], swi2[:], sx2ib[:, c0:c1],
            ALU.mult, ALU.mult),
            reads=[ps_[:], swi2[:], sx2ib[:]], writes=[ochunk[:]], tag="l1.oe1")
        em.emit("G", lambda e, ot=ot, c0=c0, c1=c1: e.dma_start(
            out=y_t[ot * 128:(ot + 1) * 128, c0:c1], in_=ochunk[:, 0:c1 - c0],
            accum_op=ALU.add),
            reads=[ochunk[:]], writes=[f"o.y.{ot}"], dma=True, tag="l1.oacc")
    emit_bitlinear_mm(p, w2, NDH, ND, hq, sw2, T, evac_o, tag="l1.w2")
    for j in range(NDH):
        p.free(hq[j])
    p.free(ochunk, sx2ib, an2w, sw1, swi1, sw2, swi2)
    for kk, v in sm.items():
        if kk not in sm_freed:
            p.free(v)
    p.phase()

    if reload:
        for k in range(ND):
            xt[k] = p.buf(f"x{k}b", [128, T], F32)
            em.emit("Q", lambda e, k=k, t=xt[k]: e.dma_start(
                out=t[:], in_=y_t[k * 128:(k + 1) * 128, :]),
                reads=[f"o.y.{k}"], writes=[xt[k][:]], dma=True)


def emit_wscale(p, wdram, nO, bw, nelem, tag, scratch_out=None):
    em = p.em
    wb = [p.buf(p.u(tag + ".w"), [128, 512], F32) for i in range(2)]
    ab = p.buf(p.u(tag + ".a"), [128, 512], F32)
    pscale = p.psum(3)
    nch = cdiv(bw, 512)
    for ot in range(nO):
        for c in range(nch):
            c0, c1 = c * 512, min((c + 1) * 512, bw)
            b = wb[(ot * nch + c) % 2]
            em.emit("G", lambda e, ot=ot, b=b, c0=c0, c1=c1: e.dma_start(
                out=b[:, 0:c1 - c0], in_=wdram[ot][:, c0:c1]),
                writes=[b[:]], dma=True, tag=tag + ".ld")
            em.emit("S", lambda e, b=b, c0=c0, c1=c1: e.activation(
                ab[:, 0:c1 - c0], b[:, 0:c1 - c0], AF.Abs),
                reads=[b[:]], writes=[ab[:]], tag=tag + ".abs")
            em.emit("T", lambda e, ot=ot, c=c, c0=c0, c1=c1: e.matmul(
                pscale[0:1, 0:c1 - c0], p.ones_col[:], ab[:, 0:c1 - c0],
                start=(ot == 0 and c == 0), stop=(ot == nO - 1 and c == nch - 1)),
                reads=[ab[:], p.ones_col[:]], writes=[pscale[:]], tag=tag + ".mm")
    srow = p.buf(p.u(tag + ".srow"), [1, 512], F32)
    em.emit("V", lambda e: e.tensor_copy(srow[:], pscale[0:1, 0:512]),
            reads=[pscale[:]], writes=[srow[:]])
    sval = p.buf(p.u(tag + ".sv"), [1, 1], F32)
    em.emit("V", lambda e: e.reduce_sum(sval[:], srow[:, 0:min(512, bw)], axis=AX.X),
            reads=[srow[:]], writes=[sval[:]])
    em.emit("V", lambda e: e.tensor_scalar_mul(sval[:], sval[:], 1.0 / nelem),
            reads=[sval[:]], writes=[sval[:]])
    em.emit("V", lambda e: e.tensor_single_scalar(sval[:], sval[:], 1e-5, ALU.max),
            reads=[sval[:]], writes=[sval[:]])
    sinv = p.buf(p.u(tag + ".si"), [1, 1], F32)
    em.emit("V", lambda e: e.tensor_copy(sinv[:], sval[:]), reads=[sval[:]], writes=[sinv[:]])
    em.emit("V", lambda e: e.reciprocal(sval[:], sval[:]), reads=[sval[:]], writes=[sval[:]])
    swc = p.buf(p.u(tag + ".swc"), [128, 1], F32)
    swic = p.buf(p.u(tag + ".swic"), [128, 1], F32)
    p.bcast_col(sval, swc)
    p.bcast_col(sinv, swic)
    if scratch_out is not None:
        scratch_out.extend([wb[0], wb[1], ab, srow, sval, sinv])
    else:
        p.free(wb[0], wb[1], ab, srow, sval, sinv)
    return swc, swic


def emit_bitlinear_mm(p, wdram, nK, nO, xq, sw_col, T_, evac, tag):
    em = p.em
    bw = nK * 128
    wf = [p.buf(p.u(tag + ".wf"), [128, 512], F32) for _ in range(2)]
    wb = [p.buf(p.u(tag + ".w"), [128, bw], BF16) for _ in range(2)]
    qs = p.buf(p.u(tag + ".qs"), [128, 512], F32)
    nch = cdiv(T_, 512)
    for ot in range(nO):
        b = wb[ot % 2]
        for q0 in range(0, bw, 512):
            q1 = min(q0 + 512, bw)
            bf_ = wf[(q0 // 512) % 2]
            em.emit("G", lambda e, ot=ot, bf_=bf_, q0=q0, q1=q1: e.dma_start(
                out=bf_[:, 0:q1 - q0], in_=wdram[ot][:, q0:q1]),
                writes=[bf_[:]], dma=True, tag=tag + ".ld")
            em.emit("V", lambda e, bf_=bf_, q0=q0, q1=q1: e.tensor_scalar(
                qs[:, 0:q1 - q0], bf_[:, 0:q1 - q0], sw_col[:], None, ALU.mult),
                reads=[bf_[:], sw_col[:]], writes=[qs[:]], tag=tag + ".q1")
            em.emit("V", lambda e, b=b, q0=q0, q1=q1: e.tensor_single_scalar(
                b[:, q0:q1], qs[:, 0:q1 - q0], 0.5, ALU.is_ge),
                reads=[qs[:]], writes=[b[:]], tag=tag + ".q2")
            em.emit("V", lambda e, q0=q0, q1=q1: e.tensor_single_scalar(
                qs[:, 0:q1 - q0], qs[:, 0:q1 - q0], -0.5, ALU.is_ge),
                reads=[qs[:]], writes=[qs[:]], tag=tag + ".q2b")
            em.emit("V", lambda e, b=b, q0=q0, q1=q1: e.tensor_tensor(
                qs[:, 0:q1 - q0], qs[:, 0:q1 - q0], b[:, q0:q1], ALU.add),
                reads=[qs[:], b[:]], writes=[qs[:]], tag=tag + ".q3")
            em.emit("V", lambda e, b=b, q0=q0, q1=q1: e.tensor_single_scalar(
                b[:, q0:q1], qs[:, 0:q1 - q0], 1.0, ALU.subtract),
                reads=[qs[:]], writes=[b[:]], tag=tag + ".q4")
        for c in range(nch):
            c0, c1 = c * 512, min((c + 1) * 512, T_)
            ps_ = p.psum((ot * nch + c) % 2)

            def mm(e, b=b, c0=c0, c1=c1, ps_=ps_, tiles=tuple(xq)):
                last = None
                for k in range(nK):
                    last = e.matmul(ps_[:, 0:c1 - c0], b[:, k * 128:(k + 1) * 128],
                                    tiles[k][:, c0:c1], start=(k == 0), stop=(k == nK - 1))
                return last
            em.emit("T", mm, reads=[b[:]] + [x[:] for x in xq], writes=[ps_[:]],
                    tag=tag + ".mm")
            evac(ot, c0, c1, ps_)
    p.free(wf[0], wf[1], wb[0], wb[1], qs)





def emit_l2a(p, xt, y_t):
    nc, em = p.nc, p.em
    H, WIN = 16, 16
    wq_c = p.din("l2_wq_c", [ND, 128, ND * 128])
    wk_c = p.din("l2_wk_c", [ND, 128, ND * 128])
    wv_ct = p.din("l2_wv_ct", [ND, 128, D])
    wq_l = p.din("l2_wq_l", [ND, 128, ND * 128])
    wk_l = p.din("l2_wk_l", [ND, 128, ND * 128])
    wv_lt = p.din("l2_wv_lt", [ND, 128, D])
    wo_c = p.din("l2_wo_c", [ND, 128, ND * 128])
    wo_l = p.din("l2_wo_l", [ND, 128, ND * 128])
    bqc = p.din("l2_bqc", [128, ND]); bkc = p.din("l2_bkc", [128, ND])
    bvc = p.din("l2_bvc", [128, ND]); boc = p.din("l2_boc", [128, ND])
    bql = p.din("l2_bql", [128, ND]); bkl = p.din("l2_bkl", [128, ND])
    bvl = p.din("l2_bvl", [128, ND]); bol = p.din("l2_bol", [128, ND])
    xn_t = p.dout("xn_t", [D, T], BF16)
    xn_tm_t = p.dout("xn_tm_t", [T, D], BF16)
    bnsum = p.dout("bnsum", [128, 2 * ND])

    sm = {}
    for nm, srcd in (("bqc", bqc), ("bkc", bkc), ("bvc", bvc), ("boc", boc),
                     ("bql", bql), ("bkl", bkl), ("bvl", bvl), ("bol", bol)):
        dst = sm[nm] = p.buf("l2s." + nm, [128, ND], F32)
        em.emit("Q", lambda e, d=dst, s=srcd: e.dma_start(out=d[:], in_=s),
                writes=[dst[:]], dma=True)

    xn = [p.buf(f"l2.xn{k}", [128, T], BF16) for k in range(ND)]
    p.layernorm(xt, T, xn, tag="l2ln")
    for k in range(ND):
        em.emit("Q", lambda e, k=k, t=xn[k]: e.dma_start(
            out=xn_t[k * 128:(k + 1) * 128, :], in_=t[:]),
            reads=[xn[k][:]], writes=[f"o.xn.{k}"], dma=True)
    bns = p.buf("l2.bns", [128, 2 * ND], F32)
    scr = p.buf("l2.scr", [128, 512], F32)
    col = p.buf("l2.col", [128, 1], F32)
    for k in range(ND):
        em.emit("V", lambda e, k=k, t=xn[k]: e.reduce_sum(bns[:, k:k + 1], t[:], axis=AX.X),
                reads=[xn[k][:]], writes=[bns[:]], tag="l2.bn1")
        for c in range(4):
            em.emit("S", lambda e, k=k, c=c, t=xn[k]: e.activation(
                scr[:], t[:, c * 512:(c + 1) * 512], AF.Square),
                reads=[xn[k][:]], writes=[scr[:]], tag="l2.bn2")
            em.emit("V", lambda e: e.reduce_sum(col[:], scr[:], axis=AX.X),
                    reads=[scr[:]], writes=[col[:]], tag="l2.bn3")
            if c == 0:
                em.emit("V", lambda e, k=k: e.tensor_copy(bns[:, ND + k:ND + k + 1], col[:]),
                        reads=[col[:]], writes=[bns[:]], tag="l2.bn4")
            else:
                em.emit("V", lambda e, k=k: e.tensor_tensor(
                    bns[:, ND + k:ND + k + 1], bns[:, ND + k:ND + k + 1], col[:], ALU.add),
                    reads=[col[:], bns[:]], writes=[bns[:]], tag="l2.bn4")
    em.emit("Q", lambda e: e.dma_start(out=bnsum, in_=bns[:]),
            reads=[bns[:]], writes=["o.bns"], dma=True)
    ttile = p.buf("l2.ttile", [128, D], BF16)
    pbf = p.psum(7)
    for c in range(T // 128):
        for k in range(ND):
            em.emit("T", lambda e, c=c, k=k, t=xn[k]: e.transpose(
                pbf[:, k * 128:(k + 1) * 128], t[:, c * 128:(c + 1) * 128], p.ident_bf[:]),
                reads=[xn[k][:], p.ident_bf[:]], writes=[pbf[:]], tag="l2.tm1")
        em.emit("V", lambda e: e.tensor_copy(ttile[:], pbf[:]),
                reads=[pbf[:]], writes=[ttile[:]], tag="l2.tm2")
        em.emit("Q", lambda e, c=c: e.dma_start(
            out=xn_tm_t[c * 128:(c + 1) * 128, :], in_=ttile[:]),
            reads=[ttile[:]], writes=["o.xntm"], dma=True, tag="l2.tm3")
    p.free(scr, col, bns, ttile)

    for k in range(ND):
        em.emit("Q", lambda e, k=k, t=xt[k]: e.dma_start(
            out=y_t[k * 128:(k + 1) * 128, :], in_=t[:]),
            reads=[xt[k][:]], writes=[f"o.y.{k}"], dma=True)
    for k in range(ND):
        p.free(xt[k])
        xt[k] = None
    p.phase()

    if "l2cls" in SKIP:
        for k in range(ND):
            p.free(xn[k])
        for v in sm.values():
            p.free(v)
        p.phase()
        return
    kc = [p.buf(f"l2.kc{k}", [128, T], BF16) for k in range(ND)]

    def evac_kc(ot, c0, c1, ps_):
        em.emit("V", lambda e, ot=ot: e.tensor_scalar(
            kc[ot][:, c0:c1], ps_[:, 0:c1 - c0], 1.0, sm["bkc"][:, ot:ot + 1],
            ALU.mult, ALU.add),
            reads=[ps_[:], sm["bkc"][:]], writes=[kc[ot][:]], tag="l2.kc")
    if "l2nokc" not in SKIP:
        p.matmul_to(wk_c, ND, ND, xn, T, evac_kc, tag="l2kc")

    if "l2c1" in SKIP:
        return
    def make_vtm(wvt_dram, tag):
        NCH = 2 if "l2vtm2" in SKIP else T // 128
        EV = "V" if "l2vtmV" in SKIP else "S"
        vt = [p.buf(f"{tag}{c}", [128, D], BF16) for c in range(T // 128)]
        wvb = [p.buf(p.u(tag + ".w"), [128, D], BF16) for _ in range(ND)]
        for k in range(ND):
            em.emit("G", lambda e, k=k, b=wvb[k], w=wvt_dram: e.dma_start(out=b[:], in_=w[k]),
                    writes=[wvb[k][:]], dma=True, tag=tag + ".ld")
        for c in range(NCH):
            for hf in range(2):
                ps_ = p.psum(hf)

                def mmv(e, c=c, hf=hf, xs=tuple(xn), ws=tuple(wvb), ps_=ps_):
                    last = None
                    for k in range(ND):
                        last = e.matmul(ps_[:, :], xs[k][:, c * 128:(c + 1) * 128],
                                        ws[k][:, hf * 512:(hf + 1) * 512],
                                        start=(k == 0), stop=(k == ND - 1))
                    return last
                em.emit("T", mmv, reads=[w[:] for w in wvb] + [x[:] for x in xn],
                        writes=[ps_[:]], tag=tag + ".mm")
                if EV == "S":
                    em.emit("S", lambda e, c=c, hf=hf, vtc=vt[c], ps_=ps_: e.copy(
                        vtc[:, hf * 512:(hf + 1) * 512], ps_[:, :]),
                        reads=[ps_[:]], writes=[vt[c][:]], tag=tag + ".ev")
                else:
                    em.emit("V", lambda e, c=c, hf=hf, vtc=vt[c], ps_=ps_: e.tensor_copy(
                        vtc[:, hf * 512:(hf + 1) * 512], ps_[:, :]),
                        reads=[ps_[:]], writes=[vt[c][:]], tag=tag + ".ev")
        p.free(*wvb)
        return vt
    vct = make_vtm(wv_ct, "l2.vct")

    if "l2c2" in SKIP:
        return
    q0 = p.buf("l2.q0", [128, ND], F32)

    def evac_q0(ot, c0, c1, ps_):
        em.emit("V", lambda e, ot=ot: e.tensor_scalar(
            q0[:, ot:ot + 1], ps_[:, 0:1], 1.0, sm["bqc"][:, ot:ot + 1],
            ALU.mult, ALU.add),
            reads=[ps_[:], sm["bqc"][:]], writes=[q0[:]], tag="l2.q0")
    p.matmul_to(wq_c, ND, ND, [x for x in xn], 1, evac_q0, tag="l2q0")
    em.emit("V", lambda e: e.tensor_scalar_mul(q0[:], q0[:], 0.125),
            reads=[q0[:]], writes=[q0[:]])
    q0b = p.buf("l2.q0b", [128, ND], BF16)
    em.emit("V", lambda e: e.tensor_copy(q0b[:], q0[:]), reads=[q0[:]], writes=[q0b[:]])

    qblk = [p.buf(f"l2.qblk{k}", [128, 16], BF16) for k in range(ND)]
    for k in range(ND):
        em.emit("G", lambda e, t=qblk[k]: e.memset(t[:], 0.0), writes=[qblk[k][:]])
        em.emit("V", lambda e, k=k, t=qblk[k]: e.tensor_copy(
            t[0:64, 2 * k:2 * k + 1], q0b[0:64, k:k + 1]),
            reads=[q0b[:]], writes=[qblk[k][:]], tag="l2.qb1")
        em.emit("V", lambda e, k=k, t=qblk[k]: e.tensor_copy(
            t[64:128, 2 * k + 1:2 * k + 2], q0b[64:128, k:k + 1]),
            reads=[q0b[:]], writes=[qblk[k][:]], tag="l2.qb2")
    if "l2c3" in SKIP:
        return
    if "tapq0" in DBG:
        p.tap("q0", q0)
    if "tapqb" in DBG:
        p.tap("qblk0", qblk[0])
        p.tap("qblk1", qblk[1])
        p.tap("kc0", kc[0])
    sc_cls = p.buf("l2.sccls", [16, T], F32)
    for c in range(4):
        ps_ = p.psum(2)

        def mmsc(e, c=c, tiles=tuple(kc), qb=tuple(qblk), ps_=ps_):
            last = None
            for k in range(ND):
                last = e.matmul(ps_[0:16, :], qb[k][:], tiles[k][:, c * 512:(c + 1) * 512],
                                start=(k == 0), stop=(k == ND - 1))
            return last
        em.emit("T", mmsc, reads=[t[:] for t in qblk] + [t[:] for t in kc],
                writes=[ps_[:]], tag="l2.scmm")
        em.emit("V", lambda e, c=c, ps_=ps_: e.tensor_copy(sc_cls[:, c * 512:(c + 1) * 512],
                                                           ps_[0:16, :]),
                reads=[ps_[:]], writes=[sc_cls[:]], tag="l2.scev")
    for k in range(ND):
        p.free(qblk[k], kc[k])
    rmx = p.buf("l2.rmx", [16, 1], F32)
    rsm = p.buf("l2.rsm", [16, 1], F32)
    em.emit("V", lambda e: e.reduce_max(rmx[:], sc_cls[:], axis=AX.X),
            reads=[sc_cls[:]], writes=[rmx[:]])
    em.emit("V", lambda e: e.tensor_scalar_mul(rmx[:], rmx[:], -1.0),
            reads=[rmx[:]], writes=[rmx[:]])
    em.emit("S", lambda e: e.activation(sc_cls[:], sc_cls[:], AF.Exp, bias=rmx[0:16, :]),
            reads=[sc_cls[:], rmx[:]], writes=[sc_cls[:]])
    em.emit("V", lambda e: e.reduce_sum(rsm[:], sc_cls[:], axis=AX.X),
            reads=[sc_cls[:]], writes=[rsm[:]])
    em.emit("V", lambda e: e.reciprocal(rsm[:], rsm[:]), reads=[rsm[:]], writes=[rsm[:]])
    awc = p.buf("l2.awc", [16, T], BF16)
    em.emit("V", lambda e: e.tensor_scalar(awc[:], sc_cls[:], rsm[0:16, :], None, ALU.mult),
            reads=[sc_cls[:], rsm[:]], writes=[awc[:]])
    p.free(sc_cls)
    if "tapsc" in DBG:
        p.tap("sccls", sc_cls)
    if "l2c4" in SKIP:
        return
    awT = p.buf("l2.awT", [128, 16 * (T // 128)], BF16)
    pbf2 = p.psum(7)
    for c in range(T // 128):
        em.emit("T", lambda e, c=c: e.transpose(
            pbf2[0:128, 0:16], awc[0:16, c * 128:(c + 1) * 128], p.ident_bf[0:16, 0:16]),
            reads=[awc[:], p.ident_bf[:]], writes=[pbf2[:]], tag="l2.awt")
        em.emit("V", lambda e, c=c: e.tensor_copy(awT[:, c * 16:(c + 1) * 16], pbf2[0:128, 0:16]),
                reads=[pbf2[:]], writes=[awT[:]], tag="l2.awtev")
    ctxall = p.buf("l2.ctxall", [16, D], F32)
    for hf in range(2):
        ps_ = p.psum(3)

        def mmctx(e, hf=hf, vt=tuple(vct), ps_=ps_):
            last = None
            for c in range(T // 128):
                last = e.matmul(ps_[0:16, :], awT[:, c * 16:(c + 1) * 16],
                                vt[c][:, hf * 512:(hf + 1) * 512],
                                start=(c == 0), stop=(c == T // 128 - 1))
            return last
        em.emit("T", mmctx, reads=[awT[:]] + [v[:] for v in vct], writes=[ps_[:]],
                tag="l2.ctxmm")
        em.emit("V", lambda e, hf=hf, ps_=ps_: e.tensor_copy(
            ctxall[:, hf * 512:(hf + 1) * 512], ps_[0:16, :]),
            reads=[ps_[:]], writes=[ctxall[:]], tag="l2.ctxev")
    for c in range(T // 128):
        p.free(vct[c])
    p.free(awc, awT, rmx, rsm, q0, q0b)
    if "l2c5" in SKIP:
        return
    if "tapctx" in DBG:
        p.tap("ctxall", ctxall)
    ctx_fm = p.buf("l2.ctxfm", [128, ND], F32)
    for h in range(16):
        em.emit("G", lambda e, h=h: e.dma_start(
            out=ctx_fm[(h % 2) * 64:(h % 2) * 64 + 64, h // 2:h // 2 + 1],
            in_=ctxall[h:h + 1, h * 64:(h + 1) * 64]),
            reads=[ctxall[:]], writes=[ctx_fm[:]], dma=True, tag="l2.diag")
    em.emit("V", lambda e: e.tensor_tensor(ctx_fm[:], ctx_fm[:], sm["bvc"][:], ALU.add),
            reads=[ctx_fm[:], sm["bvc"][:]], writes=[ctx_fm[:]])
    ctx_fmb = p.buf("l2.ctxfmb", [128, ND], BF16)
    em.emit("V", lambda e: e.tensor_copy(ctx_fmb[:], ctx_fm[:]),
            reads=[ctx_fm[:]], writes=[ctx_fmb[:]])
    if "l2c6" in SKIP:
        return
    gcol = p.buf("l2.gcol", [128, ND], F32)

    def evac_go(ot, c0, c1, ps_):
        em.emit("V", lambda e, ot=ot: e.tensor_scalar(
            gcol[:, ot:ot + 1], ps_[:, 0:1], 1.0, sm["boc"][:, ot:ot + 1],
            ALU.mult, ALU.add),
            reads=[ps_[:], sm["boc"][:]], writes=[gcol[:]], tag="l2.go")
    p.matmul_to(wo_c, ND, ND, [ctx_fmb[:, k:k + 1].unsqueeze(0).squeeze(0) if False else ctx_fmb for k in range(ND)], 1, evac_go, tag="l2go", xsel=True)
    for ot in range(ND):
        em.emit("G", lambda e, ot=ot: e.dma_start(
            out=y_t[ot * 128:(ot + 1) * 128, 0:1], in_=gcol[:, ot:ot + 1],
            accum_op=ALU.add, allow_slow_non_contiguous=True),
            reads=[gcol[:]], writes=[f"o.y.{ot}"], dma=True, tag="l2.goacc")
    p.free(ctxall, ctx_fm, ctx_fmb, gcol)
    p.phase()

    if "l2loc" in SKIP:
        for k in range(ND):
            p.free(xn[k])
        for v in sm.values():
            p.free(v)
        p.phase()
        return
    ql = [p.buf(f"l2.ql{k}", [128, T], BF16) for k in range(ND)]

    def evac_ql(ot, c0, c1, ps_):
        em.emit("V", lambda e, ot=ot: e.tensor_scalar(
            ql[ot][:, c0:c1], ps_[:, 0:c1 - c0], sm["bql"][:, ot:ot + 1], 0.125,
            ALU.add, ALU.mult),
            reads=[ps_[:], sm["bql"][:]], writes=[ql[ot][:]], tag="l2.ql")
    p.matmul_to(wq_l, ND, ND, xn, T, evac_ql, tag="l2ql")
    kl = [p.buf(f"l2.kl{k}", [128, T], BF16) for k in range(ND)]

    def evac_kl(ot, c0, c1, ps_):
        em.emit("V", lambda e, ot=ot: e.tensor_scalar(
            kl[ot][:, c0:c1], ps_[:, 0:c1 - c0], sm["bkl"][:, ot:ot + 1], None, ALU.add),
            reads=[ps_[:], sm["bkl"][:]], writes=[kl[ot][:]], tag="l2.kl")
    p.matmul_to(wk_l, ND, ND, xn, T, evac_kl, tag="l2kl")
    vlt = make_vtm(wv_lt, "l2.vlt")
    for k in range(ND):
        p.free(xn[k])
    p.phase()

    WW = 384
    mgen = p.buf("l2.mgen", [128, WW], F32)
    mb0 = p.buf("l2.mb0", [128, WW], F32)
    NEG = -3.0e30
    em.emit("G", lambda e: e.memset(mgen[:], 0.0), writes=[mgen[:]])
    em.emit("G", lambda e: e.affine_select(mgen[:], mgen[:], pattern=[[1, WW]],
                                           compare_op=ALU.is_ge, fill=NEG,
                                           base=-113, channel_multiplier=-1),
            reads=[mgen[:]], writes=[mgen[:]])
    em.emit("G", lambda e: e.affine_select(mgen[:], mgen[:], pattern=[[-1, WW]],
                                           compare_op=ALU.is_ge, fill=NEG,
                                           base=145, channel_multiplier=1),
            reads=[mgen[:]], writes=[mgen[:]])
    em.emit("G", lambda e: e.memset(mb0[:], 0.0), writes=[mb0[:]])
    em.emit("G", lambda e: e.affine_select(mb0[:], mb0[:], pattern=[[1, WW]],
                                           compare_op=ALU.is_ge, fill=NEG,
                                           base=15, channel_multiplier=-1),
            reads=[mb0[:]], writes=[mb0[:]])
    em.emit("G", lambda e: e.affine_select(mb0[:], mb0[:], pattern=[[-1, WW]],
                                           compare_op=ALU.is_ge, fill=NEG,
                                           base=17, channel_multiplier=1),
            reads=[mb0[:]], writes=[mb0[:]])
    em.emit("G", lambda e: e.affine_select(mb0[:], mb0[:], pattern=[[1, WW]],
                                           compare_op=ALU.is_ge, fill=NEG,
                                           base=-1, channel_multiplier=0),
            reads=[mb0[:]], writes=[mb0[:]])

    ctx_loc = [p.buf(f"l2.cl{k}", [128, T], BF16) for k in range(ND)]
    aws2 = [p.buf(f"l2.aws{i}", [128, WW], F32) for i in range(2)]
    awb22 = [p.buf(f"l2.awb2{i}", [128, WW], BF16) for i in range(2)]
    awT22 = [p.buf(f"l2.awT2{i}", [128, WW], BF16) for i in range(2)]
    lrmx2 = [p.buf(f"l2.lrmx{i}", [128, 1], F32) for i in range(2)]
    lrsm2 = [p.buf(f"l2.lrsm{i}", [128, 1], F32) for i in range(2)]
    pbf3 = p.psum(7)
    for b in range(16):
        nq = 128 if b < 15 else 127
        t0 = 1 + 128 * b
        cA = max(b - 1, 0)
        chunks = [cA, cA + 1, cA + 2]
        live = [c for c in chunks if c < 16]
        msk = mb0 if b == 0 else mgen
        for h in range(16):
            th, pb = h // 2, (h % 2) * 64
            par = (b * 16 + h) % 2
            aws, awb2, awT2 = aws2[par], awb22[par], awT22[par]
            lrmx, lrsm = lrmx2[par], lrsm2[par]
            ps_s = p.psum(2 + 2 * par)

            def mms(e, b=b, th=th, pb=pb, t0=t0, nq=nq, chunks=tuple(chunks), ps_s=ps_s):
                last = None
                for jj, cc in enumerate(chunks):
                    if cc >= 16:
                        continue
                    last = e.matmul(ps_s[0:nq, jj * 128:(jj + 1) * 128],
                                    ql[th][pb:pb + 64, t0:t0 + nq],
                                    kl[th][pb:pb + 64, cc * 128:(cc + 1) * 128],
                                    start=True, stop=True)
                return last
            em.emit("T", mms, reads=[ql[th][:], kl[th][:]], writes=[ps_s[:]], tag="l2.lsc")
            if len(live) < 3:
                em.emit("V", lambda e, nq=nq, ps_s=ps_s: e.memset(ps_s[0:nq, 256:384], -3.0e30),
                        reads=[], writes=[ps_s[:]], tag="l2.lz")
            em.emit("V", lambda e, nq=nq, msk=msk, ps_s=ps_s, aws=aws: e.tensor_tensor(
                aws[0:nq, :], ps_s[0:nq, 0:WW], msk[0:nq, :], ALU.add),
                reads=[ps_s[:], msk[:]], writes=[aws[:]], tag="l2.lmask")
            em.emit("V", lambda e, nq=nq, aws=aws, lrmx=lrmx: e.reduce_max(
                lrmx[0:nq, :], aws[0:nq, :], axis=AX.X),
                reads=[aws[:]], writes=[lrmx[:]], tag="l2.lmax")
            em.emit("V", lambda e, nq=nq, lrmx=lrmx: e.tensor_scalar_mul(
                lrmx[0:nq, :], lrmx[0:nq, :], -1.0),
                reads=[lrmx[:]], writes=[lrmx[:]], tag="l2.lneg")
            em.emit("S", lambda e, nq=nq, aws=aws, lrmx=lrmx: e.activation(
                aws[0:nq, :], aws[0:nq, :], AF.Exp, bias=lrmx[0:nq, :]),
                reads=[aws[:], lrmx[:]], writes=[aws[:]], tag="l2.lexp")
            em.emit("V", lambda e, nq=nq, aws=aws, lrsm=lrsm: e.reduce_sum(
                lrsm[0:nq, :], aws[0:nq, :], axis=AX.X),
                reads=[aws[:]], writes=[lrsm[:]], tag="l2.lsum")
            em.emit("V", lambda e, nq=nq, lrsm=lrsm: e.reciprocal(
                lrsm[0:nq, :], lrsm[0:nq, :]),
                reads=[lrsm[:]], writes=[lrsm[:]], tag="l2.lrec")
            if nq < 128:
                em.emit("G", lambda e, awb2=awb2: e.memset(awb2[:], 0.0),
                        writes=[awb2[:]], tag="l2.lpad")
            em.emit("V", lambda e, nq=nq, aws=aws, lrsm=lrsm, awb2=awb2: e.tensor_scalar(
                awb2[0:nq, :], aws[0:nq, :], lrsm[0:nq, :], None, ALU.mult),
                reads=[aws[:], lrsm[:]], writes=[awb2[:]], tag="l2.lnorm")
            po = par * 384
            for j in range(3):
                em.emit("T", lambda e, j=j, po=po, awb2=awb2: e.transpose(
                    pbf3[:, po + j * 128:po + (j + 1) * 128],
                    awb2[:, j * 128:(j + 1) * 128], p.ident_bf[:]),
                    reads=[awb2[:], p.ident_bf[:]], writes=[f"ps7.{par}"], tag="l2.lt")
                em.emit("V", lambda e, j=j, po=po, awT2=awT2: e.tensor_copy(
                    awT2[:, j * 128:(j + 1) * 128], pbf3[:, po + j * 128:po + j * 128 + 128]),
                    reads=[f"ps7.{par}"], writes=[awT2[:]], tag="l2.ltev")
            ps_c = p.psum(3 + 2 * par)

            def mmc(e, h=h, nq=nq, chunks=tuple(chunks), vt=tuple(vlt), ps_c=ps_c,
                    awT2=awT2):
                last = None
                first = True
                for jj, cc in enumerate(chunks):
                    if cc >= 16:
                        continue
                    last = e.matmul(ps_c[0:64, 0:nq], vt[cc][:, h * 64:(h + 1) * 64],
                                    awT2[:, jj * 128:jj * 128 + nq],
                                    start=first, stop=(jj == len(chunks) - 1 or cc == 15))
                    first = False
                return last
            em.emit("T", mmc, reads=[vlt[c][:] for c in live] + [awT2[:]],
                    writes=[ps_c[:]], tag="l2.lctx")
            em.emit("V", lambda e, th=th, pb=pb, t0=t0, nq=nq, ps_c=ps_c: e.tensor_copy(
                ctx_loc[th][pb:pb + 64, t0:t0 + nq], ps_c[0:64, 0:nq]),
                reads=[ps_c[:]], writes=[ctx_loc[th][:]], tag="l2.lcev")
    for k in range(ND):
        p.free(ql[k], kl[k])
    for c in range(T // 128):
        p.free(vlt[c])
    p.free(aws2[0], aws2[1], awb22[0], awb22[1], awT22[0], awT22[1],
           lrmx2[0], lrmx2[1], lrsm2[0], lrsm2[1], mgen, mb0)
    for k in range(ND):
        em.emit("G", lambda e, k=k, t=ctx_loc[k]: e.memset(t[:, 0:1], 0.0),
                writes=[ctx_loc[k][:]], tag="l2.c0")
        em.emit("V", lambda e, k=k, t=ctx_loc[k]: e.tensor_scalar(
            t[:, 1:T], t[:, 1:T], sm["bvl"][:, k:k + 1], None, ALU.add),
            reads=[ctx_loc[k][:], sm["bvl"][:]], writes=[ctx_loc[k][:]], tag="l2.cbias")
    p.phase()

    och = p.buf("l2.och", [128, 512], F32)

    def evac_ol(ot, c0, c1, ps_):
        em.emit("V", lambda e, ot=ot, c0=c0, c1=c1: e.tensor_scalar(
            och[:, 0:c1 - c0], ps_[:, 0:c1 - c0], sm["bol"][:, ot:ot + 1], None, ALU.add),
            reads=[ps_[:], sm["bol"][:]], writes=[och[:]], tag="l2.ol1")
        lo = 1 if c0 == 0 else 0
        em.emit("G", lambda e, ot=ot, c0=c0, c1=c1, lo=lo: e.dma_start(
            out=y_t[ot * 128:(ot + 1) * 128, c0 + lo:c1], in_=och[:, lo:c1 - c0],
            accum_op=ALU.add),
            reads=[och[:]], writes=[f"o.y.{ot}"], dma=True, tag="l2.olacc")
    p.matmul_to(wo_l, ND, ND, ctx_loc, T, evac_ol, tag="l2ol")
    for k in range(ND):
        p.free(ctx_loc[k])
    p.free(och)
    for v in sm.values():
        p.free(v)
    p.phase()




def emit_l2b(p, xt, y_t):
    nc, em = p.nc, p.em
    NH, NK, KTOP = 8, 256, 8
    NS = T // 128
    xn_in = p.din("xn_in", [D, T], BF16)
    xn_tm_in = p.din("xn_tm_in", [T, D], BF16)
    bn_all = p.din("bn_all", [128, NCORES * 16])
    wq_d = p.din("l2_wq", [16, 128, ND * 128])
    keys_d = p.din("l2_keys", [16, 128, NK])
    gam = p.din("l2_gam", [128, ND])
    bet = p.din("l2_bet", [128, ND])
    down_d = p.din("l2_down", [65536, D])
    up_d = p.din("l2_up", [65536, D])

    xn = [p.buf(f"l2b.xn{k}", [128, T], BF16) for k in range(ND)]
    for k in range(ND):
        em.emit("Q", lambda e, k=k, t=xn[k]: e.dma_start(
            out=t[:], in_=xn_in[k * 128:(k + 1) * 128, :]),
            writes=[xn[k][:]], dma=True)
    bns = p.buf("l2b.bns", [128, NCORES * 16], F32)
    em.emit("Q", lambda e: e.dma_start(out=bns[:], in_=bn_all), writes=[bns[:]], dma=True)
    sgam = p.buf("l2b.gam", [128, ND], F32)
    sbet = p.buf("l2b.bet", [128, ND], F32)
    em.emit("Q", lambda e: e.dma_start(out=sgam[:], in_=gam), writes=[sgam[:]], dma=True)
    em.emit("Q", lambda e: e.dma_start(out=sbet[:], in_=bet), writes=[sbet[:]], dma=True)
    keys = [p.buf(f"l2b.keys{i}", [128, NK], BF16) for i in range(16)]
    for i in range(16):
        em.emit("G", lambda e, i=i, t=keys[i]: e.dma_start(out=t[:], in_=keys_d[i]),
                writes=[keys[i][:]], dma=True)

    tots = p.buf("l2b.tots", [128, 16], F32)
    em.emit("V", lambda e: e.tensor_copy(tots[:], bns[:, 0:16]), reads=[bns[:]], writes=[tots[:]])
    for c in range(1, NCORES):
        em.emit("V", lambda e, c=c: e.tensor_tensor(tots[:], tots[:], bns[:, c * 16:(c + 1) * 16],
                                                    ALU.add),
                reads=[tots[:], bns[:]], writes=[tots[:]])
    NTOK = float(NCORES * T)
    mean = p.buf("l2b.mean", [128, ND], F32)
    var = p.buf("l2b.var", [128, ND], F32)
    acoef = p.buf("l2b.ac", [128, ND], F32)
    bcoef = p.buf("l2b.bc", [128, ND], F32)
    em.emit("V", lambda e: e.tensor_scalar_mul(mean[:], tots[:, 0:8], 1.0 / NTOK),
            reads=[tots[:]], writes=[mean[:]])
    em.emit("V", lambda e: e.tensor_tensor(var[:], mean[:], mean[:], ALU.mult),
            reads=[mean[:]], writes=[var[:]])
    em.emit("V", lambda e: e.scalar_tensor_tensor(var[:], tots[:, 8:16], 1.0 / NTOK, var[:],
                                                  ALU.mult, ALU.subtract),
            reads=[tots[:], var[:]], writes=[var[:]])
    e5 = p.cval(1e-5)
    em.emit("S", lambda e: e.activation(var[:], var[:], AF.Sqrt, bias=e5[:]),
            reads=[var[:], e5[:]], writes=[var[:]])
    em.emit("V", lambda e: e.reciprocal(var[:], var[:]), reads=[var[:]], writes=[var[:]])
    em.emit("V", lambda e: e.tensor_tensor(acoef[:], var[:], sgam[:], ALU.mult),
            reads=[var[:], sgam[:]], writes=[acoef[:]])
    em.emit("V", lambda e: e.tensor_tensor(bcoef[:], mean[:], acoef[:], ALU.mult),
            reads=[mean[:], acoef[:]], writes=[bcoef[:]])
    em.emit("V", lambda e: e.tensor_tensor(bcoef[:], sbet[:], bcoef[:], ALU.subtract),
            reads=[sbet[:], bcoef[:]], writes=[bcoef[:]])
    xnb = [p.buf(f"l2b.xnb{k}", [128, T], BF16) for k in range(ND)]
    for k in range(ND):
        em.emit("V", lambda e, k=k, t=xnb[k], s=xn[k]: e.tensor_scalar(
            t[:], s[:], acoef[:, k:k + 1], bcoef[:, k:k + 1], ALU.mult, ALU.add),
            reads=[xn[k][:], acoef[:], bcoef[:]], writes=[xnb[k][:]], tag="l2b.xnb")
    for k in range(ND):
        p.free(xn[k])
    p.phase()

    qbs = [p.buf(f"l2b.qb{i}", [128, T], BF16) for i in range(2)]
    wqb = [p.buf(f"l2b.wq{i}", [128, ND * 128], BF16) for i in range(2)]
    scfs = [p.buf(f"l2b.scf{i}", [128, NK], F32) for i in range(2)]
    sc_p = [p.buf(f"l2b.scp{pp}", [128, NS * 64], F32) for pp in range(2)]
    idx_p = [p.buf(f"l2b.idxp{pp}", [128, NS * 64], U32) for pp in range(2)]
    for i in range(16):
        pp, h = i // 8, i % 8
        wb = wqb[i % 2]
        qb = qbs[i % 2]
        em.emit("G", lambda e, i=i, wb=wb: e.dma_start(out=wb[:], in_=wq_d[i]),
                writes=[wb[:]], dma=True, tag="l2b.wqld")
        for c in range(4):
            ps_ = p.psum(c % 2)

            def mmq(e, c=c, wb=wb, tiles=tuple(xnb), ps_=ps_):
                last = None
                for k in range(ND):
                    last = e.matmul(ps_[:, :], wb[:, k * 128:(k + 1) * 128],
                                    tiles[k][:, c * 512:(c + 1) * 512],
                                    start=(k == 0), stop=(k == ND - 1))
                return last
            em.emit("T", mmq, reads=[wb[:]] + [x[:] for x in xnb], writes=[ps_[:]],
                    tag="l2b.qmm")
            em.emit("S", lambda e, c=c, qb=qb, ps_=ps_: e.copy(qb[:, c * 512:(c + 1) * 512], ps_[:, :]),
                    reads=[ps_[:]], writes=[qb[:]], tag="l2b.qev")
        kt = keys[h * 2 + pp]
        for st in range(NS):
            ps2 = p.psum(2 + st % 2)
            scf = scfs[st % 2]
            em.emit("T", lambda e, st=st, qb=qb, kt=kt, ps2=ps2: e.matmul(
                ps2[:, 0:NK], qb[:, st * 128:(st + 1) * 128], kt[:],
                start=True, stop=True),
                reads=[qb[:], kt[:]], writes=[ps2[:]], tag="l2b.sim")
            em.emit("V", lambda e, scf=scf, ps2=ps2: e.tensor_copy(scf[:], ps2[:, 0:NK]),
                    reads=[ps2[:]], writes=[scf[:]], tag="l2b.simev")
            o0 = st * 64 + h * 8
            em.emit("V", lambda e, pp=pp, o0=o0, scf=scf: e.max(sc_p[pp][:, o0:o0 + 8], scf[:]),
                    reads=[scf[:]], writes=[sc_p[pp][:]], tag="l2b.max8")
            em.emit("V", lambda e, pp=pp, o0=o0, scf=scf: e.max_index(
                idx_p[pp][:, o0:o0 + 8], sc_p[pp][:, o0:o0 + 8], scf[:]),
                reads=[scf[:], sc_p[pp][:]], writes=[idx_p[pp][:]], tag="l2b.idx8")
    p.free(qbs[0], qbs[1], wqb[0], wqb[1], scfs[0], scfs[1])
    for i in range(16):
        p.free(keys[i])
    p.phase()

    allsc = p.buf("l2b.allsc", [128, 512], F32)
    allidx = p.buf("l2b.allidx", [128, 512], U32)
    tmpu = p.buf("l2b.tmpu", [128, 512], U32)
    scg = p.buf("l2b.scg", [128, 64], F32)
    pkg = p.buf("l2b.pkg", [128, 64], U32)
    rowg = p.buf("l2b.rowg", [128, 64], U32)
    colg = p.buf("l2b.colg", [128, 64], U32)
    accr = p.buf("l2b.accr", [128, 64], U32)
    accc = p.buf("l2b.accc", [128, 64], U32)
    mm_ = p.buf("l2b.mm", [128, 64], U32)
    idxall = p.buf("l2b.idxall", [128, NS * 64], I32)
    wsg = p.buf("l2b.wsg", [128, NS * 64], F32)
    for st in range(NS):
        s0 = st * 64
        A = sc_p[0][:, s0:s0 + 64].rearrange("p (h i) -> p h i", h=8).unsqueeze(3).broadcast_to((128, 8, 8, 8))
        B = sc_p[1][:, s0:s0 + 64].rearrange("p (h j) -> p h j", h=8).unsqueeze(2).broadcast_to((128, 8, 8, 8))
        em.emit("V", lambda e, A=A, B=B: e.tensor_tensor(
            allsc[:].rearrange("p (h i j) -> p h i j", h=8, i=8), A, B, ALU.add),
            reads=[sc_p[0][:], sc_p[1][:]], writes=[allsc[:]], tag="l2b.comb")
        em.emit("V", lambda e, s0=s0: e.tensor_single_scalar(
            tmpu[:, 0:64], idx_p[0][:, s0:s0 + 64], 8, ALU.logical_shift_left),
            reads=[idx_p[0][:]], writes=[tmpu[:]], tag="l2b.shl")
        Ai = tmpu[:, 0:64].rearrange("p (h i) -> p h i", h=8).unsqueeze(3).broadcast_to((128, 8, 8, 8))
        Bi = idx_p[1][:, s0:s0 + 64].rearrange("p (h j) -> p h j", h=8).unsqueeze(2).broadcast_to((128, 8, 8, 8))
        em.emit("V", lambda e, Ai=Ai, Bi=Bi: e.tensor_tensor(
            allidx[:].rearrange("p (h i j) -> p h i j", h=8, i=8), Ai, Bi, ALU.bitwise_or),
            reads=[tmpu[:], idx_p[1][:]], writes=[allidx[:]], tag="l2b.or")
        for h in range(NH):
            em.emit("V", lambda e, h=h: e.max(scg[:, h * 8:(h + 1) * 8],
                                              allsc[:, h * 64:(h + 1) * 64]),
                    reads=[allsc[:]], writes=[scg[:]], tag="l2b.m2")
            em.emit("V", lambda e, h=h: e.max_index(
                pkg[:, h * 8:(h + 1) * 8], scg[:, h * 8:(h + 1) * 8],
                allsc[:, h * 64:(h + 1) * 64]),
                reads=[allsc[:], scg[:]], writes=[pkg[:]], tag="l2b.i2")
        em.emit("V", lambda e: e.tensor_single_scalar(rowg[:, 0:64], pkg[:, 0:64], 3,
                                                      ALU.logical_shift_right),
                reads=[pkg[:]], writes=[rowg[:]], tag="l2b.row")
        em.emit("V", lambda e: e.tensor_single_scalar(colg[:, 0:64], pkg[:, 0:64], 7,
                                                      ALU.bitwise_and),
                reads=[pkg[:]], writes=[colg[:]], tag="l2b.col")
        for w in range(8):
            i0w = tmpu[:, 0:64].rearrange("p (h i) -> p h i", h=8)[:, :, w:w + 1].broadcast_to((128, 8, 8))
            i1w = idx_p[1][:, s0:s0 + 64].rearrange("p (h j) -> p h j", h=8)[:, :, w:w + 1].broadcast_to((128, 8, 8))
            em.emit("V", lambda e, w=w: e.tensor_single_scalar(
                mm_[:, 0:64], rowg[:, 0:64], w, ALU.is_equal),
                reads=[rowg[:]], writes=[mm_[:]], tag="l2b.oh1")
            em.emit("V", lambda e, w=w, i0w=i0w: e.tensor_tensor(
                mm_[:].rearrange("p (h i) -> p h i", h=8),
                mm_[:].rearrange("p (h i) -> p h i", h=8), i0w, ALU.mult),
                reads=[mm_[:], tmpu[:]], writes=[mm_[:]], tag="l2b.oh2")
            if w == 0:
                em.emit("V", lambda e: e.tensor_copy(accr[:], mm_[:]),
                        reads=[mm_[:]], writes=[accr[:]], tag="l2b.oh3")
            else:
                em.emit("V", lambda e: e.tensor_tensor(accr[:], accr[:], mm_[:], ALU.add),
                        reads=[mm_[:], accr[:]], writes=[accr[:]], tag="l2b.oh3")
            em.emit("V", lambda e, w=w: e.tensor_single_scalar(
                mm_[:, 0:64], colg[:, 0:64], w, ALU.is_equal),
                reads=[colg[:]], writes=[mm_[:]], tag="l2b.oh4")
            em.emit("V", lambda e, w=w, i1w=i1w: e.tensor_tensor(
                mm_[:].rearrange("p (h i) -> p h i", h=8),
                mm_[:].rearrange("p (h i) -> p h i", h=8), i1w, ALU.mult),
                reads=[mm_[:], idx_p[1][:]], writes=[mm_[:]], tag="l2b.oh5")
            if w == 0:
                em.emit("V", lambda e: e.tensor_copy(accc[:], mm_[:]),
                        reads=[mm_[:]], writes=[accc[:]], tag="l2b.oh6")
            else:
                em.emit("V", lambda e: e.tensor_tensor(accc[:], accc[:], mm_[:], ALU.add),
                        reads=[mm_[:], accc[:]], writes=[accc[:]], tag="l2b.oh6")
        em.emit("V", lambda e: e.tensor_tensor(accr[:], accr[:], accc[:], ALU.bitwise_or),
                reads=[accr[:], accc[:]], writes=[accr[:]], tag="l2b.fin")
        em.emit("V", lambda e, s0=s0: e.tensor_copy(idxall[:, s0:s0 + 64], accr[:]),
                reads=[accr[:]], writes=[idxall[:]], tag="l2b.fin2")
        em.emit("S", lambda e, s0=s0: e.activation(wsg[:, s0:s0 + 64], scg[:, 0:64], AF.Sigmoid),
                reads=[scg[:]], writes=[wsg[:]], tag="l2b.sig")
    p.free(allsc, allidx, tmpu, scg, pkg, rowg, colg, accr, accc, mm_)
    p.free(sc_p[0], sc_p[1], idx_p[0], idx_p[1])
    p.phase()

    NB = 6
    xtm = [p.buf(f"l2b.xtm{i}", [128, D], BF16) for i in range(2)]
    gd = [p.buf(f"l2b.gd{i}", [128, D], BF16) for i in range(NB)]
    gu = [p.buf(f"l2b.gu{i}", [128, D], BF16) for i in range(NB)]
    hmid = [p.buf(f"l2b.hmid{i}", [128, 64], F32) for i in range(2)]
    wfin = [p.buf(f"l2b.wfin{i}", [128, 64], F32) for i in range(2)]
    dscr = p.buf("l2b.dscr", [128, D], F32)
    dg = [p.buf(f"l2b.dg{i}", [128, 128], BF16) for i in range(2)]
    otm = [p.buf(f"l2b.otm{i}", [128, D], F32) for i in range(2)]
    for st in range(NS):
        pi = st % 2
        xb, hm, wf, ot = xtm[pi], hmid[pi], wfin[pi], otm[pi]
        pa, pb = (p.psum(0), p.psum(1)) if pi == 0 else (p.psum(2), p.psum(3))
        em.emit("Q", lambda e, st=st, xb=xb: e.dma_start(
            out=xb[:], in_=xn_tm_in[st * 128:(st + 1) * 128, :]),
            writes=[xb[:]], dma=True, tag="l2b.xtmld")
        for j in range(64):
            g = gd[j % NB]
            em.emit("G", lambda e, st=st, j=j, g=g: e.indirect_dma_start(
                g[:], None, down_d,
                IndirectOffsetOnAxis(ap=idxall[:, st * 64 + j:st * 64 + j + 1], axis=0)),
                reads=[idxall[:]], writes=[g[:]], dma=True, tag="l2b.gdown")
            em.emit("V", lambda e, j=j, g=g, xb=xb, hm=hm: e.scalar_tensor_tensor(
                dscr[:], g[:], 1.0, xb[:], ALU.mult, ALU.mult,
                accum_out=hm[:, j:j + 1]),
                reads=[g[:], xb[:]], writes=[dscr[:], hm[:]], tag="l2b.dot")
        em.emit("S", lambda e, hm=hm, wf=wf: e.activation(wf[:], hm[:], AF.Gelu),
                reads=[hm[:]], writes=[wf[:]], tag="l2b.gelu")
        em.emit("V", lambda e, st=st, wf=wf: e.tensor_tensor(
            wf[:], wf[:], wsg[:, st * 64:(st + 1) * 64], ALU.mult),
            reads=[wf[:], wsg[:]], writes=[wf[:]], tag="l2b.wm")
        for j in range(64):
            g = gu[j % NB]
            d_ = dg[j % 2]
            em.emit("G", lambda e, st=st, j=j, g=g: e.indirect_dma_start(
                g[:], None, up_d,
                IndirectOffsetOnAxis(ap=idxall[:, st * 64 + j:st * 64 + j + 1], axis=0)),
                reads=[idxall[:]], writes=[g[:]], dma=True, tag="l2b.gup")
            em.emit("V", lambda e, j=j, d_=d_, wf=wf: e.tensor_tensor(
                d_[:], wf[:, j:j + 1].broadcast_to((128, 128)), p.ident_bf[:], ALU.mult),
                reads=[wf[:], p.ident_bf[:]], writes=[d_[:]], tag="l2b.dg")

            def mmup(e, j=j, g=g, d_=d_, pa=pa, pb=pb):
                e.matmul(pa[:, :], d_[:], g[:, 0:512],
                         start=(j == 0), stop=(j == 63))
                return e.matmul(pb[:, :], d_[:], g[:, 512:1024],
                                start=(j == 0), stop=(j == 63))
            em.emit("T", mmup, reads=[d_[:], g[:]],
                    writes=[pa[:], pb[:]], tag="l2b.upmm")
        for hf in range(2):
            ph = pa if hf == 0 else pb
            em.emit("V", lambda e, hf=hf, ph=ph, ot=ot: e.tensor_copy(
                ot[:, hf * 512:(hf + 1) * 512], ph[:, :]),
                reads=[ph[:]], writes=[ot[:]], tag="l2b.oev")
        for k in range(ND):
            ps2 = p.psum(4 + k % 2)
            em.emit("T", lambda e, k=k, ps2=ps2, ot=ot: e.transpose(
                ps2[:, 0:128], ot[:, k * 128:(k + 1) * 128], p.ident[:]),
                reads=[ot[:], p.ident[:]], writes=[ps2[:]], tag="l2b.otp")
            em.emit("V", lambda e, k=k, st=st, ps2=ps2, t=xt[k]: e.tensor_tensor(
                t[:, st * 128:(st + 1) * 128], t[:, st * 128:(st + 1) * 128],
                ps2[:, 0:128], ALU.add),
                reads=[ps2[:], xt[k][:]], writes=[xt[k][:]], tag="l2b.oadd")
    p.free(xtm[0], xtm[1], hmid[0], hmid[1], wfin[0], wfin[1], dscr,
           dg[0], dg[1], otm[0], otm[1], *gd, *gu)
    for k in range(ND):
        p.free(xnb[k])
    p.free(idxall, wsg, bns, sgam, sbet, tots, mean, var, acoef, bcoef)
    p.phase()




def emit_l3(p, xt, y_t):
    nc, em = p.nc, p.em
    DH, NDH = 4096, 32
    tdec = p.din("l3_tdec", [64, 1])
    tfir = p.din("l3_tfir", [64, 1])
    w_k = p.din("l3_wk", [1, 128, ND * 64])
    w_v = p.din("l3_wv", [1, 128, ND * 64])
    w_o = p.din("l3_wo", [ND, 64, 128])
    tsg = p.din("l3_tsg", [128, ND])
    lnw = p.din("l3_lnw", [128, ND])
    lnb = p.din("l3_lnb", [128, ND])
    w1d = p.din("l3_w1", [NDH, 128, ND * 128])
    w2d = p.din("l3_w2", [NDH, 128, ND * 128])
    h1d = p.din("l3_h1", [NDH, 128, NDH * 128])
    h2d = p.din("l3_h2", [NDH, 128, NDH * 128])
    w3d = p.din("l3_w3", [ND, 128, NDH * 128])
    s1 = p.din("l3_s1", [128, NDH])
    s2 = p.din("l3_s2", [128, NDH])
    sh1 = p.din("l3_sh1", [128, NDH])
    sh2 = p.din("l3_sh2", [128, NDH])
    s3 = p.din("l3_s3", [128, ND])

    sm = {}
    for nm, srcd, sh in (("tdec", tdec, [64, 1]), ("tfir", tfir, [64, 1]),
                         ("tsg", tsg, [128, ND]), ("lnw", lnw, [128, ND]),
                         ("lnb", lnb, [128, ND]), ("s1", s1, [128, NDH]),
                         ("s2", s2, [128, NDH]), ("sh1", sh1, [128, NDH]),
                         ("sh2", sh2, [128, NDH]), ("s3", s3, [128, ND])):
        dst = sm[nm] = p.buf("l3s." + nm, sh, F32)
        em.emit("Q", lambda e, d=dst, s=srcd: e.dma_start(out=d[:], in_=s),
                writes=[dst[:]], dma=True)

    xn = [p.buf(f"l3.xn{k}", [128, T], BF16) for k in range(ND)]
    p.layernorm(xt, T, xn, tag="l3ln")
    for k in range(ND):
        em.emit("Q", lambda e, k=k, t=xt[k]: e.dma_start(
            out=y_t[k * 128:(k + 1) * 128, :], in_=t[:]),
            reads=[xt[k][:]], writes=[f"o.y.{k}"], dma=True)
    for k in range(ND):
        p.free(xt[k])
        xt[k] = None
    p.phase()

    gcol = p.buf("l3.gcol", [128, ND], F32)
    em.emit("S", lambda e: e.activation(gcol[:], sm["tsg"][:], AF.Sigmoid),
            reads=[sm["tsg"][:]], writes=[gcol[:]])
    SH = T // 2
    xs = [p.buf(f"l3.xs{k}", [128, T], BF16) for k in range(ND)]
    dtm = p.buf("l3.dtm", [128, T], F32)
    for k in range(ND):
        em.emit("V", lambda e, k=k, t=xn[k]: e.tensor_tensor(
            dtm[:, 0:SH], t[:, SH:T], t[:, 0:SH], ALU.subtract),
            reads=[xn[k][:]], writes=[dtm[:]], tag="l3.sh1")
        em.emit("V", lambda e, k=k, t=xn[k]: e.tensor_tensor(
            dtm[:, SH:T], t[:, 0:SH], t[:, SH:T], ALU.subtract),
            reads=[xn[k][:]], writes=[dtm[:]], tag="l3.sh2")
        em.emit("V", lambda e, k=k, t=xn[k], o=xs[k]: e.scalar_tensor_tensor(
            o[:], dtm[:], gcol[:, k:k + 1], t[:], ALU.mult, ALU.add),
            reads=[dtm[:], gcol[:], xn[k][:]], writes=[xs[k][:]], tag="l3.shm")
    p.free(dtm)
    xm = [p.buf(f"l3.xm{k}", [128, T], BF16) for k in range(ND)]
    p.layernorm(xs, T, xm, tag="l3ln2")
    for k in range(ND):
        em.emit("V", lambda e, k=k, t=xm[k]: e.tensor_scalar(
            t[:], t[:], sm["lnw"][:, k:k + 1], sm["lnb"][:, k:k + 1], ALU.mult, ALU.add),
            reads=[xm[k][:], sm["lnw"][:], sm["lnb"][:]], writes=[xm[k][:]], tag="l3.lnwb")
    for k in range(ND):
        p.free(xs[k])
    bscr3 = p.buf("l3.bscr3", [128, 512], F32)
    kv = p.buf("l3.k", [64, T], F32)
    vv = p.buf("l3.v", [64, T], F32)

    def evac_k3(ot, c0, c1, ps_):
        em.emit("V", lambda e: e.tensor_copy(kv[:, c0:c1], ps_[0:64, 0:c1 - c0]),
                reads=[ps_[:]], writes=[kv[:]], tag="l3.kev")
    p.matmul_to(w_k, ND, 1, xm, T, evac_k3, tag="l3k", ow=64)

    def evac_v3(ot, c0, c1, ps_):
        em.emit("V", lambda e: e.tensor_copy(vv[:, c0:c1], ps_[0:64, 0:c1 - c0]),
                reads=[ps_[:]], writes=[vv[:]], tag="l3.vev")
    p.matmul_to(w_v, ND, 1, xm, T, evac_v3, tag="l3v", ow=64)
    etf = p.buf("l3.etf", [64, 1], F32)
    em.emit("S", lambda e: e.activation(etf[:], sm["tfir"][0:64, :], AF.Exp),
            reads=[sm["tfir"][:]], writes=[etf[:]])
    em.emit("V", lambda e: e.tensor_scalar_mul(etf[:], etf[:], -1.0),
            reads=[etf[:]], writes=[etf[:]])
    em.emit("S", lambda e: e.activation(kv[:], kv[:], AF.Exp, scale=etf[0:64, :]),
            reads=[kv[:], etf[:]], writes=[kv[:]], tag="l3.wkv1")
    em.emit("V", lambda e: e.tensor_tensor(kv[:], kv[:], vv[:], ALU.mult),
            reads=[kv[:], vv[:]], writes=[kv[:]], tag="l3.wkv2")
    wdec = p.buf("l3.wdec", [64, 1], F32)
    em.emit("S", lambda e: e.activation(wdec[:], sm["tdec"][0:64, :], AF.Exp),
            reads=[sm["tdec"][:]], writes=[wdec[:]])
    wdb = p.buf("l3.wdb", [64, T], F32)
    em.emit("V", lambda e: e.tensor_copy(wdb[:], wdec[0:64, :].broadcast_to((64, T))),
            reads=[wdec[:]], writes=[wdb[:]], tag="l3.wdb")
    em.emit("V", lambda e: e.tensor_tensor_scan(vv[:], wdb[:], kv[:], 0.0, ALU.mult, ALU.add),
            reads=[wdb[:], kv[:]], writes=[vv[:]], tag="l3.scan")
    vvb = p.buf("l3.vvb", [64, T], BF16)
    em.emit("V", lambda e: e.tensor_copy(vvb[:], vv[:]), reads=[vv[:]], writes=[vvb[:]])
    wob = p.buf("l3.wob", [64, ND * 128], BF16)
    em.emit("G", lambda e: e.dma_start(out=wob[:].rearrange("b (a c) -> b a c", a=ND), in_=w_o.rearrange("a b c -> b a c")),
            writes=[wob[:]], dma=True, tag="l3.wold")
    for ot in range(ND):
        for c in range(4):
            ps_ = p.psum((ot * 4 + c) % 2)
            em.emit("T", lambda e, ot=ot, c=c, ps_=ps_: e.matmul(
                ps_[:, :], wob[0:64, ot * 128:(ot + 1) * 128],
                vvb[0:64, c * 512:(c + 1) * 512], start=True, stop=True),
                reads=[wob[:], vvb[:]], writes=[ps_[:]], tag="l3.omm")
            em.emit("V", lambda e, ps_=ps_: e.tensor_copy(bscr3[:], ps_[:, :]),
                    reads=[ps_[:]], writes=[bscr3[:]], tag="l3.oev")
            em.emit("G", lambda e, ot=ot, c=c: e.dma_start(
                out=y_t[ot * 128:(ot + 1) * 128, c * 512:(c + 1) * 512],
                in_=bscr3[:], accum_op=ALU.add),
                reads=[bscr3[:]], writes=[f"o.y.{ot}"], dma=True, tag="l3.oacc")
    for k in range(ND):
        p.free(xm[k])
    p.free(kv, vv, vvb, wob, wdb, wdec, etf, gcol, bscr3)
    p.phase()

    och = p.buf("l3.och", [128, 512], F32)
    bscr = p.buf("l3.bscr", [128, 512], F32)
    for half in range(2):
        h0 = half * (T // 2)
        TH = T // 2
        xnh = [xn[k][:, h0:h0 + TH] for k in range(ND)]
        hT = [p.buf(f"l3.h{j}", [128, TH], BF16) for j in range(NDH)]
        w1b = [p.buf(p.u("l3.w1b"), [128, ND * 128], BF16) for _ in range(2)]
        w2b = [p.buf(p.u("l3.w2b"), [128, ND * 128], BF16) for _ in range(2)]
        for ot in range(NDH):
            b1, b2 = w1b[ot % 2], w2b[ot % 2]
            em.emit("G", lambda e, ot=ot, b1=b1: e.dma_start(out=b1[:], in_=w1d[ot]),
                    writes=[b1[:]], dma=True, tag="l3.w1ld")
            em.emit("G", lambda e, ot=ot, b2=b2: e.dma_start(out=b2[:], in_=w2d[ot]),
                    writes=[b2[:]], dma=True, tag="l3.w2ld")
            for c in range(TH // 512):
                c0 = c * 512
                psA, psB = p.psum(0), p.psum(1)

                def mm1(e, b1=b1, c0=c0, xs_=tuple(xn), psA=psA, h0=h0):
                    last = None
                    for k in range(ND):
                        last = e.matmul(psA[:, :], b1[:, k * 128:(k + 1) * 128],
                                        xs_[k][:, h0 + c0:h0 + c0 + 512],
                                        start=(k == 0), stop=(k == ND - 1))
                    return last
                em.emit("T", mm1, reads=[b1[:]] + [x[:] for x in xn], writes=[psA[:]],
                        tag="l3.mm1")

                def mm2(e, b2=b2, c0=c0, xs_=tuple(xn), psB=psB, h0=h0):
                    last = None
                    for k in range(ND):
                        last = e.matmul(psB[:, :], b2[:, k * 128:(k + 1) * 128],
                                        xs_[k][:, h0 + c0:h0 + c0 + 512],
                                        start=(k == 0), stop=(k == ND - 1))
                    return last
                em.emit("T", mm2, reads=[b2[:]] + [x[:] for x in xn], writes=[psB[:]],
                        tag="l3.mm2")
                em.emit("S", lambda e, ot=ot, psA=psA: e.activation(
                    bscr[:], psA[:, :], AF.Silu, scale=sm["s1"][:, ot:ot + 1]),
                    reads=[psA[:], sm["s1"][:]], writes=[bscr[:]], tag="l3.silu")
                em.emit("V", lambda e, ot=ot, psB=psB: e.tensor_scalar(
                    och[:], psB[:, :], sm["s2"][:, ot:ot + 1], None, ALU.mult),
                    reads=[psB[:], sm["s2"][:]], writes=[och[:]], tag="l3.s2m")
                em.emit("V", lambda e, ot=ot, c0=c0, ht=hT[ot]: e.tensor_tensor(
                    ht[:, c0:c0 + 512], bscr[:], och[:], ALU.mult),
                    reads=[bscr[:], och[:]], writes=[hT[ot][:]], tag="l3.hmul")
        p.free(w1b[0], w1b[1], w2b[0], w2b[1])
        g1 = [p.buf(f"l3.g1{j}", [128, TH], BF16) for j in range(NDH)]

        def big_mm(wdram, scol, src_tiles, dst_tiles, tag):
            wb = [p.buf(p.u(tag + ".w"), [128, NDH * 128], BF16) for _ in range(2)]
            for ot in range(NDH):
                b = wb[ot % 2]
                em.emit("G", lambda e, ot=ot, b=b: e.dma_start(out=b[:], in_=wdram[ot]),
                        writes=[b[:]], dma=True, tag=tag + ".ld")
                for c in range(TH // 512):
                    c0 = c * 512
                    ps_ = p.psum((ot * (TH // 512) + c) % 2)

                    def mmb(e, b=b, c0=c0, ts=tuple(src_tiles), ps_=ps_):
                        last = None
                        for k in range(NDH):
                            last = e.matmul(ps_[:, :], b[:, k * 128:(k + 1) * 128],
                                            ts[k][:, c0:c0 + 512],
                                            start=(k == 0), stop=(k == NDH - 1))
                        return last
                    em.emit("T", mmb, reads=[b[:]] + [t[:] for t in src_tiles],
                            writes=[ps_[:]], tag=tag + ".mm")
                    em.emit("S", lambda e, ot=ot, c0=c0, dt_=dst_tiles[ot], ps_=ps_: e.activation(
                        dt_[:, c0:c0 + 512], ps_[:, :], AF.Gelu, scale=scol[:, ot:ot + 1]),
                        reads=[ps_[:], scol], writes=[dst_tiles[ot][:]], tag=tag + ".ev")
            p.free(wb[0], wb[1])
        big_mm(h1d, sm["sh1"][:], hT, g1, "l3.h1")
        big_mm(h2d, sm["sh2"][:], g1, hT, "l3.h2")
        for j in range(NDH):
            p.free(g1[j])
        w3b = [p.buf(p.u("l3.w3b"), [128, NDH * 128], BF16) for _ in range(2)]
        for ot in range(ND):
            b = w3b[ot % 2]
            em.emit("G", lambda e, ot=ot, b=b: e.dma_start(out=b[:], in_=w3d[ot]),
                    writes=[b[:]], dma=True, tag="l3.w3ld")
            for c in range(TH // 512):
                c0 = c * 512
                ps_ = p.psum((ot * (TH // 512) + c) % 2)

                def mm3(e, b=b, c0=c0, ts=tuple(hT), ps_=ps_):
                    last = None
                    for k in range(NDH):
                        last = e.matmul(ps_[:, :], b[:, k * 128:(k + 1) * 128],
                                        ts[k][:, c0:c0 + 512],
                                        start=(k == 0), stop=(k == NDH - 1))
                    return last
                em.emit("T", mm3, reads=[b[:]] + [t[:] for t in hT], writes=[ps_[:]],
                        tag="l3.mm3")
                em.emit("V", lambda e, ot=ot, ps_=ps_: e.tensor_scalar(
                    och[:], ps_[:, :], sm["s3"][:, ot:ot + 1], None, ALU.mult),
                    reads=[ps_[:], sm["s3"][:]], writes=[och[:]], tag="l3.oev")
                em.emit("G", lambda e, ot=ot, c0=c0, h0=h0: e.dma_start(
                    out=y_t[ot * 128:(ot + 1) * 128, h0 + c0:h0 + c0 + 512],
                    in_=och[:], accum_op=ALU.add),
                    reads=[och[:]], writes=[f"o.y.{ot}"], dma=True, tag="l3.oacc")
        p.free(w3b[0], w3b[1])
        for j in range(NDH):
            p.free(hT[j])
        p.phase()
    for k in range(ND):
        p.free(xn[k])
    for v in sm.values():
        p.free(v)
    p.phase()

def build_chain(emitters):
    p = Prog()
    em = p.em
    x_t = p.din("x_t", [D, T])
    y_t = p.dout("y_t", [D, T])
    p.consts()
    xt = [p.buf(f"x{k}", [128, T], F32) for k in range(ND)]
    for k in range(ND):
        em.emit("Q", lambda e, k=k, t=xt[k]: e.dma_start(
            out=t[:], in_=x_t[k * 128:(k + 1) * 128, :]),
            writes=[xt[k][:]], dma=True)
    for emit_fn in emitters:
        emit_fn(p, xt, y_t)
    for k in range(ND):
        if xt[k] is not None:
            em.emit("Q", lambda e, k=k: e.dma_start(out=y_t[k * 128:(k + 1) * 128, :],
                                                    in_=xt[k][:]),
                    reads=[xt[k][:]], writes=[f"o.y.{k}"], dma=True)
    return p.finish()



def colblocks(wt):
    K, O = wt.shape
    nK, nO = K // 128, O // 128
    a = wt.reshape(nK, 128, nO, 128).transpose(2, 1, 0, 3).reshape(nO, 128, nK * 128)
    return np.ascontiguousarray(a)


def pcol(v):
    n = v.shape[0]
    return np.ascontiguousarray(v.reshape(n // 128, 128).T)


def prep_l0_weights(ins):
    f = lambda k: np.ascontiguousarray(np.asarray(ins[k], dtype=np.float32))
    w = {}
    w["l0_w_in"] = colblocks(np.ascontiguousarray(f("L0_attn_in_proj_w").T))
    xpt = np.ascontiguousarray(f("L0_attn_x_proj_w").T)
    K = xpt.shape[0]
    w["l0_w_xp"] = np.ascontiguousarray(
        xpt.reshape(K // 128, 128, 96).transpose(1, 0, 2).reshape(1, 128, (K // 128) * 96))
    w["l0_w_out"] = colblocks(np.ascontiguousarray(f("L0_attn_out_proj_w").T))
    w["l0_w_dt"] = np.ascontiguousarray(f("L0_attn_dt_proj_w").reshape(16, 128, 64))
    w["l0_dt_b"] = np.ascontiguousarray(f("L0_attn_dt_proj_b").reshape(1, 2048))
    cw = f("L0_attn_conv_w").reshape(2048, 3)
    w["l0_conv_w"] = np.ascontiguousarray(cw.reshape(16, 128, 3).transpose(1, 0, 2).reshape(128, 48))
    w["l0_conv_b"] = pcol(f("L0_attn_conv_b"))
    w["l0_dpar"] = pcol(f("L0_attn_D_param"))
    w["l0_w_r"] = colblocks(np.ascontiguousarray(f("L0_ffn_W_r").T))
    w["l0_w_k"] = colblocks(np.ascontiguousarray(f("L0_ffn_W_k").T))
    w["l0_w_v"] = colblocks(np.ascontiguousarray(f("L0_ffn_W_v").T))
    w["l0_mu_r"] = pcol(f("L0_ffn_mu_r"))
    w["l0_mu_k"] = pcol(f("L0_ffn_mu_k"))
    return w


def prep_l1_weights(ins):
    f = lambda k: np.ascontiguousarray(np.asarray(ins[k], dtype=np.float32))
    w = {}
    w["l1_pr"] = pcol(f("L1_attn_phazor_real"))
    w["l1_pi"] = pcol(f("L1_attn_phazor_imag"))
    w["l1_ar"] = pcol(f("L1_attn_phazor_init_real"))
    w["l1_ai"] = pcol(f("L1_attn_phazor_init_imag"))
    w["l1_lr"] = pcol(f("L1_attn_last_conv_real"))
    w["l1_li"] = pcol(f("L1_attn_last_conv_imag"))
    w["l1_n1w"] = pcol(f("L1_ffn_norm1_w"))
    w["l1_w1"] = colblocks(np.ascontiguousarray(f("L1_ffn_w1").T))
    w["l1_asc"] = f("L1_ffn_act_scale").reshape(1, 1)
    w["l1_abi"] = f("L1_ffn_act_bias").reshape(1, 1)
    w["l1_n2w"] = pcol(f("L1_ffn_norm2_w"))
    w["l1_w2"] = colblocks(np.ascontiguousarray(f("L1_ffn_w2").T))
    return w


def prep_l2a_weights(ins):
    f = lambda k: np.ascontiguousarray(np.asarray(ins[k], dtype=np.float32))
    w = {}
    iwc, ibc = f("L2_attn_in_w_cls"), f("L2_attn_in_b_cls")
    iwl, ibl = f("L2_attn_in_w_loc"), f("L2_attn_in_b_loc")
    E = 1024
    w["l2_wq_c"] = colblocks(np.ascontiguousarray(iwc[0:E].T))
    w["l2_wk_c"] = colblocks(np.ascontiguousarray(iwc[E:2 * E].T))
    w["l2_wv_ct"] = np.ascontiguousarray(iwc[2 * E:3 * E].T.reshape(8, 128, E))
    w["l2_wq_l"] = colblocks(np.ascontiguousarray(iwl[0:E].T))
    w["l2_wk_l"] = colblocks(np.ascontiguousarray(iwl[E:2 * E].T))
    w["l2_wv_lt"] = np.ascontiguousarray(iwl[2 * E:3 * E].T.reshape(8, 128, E))
    w["l2_wo_c"] = colblocks(np.ascontiguousarray(f("L2_attn_out_w_cls").T))
    w["l2_wo_l"] = colblocks(np.ascontiguousarray(f("L2_attn_out_w_loc").T))
    w["l2_bqc"] = pcol(ibc[0:E]); w["l2_bkc"] = pcol(ibc[E:2 * E])
    w["l2_bvc"] = pcol(ibc[2 * E:3 * E]); w["l2_boc"] = pcol(f("L2_attn_out_b_cls"))
    w["l2_bql"] = pcol(ibl[0:E]); w["l2_bkl"] = pcol(ibl[E:2 * E])
    w["l2_bvl"] = pcol(ibl[2 * E:3 * E]); w["l2_bol"] = pcol(f("L2_attn_out_b_loc"))
    return w


def run_l2a(inputs, x, ncores=NCORES, raw=False):
    w = prep_l2a_weights(inputs)
    nc = build_chain([emit_l2a])
    in_maps = []
    for c in range(ncores):
        m = {"x_t": np.ascontiguousarray(x[c].T)}
        m.update(w)
        in_maps.append(m)
    res = run_bass_kernel_spmd(nc, in_maps, core_ids=list(range(ncores)))
    if raw:
        return res
    return (np.stack([np.asarray(res.results[c]["y_t"]).T for c in range(ncores)]),
            np.stack([np.asarray(res.results[c]["xn_t"]) for c in range(ncores)]),
            np.stack([np.asarray(res.results[c]["xn_tm_t"]) for c in range(ncores)]),
            np.stack([np.asarray(res.results[c]["bnsum"]) for c in range(ncores)]))


def prep_l2b_weights(ins):
    f = lambda k: np.ascontiguousarray(np.asarray(ins[k], dtype=np.float32))
    w = {}
    w["l2_wq"] = colblocks(np.ascontiguousarray(f("L2_ffn_w_q").T))
    w["l2_keys"] = np.ascontiguousarray(
        f("L2_ffn_keys_p").transpose(0, 2, 3, 1).reshape(16, 128, 256))
    w["l2_gam"] = pcol(f("L2_ffn_bn_gamma"))
    w["l2_bet"] = pcol(f("L2_ffn_bn_beta"))
    w["l2_down"] = f("L2_ffn_down")
    w["l2_up"] = f("L2_ffn_up")
    return w


def run_l2b(inputs, x3p, xn_t, xn_tm, bnsum, ncores=NCORES, raw=False):
    w = prep_l2b_weights(inputs)
    bn_cat = np.ascontiguousarray(np.concatenate(list(bnsum), axis=1))
    nc = build_chain([emit_l2b])
    in_maps = []
    for c in range(ncores):
        m = {"x_t": np.ascontiguousarray(x3p[c].T), "xn_in": xn_t[c],
             "xn_tm_in": xn_tm[c], "bn_all": bn_cat}
        m.update(w)
        in_maps.append(m)
    res = run_bass_kernel_spmd(nc, in_maps, core_ids=list(range(ncores)))
    if raw:
        return res
    return np.stack([np.asarray(res.results[c]["y_t"]).T for c in range(ncores)])


def prep_l3_weights(ins):
    f = lambda k: np.ascontiguousarray(np.asarray(ins[k], dtype=np.float32))
    w = {}
    w["l3_tdec"] = f("L3_attn_time_decay").reshape(64, 1)
    w["l3_tfir"] = f("L3_attn_time_first").reshape(64, 1)
    kT = np.ascontiguousarray(f("L3_attn_key_w").T)
    w["l3_wk"] = np.ascontiguousarray(
        kT.reshape(8, 128, 64).transpose(1, 0, 2).reshape(1, 128, 8 * 64))
    vT = np.ascontiguousarray(f("L3_attn_value_w").T)
    w["l3_wv"] = np.ascontiguousarray(
        vT.reshape(8, 128, 64).transpose(1, 0, 2).reshape(1, 128, 8 * 64))
    w["l3_wo"] = np.ascontiguousarray(f("L3_attn_output_w").T.reshape(64, 8, 128).transpose(1, 0, 2))
    w["l3_tsg"] = pcol(f("L3_attn_time_shift_gates"))
    w["l3_lnw"] = pcol(f("L3_attn_ln_w"))
    w["l3_lnb"] = pcol(f("L3_attn_ln_b"))
    w["l3_w1"] = colblocks(np.ascontiguousarray(f("L3_ffn_w1_w").T))
    w["l3_w2"] = colblocks(np.ascontiguousarray(f("L3_ffn_w2_w").T))
    w["l3_h1"] = colblocks(np.ascontiguousarray(f("L3_ffn_h1_w").T))
    w["l3_h2"] = colblocks(np.ascontiguousarray(f("L3_ffn_h2_w").T))
    w["l3_w3"] = colblocks(np.ascontiguousarray(f("L3_ffn_w3_w").T))
    w["l3_s1"] = pcol(f("L3_ffn_w1_s"))
    w["l3_s2"] = pcol(f("L3_ffn_w2_s"))
    w["l3_sh1"] = pcol(f("L3_ffn_h1_s"))
    w["l3_sh2"] = pcol(f("L3_ffn_h2_s"))
    w["l3_s3"] = pcol(f("L3_ffn_w3_s"))
    return w


def run_l3(inputs, x, ncores=NCORES, raw=False):
    w = prep_l3_weights(inputs)
    nc = build_chain([emit_l3])
    in_maps = []
    for c in range(ncores):
        m = {"x_t": np.ascontiguousarray(x[c].T)}
        m.update(w)
        in_maps.append(m)
    res = run_bass_kernel_spmd(nc, in_maps, core_ids=list(range(ncores)))
    if raw:
        return res
    return np.stack([np.asarray(res.results[c]["y_t"]).T for c in range(ncores)])


def run_l1(inputs, x, ncores=NCORES, raw=False):
    w = prep_l1_weights(inputs)
    nc = build_chain([emit_l1])
    in_maps = []
    for c in range(ncores):
        m = {"x_t": np.ascontiguousarray(x[c].T)}
        m.update(w)
        in_maps.append(m)
    res = run_bass_kernel_spmd(nc, in_maps, core_ids=list(range(ncores)))
    if raw:
        return res
    return np.stack([np.asarray(res.results[c]["y_t"]).T for c in range(ncores)])


def run_l0(inputs, x, ncores=NCORES, raw=False):
    w = prep_l0_weights(inputs)
    nc = build_chain([emit_l0])
    in_maps = []
    for c in range(ncores):
        m = {"x_t": np.ascontiguousarray(x[c].T)}
        m.update(w)
        in_maps.append(m)
    res = run_bass_kernel_spmd(nc, in_maps, core_ids=list(range(ncores)))
    if raw:
        return res
    return np.stack([np.asarray(res.results[c]["y_t"]).T for c in range(ncores)])


import functools


def kernel(**inputs):
    x = np.asarray(inputs["L0_attn_x"], dtype=np.float32)
    w = {}
    w.update(prep_l0_weights(inputs))
    w.update(prep_l1_weights(inputs))
    w.update(prep_l2a_weights(inputs))
    nc1 = build_chain([emit_l0, functools.partial(emit_l1, reload=True), emit_l2a])
    in_maps = []
    for c in range(NCORES):
        m = {"x_t": np.ascontiguousarray(x[c].T)}
        m.update(w)
        in_maps.append(m)
    res1 = run_bass_kernel_spmd(nc1, in_maps, core_ids=list(range(NCORES)))
    y1 = [np.asarray(res1.results[c]["y_t"]) for c in range(NCORES)]
    xn_t = [np.asarray(res1.results[c]["xn_t"]) for c in range(NCORES)]
    xn_tm = [np.asarray(res1.results[c]["xn_tm_t"]) for c in range(NCORES)]
    bn_cat = np.ascontiguousarray(np.concatenate(
        [np.asarray(res1.results[c]["bnsum"]) for c in range(NCORES)], axis=1))
    w2 = {}
    w2.update(prep_l2b_weights(inputs))
    w2.update(prep_l3_weights(inputs))
    nc2 = build_chain([emit_l2b, emit_l3])
    in_maps2 = []
    for c in range(NCORES):
        m = {"x_t": y1[c], "xn_in": xn_t[c], "xn_tm_in": xn_tm[c], "bn_all": bn_cat}
        m.update(w2)
        in_maps2.append(m)
    res2 = run_bass_kernel_spmd(nc2, in_maps2, core_ids=list(range(NCORES)))
    x4 = np.stack([np.asarray(res2.results[c]["y_t"]).T for c in range(NCORES)])
    return np.ascontiguousarray(x4.astype(np.float32))


# version 08 (final)
# speedup vs baseline: 1.0109x; 1.0065x over previous
import contextlib
import math
import numpy as np
import concourse.bass as bass
import concourse.mybir as mybir
from concourse.bass_utils import run_bass_kernel_spmd
from concourse.bass_core_rust import IndirectOffsetOnAxis

F32 = mybir.dt.float32
BF16 = mybir.dt.bfloat16
I32 = mybir.dt.int32
I16 = mybir.dt.int16
U32 = mybir.dt.uint32
U8 = mybir.dt.uint8
F16 = mybir.dt.float16
AF = mybir.ActivationFunctionType
ALU = mybir.AluOpType
AX = mybir.AxisListType

DBG = []
SKIP = set()
NCORES = 8
T = 2048
D = 1024
ND = D // 128

ENGINES = ("T", "V", "S", "G", "Q")
BODY = {"T": "tensor", "V": "vector", "S": "scalar", "G": "gpsimd", "Q": "sync"}
SELF_SYNC = {"T": True, "V": True, "S": True, "G": True, "Q": True}


DMA_SEM_K = 8


class Em:

    def __init__(self, nc):
        self.nc = nc
        self.ops = {e: [] for e in ENGINES}
        self.cnt = {}
        for e in ENGINES:
            self.cnt[e] = 0
        self.dma_i = {e: 0 for e in ENGINES}
        self.res = {}
        self.done = {e: {} for e in ENGINES}
        self.floor = {}

    @staticmethod
    def key(r):
        return r if isinstance(r, str) else r.name

    def emit(self, eng, fn, reads=(), writes=(), dma=False, tag=""):
        need = dict(self.floor)

        def want(f, c):
            if f == eng and not SELF_SYNC[eng]:
                return
            if need.get(f, 0) < c:
                need[f] = c
        for r in reads:
            st = self.res.get(self.key(r))
            if st and st["w"]:
                want(*st["w"])
        for w in writes:
            st = self.res.get(self.key(w))
            if st:
                if st["w"]:
                    want(*st["w"])
                for rd in st["r"]:
                    want(*rd)
        if dma:
            s = f"{eng}d{self.dma_i[eng] % DMA_SEM_K}"
            self.dma_i[eng] += 1
            prev = self.cnt.get(s, 0)
            if prev:
                want(s, prev)
            self.cnt[s] = prev + 16
            mark = (s, prev + 16)
        else:
            self.cnt[eng] += 1
            mark = (eng, self.cnt[eng])
        waits = []
        for f, v in sorted(need.items()):
            if v > 0 and self.done[eng].get(f, 0) < v:
                waits.append((f, v))
                self.done[eng][f] = v
        self.ops[eng].append((waits, fn, mark, tag))
        for r in reads:
            k = self.key(r)
            self.res.setdefault(k, {"w": None, "r": []})["r"].append(mark)
        for w in writes:
            self.res[self.key(w)] = {"w": mark, "r": []}
        return mark

    def barrier(self):
        self.floor = dict(self.cnt)



def cdiv(a, b):
    return (a + b - 1) // b


class Prog:
    ARENA_BYTES = 206 * 1024

    def __init__(self):
        self.nc = bass.Bass("TRN2", target_bir_lowering=False)
        self.em = Em(self.nc)
        self.ctx = contextlib.ExitStack()
        self.uid = 0
        self.freelist = [(0, self.ARENA_BYTES)]
        self.dirty = []
        self.vlist = []
        self.live = {}
        self.peak = 0
        self.psums = []
        self.consts_done = False

    def u(self, pfx):
        self.uid += 1
        return f"{pfx}.{self.uid}"

    def buf(self, name, shape, dtype):
        per = int(np.prod(shape[1:])) * mybir.dt.size(dtype)
        per = (per + 63) & ~63
        fits = [(ln, i) for i, (off, ln) in enumerate(self.freelist) if ln >= per]
        fits.sort()
        for _, i in fits[:1]:
            off, ln = self.freelist[i]
            if True:
                if any(off < do + dl and do < off + per for do, dl in self.dirty):
                    self.phase()
                if ln == per:
                    self.freelist.pop(i)
                else:
                    self.freelist[i] = (off + per, ln - per)
                h = self.nc.virtual_sbuf_tensor(name, list(shape), dtype)
                self.vlist.append((h, off))
                self.live[h.name] = (off, per)
                used = self.ARENA_BYTES - sum(l for _, l in self.freelist)
                self.peak = max(self.peak, used)
                return h
        live = sorted(self.live.items(), key=lambda kv: -kv[1][1])
        raise MemoryError(f"arena OOM for {name}: {per}B, free={self.freelist}, "
                          f"live={[(k, v[1]) for k, v in live[:40]]}")

    def free(self, *handles):
        for h in handles:
            name = h.name if hasattr(h, "name") else h
            off, per = self.live.pop(name)
            self.freelist.append((off, per))
            self.dirty.append((off, per))
        self.freelist.sort()
        out = []
        for off, ln in self.freelist:
            if out and out[-1][0] + out[-1][1] == off:
                out[-1] = (out[-1][0], out[-1][1] + ln)
            else:
                out.append((off, ln))
        self.freelist = [tuple(x) for x in out]

    def phase(self):
        self.em.barrier()
        self.dirty = []

    def psum(self, i, dtype=F32):
        while len(self.psums) <= i:
            j = len(self.psums)
            if j == 7:
                self.psums.append(self.ctx.enter_context(
                    self.nc.psum_tensor("ps7bf", [128, 1024], BF16)))
            else:
                self.psums.append(self.ctx.enter_context(
                    self.nc.psum_tensor(f"ps{j}", [128, 512], F32)))
        return self.psums[i]

    def tap(self, name, h, rows=128):
        sh = list(h.shape)
        o = self.dout("dbg_" + name, sh, F32)
        if h.dtype != F32:
            if not hasattr(self, "_tapscratch"):
                self._tapscratch = self.buf("tapscratch", [128, T], F32)
            tmp = self._tapscratch
            self.em.emit("V", lambda e, h=h, t=tmp: e.tensor_copy(t[0:sh[0], 0:sh[1]], h[:]),
                         reads=[h[:]], writes=[tmp[:]])
            src_ = tmp
        else:
            src_ = h
        self.em.emit("Q", lambda e, s=src_, o=o, r=sh[0], cc=sh[1]: e.dma_start(
            out=o, in_=s[0:r, 0:cc]),
            reads=[src_[:]], writes=["dbgout." + name], dma=True)

    def din(self, name, shape, dtype=F32):
        return self.nc.dram_tensor(name, list(shape), dtype, kind="ExternalInput").ap()

    def dout(self, name, shape, dtype=F32):
        return self.nc.dram_tensor(name, list(shape), dtype, kind="ExternalOutput").ap()

    def cval(self, value, dtype=F32):
        key = (float(value), dtype)
        if not hasattr(self, "_cvals"):
            self._cvals = {}
        if key not in self._cvals:
            t = self.buf(self.u("c_v"), [128, 1], dtype)
            self.em.emit("G", lambda e, t=t, v=float(value): e.memset(t[:], v),
                         writes=[t[:]])
            self._cvals[key] = t
        return self._cvals[key]

    def consts(self):
        if self.consts_done:
            return
        self.consts_done = True
        em = self.em
        self.ones_col = self.buf("c_ones_col", [128, 1], F32)
        self.ones_row = self.buf("c_ones_row", [1, 128], F32)
        self.ones_colbf = self.buf("c_ones_colbf", [128, 1], BF16)
        em.emit("G", lambda e: e.memset(self.ones_col[:], 1.0), writes=[self.ones_col[:]])
        em.emit("G", lambda e: e.memset(self.ones_row[:], 1.0), writes=[self.ones_row[:]])
        em.emit("G", lambda e: e.memset(self.ones_colbf[:], 1.0), writes=[self.ones_colbf[:]])
        self.ident = self.buf("c_ident", [128, 128], F32)
        em.emit("G", lambda e: e.memset(self.ident[:], 1.0), writes=[self.ident[:]])
        em.emit("G", lambda e: e.affine_select(
            self.ident[:], self.ident[:], pattern=[[1, 128]], compare_op=ALU.is_equal,
            fill=0.0, base=0, channel_multiplier=-1),
            reads=[self.ident[:]], writes=[self.ident[:]])
        self.ident_bf = self.buf("c_identbf", [128, 128], BF16)
        em.emit("V", lambda e: e.tensor_copy(self.ident_bf[:], self.ident[:]),
                reads=[self.ident[:]], writes=[self.ident_bf[:]])
        self.lane = self.buf("c_lane", [128, 1], F32)
        self.lane1m = self.buf("c_lane1m", [128, 1], F32)
        em.emit("G", lambda e: e.memset(self.lane[:], 1.0), writes=[self.lane[:]])
        em.emit("V", lambda e: e.tensor_single_scalar(self.lane[:], self.lane[:], 1e-12, ALU.add),
                reads=[self.lane[:]], writes=[self.lane[:]])
        em.emit("V", lambda e: e.tensor_single_scalar(self.lane[:], self.lane[:], 1.0, ALU.is_gt),
                reads=[self.lane[:]], writes=[self.lane[:]])
        em.emit("V", lambda e: e.tensor_scalar(self.lane1m[:], self.lane[:], -1.0, 1.0,
                                               ALU.mult, ALU.add),
                reads=[self.lane[:]], writes=[self.lane1m[:]])

    def round_ip(self, xh, sh, cols, tag="rnd"):
        em = self.em

        def b1(e, x=xh[:, 0:cols], s=sh[:, 0:cols], ln=self.lane):
            return e.tensor_scalar(s, x, ln[:], M64, ALU.mult, ALU.add)
        em.emit("V", b1, reads=[xh[:], self.lane[:]], writes=[sh[:]], tag=tag + ".b1")
        em.emit("V", lambda e, s=sh[:, 0:cols]: e.tensor_single_scalar(
            s, s, M64, ALU.subtract), reads=[sh[:]], writes=[sh[:]], tag=tag + ".b2")

        def a1(e, x=xh[:, 0:cols], ln=self.lane1m):
            return e.tensor_scalar(x, x, ln[:], MAGIC, ALU.mult, ALU.add)
        em.emit("V", a1, reads=[xh[:], self.lane1m[:]], writes=[xh[:]], tag=tag + ".a1")
        em.emit("V", lambda e, x=xh[:, 0:cols]: e.tensor_single_scalar(
            x, x, MAGIC, ALU.subtract), reads=[xh[:]], writes=[xh[:]], tag=tag + ".a2")
        em.emit("V", lambda e, x=xh[:, 0:cols], s=sh[:, 0:cols]: e.tensor_tensor(
            x, x, s, ALU.add), reads=[xh[:], sh[:]], writes=[xh[:]], tag=tag + ".c")

    def row_from_parts(self, src, row, T_, tag="rfp"):
        em = self.em
        nch = T_ // 128
        scol = self.buf(self.u(tag + ".scol"), [128, nch], F32)
        pt = self.psum(3)
        for c in range(nch):
            em.emit("T", lambda e, c=c: e.transpose(
                pt[:, 0:128], src[:, c * 128:(c + 1) * 128], self.ident[:]),
                reads=[src[:], self.ident[:]], writes=[pt[:]], tag=tag + ".t")
            em.emit("V", lambda e, c=c: e.reduce_max(
                scol[:, c:c + 1], pt[:, 0:128], axis=AX.X),
                reads=[pt[:]], writes=[scol[:]], tag=tag + ".r")
        s16 = self.buf(self.u(tag + ".s16"), [max(nch, 1), 128], F32)
        em.emit("T", lambda e: e.transpose(pt[0:nch, 0:128], scol[:], self.ident[:]),
                reads=[scol[:], self.ident[:]], writes=[pt[:]], tag=tag + ".t2")
        em.emit("V", lambda e: e.tensor_copy(s16[:], pt[0:nch, 0:128]),
                reads=[pt[:]], writes=[s16[:]], tag=tag + ".c2")
        em.emit("G", lambda e: e.dma_start(out=row[0:1, 0:T_], in_=s16[:]),
                reads=[s16[:]], writes=[row[:]], dma=True, tag=tag + ".d")

    def materialize(self):
        ah = self.ctx.enter_context(self.nc.sbuf_tensor(
            "arena", [128, self.ARENA_BYTES], U8))
        base = self.nc.lookup_mloc(ah).addr
        for h, off in self.vlist:
            self.nc._tensor(h.name, h.shape, h.dtype, type="SB")
            mloc = self.nc.lookup_mloc(h)
            mloc.allocated = not self.nc.target_bir_lowering
            mloc.addr = base + off

    def finish(self):
        em = self.em
        waits = [(k, v) for k, v in sorted(em.cnt.items())
                 if "d" in k and len(k) > 1 and v > 0]
        em.ops["Q"].append((waits, lambda e: None, ("Q", em.cnt["Q"] + 1), "final"))
        em.cnt["Q"] += 1
        self.materialize()
        nc = self.nc
        sems = {k: self.ctx.enter_context(nc.semaphore("sem_" + k))
                for k, v in em.cnt.items() if v > 0}
        with nc.Block() as block:
            for eng in ENGINES:
                ops = self.em.ops[eng]

                def body(e, ops=ops, eng=eng):
                    for waits, fn, mark, tag in ops:
                        for f, v in waits:
                            e.wait_ge(sems[f], v)
                        inst = fn(e)
                        if inst is None:
                            inst = e.nop()
                        s, v = mark
                        inst.then_inc(sems[s], 16 if s != eng else 1)
                getattr(block, BODY[eng])(body)
        self.ctx.close()
        return self.nc

    def ops_for(self, eng):
        return self.em.ops[eng]

    def matmul_to(self, wdram, nK, nO, xt, T_, evac, wdtype=BF16, tag="mm",
                  ps_idx=(0, 1), xtag=None, ow=128, xsel=False):
        em = self.em
        wb = [self.buf(self.u(tag + ".w"), [128, nK * ow], wdtype) for _ in range(2)]
        nch = cdiv(T_, 512)
        xreads = [x[:] for x in xt] if xtag is None else xtag
        for ot in range(nO):
            b = wb[ot % 2]
            em.emit("G", lambda e, ot=ot, b=b: e.dma_start(out=b[:], in_=wdram[ot]),
                    writes=[b[:]], dma=True, tag=tag + ".wld")
            for c in range(nch):
                c0, c1 = c * 512, min((c + 1) * 512, T_)
                ps_ = self.psum(ps_idx[(ot * nch + c) % 2])

                def mm(e, b=b, c0=c0, c1=c1, ps_=ps_, tiles=tuple(xt)):
                    last = None
                    for k in range(nK):
                        mv = (tiles[k][:, k:k + 1] if xsel else tiles[k][:, c0:c1])
                        last = e.matmul(ps_[0:ow, 0:c1 - c0], b[:, k * ow:(k + 1) * ow],
                                        mv, start=(k == 0), stop=(k == nK - 1))
                    return last
                em.emit("T", mm, reads=[b[:]] + list(xreads), writes=[ps_[:]], tag=tag + ".mm")
                evac(ot, c0, c1, ps_)
        self.free(*wb)

    def bcast_col(self, sval, dst, tag="bcc"):
        em = self.em
        p_ = self.psum(4)
        em.emit("T", lambda e: e.matmul(p_[:, 0:1], self.ones_row[:], sval[0:1, 0:1],
                                        start=True, stop=True),
                reads=[sval[:], self.ones_row[:]], writes=[p_[:]], tag=tag)
        em.emit("S", lambda e: e.copy(dst[:], p_[:, 0:1]),
                reads=[p_[:]], writes=[dst[:]], tag=tag + "c")

    def bcast_row(self, row, dst, T_, ps_i=5, dt_src=None, tag="bc"):
        em = self.em
        p_ = self.psum(ps_i)
        for c in range(cdiv(T_, 512)):
            c0, c1 = c * 512, min((c + 1) * 512, T_)
            em.emit("T", lambda e, c0=c0, c1=c1: e.matmul(
                p_[:, 0:c1 - c0], self.ones_row[:], row[:, c0:c1], start=True, stop=True),
                reads=[row[:], self.ones_row[:]], writes=[p_[:]], tag=tag)
            em.emit("S", lambda e, c0=c0, c1=c1: e.copy(dst[:, c0:c1], p_[:, 0:c1 - c0]),
                    reads=[p_[:]], writes=[dst[:]], tag=tag + "c")

    def layernorm(self, xt, T_, xn, eps=1e-5, tag="ln", scr=None):
        em = self.em
        n = len(xt)
        nd = float(n * 128)
        ones_in = self.ones_col if xt[0].dtype == F32 else self.ones_colbf
        ssum = self.buf(self.u(tag + ".sum"), [1, T_], F32)
        ssq = self.buf(self.u(tag + ".ssq"), [1, T_], F32)
        sq = [self.buf(self.u(tag + ".sq"), [128, 512], F32) for _ in range(2)]
        p1, p2 = self.psum(5), self.psum(6)
        nch = cdiv(T_, 512)
        for c in range(nch):
            c0 = c * 512

            def mm_sum(e, c0=c0, tiles=tuple(xt), p1=p1, oc=ones_in):
                last = None
                for k in range(n):
                    last = e.matmul(p1[0:1, :], oc[:], tiles[k][:, c0:c0 + 512],
                                    start=(k == 0), stop=(k == n - 1))
                return last
            em.emit("T", mm_sum, reads=[x[:] for x in xt] + [ones_in[:]],
                    writes=[p1[:]], tag=tag + ".sm")
            em.emit("V", lambda e, c0=c0: e.tensor_copy(ssum[:, c0:c0 + 512], p1[0:1, :]),
                    reads=[p1[:]], writes=[ssum[:]], tag=tag + ".cp")
            for k in range(n):
                s = sq[k % 2]
                em.emit("S", lambda e, t=xt[k], c0=c0, s=s: e.activation(
                    s[:], t[:, c0:c0 + 512], AF.Square),
                    reads=[xt[k][:]], writes=[s[:]], tag=tag + ".sq")
                em.emit("T", lambda e, k=k, s=s: e.matmul(
                    p2[0:1, :], self.ones_col[:], s[:], start=(k == 0), stop=(k == n - 1)),
                    reads=[s[:], self.ones_col[:]], writes=[p2[:]], tag=tag + ".sqm")
            em.emit("V", lambda e, c0=c0: e.tensor_copy(ssq[:, c0:c0 + 512], p2[0:1, :]),
                    reads=[p2[:]], writes=[ssq[:]], tag=tag + ".cp2")
        mrs = self.buf(self.u(tag + ".mrs"), [1, T_], F32)
        mean, rstd = ssum, ssq
        em.emit("V", lambda e: e.tensor_scalar_mul(mean[:], ssum[:], 1.0 / nd),
                reads=[ssum[:]], writes=[mean[:]])
        em.emit("V", lambda e: e.tensor_tensor(mrs[:], mean[:], mean[:], ALU.mult),
                reads=[mean[:]], writes=[mrs[:]])
        em.emit("V", lambda e: e.scalar_tensor_tensor(
            rstd[:], ssq[:], 1.0 / nd, mrs[:], ALU.mult, ALU.subtract),
            reads=[ssq[:], mrs[:]], writes=[rstd[:]])
        ec = self.cval(eps)
        em.emit("S", lambda e: e.activation(rstd[:], rstd[:], AF.Sqrt, bias=ec[0:1, :]),
                reads=[rstd[:], ec[:]], writes=[rstd[:]])
        em.emit("V", lambda e: e.reciprocal(rstd[:], rstd[:]),
                reads=[rstd[:]], writes=[rstd[:]])
        em.emit("V", lambda e: e.tensor_tensor(mrs[:], mean[:], rstd[:], ALU.mult),
                reads=[mean[:], rstd[:]], writes=[mrs[:]])
        rb = self.buf(self.u(tag + ".rb"), [128, T_], F32)
        mb = self.buf(self.u(tag + ".mb"), [128, T_], F32)
        self.bcast_row(rstd, rb, T_, ps_i=5, tag=tag + ".b1")
        self.bcast_row(mrs, mb, T_, ps_i=6, tag=tag + ".b2")
        tmp = self.buf(self.u(tag + ".tmp"), [128, T_], F32)
        for k in range(n):
            em.emit("V", lambda e, t=xt[k]: e.tensor_tensor(tmp[:], t[:], rb[:], ALU.mult),
                    reads=[xt[k][:], rb[:]], writes=[tmp[:]])
            em.emit("V", lambda e, t=xn[k]: e.tensor_tensor(t[:], tmp[:], mb[:], ALU.subtract),
                    reads=[tmp[:], mb[:]], writes=[xn[k][:]])
        self.free(ssum, ssq, sq[0], sq[1], mrs, rb, mb, tmp)



def emit_l0(p, xt, y_t=None):
    nc, em = p.nc, p.em
    DI, NDI = 2048, 16
    w_in = p.din("l0_w_in", [32, 128, ND * 128])
    w_xp = p.din("l0_w_xp", [1, 128, NDI * 96])
    w_out = p.din("l0_w_out", [ND, 128, NDI * 128])
    w_dt = p.din("l0_w_dt", [NDI, 128, 64])
    dt_b = p.din("l0_dt_b", [1, DI])
    conv_w = p.din("l0_conv_w", [128, NDI * 3])
    conv_b = p.din("l0_conv_b", [128, NDI])
    d_par = p.din("l0_dpar", [128, NDI])
    w_r = p.din("l0_w_r", [ND, 128, ND * 128])
    w_k = p.din("l0_w_k", [ND, 128, ND * 128])
    w_v = p.din("l0_w_v", [ND, 128, ND * 128])
    mu_r = p.din("l0_mu_r", [128, ND])
    mu_k = p.din("l0_mu_k", [128, ND])

    sm = {}
    for nm, src, sh in (("convw", conv_w, [128, NDI * 3]), ("convb", conv_b, [128, NDI]),
                        ("dpar", d_par, [128, NDI]), ("mur", mu_r, [128, ND]),
                        ("muk", mu_k, [128, ND]), ("dtb", dt_b, [1, DI])):
        dst = sm[nm] = p.buf("l0s." + nm, sh, F32)
        em.emit("Q", lambda e, d=dst, s=src: e.dma_start(out=d[:], in_=s),
                writes=[dst[:]], dma=True)

    xn = [p.buf(f"l0.xn{k}", [128, T], BF16) for k in range(ND)]
    p.layernorm(xt, T, xn, tag="l0ln")
    if "xn0" in DBG:
        p.tap("xn0", xn[0])
    p.phase()

    xs0 = [p.buf(f"l0.xs{k}", [128, T], BF16) for k in range(NDI)]

    def evac_xs(ot, c0, c1, ps_):
        em.emit("S", lambda e: e.copy(xs0[ot][:, c0:c1], ps_[:, 0:c1 - c0]),
                reads=[ps_[:]], writes=[xs0[ot][:]], tag="l0.xs")
    p.matmul_to(w_in[0:NDI], ND, NDI, xn, T, evac_xs, tag="l0in")

    if "xs0" in DBG:
        p.tap("xs0", xs0[0])
    if "convw" in DBG:
        p.tap("convw", sm["convw"])
        p.tap("convb", sm["convb"])
    spare = p.buf("l0.convspare", [128, T], BF16)
    cs1 = p.buf("l0.cs1", [128, T], F32)
    cs2 = p.buf("l0.cs2", [128, T], F32)
    xc = []
    dst = spare
    for k in range(NDI):
        src = xs0[k]
        em.emit("V", lambda e, k=k, src=src: e.tensor_scalar(
            cs1[:], src[:], sm["convw"][:, 3 * k + 1:3 * k + 2],
            sm["convb"][:, k:k + 1], ALU.mult, ALU.add),
            reads=[src[:], sm["convb"][:], sm["convw"][:]], writes=[cs1[:]], tag="cv1")
        em.emit("V", lambda e, k=k, src=src: e.scalar_tensor_tensor(
            cs2[:, 0:T - 1], src[:, 1:T], sm["convw"][:, 3 * k + 2:3 * k + 3],
            cs1[:, 0:T - 1], ALU.mult, ALU.add),
            reads=[src[:], cs1[:], sm["convw"][:]], writes=[cs2[:]], tag="cv2")
        em.emit("V", lambda e: e.tensor_copy(cs2[:, T - 1:T], cs1[:, T - 1:T]),
                reads=[cs1[:]], writes=[cs2[:]], tag="cv2b")
        em.emit("V", lambda e, k=k, src=src, dst=dst: e.scalar_tensor_tensor(
            dst[:, 1:T], src[:, 0:T - 1], sm["convw"][:, 3 * k:3 * k + 1],
            cs2[:, 1:T], ALU.mult, ALU.add),
            reads=[src[:], cs2[:], sm["convw"][:]], writes=[dst[:]], tag="cv3")
        em.emit("V", lambda e, dst=dst: e.tensor_copy(dst[:, 0:1], cs2[:, 0:1]),
                reads=[cs2[:]], writes=[dst[:]], tag="cv3b")
        if k == 0 and "cs1" in DBG:
            p.tap("cs1", cs1)
            p.tap("cs2", cs2)
        xc.append(dst)
        dst = src
    p.free(dst, cs1, cs2)

    if "xc0" in DBG:
        p.tap("xc0", xc[0])
    xdbl = p.buf("l0.xdbl", [96, T], F32)

    def evac_xdbl(ot, c0, c1, ps_):
        em.emit("S", lambda e: e.copy(xdbl[:, c0:c1], ps_[0:96, 0:c1 - c0]),
                reads=[ps_[:]], writes=[xdbl[:]], tag="l0.xdbl")
    p.matmul_to(w_xp, NDI, 1, xc, T, evac_xdbl, tag="l0xp", ow=96)

    m_colbf = p.buf("l0.mcol", [64, 1], BF16)
    p_m = p.psum(5)
    wdt_b = [p.buf(f"l0.wdtb{i}", [128, 64], BF16) for i in range(2)]
    for k in range(NDI):
        b = wdt_b[k % 2]
        em.emit("G", lambda e, k=k, b=b: e.dma_start(out=b[:], in_=w_dt[k]),
                writes=[b[:]], dma=True)
        em.emit("T", lambda e, k=k, b=b: e.matmul(
            p_m[0:64, 0:1], b[:], p.ones_colbf[:], start=(k == 0), stop=(k == NDI - 1)),
            reads=[b[:], p.ones_colbf[:]], writes=[p_m[:]], tag="l0.m")
    em.emit("V", lambda e: e.tensor_copy(m_colbf[:], p_m[0:64, 0:1]),
            reads=[p_m[:]], writes=[m_colbf[:]])
    sb_sum = p.buf("l0.sbsum", [1, 1], F32)
    dtbh = sm["dtb"]
    em.emit("V", lambda e, d=dtbh: e.reduce_sum(sb_sum[:], d[:], axis=AX.X),
            reads=[dtbh[:]], writes=[sb_sum[:]])
    em.emit("V", lambda e: e.tensor_scalar_mul(sb_sum[:], sb_sum[:], 1.0 / DI),
            reads=[sb_sum[:]], writes=[sb_sum[:]])

    xdbl_bf = p.buf("l0.xdblbf", [64, T], BF16)
    em.emit("V", lambda e: e.tensor_copy(xdbl_bf[:], xdbl[0:64, :]),
            reads=[xdbl[:]], writes=[xdbl_bf[:]])
    g_row = p.buf("l0.grow", [1, T], F32)
    p_g = p.psum(6)
    for c in range(T // 512):
        c0 = c * 512
        em.emit("T", lambda e, c0=c0: e.matmul(
            p_g[0:1, :], m_colbf[:], xdbl_bf[:, c0:c0 + 512], start=True, stop=True),
            reads=[m_colbf[:], xdbl_bf[:]], writes=[p_g[:]], tag="l0.g")
        sc_c = p.cval(1.0 / DI)
        em.emit("S", lambda e, c0=c0: e.activation(
            g_row[:, c0:c0 + 512], p_g[0:1, :], AF.Sigmoid,
            bias=sb_sum[:], scale=sc_c[0:1, :]),
            reads=[p_g[:], sb_sum[:], sc_c[:]], writes=[g_row[:]], tag="l0.gs")
    if "grow" in DBG:
        p.tap("grow", g_row)
    if "xdbl" in DBG:
        p.tap("xdbl", xdbl)
    p.free(xdbl, xdbl_bf, wdt_b[0], wdt_b[1], m_colbf)

    p.free(sm.pop("dtb"))
    gb = p.buf("l0.gb", [128, T], F32)
    p.bcast_row(g_row, gb, T, ps_i=5, tag="l0.gb")
    p.free(g_row, sb_sum)

    utmp = p.buf("l0.utmp", [128, T], F32)
    ysc = p.buf("l0.ysc", [128, T], F32)
    zwb = [p.buf(f"l0.zw{i}", [128, ND * 128], BF16) for i in range(2)]
    y = []
    for k in range(NDI):
        em.emit("V", lambda e, k=k: e.tensor_tensor(utmp[:], gb[:], xc[k][:], ALU.mult),
                reads=[gb[:], xc[k][:]], writes=[utmp[:]], tag="l0.u")
        em.emit("V", lambda e, k=k: e.tensor_tensor(utmp[:], xc[k][:], utmp[:], ALU.subtract),
                reads=[xc[k][:], utmp[:]], writes=[utmp[:]], tag="l0.u2")
        em.emit("V", lambda e: e.tensor_tensor_scan(
            ysc[:], gb[:], utmp[:], 0.0, ALU.mult, ALU.add),
            reads=[gb[:], utmp[:]], writes=[ysc[:]], tag="l0.scan")
        b = zwb[k % 2]
        em.emit("G", lambda e, k=k, b=b: e.dma_start(out=b[:], in_=w_in[NDI + k]),
                writes=[b[:]], dma=True, tag="l0.zwld")
        for c in range(T // 512):
            c0 = c * 512
            ps_ = p.psum((c % 2))

            def mmz(e, b=b, c0=c0, ps_=ps_, tiles=tuple(xn)):
                last = None
                for kk in range(ND):
                    last = e.matmul(ps_[:, :], b[:, kk * 128:(kk + 1) * 128],
                                    tiles[kk][:, c0:c0 + 512], start=(kk == 0),
                                    stop=(kk == ND - 1))
                return last
            em.emit("T", mmz, reads=[b[:]] + [x[:] for x in xn], writes=[ps_[:]],
                    tag="l0.zmm")
            em.emit("S", lambda e, c0=c0, ps_=ps_: e.activation(
                ps_[:, :], ps_[:, :], AF.Silu),
                reads=[ps_[:]], writes=[ps_[:]], tag="l0.zsilu")
            em.emit("V", lambda e, c0=c0, ps_=ps_: e.tensor_tensor(
                ysc[:, c0:c0 + 512], ysc[:, c0:c0 + 512], ps_[:, :], ALU.mult),
                reads=[ps_[:], ysc[:]], writes=[ysc[:]], tag="l0.ymul")
        em.emit("V", lambda e, k=k: e.scalar_tensor_tensor(
            xc[k][:], xc[k][:], sm["dpar"][:, k:k + 1], ysc[:], ALU.mult, ALU.add),
            reads=[xc[k][:], sm["dpar"][:], ysc[:]], writes=[xc[k][:]], tag="l0.yf")
        y.append(xc[k])
    if "y0" in DBG:
        p.tap("y0", y[0])
    p.free(utmp, ysc, zwb[0], zwb[1], gb)
    p.phase()

    def evac_out(ot, c0, c1, ps_):
        em.emit("V", lambda e, t=xt[ot]: e.tensor_tensor(t[:, c0:c1], t[:, c0:c1],
                                                         ps_[:, 0:c1 - c0], ALU.add),
                reads=[ps_[:], xt[ot][:]], writes=[xt[ot][:]], tag="l0.oadd")
    p.matmul_to(w_out, NDI, ND, y, T, evac_out, tag="l0out")
    if "xattn0" in DBG:
        p.tap("xattn0", xt[0])
    p.free(*[c for c in xc])
    p.phase()

    if "l0ffn" in SKIP:
        for v in sm.values():
            p.free(v)
        p.free(*xn)
        p.phase()
        return
    dtmp = p.buf("l0.dtmp", [128, T], F32)

    def mix(dst, mucol):
        for k in range(ND):
            em.emit("V", lambda e, k=k: e.tensor_tensor(
                dtmp[:, 1:T], xn[k][:, 1:T], xn[k][:, 0:T - 1], ALU.subtract),
                reads=[xn[k][:]], writes=[dtmp[:]], tag="l0.mx1")
            em.emit("V", lambda e, k=k, dst=dst: e.scalar_tensor_tensor(
                dst[k][:, 1:T], dtmp[:, 1:T], mucol[:, k:k + 1], xn[k][:, 0:T - 1],
                ALU.mult, ALU.add),
                reads=[dtmp[:], mucol, xn[k][:]], writes=[dst[k][:]], tag="l0.mx2")
            em.emit("V", lambda e, k=k, dst=dst: e.tensor_scalar(
                dst[k][:, 0:1], xn[k][:, 0:1], mucol[:, k:k + 1], None, ALU.mult),
                reads=[xn[k][:], mucol], writes=[dst[k][:]], tag="l0.mx3")

    xr = [p.buf(f"l0.xr{k}", [128, T], BF16) for k in range(ND)]
    mix(xr, sm["mur"][:])
    rs = [p.buf(f"l0.rs{k}", [128, T], BF16) for k in range(ND)]

    def evac_r(ot, c0, c1, ps_):
        em.emit("S", lambda e: e.activation(rs[ot][:, c0:c1], ps_[:, 0:c1 - c0], AF.Sigmoid),
                reads=[ps_[:]], writes=[rs[ot][:]], tag="l0.r")
    if "xr0" in DBG:
        p.tap("xr0", xr[0])
    p.matmul_to(w_r, ND, ND, xr, T, evac_r, tag="l0wr")
    p.free(*xr)
    p.phase()

    xk = [p.buf(f"l0.xk{k}", [128, T], BF16) for k in range(ND)]
    mix(xk, sm["muk"][:])
    p.free(dtmp)
    ka = [p.buf(f"l0.ka{k}", [128, T], BF16) for k in range(ND)]

    def evac_k(ot, c0, c1, ps_):
        em.emit("S", lambda e: e.activation(ka[ot][:, c0:c1], ps_[:, 0:c1 - c0], AF.Relu),
                reads=[ps_[:]], writes=[ka[ot][:]], tag="l0.k")
        em.emit("S", lambda e: e.activation(ka[ot][:, c0:c1], ka[ot][:, c0:c1], AF.Square),
                reads=[ka[ot][:]], writes=[ka[ot][:]], tag="l0.k2")
    p.matmul_to(w_k, ND, ND, xk, T, evac_k, tag="l0wk")
    p.free(*xk)
    p.phase()

    vtmp = p.buf("l0.vtmp", [128, 512], F32)

    def evac_v(ot, c0, c1, ps_):
        em.emit("V", lambda e: e.tensor_tensor(vtmp[:, 0:c1 - c0], ps_[:, 0:c1 - c0],
                                               rs[ot][:, c0:c1], ALU.mult),
                reads=[ps_[:], rs[ot][:]], writes=[vtmp[:]], tag="l0.v")
        em.emit("V", lambda e, t=xt[ot]: e.tensor_tensor(t[:, c0:c1], t[:, c0:c1],
                                                         vtmp[:, 0:c1 - c0], ALU.add),
                reads=[vtmp[:], xt[ot][:]], writes=[xt[ot][:]], tag="l0.vadd")
    if "rs0" in DBG:
        p.tap("rs0", rs[0])
    if "ka0" in DBG:
        p.tap("ka0", ka[0])
    p.matmul_to(w_v, ND, ND, ka, T, evac_v, tag="l0wv")
    p.free(*ka, *rs, vtmp, *xn)
    for v in sm.values():
        p.free(v)
    p.phase()





MAGIC = 12582912.0
M64 = 6755399441055744.0


def emit_l1(p, xt, y_t, reload=False):
    nc, em = p.nc, p.em
    DH, NDH = 4096, 32
    pr = p.din("l1_pr", [128, ND]); pi = p.din("l1_pi", [128, ND])
    ar = p.din("l1_ar", [128, ND]); ai = p.din("l1_ai", [128, ND])
    lr = p.din("l1_lr", [128, ND]); li = p.din("l1_li", [128, ND])
    n1w = p.din("l1_n1w", [128, ND])
    w1 = p.din("l1_w1", [NDH, 128, ND * 128])
    asc = p.din("l1_asc", [1, 1]); abi = p.din("l1_abi", [1, 1])
    n2w = p.din("l1_n2w", [128, NDH])
    w2 = p.din("l1_w2", [ND, 128, NDH * 128])

    sm = {}
    for nm, srcd, sh in (("pr", pr, [128, ND]), ("pi", pi, [128, ND]),
                         ("ar", ar, [128, ND]), ("ai", ai, [128, ND]),
                         ("lr", lr, [128, ND]), ("li", li, [128, ND]),
                         ("n1w", n1w, [128, ND]), ("n2w", n2w, [128, NDH]),
                         ("asc", asc, [1, 1]), ("abi", abi, [1, 1])):
        dst = sm[nm] = p.buf("l1s." + nm, sh, F32)
        em.emit("Q", lambda e, d=dst, s=srcd: e.dma_start(out=d[:], in_=s),
                writes=[dst[:]], dma=True)

    xn = [p.buf(f"l1.xn{k}", [128, T], F32) for k in range(ND)]
    p.layernorm(xt, T, xn, tag="l1ln")
    p.phase()

    skip_attn = "l1attn" in SKIP
    def tile(nm):
        t = p.buf("l1c." + nm, [128, ND], F32)
        return t
    m2, r_, mag, minv, cosv, sinv = (tile(x) for x in
                                     ("m2", "r", "mag", "minv", "cos", "sin"))
    t1, t2 = tile("t1"), tile("t2")
    phr, phi = tile("phr"), tile("phi")
    sr, si = tile("sr"), tile("si")
    NS = 11
    mrs = p.buf("l1c.mrs", [128, ND * NS], F32)
    mis = p.buf("l1c.mis", [128, ND * NS], F32)
    nms = p.buf("l1c.nms", [128, ND * NS], F32)
    Hr = [p.buf("l1.Hr0", [128, T], F32)]
    Hi = [p.buf("l1.Hi0", [128, T], F32)]
    Hr2 = [p.buf("l1.Hr20", [128, T], F32)]
    Hi2 = [p.buf("l1.Hi20", [128, T], F32)]
    ws1_scr = []
    sw1, swi1 = emit_wscale(p, w1, NDH, ND * 128, 4096 * 1024, tag="l1.ws1",
                            scratch_out=ws1_scr)
    em.emit("S", lambda e: e.activation(m2[:], sm["pr"][:], AF.Square), reads=[sm["pr"][:]], writes=[m2[:]])
    em.emit("S", lambda e: e.activation(t1[:], sm["pi"][:], AF.Square), reads=[sm["pi"][:]], writes=[t1[:]])
    em.emit("V", lambda e: e.tensor_add(m2[:], m2[:], t1[:]), reads=[m2[:], t1[:]], writes=[m2[:]])
    cm1 = p.cval(-1.0)
    em.emit("S", lambda e: e.activation(r_[:], m2[:], AF.Exp, scale=cm1[:]),
            reads=[m2[:], cm1[:]], writes=[r_[:]])
    em.emit("S", lambda e: e.activation(mag[:], m2[:], AF.Sqrt), reads=[m2[:]], writes=[mag[:]])
    em.emit("V", lambda e: e.reciprocal(minv[:], mag[:]), reads=[mag[:]], writes=[minv[:]])
    em.emit("V", lambda e: e.tensor_mul(cosv[:], sm["pr"][:], minv[:]), reads=[sm["pr"][:], minv[:]], writes=[cosv[:]])
    em.emit("V", lambda e: e.tensor_mul(sinv[:], sm["pi"][:], minv[:]), reads=[sm["pi"][:], minv[:]], writes=[sinv[:]])
    em.emit("V", lambda e: e.tensor_mul(phr[:], r_[:], cosv[:]), reads=[r_[:], cosv[:]], writes=[phr[:]])
    em.emit("V", lambda e: e.tensor_mul(phi[:], r_[:], sinv[:]), reads=[r_[:], sinv[:]], writes=[phi[:]])
    em.emit("V", lambda e: e.tensor_mul(t1[:], sm["lr"][:], phr[:]), reads=[sm["lr"][:], phr[:]], writes=[t1[:]])
    em.emit("V", lambda e: e.tensor_mul(t2[:], sm["li"][:], phi[:]), reads=[sm["li"][:], phi[:]], writes=[t2[:]])
    em.emit("V", lambda e: e.tensor_sub(sr[:], t1[:], t2[:]), reads=[t1[:], t2[:]], writes=[sr[:]])
    em.emit("V", lambda e: e.tensor_mul(t1[:], sm["lr"][:], phi[:]), reads=[sm["lr"][:], phi[:]], writes=[t1[:]])
    em.emit("V", lambda e: e.tensor_mul(t2[:], sm["li"][:], phr[:]), reads=[sm["li"][:], phr[:]], writes=[t2[:]])
    em.emit("V", lambda e: e.tensor_add(si[:], t1[:], t2[:]), reads=[t1[:], t2[:]], writes=[si[:]])
    em.emit("V", lambda e: e.tensor_copy(mrs[:, 0:ND], phr[:]), reads=[phr[:]], writes=[mrs[:]])
    em.emit("V", lambda e: e.tensor_copy(mis[:, 0:ND], phi[:]), reads=[phi[:]], writes=[mis[:]])
    for s in range(1, NS):
        a, b = (s - 1) * ND, s * ND
        em.emit("V", lambda e, a=a, b=b: e.tensor_mul(t1[:], mrs[:, a:a + ND], mrs[:, a:a + ND]),
                reads=[mrs[:]], writes=[t1[:]])
        em.emit("V", lambda e, a=a, b=b: e.tensor_mul(t2[:], mis[:, a:a + ND], mis[:, a:a + ND]),
                reads=[mis[:]], writes=[t2[:]])
        em.emit("V", lambda e, a=a, b=b: e.tensor_sub(mrs[:, b:b + ND], t1[:], t2[:]),
                reads=[t1[:], t2[:]], writes=[mrs[:]])
        em.emit("V", lambda e, a=a, b=b: e.tensor_mul(t1[:], mrs[:, a:a + ND], mis[:, a:a + ND]),
                reads=[mrs[:], mis[:]], writes=[t1[:]])
        em.emit("V", lambda e, a=a, b=b: e.tensor_scalar_mul(mis[:, b:b + ND], t1[:], 2.0),
                reads=[t1[:]], writes=[mis[:]])
    em.emit("V", lambda e: e.tensor_scalar_mul(nms[:], mis[:], -1.0), reads=[mis[:]], writes=[nms[:]])

    for k in range(ND if not skip_attn else 0):
        g = 0
        E = "V"
        hr, hi, hr2, hi2 = Hr[g], Hi[g], Hr2[g], Hi2[g]
        em.emit(E, lambda e, k=k, hr=hr: e.tensor_scalar(
            hr[:], xn[k][:], sm["ar"][:, k:k + 1], None, ALU.mult),
            reads=[xn[k][:], sm["ar"][:]], writes=[hr[:]], tag="l1.u")
        em.emit(E, lambda e, k=k, hi=hi: e.tensor_scalar(
            hi[:], xn[k][:], sm["ai"][:, k:k + 1], None, ALU.mult),
            reads=[xn[k][:], sm["ai"][:]], writes=[hi[:]], tag="l1.u")
        em.emit(E, lambda e, k=k, hr=hr: e.tensor_tensor(
            hr[:, 0:1], hr[:, 0:1], sr[:, k:k + 1], ALU.add),
            reads=[hr[:], sr[:]], writes=[hr[:]], tag="l1.seed")
        em.emit(E, lambda e, k=k, hi=hi: e.tensor_tensor(
            hi[:, 0:1], hi[:, 0:1], si[:, k:k + 1], ALU.add),
            reads=[hi[:], si[:]], writes=[hi[:]], tag="l1.seed")
        for s in range(NS):
            m = 1 << s
            c = s * ND + k
            em.emit(E, lambda e, hr=hr, hr2=hr2, m=m: e.tensor_copy(
                hr2[:, 0:m], hr[:, 0:m]), reads=[hr[:]], writes=[hr2[:]], tag="l1.dc")
            em.emit(E, lambda e, hr=hr, hr2=hr2, m=m, c=c: e.scalar_tensor_tensor(
                hr2[:, m:T], hr[:, 0:T - m], mrs[:, c:c + 1], hr[:, m:T], ALU.mult, ALU.add),
                reads=[hr[:], mrs[:]], writes=[hr2[:]], tag="l1.d1")
            em.emit(E, lambda e, hi=hi, hr2=hr2, m=m, c=c: e.scalar_tensor_tensor(
                hr2[:, m:T], hi[:, 0:T - m], nms[:, c:c + 1], hr2[:, m:T], ALU.mult, ALU.add),
                reads=[hi[:], nms[:], hr2[:]], writes=[hr2[:]], tag="l1.d2")
            em.emit(E, lambda e, hi=hi, hi2=hi2, m=m: e.tensor_copy(
                hi2[:, 0:m], hi[:, 0:m]), reads=[hi[:]], writes=[hi2[:]], tag="l1.dc")
            em.emit(E, lambda e, hr=hr, hi=hi, hi2=hi2, m=m, c=c: e.scalar_tensor_tensor(
                hi2[:, m:T], hr[:, 0:T - m], mis[:, c:c + 1], hi[:, m:T], ALU.mult, ALU.add),
                reads=[hr[:], hi[:], mis[:]], writes=[hi2[:]], tag="l1.d3")
            em.emit(E, lambda e, hi=hi, hi2=hi2, m=m, c=c: e.scalar_tensor_tensor(
                hi2[:, m:T], hi[:, 0:T - m], mrs[:, c:c + 1], hi2[:, m:T], ALU.mult, ALU.add),
                reads=[hi[:], mrs[:], hi2[:]], writes=[hi2[:]], tag="l1.d4")
            hr, hr2 = hr2, hr
            hi, hi2 = hi2, hi
        em.emit(E, lambda e, t=xt[k], hr=hr: e.tensor_tensor(
            t[:], t[:], hr[:], ALU.add),
            reads=[xt[k][:], hr[:]], writes=[xt[k][:]], tag="l1.addx")
    p.free(Hr[0], Hi[0], Hr2[0], Hi2[0])
    sm_freed = set()
    for nm in ("pr", "pi", "ar", "ai", "lr", "li"):
        p.free(sm[nm])
        sm_freed.add(nm)
    for tl in (m2, r_, mag, minv, cosv, sinv, phr, phi, sr, si, t1, t2, mrs, mis, nms):
        p.free(tl)
    p.phase()

    for k in range(ND):
        em.emit("Q", lambda e, k=k, t=xt[k]: e.dma_start(
            out=y_t[k * 128:(k + 1) * 128, :], in_=t[:]),
            reads=[xt[k][:]], writes=[f"o.y.{k}"], dma=True)
    for k in range(ND):
        p.free(xt[k])
        xt[k] = None
    p.phase()

    if "l1ffn" in SKIP:
        if reload:
            for k in range(ND):
                xt[k] = p.buf(f"x{k}b", [128, T], F32)
                em.emit("Q", lambda e, k=k, t=xt[k]: e.dma_start(
                    out=t[:], in_=y_t[k * 128:(k + 1) * 128, :]),
                    reads=[f"o.y.{k}"], writes=[xt[k][:]], dma=True)
        return
    s2row = p.buf("l1.s2row", [1, T], F32)
    sqs = p.buf("l1.sqs", [128, 512], F32)
    p2 = p.psum(6)
    for c in range(T // 512):
        c0 = c * 512
        for k in range(ND):
            em.emit("S", lambda e, k=k, c0=c0: e.activation(
                sqs[:], xn[k][:, c0:c0 + 512], AF.Square),
                reads=[xn[k][:]], writes=[sqs[:]], tag="l1.q1s")
            em.emit("T", lambda e, k=k: e.matmul(
                p2[0:1, :], p.ones_col[:], sqs[:], start=(k == 0), stop=(k == ND - 1)),
                reads=[sqs[:], p.ones_col[:]], writes=[p2[:]], tag="l1.q1m")
        em.emit("V", lambda e, c0=c0: e.tensor_copy(s2row[:, c0:c0 + 512], p2[0:1, :]),
                reads=[p2[:]], writes=[s2row[:]], tag="l1.q1c")
    rstd1 = p.buf("l1.rstd1", [1, T], F32)
    e8 = p.cval(1e-8)
    em.emit("V", lambda e: e.tensor_scalar_mul(rstd1[:], s2row[:], 1.0 / D),
            reads=[s2row[:]], writes=[rstd1[:]])
    em.emit("S", lambda e: e.activation(rstd1[:], rstd1[:], AF.Sqrt, bias=e8[0:1, :]),
            reads=[rstd1[:], e8[:]], writes=[rstd1[:]])
    em.emit("V", lambda e: e.reciprocal(rstd1[:], rstd1[:]), reads=[rstd1[:]], writes=[rstd1[:]])
    rstd1b = p.buf("l1.rstd1b", [128, T], F32)
    p.bcast_row(rstd1, rstd1b, T, ps_i=5, tag="l1.rb")
    xnorm = [p.buf(f"l1.xm{k}", [128, T], F32) for k in range(ND)]
    maxa = p.buf("l1.maxa", [128, T], F32)
    xsc = p.buf("l1.xsc", [128, T], F32)
    for k in range(ND):
        em.emit("V", lambda e, k=k: e.tensor_mul(xsc[:], xn[k][:], rstd1b[:]),
                reads=[xn[k][:], rstd1b[:]], writes=[xsc[:]], tag="l1.xh1")
        em.emit("V", lambda e, k=k: e.tensor_scalar(
            xnorm[k][:], xsc[:], sm["n1w"][:, k:k + 1], None, ALU.mult),
            reads=[xsc[:], sm["n1w"][:]], writes=[xnorm[k][:]], tag="l1.xh2")
        p.free(xn[k])
        em.emit("S", lambda e, t=xnorm[k]: e.activation(xsc[:], t[:], AF.Abs),
                reads=[xnorm[k][:]], writes=[xsc[:]], tag="l1.mxa")
        if k == 0:
            em.emit("V", lambda e: e.tensor_copy(maxa[:], xsc[:]),
                    reads=[xsc[:]], writes=[maxa[:]], tag="l1.mx")
        else:
            em.emit("V", lambda e: e.tensor_tensor(
                maxa[:], maxa[:], xsc[:], ALU.max),
                reads=[maxa[:], xsc[:]], writes=[maxa[:]], tag="l1.mx")
    mrow = p.buf("l1.mrow", [1, T], F32)
    p.row_from_parts(maxa, mrow, T, tag="l1.rfp")
    sxrow = p.buf("l1.sxrow", [1, T], F32)
    sxirow = p.buf("l1.sxirow", [1, T], F32)
    em.emit("V", lambda e: e.tensor_single_scalar(mrow[:], mrow[:], 1e-5, ALU.max),
            reads=[mrow[:]], writes=[mrow[:]])
    em.emit("V", lambda e: e.tensor_scalar_mul(sxirow[:], mrow[:], 1.0 / 127.0),
            reads=[mrow[:]], writes=[sxirow[:]])
    em.emit("V", lambda e: e.reciprocal(sxrow[:], mrow[:]), reads=[mrow[:]], writes=[sxrow[:]])
    em.emit("V", lambda e: e.tensor_scalar_mul(sxrow[:], sxrow[:], 127.0),
            reads=[sxrow[:]], writes=[sxrow[:]])
    sxb = p.buf("l1.sxb", [128, T], F32)
    sxib = p.buf("l1.sxib", [128, T], F32)
    p.bcast_row(sxrow, sxb, T, ps_i=5, tag="l1.sb")
    p.bcast_row(sxirow, sxib, T, ps_i=6, tag="l1.sib")
    p.free(sxirow)
    xq = []
    rsc = p.buf("l1.rsc", [128, T], F32)
    for k in range(ND):
        em.emit("V", lambda e, k=k: e.tensor_mul(xsc[:], xnorm[k][:], sxb[:]),
                reads=[xnorm[k][:], sxb[:]], writes=[xsc[:]], tag="l1.qq1")
        p.round_ip(xsc, rsc, T, tag="l1.qq2")
        em.emit("V", lambda e: e.tensor_single_scalar(xsc[:], xsc[:], -128.0, ALU.max),
                reads=[xsc[:]], writes=[xsc[:]], tag="l1.qq3")
        xqk = p.buf(f"l1.xq{k}", [128, T], BF16)
        em.emit("V", lambda e, t=xqk: e.tensor_single_scalar(t[:], xsc[:], 127.0, ALU.min),
                reads=[xsc[:]], writes=[xqk[:]], tag="l1.qq4")
        xq.append(xqk)
        p.free(xnorm[k])
    p.free(maxa, mrow, sxrow, rstd1, rstd1b, s2row, sqs, xsc, sxb, rsc)
    p.phase()

    p.free(*ws1_scr)
    h = [p.buf(f"l1.h{j}", [128, T], F16) for j in range(NDH)]
    s2h = p.buf("l1.s2h", [1, T], F32)
    em.emit("V", lambda e: e.memset(s2h[:], 0.0), writes=[s2h[:]])
    maxh = p.buf("l1.maxh", [128, T], F32)
    em.emit("V", lambda e: e.memset(maxh[:], 0.0), writes=[maxh[:]])
    an2w = p.buf("l1.an2w", [128, NDH], F32)
    em.emit("S", lambda e: e.activation(an2w[:], sm["n2w"][:], AF.Abs),
            reads=[sm["n2w"][:]], writes=[an2w[:]])
    ascb = p.buf("l1.ascb", [128, 1], F32)
    abib = p.buf("l1.abib", [128, 1], F32)
    p.bcast_col(sm["asc"], ascb)
    p.bcast_col(sm["abi"], abib)
    hevts = [p.buf(f"l1.hev{i}", [128, 512], F16) for i in range(2)]
    sqh = p.buf("l1.sqh", [128, 512], F32)
    p2bs = [p.psum(6), p.psum(2)]

    def evac_h(ot, c0, c1, ps_):
        ri = (ot * 4 + c0 // 512) % 2
        hevt, p2b = hevts[ri], p2bs[ri]
        em.emit("V", lambda e, hevt=hevt: e.scalar_tensor_tensor(
            hevt[:, 0:c1 - c0], ps_[:, 0:c1 - c0], swi1[:], sxib[:, c0:c1],
            ALU.mult, ALU.mult),
            reads=[ps_[:], swi1[:], sxib[:]], writes=[hevt[:]], tag="l1.he1")
        em.emit("S", lambda e, hevt=hevt: e.activation(
            hevt[:, 0:c1 - c0], hevt[:, 0:c1 - c0], AF.Relu),
            reads=[hevt[:]], writes=[hevt[:]], tag="l1.he2")
        em.emit("S", lambda e, hevt=hevt: e.activation(
            hevt[:, 0:c1 - c0], hevt[:, 0:c1 - c0], AF.Square),
            reads=[hevt[:]], writes=[hevt[:]], tag="l1.he3")
        em.emit("V", lambda e, hevt=hevt: e.tensor_scalar(
            h[ot][:, c0:c1], hevt[:, 0:c1 - c0], ascb[:], abib[:], ALU.mult, ALU.add),
            reads=[hevt[:], ascb[:], abib[:]], writes=[h[ot][:]], tag="l1.he4")
        em.emit("S", lambda e, sqh=sqh: e.activation(sqh[:], h[ot][:, c0:c1], AF.Square),
                reads=[h[ot][:]], writes=[sqh[:]], tag="l1.he5")
        em.emit("T", lambda e, sqh=sqh, p2b=p2b: e.matmul(
            p2b[0:1, 0:c1 - c0], p.ones_col[:], sqh[:, 0:c1 - c0],
            start=True, stop=True),
            reads=[sqh[:], p.ones_col[:]], writes=[p2b[:]], tag="l1.he6")
        em.emit("V", lambda e, p2b=p2b: e.tensor_tensor(s2h[:, c0:c1], s2h[:, c0:c1],
                                                        p2b[0:1, 0:c1 - c0], ALU.add),
                reads=[p2b[:], s2h[:]], writes=[s2h[:]], tag="l1.he7")
        em.emit("S", lambda e, ot=ot, sqh=sqh: e.activation(
            sqh[:, 0:c1 - c0], h[ot][:, c0:c1], AF.Abs),
            reads=[h[ot][:]], writes=[sqh[:]], tag="l1.he8a")
        em.emit("V", lambda e, ot=ot, sqh=sqh: e.tensor_scalar(
            sqh[:, 0:c1 - c0], sqh[:, 0:c1 - c0], an2w[:, ot:ot + 1], None, ALU.mult),
            reads=[sqh[:], an2w[:]], writes=[sqh[:]], tag="l1.he8b")
        em.emit("V", lambda e, sqh=sqh: e.tensor_tensor(
            maxh[:, c0:c1], maxh[:, c0:c1], sqh[:, 0:c1 - c0], ALU.max),
            reads=[maxh[:], sqh[:]], writes=[maxh[:]], tag="l1.he8c")
    emit_bitlinear_mm(p, w1, ND, NDH, xq, sw1, T, evac_h, tag="l1.w1")
    ws2_scr = []
    sw2, swi2 = emit_wscale(p, w2, ND, NDH * 128, 1024 * 4096, tag="l1.ws2",
                            scratch_out=ws2_scr)
    for k in range(ND):
        p.free(xq[k])
    p.free(sxib, ascb, abib, hevts[0], hevts[1], sqh)
    p.phase()

    rstd2 = p.buf("l1.rstd2", [1, T], F32)
    em.emit("V", lambda e: e.tensor_scalar_mul(rstd2[:], s2h[:], 1.0 / DH),
            reads=[s2h[:]], writes=[rstd2[:]])
    em.emit("S", lambda e: e.activation(rstd2[:], rstd2[:], AF.Sqrt, bias=e8[0:1, :]),
            reads=[rstd2[:], e8[:]], writes=[rstd2[:]])
    em.emit("V", lambda e: e.reciprocal(rstd2[:], rstd2[:]), reads=[rstd2[:]], writes=[rstd2[:]])
    mrow2 = p.buf("l1.mrow2", [1, T], F32)
    p.row_from_parts(maxh, mrow2, T, tag="l1.rf2")
    em.emit("V", lambda e: e.tensor_tensor(mrow2[:], mrow2[:], rstd2[:], ALU.mult),
            reads=[mrow2[:], rstd2[:]], writes=[mrow2[:]])
    sx2row = p.buf("l1.sx2row", [1, T], F32)
    sx2irow = p.buf("l1.sx2irow", [1, T], F32)
    em.emit("V", lambda e: e.tensor_single_scalar(mrow2[:], mrow2[:], 1e-5, ALU.max),
            reads=[mrow2[:]], writes=[mrow2[:]])
    em.emit("V", lambda e: e.tensor_scalar_mul(sx2irow[:], mrow2[:], 1.0 / 127.0),
            reads=[mrow2[:]], writes=[sx2irow[:]])
    em.emit("V", lambda e: e.reciprocal(sx2row[:], mrow2[:]), reads=[mrow2[:]], writes=[sx2row[:]])
    em.emit("V", lambda e: e.tensor_scalar_mul(sx2row[:], sx2row[:], 127.0),
            reads=[sx2row[:]], writes=[sx2row[:]])
    em.emit("V", lambda e: e.tensor_tensor(sx2row[:], sx2row[:], rstd2[:], ALU.mult),
            reads=[sx2row[:], rstd2[:]], writes=[sx2row[:]])
    p.free(maxh, mrow2, rstd2, s2h)
    psb = p.buf("l1.psb", [128, T], F32)
    sx2ib = p.buf("l1.sx2ib", [128, T], F32)
    p.bcast_row(sx2row, psb, T, ps_i=5, tag="l1.s2b")
    p.bcast_row(sx2irow, sx2ib, T, ps_i=6, tag="l1.s2ib")
    p.free(sx2irow)
    hq = h
    hsc = p.buf("l1.hsc", [128, T], F32)
    hrs = p.buf("l1.hrs", [128, T], F32)
    for j in range(NDH):
        em.emit("V", lambda e, j=j: e.tensor_scalar(
            hsc[:], h[j][:], sm["n2w"][:, j:j + 1], None, ALU.mult),
            reads=[h[j][:], sm["n2w"][:]], writes=[hsc[:]], tag="l1.q21")
        em.emit("V", lambda e: e.tensor_mul(hsc[:], hsc[:], psb[:]),
                reads=[hsc[:], psb[:]], writes=[hsc[:]], tag="l1.q22")
        p.round_ip(hsc, hrs, T, tag="l1.q23")
        em.emit("V", lambda e: e.tensor_single_scalar(hsc[:], hsc[:], -128.0, ALU.max),
                reads=[hsc[:]], writes=[hsc[:]], tag="l1.q24")
        em.emit("V", lambda e, j=j: e.tensor_single_scalar(hq[j][:], hsc[:], 127.0, ALU.min),
                reads=[hsc[:]], writes=[hq[j][:]], tag="l1.q25")
    p.free(psb, hsc, hrs, sx2row)
    p.phase()

    p.free(*ws2_scr)
    ochunks = [p.buf(f"l1.oc{i}", [128, 512], F32) for i in range(2)]

    def evac_o(ot, c0, c1, ps_):
        ochunk = ochunks[(ot * 4 + c0 // 512) % 2]
        em.emit("V", lambda e, ochunk=ochunk: e.scalar_tensor_tensor(
            ochunk[:, 0:c1 - c0], ps_[:, 0:c1 - c0], swi2[:], sx2ib[:, c0:c1],
            ALU.mult, ALU.mult),
            reads=[ps_[:], swi2[:], sx2ib[:]], writes=[ochunk[:]], tag="l1.oe1")
        em.emit("G", lambda e, ot=ot, c0=c0, c1=c1, ochunk=ochunk: e.dma_start(
            out=y_t[ot * 128:(ot + 1) * 128, c0:c1], in_=ochunk[:, 0:c1 - c0],
            accum_op=ALU.add),
            reads=[ochunk[:]], writes=[f"o.y.{ot}"], dma=True, tag="l1.oacc")
    emit_bitlinear_mm(p, w2, NDH, ND, hq, sw2, T, evac_o, tag="l1.w2")
    for j in range(NDH):
        p.free(hq[j])
    p.free(ochunks[0], ochunks[1], sx2ib, an2w, sw1, swi1, sw2, swi2)
    for kk, v in sm.items():
        if kk not in sm_freed:
            p.free(v)
    p.phase()

    if reload:
        for k in range(ND):
            xt[k] = p.buf(f"x{k}b", [128, T], F32)
            em.emit("Q", lambda e, k=k, t=xt[k]: e.dma_start(
                out=t[:], in_=y_t[k * 128:(k + 1) * 128, :]),
                reads=[f"o.y.{k}"], writes=[xt[k][:]], dma=True)


def emit_wscale(p, wdram, nO, bw, nelem, tag, scratch_out=None):
    em = p.em
    wb = [p.buf(p.u(tag + ".w"), [128, 512], F32) for i in range(2)]
    ab = p.buf(p.u(tag + ".a"), [128, 512], F32)
    pscale = p.psum(3)
    nch = cdiv(bw, 512)
    for ot in range(nO):
        for c in range(nch):
            c0, c1 = c * 512, min((c + 1) * 512, bw)
            b = wb[(ot * nch + c) % 2]
            em.emit("G", lambda e, ot=ot, b=b, c0=c0, c1=c1: e.dma_start(
                out=b[:, 0:c1 - c0], in_=wdram[ot][:, c0:c1]),
                writes=[b[:]], dma=True, tag=tag + ".ld")
            em.emit("S", lambda e, b=b, c0=c0, c1=c1: e.activation(
                ab[:, 0:c1 - c0], b[:, 0:c1 - c0], AF.Abs),
                reads=[b[:]], writes=[ab[:]], tag=tag + ".abs")
            em.emit("T", lambda e, ot=ot, c=c, c0=c0, c1=c1: e.matmul(
                pscale[0:1, 0:c1 - c0], p.ones_col[:], ab[:, 0:c1 - c0],
                start=(ot == 0 and c == 0), stop=(ot == nO - 1 and c == nch - 1)),
                reads=[ab[:], p.ones_col[:]], writes=[pscale[:]], tag=tag + ".mm")
    srow = p.buf(p.u(tag + ".srow"), [1, 512], F32)
    em.emit("V", lambda e: e.tensor_copy(srow[:], pscale[0:1, 0:512]),
            reads=[pscale[:]], writes=[srow[:]])
    sval = p.buf(p.u(tag + ".sv"), [1, 1], F32)
    em.emit("V", lambda e: e.reduce_sum(sval[:], srow[:, 0:min(512, bw)], axis=AX.X),
            reads=[srow[:]], writes=[sval[:]])
    em.emit("V", lambda e: e.tensor_scalar_mul(sval[:], sval[:], 1.0 / nelem),
            reads=[sval[:]], writes=[sval[:]])
    em.emit("V", lambda e: e.tensor_single_scalar(sval[:], sval[:], 1e-5, ALU.max),
            reads=[sval[:]], writes=[sval[:]])
    sinv = p.buf(p.u(tag + ".si"), [1, 1], F32)
    em.emit("V", lambda e: e.tensor_copy(sinv[:], sval[:]), reads=[sval[:]], writes=[sinv[:]])
    em.emit("V", lambda e: e.reciprocal(sval[:], sval[:]), reads=[sval[:]], writes=[sval[:]])
    swc = p.buf(p.u(tag + ".swc"), [128, 1], F32)
    swic = p.buf(p.u(tag + ".swic"), [128, 1], F32)
    p.bcast_col(sval, swc)
    p.bcast_col(sinv, swic)
    if scratch_out is not None:
        scratch_out.extend([wb[0], wb[1], ab, srow, sval, sinv])
    else:
        p.free(wb[0], wb[1], ab, srow, sval, sinv)
    return swc, swic


def emit_bitlinear_mm(p, wdram, nK, nO, xq, sw_col, T_, evac, tag):
    em = p.em
    bw = nK * 128
    wf = [p.buf(p.u(tag + ".wf"), [128, 512], F32) for _ in range(2)]
    wb = [p.buf(p.u(tag + ".w"), [128, bw], BF16) for _ in range(2)]
    qs = p.buf(p.u(tag + ".qs"), [128, 512], F32)
    nch = cdiv(T_, 512)
    ci = 0
    for ot in range(nO):
        b = wb[ot % 2]
        for q0 in range(0, bw, 512):
            q1 = min(q0 + 512, bw)
            bf_ = wf[ci % 2]
            ci += 1
            em.emit("G", lambda e, ot=ot, bf_=bf_, q0=q0, q1=q1: e.dma_start(
                out=bf_[:, 0:q1 - q0], in_=wdram[ot][:, q0:q1]),
                writes=[bf_[:]], dma=True, tag=tag + ".ld")
            em.emit("V", lambda e, bf_=bf_, q0=q0, q1=q1: e.tensor_scalar(
                qs[:, 0:q1 - q0], bf_[:, 0:q1 - q0], sw_col[:], None, ALU.mult),
                reads=[bf_[:], sw_col[:]], writes=[qs[:]], tag=tag + ".q1")
            em.emit("V", lambda e, b=b, q0=q0, q1=q1: e.tensor_single_scalar(
                b[:, q0:q1], qs[:, 0:q1 - q0], 0.5, ALU.is_ge),
                reads=[qs[:]], writes=[b[:]], tag=tag + ".q2")
            em.emit("V", lambda e, q0=q0, q1=q1: e.tensor_single_scalar(
                qs[:, 0:q1 - q0], qs[:, 0:q1 - q0], -0.5, ALU.is_ge),
                reads=[qs[:]], writes=[qs[:]], tag=tag + ".q2b")
            em.emit("V", lambda e, b=b, q0=q0, q1=q1: e.tensor_tensor(
                qs[:, 0:q1 - q0], qs[:, 0:q1 - q0], b[:, q0:q1], ALU.add),
                reads=[qs[:], b[:]], writes=[qs[:]], tag=tag + ".q3")
            em.emit("V", lambda e, b=b, q0=q0, q1=q1: e.tensor_single_scalar(
                b[:, q0:q1], qs[:, 0:q1 - q0], 1.0, ALU.subtract),
                reads=[qs[:]], writes=[b[:]], tag=tag + ".q4")
        for c in range(nch):
            c0, c1 = c * 512, min((c + 1) * 512, T_)
            ps_ = p.psum((ot * nch + c) % 2)

            def mm(e, b=b, c0=c0, c1=c1, ps_=ps_, tiles=tuple(xq)):
                last = None
                for k in range(nK):
                    last = e.matmul(ps_[:, 0:c1 - c0], b[:, k * 128:(k + 1) * 128],
                                    tiles[k][:, c0:c1], start=(k == 0), stop=(k == nK - 1))
                return last
            em.emit("T", mm, reads=[b[:]] + [x[:] for x in xq], writes=[ps_[:]],
                    tag=tag + ".mm")
            evac(ot, c0, c1, ps_)
    p.free(*wf, *wb, qs)





def emit_l2a(p, xt, y_t):
    nc, em = p.nc, p.em
    H, WIN = 16, 16
    wq_c = p.din("l2_wq_c", [ND, 128, ND * 128])
    wk_c = p.din("l2_wk_c", [ND, 128, ND * 128])
    wv_ct = p.din("l2_wv_ct", [ND, 128, D])
    wq_l = p.din("l2_wq_l", [ND, 128, ND * 128])
    wk_l = p.din("l2_wk_l", [ND, 128, ND * 128])
    wv_lt = p.din("l2_wv_lt", [ND, 128, D])
    wo_c = p.din("l2_wo_c", [ND, 128, ND * 128])
    wo_l = p.din("l2_wo_l", [ND, 128, ND * 128])
    bqc = p.din("l2_bqc", [128, ND]); bkc = p.din("l2_bkc", [128, ND])
    bvc = p.din("l2_bvc", [128, ND]); boc = p.din("l2_boc", [128, ND])
    bql = p.din("l2_bql", [128, ND]); bkl = p.din("l2_bkl", [128, ND])
    bvl = p.din("l2_bvl", [128, ND]); bol = p.din("l2_bol", [128, ND])
    xn_t = p.dout("xn_t", [D, T], BF16)
    xn_tm_t = p.dout("xn_tm_t", [T, D], BF16)
    bnsum = p.dout("bnsum", [128, 2 * ND])

    sm = {}
    for nm, srcd in (("bqc", bqc), ("bkc", bkc), ("bvc", bvc), ("boc", boc),
                     ("bql", bql), ("bkl", bkl), ("bvl", bvl), ("bol", bol)):
        dst = sm[nm] = p.buf("l2s." + nm, [128, ND], F32)
        em.emit("Q", lambda e, d=dst, s=srcd: e.dma_start(out=d[:], in_=s),
                writes=[dst[:]], dma=True)

    xn = [p.buf(f"l2.xn{k}", [128, T], BF16) for k in range(ND)]
    p.layernorm(xt, T, xn, tag="l2ln")
    for k in range(ND):
        em.emit("Q", lambda e, k=k, t=xn[k]: e.dma_start(
            out=xn_t[k * 128:(k + 1) * 128, :], in_=t[:]),
            reads=[xn[k][:]], writes=[f"o.xn.{k}"], dma=True)
    bns = p.buf("l2.bns", [128, 2 * ND], F32)
    scr = p.buf("l2.scr", [128, 512], F32)
    col = p.buf("l2.col", [128, 1], F32)
    for k in range(ND):
        em.emit("V", lambda e, k=k, t=xn[k]: e.reduce_sum(bns[:, k:k + 1], t[:], axis=AX.X),
                reads=[xn[k][:]], writes=[bns[:]], tag="l2.bn1")
        for c in range(4):
            em.emit("S", lambda e, k=k, c=c, t=xn[k]: e.activation(
                scr[:], t[:, c * 512:(c + 1) * 512], AF.Square),
                reads=[xn[k][:]], writes=[scr[:]], tag="l2.bn2")
            em.emit("V", lambda e: e.reduce_sum(col[:], scr[:], axis=AX.X),
                    reads=[scr[:]], writes=[col[:]], tag="l2.bn3")
            if c == 0:
                em.emit("V", lambda e, k=k: e.tensor_copy(bns[:, ND + k:ND + k + 1], col[:]),
                        reads=[col[:]], writes=[bns[:]], tag="l2.bn4")
            else:
                em.emit("V", lambda e, k=k: e.tensor_tensor(
                    bns[:, ND + k:ND + k + 1], bns[:, ND + k:ND + k + 1], col[:], ALU.add),
                    reads=[col[:], bns[:]], writes=[bns[:]], tag="l2.bn4")
    em.emit("Q", lambda e: e.dma_start(out=bnsum, in_=bns[:]),
            reads=[bns[:]], writes=["o.bns"], dma=True)
    ttile = p.buf("l2.ttile", [128, D], BF16)
    pbf = p.psum(7)
    for c in range(T // 128):
        for k in range(ND):
            em.emit("T", lambda e, c=c, k=k, t=xn[k]: e.transpose(
                pbf[:, k * 128:(k + 1) * 128], t[:, c * 128:(c + 1) * 128], p.ident_bf[:]),
                reads=[xn[k][:], p.ident_bf[:]], writes=[pbf[:]], tag="l2.tm1")
        em.emit("V", lambda e: e.tensor_copy(ttile[:], pbf[:]),
                reads=[pbf[:]], writes=[ttile[:]], tag="l2.tm2")
        em.emit("Q", lambda e, c=c: e.dma_start(
            out=xn_tm_t[c * 128:(c + 1) * 128, :], in_=ttile[:]),
            reads=[ttile[:]], writes=["o.xntm"], dma=True, tag="l2.tm3")
    p.free(scr, col, bns, ttile)

    for k in range(ND):
        em.emit("Q", lambda e, k=k, t=xt[k]: e.dma_start(
            out=y_t[k * 128:(k + 1) * 128, :], in_=t[:]),
            reads=[xt[k][:]], writes=[f"o.y.{k}"], dma=True)
    for k in range(ND):
        p.free(xt[k])
        xt[k] = None
    p.phase()

    if "l2cls" in SKIP:
        for k in range(ND):
            p.free(xn[k])
        for v in sm.values():
            p.free(v)
        p.phase()
        return
    kc = [p.buf(f"l2.kc{k}", [128, T], BF16) for k in range(ND)]

    def evac_kc(ot, c0, c1, ps_):
        em.emit("V", lambda e, ot=ot: e.tensor_scalar(
            kc[ot][:, c0:c1], ps_[:, 0:c1 - c0], 1.0, sm["bkc"][:, ot:ot + 1],
            ALU.mult, ALU.add),
            reads=[ps_[:], sm["bkc"][:]], writes=[kc[ot][:]], tag="l2.kc")
    if "l2nokc" not in SKIP:
        p.matmul_to(wk_c, ND, ND, xn, T, evac_kc, tag="l2kc")

    if "l2c1" in SKIP:
        return
    def make_vtm(wvt_dram, tag):
        NCH = 2 if "l2vtm2" in SKIP else T // 128
        EV = "V" if "l2vtmV" in SKIP else "S"
        vt = [p.buf(f"{tag}{c}", [128, D], BF16) for c in range(T // 128)]
        wvb = [p.buf(p.u(tag + ".w"), [128, D], BF16) for _ in range(ND)]
        for k in range(ND):
            em.emit("G", lambda e, k=k, b=wvb[k], w=wvt_dram: e.dma_start(out=b[:], in_=w[k]),
                    writes=[wvb[k][:]], dma=True, tag=tag + ".ld")
        for c in range(NCH):
            for hf in range(2):
                ps_ = p.psum(hf)

                def mmv(e, c=c, hf=hf, xs=tuple(xn), ws=tuple(wvb), ps_=ps_):
                    last = None
                    for k in range(ND):
                        last = e.matmul(ps_[:, :], xs[k][:, c * 128:(c + 1) * 128],
                                        ws[k][:, hf * 512:(hf + 1) * 512],
                                        start=(k == 0), stop=(k == ND - 1))
                    return last
                em.emit("T", mmv, reads=[w[:] for w in wvb] + [x[:] for x in xn],
                        writes=[ps_[:]], tag=tag + ".mm")
                if EV == "S":
                    em.emit("S", lambda e, c=c, hf=hf, vtc=vt[c], ps_=ps_: e.copy(
                        vtc[:, hf * 512:(hf + 1) * 512], ps_[:, :]),
                        reads=[ps_[:]], writes=[vt[c][:]], tag=tag + ".ev")
                else:
                    em.emit("V", lambda e, c=c, hf=hf, vtc=vt[c], ps_=ps_: e.tensor_copy(
                        vtc[:, hf * 512:(hf + 1) * 512], ps_[:, :]),
                        reads=[ps_[:]], writes=[vt[c][:]], tag=tag + ".ev")
        p.free(*wvb)
        return vt
    vct = make_vtm(wv_ct, "l2.vct")

    if "l2c2" in SKIP:
        return
    q0 = p.buf("l2.q0", [128, ND], F32)

    def evac_q0(ot, c0, c1, ps_):
        em.emit("V", lambda e, ot=ot: e.tensor_scalar(
            q0[:, ot:ot + 1], ps_[:, 0:1], 1.0, sm["bqc"][:, ot:ot + 1],
            ALU.mult, ALU.add),
            reads=[ps_[:], sm["bqc"][:]], writes=[q0[:]], tag="l2.q0")
    p.matmul_to(wq_c, ND, ND, [x for x in xn], 1, evac_q0, tag="l2q0")
    em.emit("V", lambda e: e.tensor_scalar_mul(q0[:], q0[:], 0.125),
            reads=[q0[:]], writes=[q0[:]])
    q0b = p.buf("l2.q0b", [128, ND], BF16)
    em.emit("V", lambda e: e.tensor_copy(q0b[:], q0[:]), reads=[q0[:]], writes=[q0b[:]])

    qblk = [p.buf(f"l2.qblk{k}", [128, 16], BF16) for k in range(ND)]
    for k in range(ND):
        em.emit("G", lambda e, t=qblk[k]: e.memset(t[:], 0.0), writes=[qblk[k][:]])
        em.emit("V", lambda e, k=k, t=qblk[k]: e.tensor_copy(
            t[0:64, 2 * k:2 * k + 1], q0b[0:64, k:k + 1]),
            reads=[q0b[:]], writes=[qblk[k][:]], tag="l2.qb1")
        em.emit("V", lambda e, k=k, t=qblk[k]: e.tensor_copy(
            t[64:128, 2 * k + 1:2 * k + 2], q0b[64:128, k:k + 1]),
            reads=[q0b[:]], writes=[qblk[k][:]], tag="l2.qb2")
    if "l2c3" in SKIP:
        return
    if "tapq0" in DBG:
        p.tap("q0", q0)
    if "tapqb" in DBG:
        p.tap("qblk0", qblk[0])
        p.tap("qblk1", qblk[1])
        p.tap("kc0", kc[0])
    sc_cls = p.buf("l2.sccls", [16, T], F32)
    for c in range(4):
        ps_ = p.psum(2)

        def mmsc(e, c=c, tiles=tuple(kc), qb=tuple(qblk), ps_=ps_):
            last = None
            for k in range(ND):
                last = e.matmul(ps_[0:16, :], qb[k][:], tiles[k][:, c * 512:(c + 1) * 512],
                                start=(k == 0), stop=(k == ND - 1))
            return last
        em.emit("T", mmsc, reads=[t[:] for t in qblk] + [t[:] for t in kc],
                writes=[ps_[:]], tag="l2.scmm")
        em.emit("V", lambda e, c=c, ps_=ps_: e.tensor_copy(sc_cls[:, c * 512:(c + 1) * 512],
                                                           ps_[0:16, :]),
                reads=[ps_[:]], writes=[sc_cls[:]], tag="l2.scev")
    for k in range(ND):
        p.free(qblk[k], kc[k])
    rmx = p.buf("l2.rmx", [16, 1], F32)
    rsm = p.buf("l2.rsm", [16, 1], F32)
    em.emit("V", lambda e: e.reduce_max(rmx[:], sc_cls[:], axis=AX.X),
            reads=[sc_cls[:]], writes=[rmx[:]])
    em.emit("V", lambda e: e.tensor_scalar_mul(rmx[:], rmx[:], -1.0),
            reads=[rmx[:]], writes=[rmx[:]])
    em.emit("S", lambda e: e.activation(sc_cls[:], sc_cls[:], AF.Exp, bias=rmx[0:16, :]),
            reads=[sc_cls[:], rmx[:]], writes=[sc_cls[:]])
    em.emit("V", lambda e: e.reduce_sum(rsm[:], sc_cls[:], axis=AX.X),
            reads=[sc_cls[:]], writes=[rsm[:]])
    em.emit("V", lambda e: e.reciprocal(rsm[:], rsm[:]), reads=[rsm[:]], writes=[rsm[:]])
    awc = p.buf("l2.awc", [16, T], BF16)
    em.emit("V", lambda e: e.tensor_scalar(awc[:], sc_cls[:], rsm[0:16, :], None, ALU.mult),
            reads=[sc_cls[:], rsm[:]], writes=[awc[:]])
    p.free(sc_cls)
    if "tapsc" in DBG:
        p.tap("sccls", sc_cls)
    if "l2c4" in SKIP:
        return
    awT = p.buf("l2.awT", [128, 16 * (T // 128)], BF16)
    pbf2 = p.psum(7)
    for c in range(T // 128):
        em.emit("T", lambda e, c=c: e.transpose(
            pbf2[0:128, 0:16], awc[0:16, c * 128:(c + 1) * 128], p.ident_bf[0:16, 0:16]),
            reads=[awc[:], p.ident_bf[:]], writes=[pbf2[:]], tag="l2.awt")
        em.emit("V", lambda e, c=c: e.tensor_copy(awT[:, c * 16:(c + 1) * 16], pbf2[0:128, 0:16]),
                reads=[pbf2[:]], writes=[awT[:]], tag="l2.awtev")
    ctxall = p.buf("l2.ctxall", [16, D], F32)
    for hf in range(2):
        ps_ = p.psum(3)

        def mmctx(e, hf=hf, vt=tuple(vct), ps_=ps_):
            last = None
            for c in range(T // 128):
                last = e.matmul(ps_[0:16, :], awT[:, c * 16:(c + 1) * 16],
                                vt[c][:, hf * 512:(hf + 1) * 512],
                                start=(c == 0), stop=(c == T // 128 - 1))
            return last
        em.emit("T", mmctx, reads=[awT[:]] + [v[:] for v in vct], writes=[ps_[:]],
                tag="l2.ctxmm")
        em.emit("V", lambda e, hf=hf, ps_=ps_: e.tensor_copy(
            ctxall[:, hf * 512:(hf + 1) * 512], ps_[0:16, :]),
            reads=[ps_[:]], writes=[ctxall[:]], tag="l2.ctxev")
    for c in range(T // 128):
        p.free(vct[c])
    p.free(awc, awT, rmx, rsm, q0, q0b)
    if "l2c5" in SKIP:
        return
    if "tapctx" in DBG:
        p.tap("ctxall", ctxall)
    ctx_fm = p.buf("l2.ctxfm", [128, ND], F32)
    for h in range(16):
        em.emit("G", lambda e, h=h: e.dma_start(
            out=ctx_fm[(h % 2) * 64:(h % 2) * 64 + 64, h // 2:h // 2 + 1],
            in_=ctxall[h:h + 1, h * 64:(h + 1) * 64]),
            reads=[ctxall[:]], writes=[ctx_fm[:]], dma=True, tag="l2.diag")
    em.emit("V", lambda e: e.tensor_tensor(ctx_fm[:], ctx_fm[:], sm["bvc"][:], ALU.add),
            reads=[ctx_fm[:], sm["bvc"][:]], writes=[ctx_fm[:]])
    ctx_fmb = p.buf("l2.ctxfmb", [128, ND], BF16)
    em.emit("V", lambda e: e.tensor_copy(ctx_fmb[:], ctx_fm[:]),
            reads=[ctx_fm[:]], writes=[ctx_fmb[:]])
    if "l2c6" in SKIP:
        return
    gcol = p.buf("l2.gcol", [128, ND], F32)

    def evac_go(ot, c0, c1, ps_):
        em.emit("V", lambda e, ot=ot: e.tensor_scalar(
            gcol[:, ot:ot + 1], ps_[:, 0:1], 1.0, sm["boc"][:, ot:ot + 1],
            ALU.mult, ALU.add),
            reads=[ps_[:], sm["boc"][:]], writes=[gcol[:]], tag="l2.go")
    p.matmul_to(wo_c, ND, ND, [ctx_fmb[:, k:k + 1].unsqueeze(0).squeeze(0) if False else ctx_fmb for k in range(ND)], 1, evac_go, tag="l2go", xsel=True)
    for ot in range(ND):
        em.emit("G", lambda e, ot=ot: e.dma_start(
            out=y_t[ot * 128:(ot + 1) * 128, 0:1], in_=gcol[:, ot:ot + 1],
            accum_op=ALU.add, allow_slow_non_contiguous=True),
            reads=[gcol[:]], writes=[f"o.y.{ot}"], dma=True, tag="l2.goacc")
    p.free(ctxall, ctx_fm, ctx_fmb, gcol)
    p.phase()

    if "l2loc" in SKIP:
        for k in range(ND):
            p.free(xn[k])
        for v in sm.values():
            p.free(v)
        p.phase()
        return
    ql = [p.buf(f"l2.ql{k}", [128, T], BF16) for k in range(ND)]

    def evac_ql(ot, c0, c1, ps_):
        em.emit("V", lambda e, ot=ot: e.tensor_scalar(
            ql[ot][:, c0:c1], ps_[:, 0:c1 - c0], sm["bql"][:, ot:ot + 1], 0.125,
            ALU.add, ALU.mult),
            reads=[ps_[:], sm["bql"][:]], writes=[ql[ot][:]], tag="l2.ql")
    p.matmul_to(wq_l, ND, ND, xn, T, evac_ql, tag="l2ql")
    kl = [p.buf(f"l2.kl{k}", [128, T], BF16) for k in range(ND)]

    def evac_kl(ot, c0, c1, ps_):
        em.emit("V", lambda e, ot=ot: e.tensor_scalar(
            kl[ot][:, c0:c1], ps_[:, 0:c1 - c0], sm["bkl"][:, ot:ot + 1], None, ALU.add),
            reads=[ps_[:], sm["bkl"][:]], writes=[kl[ot][:]], tag="l2.kl")
    p.matmul_to(wk_l, ND, ND, xn, T, evac_kl, tag="l2kl")
    vlt = make_vtm(wv_lt, "l2.vlt")
    for k in range(ND):
        p.free(xn[k])
    p.phase()

    WW = 384
    mgen = p.buf("l2.mgen", [128, WW], F32)
    mb0 = p.buf("l2.mb0", [128, WW], F32)
    NEG = -3.0e30
    em.emit("G", lambda e: e.memset(mgen[:], 0.0), writes=[mgen[:]])
    em.emit("G", lambda e: e.affine_select(mgen[:], mgen[:], pattern=[[1, WW]],
                                           compare_op=ALU.is_ge, fill=NEG,
                                           base=-113, channel_multiplier=-1),
            reads=[mgen[:]], writes=[mgen[:]])
    em.emit("G", lambda e: e.affine_select(mgen[:], mgen[:], pattern=[[-1, WW]],
                                           compare_op=ALU.is_ge, fill=NEG,
                                           base=145, channel_multiplier=1),
            reads=[mgen[:]], writes=[mgen[:]])
    em.emit("G", lambda e: e.memset(mb0[:], 0.0), writes=[mb0[:]])
    em.emit("G", lambda e: e.affine_select(mb0[:], mb0[:], pattern=[[1, WW]],
                                           compare_op=ALU.is_ge, fill=NEG,
                                           base=15, channel_multiplier=-1),
            reads=[mb0[:]], writes=[mb0[:]])
    em.emit("G", lambda e: e.affine_select(mb0[:], mb0[:], pattern=[[-1, WW]],
                                           compare_op=ALU.is_ge, fill=NEG,
                                           base=17, channel_multiplier=1),
            reads=[mb0[:]], writes=[mb0[:]])
    em.emit("G", lambda e: e.affine_select(mb0[:], mb0[:], pattern=[[1, WW]],
                                           compare_op=ALU.is_ge, fill=NEG,
                                           base=-1, channel_multiplier=0),
            reads=[mb0[:]], writes=[mb0[:]])

    ctx_loc = [p.buf(f"l2.cl{k}", [128, T], BF16) for k in range(ND)]
    aws2 = [p.buf(f"l2.aws{i}", [128, WW], F32) for i in range(2)]
    awb22 = [p.buf(f"l2.awb2{i}", [128, WW], BF16) for i in range(2)]
    awT22 = [p.buf(f"l2.awT2{i}", [128, WW], BF16) for i in range(2)]
    lrmx2 = [p.buf(f"l2.lrmx{i}", [128, 1], F32) for i in range(2)]
    lrsm2 = [p.buf(f"l2.lrsm{i}", [128, 1], F32) for i in range(2)]
    pbf3 = p.psum(7)
    for b in range(16):
        nq = 128 if b < 15 else 127
        t0 = 1 + 128 * b
        cA = max(b - 1, 0)
        chunks = [cA, cA + 1, cA + 2]
        live = [c for c in chunks if c < 16]
        msk = mb0 if b == 0 else mgen
        for h in range(16):
            th, pb = h // 2, (h % 2) * 64
            par = (b * 16 + h) % 2
            aws, awb2, awT2 = aws2[par], awb22[par], awT22[par]
            lrmx, lrsm = lrmx2[par], lrsm2[par]
            ps_s = p.psum(2 + 2 * par)

            def mms(e, b=b, th=th, pb=pb, t0=t0, nq=nq, chunks=tuple(chunks), ps_s=ps_s):
                last = None
                for jj, cc in enumerate(chunks):
                    if cc >= 16:
                        continue
                    last = e.matmul(ps_s[0:nq, jj * 128:(jj + 1) * 128],
                                    ql[th][pb:pb + 64, t0:t0 + nq],
                                    kl[th][pb:pb + 64, cc * 128:(cc + 1) * 128],
                                    start=True, stop=True)
                return last
            em.emit("T", mms, reads=[ql[th][:], kl[th][:]], writes=[ps_s[:]], tag="l2.lsc")
            if len(live) < 3:
                em.emit("V", lambda e, nq=nq, ps_s=ps_s: e.memset(ps_s[0:nq, 256:384], -3.0e30),
                        reads=[], writes=[ps_s[:]], tag="l2.lz")
            em.emit("V", lambda e, nq=nq, msk=msk, ps_s=ps_s, aws=aws: e.tensor_tensor(
                aws[0:nq, :], ps_s[0:nq, 0:WW], msk[0:nq, :], ALU.add),
                reads=[ps_s[:], msk[:]], writes=[aws[:]], tag="l2.lmask")
            em.emit("V", lambda e, nq=nq, aws=aws, lrmx=lrmx: e.reduce_max(
                lrmx[0:nq, :], aws[0:nq, :], axis=AX.X),
                reads=[aws[:]], writes=[lrmx[:]], tag="l2.lmax")
            em.emit("V", lambda e, nq=nq, lrmx=lrmx: e.tensor_scalar_mul(
                lrmx[0:nq, :], lrmx[0:nq, :], -1.0),
                reads=[lrmx[:]], writes=[lrmx[:]], tag="l2.lneg")
            em.emit("S", lambda e, nq=nq, aws=aws, lrmx=lrmx: e.activation(
                aws[0:nq, :], aws[0:nq, :], AF.Exp, bias=lrmx[0:nq, :]),
                reads=[aws[:], lrmx[:]], writes=[aws[:]], tag="l2.lexp")
            em.emit("V", lambda e, nq=nq, aws=aws, lrsm=lrsm: e.reduce_sum(
                lrsm[0:nq, :], aws[0:nq, :], axis=AX.X),
                reads=[aws[:]], writes=[lrsm[:]], tag="l2.lsum")
            em.emit("V", lambda e, nq=nq, lrsm=lrsm: e.reciprocal(
                lrsm[0:nq, :], lrsm[0:nq, :]),
                reads=[lrsm[:]], writes=[lrsm[:]], tag="l2.lrec")
            if nq < 128:
                em.emit("G", lambda e, awb2=awb2: e.memset(awb2[:], 0.0),
                        writes=[awb2[:]], tag="l2.lpad")
            em.emit("V", lambda e, nq=nq, aws=aws, lrsm=lrsm, awb2=awb2: e.tensor_scalar(
                awb2[0:nq, :], aws[0:nq, :], lrsm[0:nq, :], None, ALU.mult),
                reads=[aws[:], lrsm[:]], writes=[awb2[:]], tag="l2.lnorm")
            po = par * 384
            for j in range(3):
                em.emit("T", lambda e, j=j, po=po, awb2=awb2: e.transpose(
                    pbf3[:, po + j * 128:po + (j + 1) * 128],
                    awb2[:, j * 128:(j + 1) * 128], p.ident_bf[:]),
                    reads=[awb2[:], p.ident_bf[:]], writes=[f"ps7.{par}"], tag="l2.lt")
                em.emit("V", lambda e, j=j, po=po, awT2=awT2: e.tensor_copy(
                    awT2[:, j * 128:(j + 1) * 128], pbf3[:, po + j * 128:po + j * 128 + 128]),
                    reads=[f"ps7.{par}"], writes=[awT2[:]], tag="l2.ltev")
            ps_c = p.psum(3 + 2 * par)

            def mmc(e, h=h, nq=nq, chunks=tuple(chunks), vt=tuple(vlt), ps_c=ps_c,
                    awT2=awT2):
                last = None
                first = True
                for jj, cc in enumerate(chunks):
                    if cc >= 16:
                        continue
                    last = e.matmul(ps_c[0:64, 0:nq], vt[cc][:, h * 64:(h + 1) * 64],
                                    awT2[:, jj * 128:jj * 128 + nq],
                                    start=first, stop=(jj == len(chunks) - 1 or cc == 15))
                    first = False
                return last
            em.emit("T", mmc, reads=[vlt[c][:] for c in live] + [awT2[:]],
                    writes=[ps_c[:]], tag="l2.lctx")
            em.emit("V", lambda e, th=th, pb=pb, t0=t0, nq=nq, ps_c=ps_c: e.tensor_copy(
                ctx_loc[th][pb:pb + 64, t0:t0 + nq], ps_c[0:64, 0:nq]),
                reads=[ps_c[:]], writes=[ctx_loc[th][:]], tag="l2.lcev")
    for k in range(ND):
        p.free(ql[k], kl[k])
    for c in range(T // 128):
        p.free(vlt[c])
    p.free(aws2[0], aws2[1], awb22[0], awb22[1], awT22[0], awT22[1],
           lrmx2[0], lrmx2[1], lrsm2[0], lrsm2[1], mgen, mb0)
    for k in range(ND):
        em.emit("G", lambda e, k=k, t=ctx_loc[k]: e.memset(t[:, 0:1], 0.0),
                writes=[ctx_loc[k][:]], tag="l2.c0")
        em.emit("V", lambda e, k=k, t=ctx_loc[k]: e.tensor_scalar(
            t[:, 1:T], t[:, 1:T], sm["bvl"][:, k:k + 1], None, ALU.add),
            reads=[ctx_loc[k][:], sm["bvl"][:]], writes=[ctx_loc[k][:]], tag="l2.cbias")
    p.phase()

    och = p.buf("l2.och", [128, 512], F32)

    def evac_ol(ot, c0, c1, ps_):
        em.emit("V", lambda e, ot=ot, c0=c0, c1=c1: e.tensor_scalar(
            och[:, 0:c1 - c0], ps_[:, 0:c1 - c0], sm["bol"][:, ot:ot + 1], None, ALU.add),
            reads=[ps_[:], sm["bol"][:]], writes=[och[:]], tag="l2.ol1")
        lo = 1 if c0 == 0 else 0
        em.emit("G", lambda e, ot=ot, c0=c0, c1=c1, lo=lo: e.dma_start(
            out=y_t[ot * 128:(ot + 1) * 128, c0 + lo:c1], in_=och[:, lo:c1 - c0],
            accum_op=ALU.add),
            reads=[och[:]], writes=[f"o.y.{ot}"], dma=True, tag="l2.olacc")
    p.matmul_to(wo_l, ND, ND, ctx_loc, T, evac_ol, tag="l2ol")
    for k in range(ND):
        p.free(ctx_loc[k])
    p.free(och)
    for v in sm.values():
        p.free(v)
    p.phase()




def emit_l2b(p, xt, y_t):
    nc, em = p.nc, p.em
    NH, NK, KTOP = 8, 256, 8
    NS = T // 128
    xn_in = p.din("xn_in", [D, T], BF16)
    xn_tm_in = p.din("xn_tm_in", [T, D], BF16)
    bn_all = p.din("bn_all", [128, NCORES * 16])
    wq_d = p.din("l2_wq", [16, 128, ND * 128])
    keys_d = p.din("l2_keys", [16, 128, NK])
    gam = p.din("l2_gam", [128, ND])
    bet = p.din("l2_bet", [128, ND])
    down_d = p.din("l2_down", [65536, D])
    up_d = p.din("l2_up", [65536, D])

    xn = [p.buf(f"l2b.xn{k}", [128, T], BF16) for k in range(ND)]
    for k in range(ND):
        em.emit("Q", lambda e, k=k, t=xn[k]: e.dma_start(
            out=t[:], in_=xn_in[k * 128:(k + 1) * 128, :]),
            writes=[xn[k][:]], dma=True)
    bns = p.buf("l2b.bns", [128, NCORES * 16], F32)
    em.emit("Q", lambda e: e.dma_start(out=bns[:], in_=bn_all), writes=[bns[:]], dma=True)
    sgam = p.buf("l2b.gam", [128, ND], F32)
    sbet = p.buf("l2b.bet", [128, ND], F32)
    em.emit("Q", lambda e: e.dma_start(out=sgam[:], in_=gam), writes=[sgam[:]], dma=True)
    em.emit("Q", lambda e: e.dma_start(out=sbet[:], in_=bet), writes=[sbet[:]], dma=True)
    keys = [p.buf(f"l2b.keys{i}", [128, NK], BF16) for i in range(16)]
    for i in range(16):
        em.emit("G", lambda e, i=i, t=keys[i]: e.dma_start(out=t[:], in_=keys_d[i]),
                writes=[keys[i][:]], dma=True)

    tots = p.buf("l2b.tots", [128, 16], F32)
    em.emit("V", lambda e: e.tensor_copy(tots[:], bns[:, 0:16]), reads=[bns[:]], writes=[tots[:]])
    for c in range(1, NCORES):
        em.emit("V", lambda e, c=c: e.tensor_tensor(tots[:], tots[:], bns[:, c * 16:(c + 1) * 16],
                                                    ALU.add),
                reads=[tots[:], bns[:]], writes=[tots[:]])
    NTOK = float(NCORES * T)
    mean = p.buf("l2b.mean", [128, ND], F32)
    var = p.buf("l2b.var", [128, ND], F32)
    acoef = p.buf("l2b.ac", [128, ND], F32)
    bcoef = p.buf("l2b.bc", [128, ND], F32)
    em.emit("V", lambda e: e.tensor_scalar_mul(mean[:], tots[:, 0:8], 1.0 / NTOK),
            reads=[tots[:]], writes=[mean[:]])
    em.emit("V", lambda e: e.tensor_tensor(var[:], mean[:], mean[:], ALU.mult),
            reads=[mean[:]], writes=[var[:]])
    em.emit("V", lambda e: e.scalar_tensor_tensor(var[:], tots[:, 8:16], 1.0 / NTOK, var[:],
                                                  ALU.mult, ALU.subtract),
            reads=[tots[:], var[:]], writes=[var[:]])
    e5 = p.cval(1e-5)
    em.emit("S", lambda e: e.activation(var[:], var[:], AF.Sqrt, bias=e5[:]),
            reads=[var[:], e5[:]], writes=[var[:]])
    em.emit("V", lambda e: e.reciprocal(var[:], var[:]), reads=[var[:]], writes=[var[:]])
    em.emit("V", lambda e: e.tensor_tensor(acoef[:], var[:], sgam[:], ALU.mult),
            reads=[var[:], sgam[:]], writes=[acoef[:]])
    em.emit("V", lambda e: e.tensor_tensor(bcoef[:], mean[:], acoef[:], ALU.mult),
            reads=[mean[:], acoef[:]], writes=[bcoef[:]])
    em.emit("V", lambda e: e.tensor_tensor(bcoef[:], sbet[:], bcoef[:], ALU.subtract),
            reads=[sbet[:], bcoef[:]], writes=[bcoef[:]])
    xnb = [p.buf(f"l2b.xnb{k}", [128, T], BF16) for k in range(ND)]
    for k in range(ND):
        em.emit("V", lambda e, k=k, t=xnb[k], s=xn[k]: e.tensor_scalar(
            t[:], s[:], acoef[:, k:k + 1], bcoef[:, k:k + 1], ALU.mult, ALU.add),
            reads=[xn[k][:], acoef[:], bcoef[:]], writes=[xnb[k][:]], tag="l2b.xnb")
    for k in range(ND):
        p.free(xn[k])
    p.phase()

    qbs = [p.buf(f"l2b.qb{i}", [128, T], BF16) for i in range(2)]
    wqb = [p.buf(f"l2b.wq{i}", [128, ND * 128], BF16) for i in range(2)]
    scfs = [p.buf(f"l2b.scf{i}", [128, NK], F32) for i in range(2)]
    sc_p = [p.buf(f"l2b.scp{pp}", [128, NS * 64], F32) for pp in range(2)]
    idx_p = [p.buf(f"l2b.idxp{pp}", [128, NS * 64], U32) for pp in range(2)]
    for i in range(16):
        pp, h = i // 8, i % 8
        wb = wqb[i % 2]
        qb = qbs[i % 2]
        em.emit("G", lambda e, i=i, wb=wb: e.dma_start(out=wb[:], in_=wq_d[i]),
                writes=[wb[:]], dma=True, tag="l2b.wqld")
        for c in range(4):
            ps_ = p.psum(c % 2)

            def mmq(e, c=c, wb=wb, tiles=tuple(xnb), ps_=ps_):
                last = None
                for k in range(ND):
                    last = e.matmul(ps_[:, :], wb[:, k * 128:(k + 1) * 128],
                                    tiles[k][:, c * 512:(c + 1) * 512],
                                    start=(k == 0), stop=(k == ND - 1))
                return last
            em.emit("T", mmq, reads=[wb[:]] + [x[:] for x in xnb], writes=[ps_[:]],
                    tag="l2b.qmm")
            em.emit("S", lambda e, c=c, qb=qb, ps_=ps_: e.copy(qb[:, c * 512:(c + 1) * 512], ps_[:, :]),
                    reads=[ps_[:]], writes=[qb[:]], tag="l2b.qev")
        kt = keys[h * 2 + pp]
        for st in range(NS):
            ps2 = p.psum(2 + st % 2)
            scf = scfs[st % 2]
            em.emit("T", lambda e, st=st, qb=qb, kt=kt, ps2=ps2: e.matmul(
                ps2[:, 0:NK], qb[:, st * 128:(st + 1) * 128], kt[:],
                start=True, stop=True),
                reads=[qb[:], kt[:]], writes=[ps2[:]], tag="l2b.sim")
            em.emit("V", lambda e, scf=scf, ps2=ps2: e.tensor_copy(scf[:], ps2[:, 0:NK]),
                    reads=[ps2[:]], writes=[scf[:]], tag="l2b.simev")
            o0 = st * 64 + h * 8
            em.emit("V", lambda e, pp=pp, o0=o0, scf=scf: e.max(sc_p[pp][:, o0:o0 + 8], scf[:]),
                    reads=[scf[:]], writes=[sc_p[pp][:]], tag="l2b.max8")
            em.emit("V", lambda e, pp=pp, o0=o0, scf=scf: e.max_index(
                idx_p[pp][:, o0:o0 + 8], sc_p[pp][:, o0:o0 + 8], scf[:]),
                reads=[scf[:], sc_p[pp][:]], writes=[idx_p[pp][:]], tag="l2b.idx8")
    p.free(qbs[0], qbs[1], wqb[0], wqb[1], scfs[0], scfs[1])
    for i in range(16):
        p.free(keys[i])
    p.phase()

    allsc = p.buf("l2b.allsc", [128, 512], F32)
    allidx = p.buf("l2b.allidx", [128, 512], U32)
    tmpu = p.buf("l2b.tmpu", [128, 512], U32)
    scg = p.buf("l2b.scg", [128, 64], F32)
    pkg = p.buf("l2b.pkg", [128, 64], U32)
    rowg = p.buf("l2b.rowg", [128, 64], U32)
    colg = p.buf("l2b.colg", [128, 64], U32)
    accr = p.buf("l2b.accr", [128, 64], U32)
    accc = p.buf("l2b.accc", [128, 64], U32)
    mm_ = p.buf("l2b.mm", [128, 64], U32)
    idxall = p.buf("l2b.idxall", [128, NS * 64], I32)
    wsg = p.buf("l2b.wsg", [128, NS * 64], F32)
    for st in range(NS):
        s0 = st * 64
        A = sc_p[0][:, s0:s0 + 64].rearrange("p (h i) -> p h i", h=8).unsqueeze(3).broadcast_to((128, 8, 8, 8))
        B = sc_p[1][:, s0:s0 + 64].rearrange("p (h j) -> p h j", h=8).unsqueeze(2).broadcast_to((128, 8, 8, 8))
        em.emit("V", lambda e, A=A, B=B: e.tensor_tensor(
            allsc[:].rearrange("p (h i j) -> p h i j", h=8, i=8), A, B, ALU.add),
            reads=[sc_p[0][:], sc_p[1][:]], writes=[allsc[:]], tag="l2b.comb")
        em.emit("V", lambda e, s0=s0: e.tensor_single_scalar(
            tmpu[:, 0:64], idx_p[0][:, s0:s0 + 64], 8, ALU.logical_shift_left),
            reads=[idx_p[0][:]], writes=[tmpu[:]], tag="l2b.shl")
        Ai = tmpu[:, 0:64].rearrange("p (h i) -> p h i", h=8).unsqueeze(3).broadcast_to((128, 8, 8, 8))
        Bi = idx_p[1][:, s0:s0 + 64].rearrange("p (h j) -> p h j", h=8).unsqueeze(2).broadcast_to((128, 8, 8, 8))
        em.emit("V", lambda e, Ai=Ai, Bi=Bi: e.tensor_tensor(
            allidx[:].rearrange("p (h i j) -> p h i j", h=8, i=8), Ai, Bi, ALU.bitwise_or),
            reads=[tmpu[:], idx_p[1][:]], writes=[allidx[:]], tag="l2b.or")
        for h in range(NH):
            em.emit("V", lambda e, h=h: e.max(scg[:, h * 8:(h + 1) * 8],
                                              allsc[:, h * 64:(h + 1) * 64]),
                    reads=[allsc[:]], writes=[scg[:]], tag="l2b.m2")
            em.emit("V", lambda e, h=h: e.max_index(
                pkg[:, h * 8:(h + 1) * 8], scg[:, h * 8:(h + 1) * 8],
                allsc[:, h * 64:(h + 1) * 64]),
                reads=[allsc[:], scg[:]], writes=[pkg[:]], tag="l2b.i2")
        em.emit("V", lambda e: e.tensor_single_scalar(rowg[:, 0:64], pkg[:, 0:64], 3,
                                                      ALU.logical_shift_right),
                reads=[pkg[:]], writes=[rowg[:]], tag="l2b.row")
        em.emit("V", lambda e: e.tensor_single_scalar(colg[:, 0:64], pkg[:, 0:64], 7,
                                                      ALU.bitwise_and),
                reads=[pkg[:]], writes=[colg[:]], tag="l2b.col")
        for w in range(8):
            i0w = tmpu[:, 0:64].rearrange("p (h i) -> p h i", h=8)[:, :, w:w + 1].broadcast_to((128, 8, 8))
            i1w = idx_p[1][:, s0:s0 + 64].rearrange("p (h j) -> p h j", h=8)[:, :, w:w + 1].broadcast_to((128, 8, 8))
            em.emit("V", lambda e, w=w: e.tensor_single_scalar(
                mm_[:, 0:64], rowg[:, 0:64], w, ALU.is_equal),
                reads=[rowg[:]], writes=[mm_[:]], tag="l2b.oh1")
            em.emit("V", lambda e, w=w, i0w=i0w: e.tensor_tensor(
                mm_[:].rearrange("p (h i) -> p h i", h=8),
                mm_[:].rearrange("p (h i) -> p h i", h=8), i0w, ALU.mult),
                reads=[mm_[:], tmpu[:]], writes=[mm_[:]], tag="l2b.oh2")
            if w == 0:
                em.emit("V", lambda e: e.tensor_copy(accr[:], mm_[:]),
                        reads=[mm_[:]], writes=[accr[:]], tag="l2b.oh3")
            else:
                em.emit("V", lambda e: e.tensor_tensor(accr[:], accr[:], mm_[:], ALU.add),
                        reads=[mm_[:], accr[:]], writes=[accr[:]], tag="l2b.oh3")
            em.emit("V", lambda e, w=w: e.tensor_single_scalar(
                mm_[:, 0:64], colg[:, 0:64], w, ALU.is_equal),
                reads=[colg[:]], writes=[mm_[:]], tag="l2b.oh4")
            em.emit("V", lambda e, w=w, i1w=i1w: e.tensor_tensor(
                mm_[:].rearrange("p (h i) -> p h i", h=8),
                mm_[:].rearrange("p (h i) -> p h i", h=8), i1w, ALU.mult),
                reads=[mm_[:], idx_p[1][:]], writes=[mm_[:]], tag="l2b.oh5")
            if w == 0:
                em.emit("V", lambda e: e.tensor_copy(accc[:], mm_[:]),
                        reads=[mm_[:]], writes=[accc[:]], tag="l2b.oh6")
            else:
                em.emit("V", lambda e: e.tensor_tensor(accc[:], accc[:], mm_[:], ALU.add),
                        reads=[mm_[:], accc[:]], writes=[accc[:]], tag="l2b.oh6")
        em.emit("V", lambda e: e.tensor_tensor(accr[:], accr[:], accc[:], ALU.bitwise_or),
                reads=[accr[:], accc[:]], writes=[accr[:]], tag="l2b.fin")
        em.emit("V", lambda e, s0=s0: e.tensor_copy(idxall[:, s0:s0 + 64], accr[:]),
                reads=[accr[:]], writes=[idxall[:]], tag="l2b.fin2")
        em.emit("S", lambda e, s0=s0: e.activation(wsg[:, s0:s0 + 64], scg[:, 0:64], AF.Sigmoid),
                reads=[scg[:]], writes=[wsg[:]], tag="l2b.sig")
    p.free(allsc, allidx, tmpu, scg, pkg, rowg, colg, accr, accc, mm_)
    p.free(sc_p[0], sc_p[1], idx_p[0], idx_p[1])
    p.phase()

    NB = 6
    xtm = [p.buf(f"l2b.xtm{i}", [128, D], BF16) for i in range(2)]
    gd = [p.buf(f"l2b.gd{i}", [128, D], BF16) for i in range(NB)]
    gu = [p.buf(f"l2b.gu{i}", [128, D], BF16) for i in range(NB)]
    hmid = [p.buf(f"l2b.hmid{i}", [128, 64], F32) for i in range(2)]
    wfin = [p.buf(f"l2b.wfin{i}", [128, 64], F32) for i in range(2)]
    dscr = p.buf("l2b.dscr", [128, D], F32)
    dg = [p.buf(f"l2b.dg{i}", [128, 128], BF16) for i in range(2)]
    otm = [p.buf(f"l2b.otm{i}", [128, D], F32) for i in range(2)]
    for st in range(NS):
        pi = st % 2
        xb, hm, wf, ot = xtm[pi], hmid[pi], wfin[pi], otm[pi]
        pa, pb = (p.psum(0), p.psum(1)) if pi == 0 else (p.psum(2), p.psum(3))
        em.emit("Q", lambda e, st=st, xb=xb: e.dma_start(
            out=xb[:], in_=xn_tm_in[st * 128:(st + 1) * 128, :]),
            writes=[xb[:]], dma=True, tag="l2b.xtmld")
        for j in range(64):
            g = gd[j % NB]
            em.emit("G", lambda e, st=st, j=j, g=g: e.indirect_dma_start(
                g[:], None, down_d,
                IndirectOffsetOnAxis(ap=idxall[:, st * 64 + j:st * 64 + j + 1], axis=0)),
                reads=[idxall[:]], writes=[g[:]], dma=True, tag="l2b.gdown")
            em.emit("V", lambda e, j=j, g=g, xb=xb, hm=hm: e.scalar_tensor_tensor(
                dscr[:], g[:], 1.0, xb[:], ALU.mult, ALU.mult,
                accum_out=hm[:, j:j + 1]),
                reads=[g[:], xb[:]], writes=[dscr[:], hm[:]], tag="l2b.dot")
        em.emit("S", lambda e, hm=hm, wf=wf: e.activation(wf[:], hm[:], AF.Gelu),
                reads=[hm[:]], writes=[wf[:]], tag="l2b.gelu")
        em.emit("V", lambda e, st=st, wf=wf: e.tensor_tensor(
            wf[:], wf[:], wsg[:, st * 64:(st + 1) * 64], ALU.mult),
            reads=[wf[:], wsg[:]], writes=[wf[:]], tag="l2b.wm")
        for j in range(64):
            g = gu[j % NB]
            d_ = dg[j % 2]
            em.emit("G", lambda e, st=st, j=j, g=g: e.indirect_dma_start(
                g[:], None, up_d,
                IndirectOffsetOnAxis(ap=idxall[:, st * 64 + j:st * 64 + j + 1], axis=0)),
                reads=[idxall[:]], writes=[g[:]], dma=True, tag="l2b.gup")
            em.emit("V", lambda e, j=j, d_=d_, wf=wf: e.tensor_tensor(
                d_[:], wf[:, j:j + 1].broadcast_to((128, 128)), p.ident_bf[:], ALU.mult),
                reads=[wf[:], p.ident_bf[:]], writes=[d_[:]], tag="l2b.dg")

            def mmup(e, j=j, g=g, d_=d_, pa=pa, pb=pb):
                e.matmul(pa[:, :], d_[:], g[:, 0:512],
                         start=(j == 0), stop=(j == 63))
                return e.matmul(pb[:, :], d_[:], g[:, 512:1024],
                                start=(j == 0), stop=(j == 63))
            em.emit("T", mmup, reads=[d_[:], g[:]],
                    writes=[pa[:], pb[:]], tag="l2b.upmm")
        for hf in range(2):
            ph = pa if hf == 0 else pb
            em.emit("V", lambda e, hf=hf, ph=ph, ot=ot: e.tensor_copy(
                ot[:, hf * 512:(hf + 1) * 512], ph[:, :]),
                reads=[ph[:]], writes=[ot[:]], tag="l2b.oev")
        for k in range(ND):
            ps2 = p.psum(4 + k % 2)
            em.emit("T", lambda e, k=k, ps2=ps2, ot=ot: e.transpose(
                ps2[:, 0:128], ot[:, k * 128:(k + 1) * 128], p.ident[:]),
                reads=[ot[:], p.ident[:]], writes=[ps2[:]], tag="l2b.otp")
            em.emit("V", lambda e, k=k, st=st, ps2=ps2, t=xt[k]: e.tensor_tensor(
                t[:, st * 128:(st + 1) * 128], t[:, st * 128:(st + 1) * 128],
                ps2[:, 0:128], ALU.add),
                reads=[ps2[:], xt[k][:]], writes=[xt[k][:]], tag="l2b.oadd")
    p.free(xtm[0], xtm[1], hmid[0], hmid[1], wfin[0], wfin[1], dscr,
           dg[0], dg[1], otm[0], otm[1], *gd, *gu)
    for k in range(ND):
        p.free(xnb[k])
    p.free(idxall, wsg, bns, sgam, sbet, tots, mean, var, acoef, bcoef)
    p.phase()




def emit_l3(p, xt, y_t):
    nc, em = p.nc, p.em
    DH, NDH = 4096, 32
    tdec = p.din("l3_tdec", [64, 1])
    tfir = p.din("l3_tfir", [64, 1])
    w_k = p.din("l3_wk", [1, 128, ND * 64])
    w_v = p.din("l3_wv", [1, 128, ND * 64])
    w_o = p.din("l3_wo", [ND, 64, 128])
    tsg = p.din("l3_tsg", [128, ND])
    lnw = p.din("l3_lnw", [128, ND])
    lnb = p.din("l3_lnb", [128, ND])
    w1d = p.din("l3_w1", [NDH, 128, ND * 128])
    w2d = p.din("l3_w2", [NDH, 128, ND * 128])
    h1d = p.din("l3_h1", [NDH, 128, NDH * 128])
    h2d = p.din("l3_h2", [NDH, 128, NDH * 128])
    w3d = p.din("l3_w3", [ND, 128, NDH * 128])
    s1 = p.din("l3_s1", [128, NDH])
    s2 = p.din("l3_s2", [128, NDH])
    sh1 = p.din("l3_sh1", [128, NDH])
    sh2 = p.din("l3_sh2", [128, NDH])
    s3 = p.din("l3_s3", [128, ND])

    sm = {}
    for nm, srcd, sh in (("tdec", tdec, [64, 1]), ("tfir", tfir, [64, 1]),
                         ("tsg", tsg, [128, ND]), ("lnw", lnw, [128, ND]),
                         ("lnb", lnb, [128, ND]), ("s1", s1, [128, NDH]),
                         ("s2", s2, [128, NDH]), ("sh1", sh1, [128, NDH]),
                         ("sh2", sh2, [128, NDH]), ("s3", s3, [128, ND])):
        dst = sm[nm] = p.buf("l3s." + nm, sh, F32)
        em.emit("Q", lambda e, d=dst, s=srcd: e.dma_start(out=d[:], in_=s),
                writes=[dst[:]], dma=True)

    xn = [p.buf(f"l3.xn{k}", [128, T], BF16) for k in range(ND)]
    p.layernorm(xt, T, xn, tag="l3ln")
    for k in range(ND):
        em.emit("Q", lambda e, k=k, t=xt[k]: e.dma_start(
            out=y_t[k * 128:(k + 1) * 128, :], in_=t[:]),
            reads=[xt[k][:]], writes=[f"o.y.{k}"], dma=True)
    for k in range(ND):
        p.free(xt[k])
        xt[k] = None
    p.phase()

    gcol = p.buf("l3.gcol", [128, ND], F32)
    em.emit("S", lambda e: e.activation(gcol[:], sm["tsg"][:], AF.Sigmoid),
            reads=[sm["tsg"][:]], writes=[gcol[:]])
    SH = T // 2
    xs = [p.buf(f"l3.xs{k}", [128, T], BF16) for k in range(ND)]
    dtm = p.buf("l3.dtm", [128, T], F32)
    for k in range(ND):
        em.emit("V", lambda e, k=k, t=xn[k]: e.tensor_tensor(
            dtm[:, 0:SH], t[:, SH:T], t[:, 0:SH], ALU.subtract),
            reads=[xn[k][:]], writes=[dtm[:]], tag="l3.sh1")
        em.emit("V", lambda e, k=k, t=xn[k]: e.tensor_tensor(
            dtm[:, SH:T], t[:, 0:SH], t[:, SH:T], ALU.subtract),
            reads=[xn[k][:]], writes=[dtm[:]], tag="l3.sh2")
        em.emit("V", lambda e, k=k, t=xn[k], o=xs[k]: e.scalar_tensor_tensor(
            o[:], dtm[:], gcol[:, k:k + 1], t[:], ALU.mult, ALU.add),
            reads=[dtm[:], gcol[:], xn[k][:]], writes=[xs[k][:]], tag="l3.shm")
    p.free(dtm)
    xm = [p.buf(f"l3.xm{k}", [128, T], BF16) for k in range(ND)]
    p.layernorm(xs, T, xm, tag="l3ln2")
    for k in range(ND):
        em.emit("V", lambda e, k=k, t=xm[k]: e.tensor_scalar(
            t[:], t[:], sm["lnw"][:, k:k + 1], sm["lnb"][:, k:k + 1], ALU.mult, ALU.add),
            reads=[xm[k][:], sm["lnw"][:], sm["lnb"][:]], writes=[xm[k][:]], tag="l3.lnwb")
    for k in range(ND):
        p.free(xs[k])
    bscr3 = p.buf("l3.bscr3", [128, 512], F32)
    kv = p.buf("l3.k", [64, T], F32)
    vv = p.buf("l3.v", [64, T], F32)

    def evac_k3(ot, c0, c1, ps_):
        em.emit("V", lambda e: e.tensor_copy(kv[:, c0:c1], ps_[0:64, 0:c1 - c0]),
                reads=[ps_[:]], writes=[kv[:]], tag="l3.kev")
    p.matmul_to(w_k, ND, 1, xm, T, evac_k3, tag="l3k", ow=64)

    def evac_v3(ot, c0, c1, ps_):
        em.emit("V", lambda e: e.tensor_copy(vv[:, c0:c1], ps_[0:64, 0:c1 - c0]),
                reads=[ps_[:]], writes=[vv[:]], tag="l3.vev")
    p.matmul_to(w_v, ND, 1, xm, T, evac_v3, tag="l3v", ow=64)
    etf = p.buf("l3.etf", [64, 1], F32)
    em.emit("S", lambda e: e.activation(etf[:], sm["tfir"][0:64, :], AF.Exp),
            reads=[sm["tfir"][:]], writes=[etf[:]])
    em.emit("V", lambda e: e.tensor_scalar_mul(etf[:], etf[:], -1.0),
            reads=[etf[:]], writes=[etf[:]])
    em.emit("S", lambda e: e.activation(kv[:], kv[:], AF.Exp, scale=etf[0:64, :]),
            reads=[kv[:], etf[:]], writes=[kv[:]], tag="l3.wkv1")
    em.emit("V", lambda e: e.tensor_tensor(kv[:], kv[:], vv[:], ALU.mult),
            reads=[kv[:], vv[:]], writes=[kv[:]], tag="l3.wkv2")
    wdec = p.buf("l3.wdec", [64, 1], F32)
    em.emit("S", lambda e: e.activation(wdec[:], sm["tdec"][0:64, :], AF.Exp),
            reads=[sm["tdec"][:]], writes=[wdec[:]])
    wdb = p.buf("l3.wdb", [64, T], F32)
    em.emit("V", lambda e: e.tensor_copy(wdb[:], wdec[0:64, :].broadcast_to((64, T))),
            reads=[wdec[:]], writes=[wdb[:]], tag="l3.wdb")
    em.emit("V", lambda e: e.tensor_tensor_scan(vv[:], wdb[:], kv[:], 0.0, ALU.mult, ALU.add),
            reads=[wdb[:], kv[:]], writes=[vv[:]], tag="l3.scan")
    vvb = p.buf("l3.vvb", [64, T], BF16)
    em.emit("V", lambda e: e.tensor_copy(vvb[:], vv[:]), reads=[vv[:]], writes=[vvb[:]])
    wob = p.buf("l3.wob", [64, ND * 128], BF16)
    em.emit("G", lambda e: e.dma_start(out=wob[:].rearrange("b (a c) -> b a c", a=ND), in_=w_o.rearrange("a b c -> b a c")),
            writes=[wob[:]], dma=True, tag="l3.wold")
    for ot in range(ND):
        for c in range(4):
            ps_ = p.psum((ot * 4 + c) % 2)
            em.emit("T", lambda e, ot=ot, c=c, ps_=ps_: e.matmul(
                ps_[:, :], wob[0:64, ot * 128:(ot + 1) * 128],
                vvb[0:64, c * 512:(c + 1) * 512], start=True, stop=True),
                reads=[wob[:], vvb[:]], writes=[ps_[:]], tag="l3.omm")
            em.emit("V", lambda e, ps_=ps_: e.tensor_copy(bscr3[:], ps_[:, :]),
                    reads=[ps_[:]], writes=[bscr3[:]], tag="l3.oev")
            em.emit("G", lambda e, ot=ot, c=c: e.dma_start(
                out=y_t[ot * 128:(ot + 1) * 128, c * 512:(c + 1) * 512],
                in_=bscr3[:], accum_op=ALU.add),
                reads=[bscr3[:]], writes=[f"o.y.{ot}"], dma=True, tag="l3.oacc")
    for k in range(ND):
        p.free(xm[k])
    p.free(kv, vv, vvb, wob, wdb, wdec, etf, gcol, bscr3)
    p.phase()

    och = p.buf("l3.och", [128, 512], F32)
    bscr = p.buf("l3.bscr", [128, 512], F32)
    for half in range(2):
        h0 = half * (T // 2)
        TH = T // 2
        xnh = [xn[k][:, h0:h0 + TH] for k in range(ND)]
        hT = [p.buf(f"l3.h{j}", [128, TH], BF16) for j in range(NDH)]
        w1b = [p.buf(p.u("l3.w1b"), [128, ND * 128], BF16) for _ in range(2)]
        w2b = [p.buf(p.u("l3.w2b"), [128, ND * 128], BF16) for _ in range(2)]
        for ot in range(NDH):
            b1, b2 = w1b[ot % 2], w2b[ot % 2]
            em.emit("G", lambda e, ot=ot, b1=b1: e.dma_start(out=b1[:], in_=w1d[ot]),
                    writes=[b1[:]], dma=True, tag="l3.w1ld")
            em.emit("G", lambda e, ot=ot, b2=b2: e.dma_start(out=b2[:], in_=w2d[ot]),
                    writes=[b2[:]], dma=True, tag="l3.w2ld")
            for c in range(TH // 512):
                c0 = c * 512
                psA, psB = p.psum(0), p.psum(1)

                def mm1(e, b1=b1, c0=c0, xs_=tuple(xn), psA=psA, h0=h0):
                    last = None
                    for k in range(ND):
                        last = e.matmul(psA[:, :], b1[:, k * 128:(k + 1) * 128],
                                        xs_[k][:, h0 + c0:h0 + c0 + 512],
                                        start=(k == 0), stop=(k == ND - 1))
                    return last
                em.emit("T", mm1, reads=[b1[:]] + [x[:] for x in xn], writes=[psA[:]],
                        tag="l3.mm1")

                def mm2(e, b2=b2, c0=c0, xs_=tuple(xn), psB=psB, h0=h0):
                    last = None
                    for k in range(ND):
                        last = e.matmul(psB[:, :], b2[:, k * 128:(k + 1) * 128],
                                        xs_[k][:, h0 + c0:h0 + c0 + 512],
                                        start=(k == 0), stop=(k == ND - 1))
                    return last
                em.emit("T", mm2, reads=[b2[:]] + [x[:] for x in xn], writes=[psB[:]],
                        tag="l3.mm2")
                em.emit("S", lambda e, ot=ot, psA=psA: e.activation(
                    bscr[:], psA[:, :], AF.Silu, scale=sm["s1"][:, ot:ot + 1]),
                    reads=[psA[:], sm["s1"][:]], writes=[bscr[:]], tag="l3.silu")
                em.emit("V", lambda e, ot=ot, psB=psB: e.tensor_scalar(
                    och[:], psB[:, :], sm["s2"][:, ot:ot + 1], None, ALU.mult),
                    reads=[psB[:], sm["s2"][:]], writes=[och[:]], tag="l3.s2m")
                em.emit("V", lambda e, ot=ot, c0=c0, ht=hT[ot]: e.tensor_tensor(
                    ht[:, c0:c0 + 512], bscr[:], och[:], ALU.mult),
                    reads=[bscr[:], och[:]], writes=[hT[ot][:]], tag="l3.hmul")
        p.free(w1b[0], w1b[1], w2b[0], w2b[1])
        g1 = [p.buf(f"l3.g1{j}", [128, TH], BF16) for j in range(NDH)]

        def big_mm(wdram, scol, src_tiles, dst_tiles, tag):
            wb = [p.buf(p.u(tag + ".w"), [128, NDH * 128], BF16) for _ in range(2)]
            for ot in range(NDH):
                b = wb[ot % 2]
                em.emit("G", lambda e, ot=ot, b=b: e.dma_start(out=b[:], in_=wdram[ot]),
                        writes=[b[:]], dma=True, tag=tag + ".ld")
                for c in range(TH // 512):
                    c0 = c * 512
                    ps_ = p.psum((ot * (TH // 512) + c) % 2)

                    def mmb(e, b=b, c0=c0, ts=tuple(src_tiles), ps_=ps_):
                        last = None
                        for k in range(NDH):
                            last = e.matmul(ps_[:, :], b[:, k * 128:(k + 1) * 128],
                                            ts[k][:, c0:c0 + 512],
                                            start=(k == 0), stop=(k == NDH - 1))
                        return last
                    em.emit("T", mmb, reads=[b[:]] + [t[:] for t in src_tiles],
                            writes=[ps_[:]], tag=tag + ".mm")
                    em.emit("S", lambda e, ot=ot, c0=c0, dt_=dst_tiles[ot], ps_=ps_: e.activation(
                        dt_[:, c0:c0 + 512], ps_[:, :], AF.Gelu, scale=scol[:, ot:ot + 1]),
                        reads=[ps_[:], scol], writes=[dst_tiles[ot][:]], tag=tag + ".ev")
            p.free(wb[0], wb[1])
        big_mm(h1d, sm["sh1"][:], hT, g1, "l3.h1")
        big_mm(h2d, sm["sh2"][:], g1, hT, "l3.h2")
        for j in range(NDH):
            p.free(g1[j])
        w3b = [p.buf(p.u("l3.w3b"), [128, NDH * 128], BF16) for _ in range(2)]
        for ot in range(ND):
            b = w3b[ot % 2]
            em.emit("G", lambda e, ot=ot, b=b: e.dma_start(out=b[:], in_=w3d[ot]),
                    writes=[b[:]], dma=True, tag="l3.w3ld")
            for c in range(TH // 512):
                c0 = c * 512
                ps_ = p.psum((ot * (TH // 512) + c) % 2)

                def mm3(e, b=b, c0=c0, ts=tuple(hT), ps_=ps_):
                    last = None
                    for k in range(NDH):
                        last = e.matmul(ps_[:, :], b[:, k * 128:(k + 1) * 128],
                                        ts[k][:, c0:c0 + 512],
                                        start=(k == 0), stop=(k == NDH - 1))
                    return last
                em.emit("T", mm3, reads=[b[:]] + [t[:] for t in hT], writes=[ps_[:]],
                        tag="l3.mm3")
                em.emit("V", lambda e, ot=ot, ps_=ps_: e.tensor_scalar(
                    och[:], ps_[:, :], sm["s3"][:, ot:ot + 1], None, ALU.mult),
                    reads=[ps_[:], sm["s3"][:]], writes=[och[:]], tag="l3.oev")
                em.emit("G", lambda e, ot=ot, c0=c0, h0=h0: e.dma_start(
                    out=y_t[ot * 128:(ot + 1) * 128, h0 + c0:h0 + c0 + 512],
                    in_=och[:], accum_op=ALU.add),
                    reads=[och[:]], writes=[f"o.y.{ot}"], dma=True, tag="l3.oacc")
        p.free(w3b[0], w3b[1])
        for j in range(NDH):
            p.free(hT[j])
        p.phase()
    for k in range(ND):
        p.free(xn[k])
    for v in sm.values():
        p.free(v)
    p.phase()

def build_chain(emitters):
    p = Prog()
    em = p.em
    x_t = p.din("x_t", [D, T])
    y_t = p.dout("y_t", [D, T])
    p.consts()
    xt = [p.buf(f"x{k}", [128, T], F32) for k in range(ND)]
    for k in range(ND):
        em.emit("Q", lambda e, k=k, t=xt[k]: e.dma_start(
            out=t[:], in_=x_t[k * 128:(k + 1) * 128, :]),
            writes=[xt[k][:]], dma=True)
    for emit_fn in emitters:
        emit_fn(p, xt, y_t)
    for k in range(ND):
        if xt[k] is not None:
            em.emit("Q", lambda e, k=k: e.dma_start(out=y_t[k * 128:(k + 1) * 128, :],
                                                    in_=xt[k][:]),
                    reads=[xt[k][:]], writes=[f"o.y.{k}"], dma=True)
    return p.finish()



def colblocks(wt):
    K, O = wt.shape
    nK, nO = K // 128, O // 128
    a = wt.reshape(nK, 128, nO, 128).transpose(2, 1, 0, 3).reshape(nO, 128, nK * 128)
    return np.ascontiguousarray(a)


def pcol(v):
    n = v.shape[0]
    return np.ascontiguousarray(v.reshape(n // 128, 128).T)


def prep_l0_weights(ins):
    f = lambda k: np.ascontiguousarray(np.asarray(ins[k], dtype=np.float32))
    w = {}
    w["l0_w_in"] = colblocks(np.ascontiguousarray(f("L0_attn_in_proj_w").T))
    xpt = np.ascontiguousarray(f("L0_attn_x_proj_w").T)
    K = xpt.shape[0]
    w["l0_w_xp"] = np.ascontiguousarray(
        xpt.reshape(K // 128, 128, 96).transpose(1, 0, 2).reshape(1, 128, (K // 128) * 96))
    w["l0_w_out"] = colblocks(np.ascontiguousarray(f("L0_attn_out_proj_w").T))
    w["l0_w_dt"] = np.ascontiguousarray(f("L0_attn_dt_proj_w").reshape(16, 128, 64))
    w["l0_dt_b"] = np.ascontiguousarray(f("L0_attn_dt_proj_b").reshape(1, 2048))
    cw = f("L0_attn_conv_w").reshape(2048, 3)
    w["l0_conv_w"] = np.ascontiguousarray(cw.reshape(16, 128, 3).transpose(1, 0, 2).reshape(128, 48))
    w["l0_conv_b"] = pcol(f("L0_attn_conv_b"))
    w["l0_dpar"] = pcol(f("L0_attn_D_param"))
    w["l0_w_r"] = colblocks(np.ascontiguousarray(f("L0_ffn_W_r").T))
    w["l0_w_k"] = colblocks(np.ascontiguousarray(f("L0_ffn_W_k").T))
    w["l0_w_v"] = colblocks(np.ascontiguousarray(f("L0_ffn_W_v").T))
    w["l0_mu_r"] = pcol(f("L0_ffn_mu_r"))
    w["l0_mu_k"] = pcol(f("L0_ffn_mu_k"))
    return w


def prep_l1_weights(ins):
    f = lambda k: np.ascontiguousarray(np.asarray(ins[k], dtype=np.float32))
    w = {}
    w["l1_pr"] = pcol(f("L1_attn_phazor_real"))
    w["l1_pi"] = pcol(f("L1_attn_phazor_imag"))
    w["l1_ar"] = pcol(f("L1_attn_phazor_init_real"))
    w["l1_ai"] = pcol(f("L1_attn_phazor_init_imag"))
    w["l1_lr"] = pcol(f("L1_attn_last_conv_real"))
    w["l1_li"] = pcol(f("L1_attn_last_conv_imag"))
    w["l1_n1w"] = pcol(f("L1_ffn_norm1_w"))
    w["l1_w1"] = colblocks(np.ascontiguousarray(f("L1_ffn_w1").T))
    w["l1_asc"] = f("L1_ffn_act_scale").reshape(1, 1)
    w["l1_abi"] = f("L1_ffn_act_bias").reshape(1, 1)
    w["l1_n2w"] = pcol(f("L1_ffn_norm2_w"))
    w["l1_w2"] = colblocks(np.ascontiguousarray(f("L1_ffn_w2").T))
    return w


def prep_l2a_weights(ins):
    f = lambda k: np.ascontiguousarray(np.asarray(ins[k], dtype=np.float32))
    w = {}
    iwc, ibc = f("L2_attn_in_w_cls"), f("L2_attn_in_b_cls")
    iwl, ibl = f("L2_attn_in_w_loc"), f("L2_attn_in_b_loc")
    E = 1024
    w["l2_wq_c"] = colblocks(np.ascontiguousarray(iwc[0:E].T))
    w["l2_wk_c"] = colblocks(np.ascontiguousarray(iwc[E:2 * E].T))
    w["l2_wv_ct"] = np.ascontiguousarray(iwc[2 * E:3 * E].T.reshape(8, 128, E))
    w["l2_wq_l"] = colblocks(np.ascontiguousarray(iwl[0:E].T))
    w["l2_wk_l"] = colblocks(np.ascontiguousarray(iwl[E:2 * E].T))
    w["l2_wv_lt"] = np.ascontiguousarray(iwl[2 * E:3 * E].T.reshape(8, 128, E))
    w["l2_wo_c"] = colblocks(np.ascontiguousarray(f("L2_attn_out_w_cls").T))
    w["l2_wo_l"] = colblocks(np.ascontiguousarray(f("L2_attn_out_w_loc").T))
    w["l2_bqc"] = pcol(ibc[0:E]); w["l2_bkc"] = pcol(ibc[E:2 * E])
    w["l2_bvc"] = pcol(ibc[2 * E:3 * E]); w["l2_boc"] = pcol(f("L2_attn_out_b_cls"))
    w["l2_bql"] = pcol(ibl[0:E]); w["l2_bkl"] = pcol(ibl[E:2 * E])
    w["l2_bvl"] = pcol(ibl[2 * E:3 * E]); w["l2_bol"] = pcol(f("L2_attn_out_b_loc"))
    return w


def run_l2a(inputs, x, ncores=NCORES, raw=False):
    w = prep_l2a_weights(inputs)
    nc = build_chain([emit_l2a])
    in_maps = []
    for c in range(ncores):
        m = {"x_t": np.ascontiguousarray(x[c].T)}
        m.update(w)
        in_maps.append(m)
    res = run_bass_kernel_spmd(nc, in_maps, core_ids=list(range(ncores)))
    if raw:
        return res
    return (np.stack([np.asarray(res.results[c]["y_t"]).T for c in range(ncores)]),
            np.stack([np.asarray(res.results[c]["xn_t"]) for c in range(ncores)]),
            np.stack([np.asarray(res.results[c]["xn_tm_t"]) for c in range(ncores)]),
            np.stack([np.asarray(res.results[c]["bnsum"]) for c in range(ncores)]))


def prep_l2b_weights(ins):
    f = lambda k: np.ascontiguousarray(np.asarray(ins[k], dtype=np.float32))
    w = {}
    w["l2_wq"] = colblocks(np.ascontiguousarray(f("L2_ffn_w_q").T))
    w["l2_keys"] = np.ascontiguousarray(
        f("L2_ffn_keys_p").transpose(0, 2, 3, 1).reshape(16, 128, 256))
    w["l2_gam"] = pcol(f("L2_ffn_bn_gamma"))
    w["l2_bet"] = pcol(f("L2_ffn_bn_beta"))
    w["l2_down"] = f("L2_ffn_down")
    w["l2_up"] = f("L2_ffn_up")
    return w


def run_l2b(inputs, x3p, xn_t, xn_tm, bnsum, ncores=NCORES, raw=False):
    w = prep_l2b_weights(inputs)
    bn_cat = np.ascontiguousarray(np.concatenate(list(bnsum), axis=1))
    nc = build_chain([emit_l2b])
    in_maps = []
    for c in range(ncores):
        m = {"x_t": np.ascontiguousarray(x3p[c].T), "xn_in": xn_t[c],
             "xn_tm_in": xn_tm[c], "bn_all": bn_cat}
        m.update(w)
        in_maps.append(m)
    res = run_bass_kernel_spmd(nc, in_maps, core_ids=list(range(ncores)))
    if raw:
        return res
    return np.stack([np.asarray(res.results[c]["y_t"]).T for c in range(ncores)])


def prep_l3_weights(ins):
    f = lambda k: np.ascontiguousarray(np.asarray(ins[k], dtype=np.float32))
    w = {}
    w["l3_tdec"] = f("L3_attn_time_decay").reshape(64, 1)
    w["l3_tfir"] = f("L3_attn_time_first").reshape(64, 1)
    kT = np.ascontiguousarray(f("L3_attn_key_w").T)
    w["l3_wk"] = np.ascontiguousarray(
        kT.reshape(8, 128, 64).transpose(1, 0, 2).reshape(1, 128, 8 * 64))
    vT = np.ascontiguousarray(f("L3_attn_value_w").T)
    w["l3_wv"] = np.ascontiguousarray(
        vT.reshape(8, 128, 64).transpose(1, 0, 2).reshape(1, 128, 8 * 64))
    w["l3_wo"] = np.ascontiguousarray(f("L3_attn_output_w").T.reshape(64, 8, 128).transpose(1, 0, 2))
    w["l3_tsg"] = pcol(f("L3_attn_time_shift_gates"))
    w["l3_lnw"] = pcol(f("L3_attn_ln_w"))
    w["l3_lnb"] = pcol(f("L3_attn_ln_b"))
    w["l3_w1"] = colblocks(np.ascontiguousarray(f("L3_ffn_w1_w").T))
    w["l3_w2"] = colblocks(np.ascontiguousarray(f("L3_ffn_w2_w").T))
    w["l3_h1"] = colblocks(np.ascontiguousarray(f("L3_ffn_h1_w").T))
    w["l3_h2"] = colblocks(np.ascontiguousarray(f("L3_ffn_h2_w").T))
    w["l3_w3"] = colblocks(np.ascontiguousarray(f("L3_ffn_w3_w").T))
    w["l3_s1"] = pcol(f("L3_ffn_w1_s"))
    w["l3_s2"] = pcol(f("L3_ffn_w2_s"))
    w["l3_sh1"] = pcol(f("L3_ffn_h1_s"))
    w["l3_sh2"] = pcol(f("L3_ffn_h2_s"))
    w["l3_s3"] = pcol(f("L3_ffn_w3_s"))
    return w


def run_l3(inputs, x, ncores=NCORES, raw=False):
    w = prep_l3_weights(inputs)
    nc = build_chain([emit_l3])
    in_maps = []
    for c in range(ncores):
        m = {"x_t": np.ascontiguousarray(x[c].T)}
        m.update(w)
        in_maps.append(m)
    res = run_bass_kernel_spmd(nc, in_maps, core_ids=list(range(ncores)))
    if raw:
        return res
    return np.stack([np.asarray(res.results[c]["y_t"]).T for c in range(ncores)])


def run_l1(inputs, x, ncores=NCORES, raw=False):
    w = prep_l1_weights(inputs)
    nc = build_chain([emit_l1])
    in_maps = []
    for c in range(ncores):
        m = {"x_t": np.ascontiguousarray(x[c].T)}
        m.update(w)
        in_maps.append(m)
    res = run_bass_kernel_spmd(nc, in_maps, core_ids=list(range(ncores)))
    if raw:
        return res
    return np.stack([np.asarray(res.results[c]["y_t"]).T for c in range(ncores)])


def run_l0(inputs, x, ncores=NCORES, raw=False):
    w = prep_l0_weights(inputs)
    nc = build_chain([emit_l0])
    in_maps = []
    for c in range(ncores):
        m = {"x_t": np.ascontiguousarray(x[c].T)}
        m.update(w)
        in_maps.append(m)
    res = run_bass_kernel_spmd(nc, in_maps, core_ids=list(range(ncores)))
    if raw:
        return res
    return np.stack([np.asarray(res.results[c]["y_t"]).T for c in range(ncores)])


import functools


def kernel(**inputs):
    x = np.asarray(inputs["L0_attn_x"], dtype=np.float32)
    w = {}
    w.update(prep_l0_weights(inputs))
    w.update(prep_l1_weights(inputs))
    w.update(prep_l2a_weights(inputs))
    nc1 = build_chain([emit_l0, functools.partial(emit_l1, reload=True), emit_l2a])
    in_maps = []
    for c in range(NCORES):
        m = {"x_t": np.ascontiguousarray(x[c].T)}
        m.update(w)
        in_maps.append(m)
    res1 = run_bass_kernel_spmd(nc1, in_maps, core_ids=list(range(NCORES)))
    y1 = [np.asarray(res1.results[c]["y_t"]) for c in range(NCORES)]
    xn_t = [np.asarray(res1.results[c]["xn_t"]) for c in range(NCORES)]
    xn_tm = [np.asarray(res1.results[c]["xn_tm_t"]) for c in range(NCORES)]
    bn_cat = np.ascontiguousarray(np.concatenate(
        [np.asarray(res1.results[c]["bnsum"]) for c in range(NCORES)], axis=1))
    w2 = {}
    w2.update(prep_l2b_weights(inputs))
    w2.update(prep_l3_weights(inputs))
    nc2 = build_chain([emit_l2b, emit_l3])
    in_maps2 = []
    for c in range(NCORES):
        m = {"x_t": y1[c], "xn_in": xn_t[c], "xn_tm_in": xn_tm[c], "bn_all": bn_cat}
        m.update(w2)
        in_maps2.append(m)
    res2 = run_bass_kernel_spmd(nc2, in_maps2, core_ids=list(range(NCORES)))
    x4 = np.stack([np.asarray(res2.results[c]["y_t"]).T for c in range(NCORES)])
    return np.ascontiguousarray(x4.astype(np.float32))
